# version 01
# baseline (speedup 1.0000x reference)
import numpy as np
import ml_dtypes
from contextlib import ExitStack

import concourse.bass as bass
import concourse.mybir as mybir
from concourse.bass_utils import run_bass_kernel_spmd

F32 = mybir.dt.float32
BF16 = mybir.dt.bfloat16
AF = mybir.ActivationFunctionType
ALU = mybir.AluOpType
AX = mybir.AxisListType

NT = 2176
NOWN = 2048
HALO = 128
D = 1024
DC = 8
EPS = 1e-5
NCORES = 8
NTB = 544
HID = 4096
HC = 32


def tch(n, w=512):
    out = []
    o = 0
    while o < n:
        c = min(w, n - o)
        out.append((o, c))
        o += c
    return out


class Bld:

    def __init__(self, nc, ctx):
        self.nc = nc
        self.ctx = ctx
        self.sem = {}
        self.cnt = {}
        for name in ("sq", "gq", "mm", "vv", "ss", "gc"):
            self.sem[name] = ctx.enter_context(nc.semaphore(name))
            self.cnt[name] = 0
        self._P0 = ctx.enter_context(nc.psum_tensor("P0", [128, 512], F32))
        self._P1 = ctx.enter_context(nc.psum_tensor("P1", [128, 512], F32))
        self.PW0 = ctx.enter_context(nc.psum_tensor("PW0", [128, 1024], F32))
        self.PW1 = ctx.enter_context(nc.psum_tensor("PW1", [128, 1024], F32))
        self.P = [self._P0[:, :], self._P1[:, :], self.PW0[:, 0:512], self.PW1[:, 0:512],
                  self.PW0[:, 512:1024], self.PW1[:, 512:1024]]
        self.PT = ctx.enter_context(nc.psum_tensor("PT", [128, 128], BF16))
        self.PTF = self.PW0[:, 0:128]
        self.gq_marks = {}

    def inc(self, name, inst, k=1):
        inst.then_inc(self.sem[name], k)
        self.cnt[name] += k
        return self.cnt[name]

    def V(self, v, f):
        self.w(v, "vv")
        return self.inc("vv", f())

    def S(self, s, f):
        self.w(s, "ss")
        return self.inc("ss", f())

    def w(self, eng, name, val=None):
        v = self.cnt[name] if val is None else val
        if v > 0:
            eng.wait_ge(self.sem[name], v)

    def phase(self, **closures):
        order = closures.pop("order", ("sync", "gpsimd", "tensor", "scalar", "vector"))
        with self.nc.Block() as blk:
            for name in order:
                fn = closures.get(name)
                if fn is not None:
                    getattr(blk, name)(fn)


def alloc_consts(bld, E):
    nc = bld.nc
    bld.identf = E(nc.sbuf_tensor("c_identf", [128, 128], F32))
    bld.identb = E(nc.sbuf_tensor("c_identb", [128, 128], BF16))
    bld.onescol = E(nc.sbuf_tensor("c_onescol", [128, 1], F32))
    bld.onescol_b = E(nc.sbuf_tensor("c_onescol_b", [128, 1], BF16))
    bld.onesrow = E(nc.sbuf_tensor("c_onesrow", [1, 128], F32))
    bld.onesrow_b = E(nc.sbuf_tensor("c_onesrow_b", [1, 128], BF16))


def emit_load_consts(bld, dram):
    def g(g):
        for dst, src in [(bld.identf, dram["identf"]), (bld.identb, dram["identf"]),
                         (bld.onescol, dram["ones_col"]), (bld.onescol_b, dram["ones_col"]),
                         (bld.onesrow, dram["ones_row"]), (bld.onesrow_b, dram["ones_row"])]:
            bld.inc("gq", g.dma_start(out=dst[:, :], in_=src[:, :]), 16)
    bld.phase(gpsimd=g)


def emit_ln_fm(bld, scr, src, src_stride, t0, n, dst, dst_stride, dt0):
    pst1, pst2, pmb, prb = bld.P[4], bld.P[5], bld.P[2], bld.P[3]
    for (o, w) in tch(n):
        def vv(v, o=o, w=w):
            ss_base = bld.cnt["ss"]
            mm_base = bld.cnt["mm"]
            for c in range(DC):
                if c == 0:
                    bld.w(v, "ss")
                    bld.w(v, "mm")
                else:
                    v.wait_ge(bld.sem["ss"], ss_base + c)
                    v.wait_ge(bld.sem["mm"], mm_base + 2 * c)
                bld.w(v, "vv")
                bld.inc("vv", v.tensor_copy(
                    scr["xb"][:, 0:w], src[:, c * src_stride + t0 + o: c * src_stride + t0 + o + w]))

        def ss(s, o=o, w=w):
            vv_base = bld.cnt["vv"] - DC
            mm_base = bld.cnt["mm"]
            for c in range(DC):
                s.wait_ge(bld.sem["vv"], vv_base + c + 1)
                if c > 0:
                    s.wait_ge(bld.sem["mm"], mm_base + 2 * c)
                bld.w(s, "ss")
                bld.inc("ss", s.activation(out=scr["sq"][:, 0:w], in_=scr["xb"][:, 0:w], func=AF.Square))

        def tt(t, o=o, w=w):
            vv_base = bld.cnt["vv"] - DC
            ss_base = bld.cnt["ss"] - DC
            bld.w(t, "mm")
            for c in range(DC):
                t.wait_ge(bld.sem["vv"], vv_base + c + 1)
                t.wait_ge(bld.sem["ss"], ss_base + c + 1)
                bld.inc("mm", t.matmul(pst1[0:1, 0:w], bld.onescol_b[:, :], scr["xb"][:, 0:w],
                                       start=(c == 0), stop=(c == DC - 1)))
                bld.inc("mm", t.matmul(pst2[0:1, 0:w], bld.onescol_b[:, :], scr["sq"][:, 0:w],
                                       start=(c == 0), stop=(c == DC - 1)))
        bld.phase(vector=vv, scalar=ss, tensor=tt, order=("vector", "scalar", "tensor"))

        def vv2(v, o=o, w=w):
            bld.w(v, "mm")
            bld.w(v, "vv")
            bld.inc("vv", v.tensor_scalar(scr["m_r"][:, 0:w], pst1[0:1, 0:w], 1.0 / D, None, ALU.mult))
            bld.w(v, "vv")
            bld.inc("vv", v.tensor_scalar(scr["a_r"][:, 0:w], pst2[0:1, 0:w], 1.0 / D, None, ALU.mult))
            bld.w(v, "vv")
            bld.inc("vv", v.tensor_tensor(scr["b_r"][:, 0:w], scr["m_r"][:, 0:w], scr["m_r"][:, 0:w], ALU.mult))
            bld.w(v, "vv")
            bld.inc("vv", v.tensor_tensor(scr["a_r"][:, 0:w], scr["a_r"][:, 0:w], scr["b_r"][:, 0:w], ALU.subtract))
            bld.w(v, "vv")
            bld.inc("vv", v.tensor_scalar(scr["a_r"][:, 0:w], scr["a_r"][:, 0:w], EPS, None, ALU.add))
        bld.phase(vector=vv2, order=("vector",))

        def ss2(s, o=o, w=w):
            bld.w(s, "vv")
            bld.w(s, "ss")
            bld.inc("ss", s.activation(out=scr["b_r"][:, 0:w], in_=scr["a_r"][:, 0:w], func=AF.Sqrt))
        bld.phase(scalar=ss2, order=("scalar",))

        def vv3(v, o=o, w=w):
            bld.w(v, "ss")
            bld.w(v, "vv")
            bld.inc("vv", v.reciprocal(scr["a_r"][:, 0:w], scr["b_r"][:, 0:w]))
        bld.phase(vector=vv3, order=("vector",))

        def tt4(t, o=o, w=w):
            bld.w(t, "vv")
            bld.w(t, "mm")
            bld.inc("mm", t.matmul(pmb[:, 0:w], bld.onesrow[:, :], scr["m_r"][:, 0:w], start=True, stop=True))
            bld.inc("mm", t.matmul(prb[:, 0:w], bld.onesrow[:, :], scr["a_r"][:, 0:w], start=True, stop=True))

        def vv4(v, o=o, w=w):
            bld.w(v, "mm")
            for c in range(DC):
                bld.w(v, "vv")
                bld.inc("vv", v.tensor_tensor(scr["tmp"][:, 0:w],
                                              src[:, c * src_stride + t0 + o: c * src_stride + t0 + o + w],
                                              pmb[:, 0:w], ALU.subtract))
                bld.w(v, "vv")
                bld.inc("vv", v.tensor_tensor(dst[:, c * dst_stride + dt0 + o: c * dst_stride + dt0 + o + w],
                                              scr["tmp"][:, 0:w], prb[:, 0:w], ALU.mult))
        bld.phase(tensor=tt4, vector=vv4, order=("tensor", "vector"))


def alloc_ln_scr(bld, E, tag="", tmp=None):
    nc = bld.nc
    return {
        "xb": E(nc.sbuf_tensor(f"lnxb{tag}", [128, 512], BF16)),
        "sq": E(nc.sbuf_tensor(f"lnsq{tag}", [128, 512], BF16)),
        "m_r": E(nc.sbuf_tensor(f"lnm{tag}", [1, 512], F32)),
        "a_r": E(nc.sbuf_tensor(f"lna{tag}", [1, 512], F32)),
        "b_r": E(nc.sbuf_tensor(f"lnb{tag}", [1, 512], F32)),
        "tmp": tmp if tmp is not None else E(nc.sbuf_tensor(f"lntmp{tag}", [128, 512], F32)),
    }


def emit_small_transpose(bld, src_ap, rows, cols, evac_cb):
    def tt(t):
        bld.w(t, "gq")
        bld.w(t, "sq")
        bld.w(t, "vv")
        bld.w(t, "mm")
        bld.inc("mm", t.matmul(bld.PT[0:cols, 0:rows], src_ap, bld.identb[0:rows, 0:rows],
                               start=True, stop=True, is_transpose=True))

    def vv(v):
        bld.w(v, "mm")
        bld.w(v, "sq")
        evac_cb(v)
    bld.phase(tensor=tt, vector=vv, order=("tensor", "vector"))


DBG = False


def emit_dbg(bld, dram, key, src_ap):
    if not DBG or key not in dram:
        return
    def sy(sy):
        bld.w(sy, "vv"); bld.w(sy, "mm"); bld.w(sy, "ss")
        bld.inc("sq", sy.dma_start(out=dram[key][:, :], in_=src_ap), 16)
        bld.w(sy, "sq")
    bld.phase(sync=sy)


def emit_l0ffn(bld, resid, dram):
    nc = bld.nc
    with ExitStack() as ctx:
        E = ctx.enter_context
        ln2 = E(nc.sbuf_tensor("ln2", [128, DC * NTB], BF16))
        r2T = E(nc.sbuf_tensor("r2T", [128, HC * NTB], BF16))
        vT = E(nc.sbuf_tensor("vT", [128, DC * 16], BF16))
        fbT = E(nc.sbuf_tensor("fbT", [128, 16], BF16))
        uTg = E(nc.sbuf_tensor("uTg", [128, HID], BF16))
        s1p = E(nc.sbuf_tensor("s1p", [128, NT], BF16))
        s2p = E(nc.sbuf_tensor("s2p", [128, NT], BF16))
        v16b = E(nc.sbuf_tensor("v16b", [16, D], BF16))
        fbb = E(nc.sbuf_tensor("fbb", [16, 16], BF16))
        ub = E(nc.sbuf_tensor("ub", [128, 32 * 16], BF16))
        gcol = E(nc.sbuf_tensor("gcol", [16, 1], F32))
        gsb = E(nc.sbuf_tensor("gsb", [1, 1], F32))
        wt = E(nc.sbuf_tensor("wt", [128, 1024], F32))
        wpt = E(nc.sbuf_tensor("wpt", [128, HID], F32))
        red = E(nc.sbuf_tensor("red", [128, 1], F32))
        hs = E(nc.sbuf_tensor("hs", [128, 1], F32))
        scl = E(nc.sbuf_tensor("scl", [128, 1], F32))
        sgn = E(nc.sbuf_tensor("sgn", [128, HID], BF16))
        absq = E(nc.sbuf_tensor("absq", [128, HID], F32))
        hs2 = E(nc.sbuf_tensor("hs2", [128, 1], F32))
        weff = E(nc.sbuf_tensor("weff", [128, HID], BF16))
        strip = E(nc.sbuf_tensor("strip", [128, DC * 128], BF16))
        wpstrip = E(nc.sbuf_tensor("wpstrip", [128, HC * 128], BF16))
        rb = E(nc.sbuf_tensor("rbuf", [128, 512], BF16))
        lns = alloc_ln_scr(bld, E, "f", tmp=absq)
        ph, ph2, po0, po1 = bld.P[0], bld.P[1], bld.P[2], bld.P[3]
        ps16 = bld.P[4]

        def g0(g):
            bld.inc("gq", g.dma_start(out=v16b[:, :], in_=dram["L0_ffn_v"][:, :]), 16)
            bld.inc("gq", g.dma_start(out=fbb[:, :], in_=dram["L0_ffn_fast_b"][:, :]), 16)
            usrc = dram["L0_ffn_u"].rearrange("(a p) j -> p a j", p=128)
            bld.inc("gq", g.dma_start(out=ub[:, :].rearrange("p (a j) -> p a j", j=16), in_=usrc), 16)
            bld.inc("gq", g.dma_start(out=gsb[:, :], in_=dram["L0_ffn_gate"][None, :]), 16)

        def v0(v):
            bld.w(v, "vv")
            bld.inc("vv", v.memset(uTg[:, :], 0.0))
            bld.w(v, "vv")
            bld.inc("vv", v.memset(fbT[:, :], 0.0))
            bld.w(v, "vv")
            bld.inc("vv", v.memset(s1p[:, :], 0.0))
            bld.w(v, "vv")
            bld.inc("vv", v.memset(s2p[:, :], 0.0))
        bld.phase(gpsimd=g0, vector=v0)

        for c in range(DC):
            def mk(c):
                return lambda v: bld.inc("vv", v.tensor_copy(vT[:, c * 16:(c + 1) * 16], bld.PT[0:128, 0:16]))
            emit_small_transpose(bld, v16b[:, c * 128:(c + 1) * 128], 16, 128, mk(c))
        emit_small_transpose(bld, fbb[:, :], 16, 16,
                             lambda v: bld.inc("vv", v.tensor_copy(fbT[0:16, 0:16], bld.PT[0:16, 0:16])))
        for a in range(32):
            def mk2(a):
                return lambda v: bld.inc("vv", v.tensor_copy(uTg[0:16, a * 128:(a + 1) * 128], bld.PT[0:16, 0:128]))
            emit_small_transpose(bld, ub[:, a * 16:(a + 1) * 16], 128, 16, mk2(a))

        def tg(t):
            bld.w(t, "gq")
            bld.w(t, "vv")
            bld.w(t, "mm")
            bld.inc("mm", t.matmul(ps16[0:16, 0:1], bld.onesrow[:, 0:16], gsb[:, :], start=True, stop=True))

        def vg(v):
            bld.w(v, "mm")
            bld.w(v, "vv")
            bld.inc("vv", v.tensor_copy(gcol[:, :], ps16[0:16, 0:1]))
            bld.w(v, "vv")
            bld.inc("vv", v.tensor_scalar(uTg[0:16, :], uTg[0:16, :], gcol[0:16, :], None, ALU.mult))
        bld.phase(tensor=tg, vector=vg, order=("tensor", "vector"))

        for blk in range(NT // NTB):
            t0 = blk * NTB
            emit_ln_fm(bld, lns, resid, NT, t0, NTB, ln2, NTB, 0)
            if blk == 0:
                emit_dbg(bld, dram, "dbg_ln2", ln2[:, :])

            for (o, w) in tch(NTB):
                def ts1(t, o=o, w=w):
                    bld.w(t, "vv")
                    bld.w(t, "mm")
                    for c in range(DC):
                        bld.inc("mm", t.matmul(ps16[0:16, 0:w], vT[:, c * 16:(c + 1) * 16],
                                               ln2[:, c * NTB + o: c * NTB + o + w],
                                               start=(c == 0), stop=(c == DC - 1)))

                def vs1(v, o=o, w=w):
                    bld.w(v, "mm")
                    bld.w(v, "vv")
                    bld.inc("vv", v.tensor_copy(s1p[0:16, t0 + o: t0 + o + w], ps16[0:16, 0:w]))
                bld.phase(tensor=ts1, vector=vs1, order=("tensor", "vector"))

                def ts2(t, o=o, w=w):
                    bld.w(t, "vv")
                    bld.inc("mm", t.matmul(ps16[0:16, 0:w], fbT[:, 0:16], s1p[:, t0 + o: t0 + o + w],
                                           start=True, stop=True))

                def vs2(v, o=o, w=w):
                    bld.w(v, "mm")
                    bld.w(v, "vv")
                    bld.inc("vv", v.tensor_copy(s2p[0:16, t0 + o: t0 + o + w], ps16[0:16, 0:w]))
                bld.phase(tensor=ts2, vector=vs2, order=("tensor", "vector"))
            if blk == 0:
                emit_dbg(bld, dram, "dbg_s1", s1p[:, :])
                emit_dbg(bld, dram, "dbg_s2", s2p[:, :])

            for hc in range(HC):
                if blk == 0:
                    def sy(sy, hc=hc):
                        bld.w(sy, "mm")
                        bld.w(sy, "ss")
                        bld.w(sy, "vv")
                        bld.inc("sq", sy.dma_start(out=wt[:, :],
                                                   in_=dram["L0_ffn_w_fc"][hc * 128:(hc + 1) * 128, :]), 16)

                    def vp(v, hc=hc):
                        bld.w(v, "sq")
                        bld.w(v, "vv")
                        bld.inc("vv", v.tensor_reduce(red[:, :], wt[:, :], AX.X, ALU.add,
                                                      apply_absolute_value=True))
                        bld.w(v, "vv")
                        bld.inc("vv", v.tensor_scalar(hs[:, :], red[:, :], 0.5 / D, 0.5e-5, ALU.mult, ALU.max))
                        bld.w(v, "vv")
                        bld.inc("vv", v.tensor_scalar(scl[:, :], red[:, :], 1.0 / D, 1e-5, ALU.mult, ALU.max))
                        bld.w(v, "vv")
                        bld.inc("vv", v.tensor_tensor(hs2[:, :], hs[:, :], hs[:, :], ALU.mult))

                    def sp(s, hc=hc):
                        bld.w(s, "sq")
                        bld.w(s, "vv")
                        bld.w(s, "ss")
                        bld.inc("ss", s.activation(out=sgn[:, 0:1024], in_=wt[:, :], func=AF.Sign))
                        bld.w(s, "ss")
                        bld.inc("ss", s.activation(out=absq[:, 0:1024], in_=wt[:, :], func=AF.Square))
                    bld.phase(sync=sy, vector=vp, scalar=sp, order=("sync", "vector", "scalar"))

                    def vp2(v, hc=hc):
                        bld.w(v, "ss")
                        bld.w(v, "vv")
                        bld.inc("vv", v.tensor_scalar(absq[:, 0:1024], absq[:, 0:1024], hs2[:, :], None, ALU.subtract))
                        bld.w(v, "vv")
                        bld.inc("vv", v.tensor_scalar(weff[:, 0:1024], absq[:, 0:1024], 0.0, None, ALU.is_ge))
                        bld.w(v, "vv")
                        bld.inc("vv", v.tensor_tensor(weff[:, 0:1024], weff[:, 0:1024], sgn[:, 0:1024], ALU.mult))
                        bld.w(v, "vv")
                        bld.inc("vv", v.tensor_scalar(weff[:, 0:1024], weff[:, 0:1024], scl[:, :], None, ALU.mult))
                    bld.phase(vector=vp2, order=("vector",))

                    for dc in range(DC):
                        def mk3(dc):
                            return lambda v: bld.inc("vv", v.tensor_copy(strip[:, dc * 128:(dc + 1) * 128],
                                                                         bld.PT[:, :]))
                        emit_small_transpose(bld, weff[:, dc * 128:(dc + 1) * 128], 128, 128, mk3(dc))

                    def sw(sy, hc=hc):
                        bld.w(sy, "vv")
                        bld.inc("sq", sy.dma_start(
                            out=dram["wfcT"].rearrange("(h x) c -> h x c", x=128)[hc], in_=strip[:, :]), 16)
                    bld.phase(sync=sw)
                    if hc == 0:
                        emit_dbg(bld, dram, "dbg_strip", strip[:, :])
                else:
                    def sr(sy, hc=hc):
                        bld.w(sy, "mm")
                        bld.inc("sq", sy.dma_start(
                            out=strip[:, :],
                            in_=dram["wfcT"].rearrange("(h x) c -> h x c", x=128)[hc]), 16)
                    bld.phase(sync=sr)

                for ti, (o, w) in enumerate(tch(NTB)):
                    pcur = ph if ti % 2 == 0 else ph2
                    def tf(t, hc=hc, o=o, w=w, pcur=pcur):
                        bld.w(t, "sq")
                        bld.w(t, "vv")
                        bld.w(t, "ss")
                        bld.w(t, "mm")
                        for c in range(DC):
                            bld.inc("mm", t.matmul(pcur[:, 0:w], strip[:, c * 128:(c + 1) * 128],
                                                   ln2[:, c * NTB + o: c * NTB + o + w],
                                                   start=(c == 0), stop=False))
                        bld.inc("mm", t.matmul(pcur[:, 0:w], uTg[:, hc * 128:(hc + 1) * 128],
                                               s2p[:, t0 + o: t0 + o + w], start=False, stop=True))

                    def sf(s, hc=hc, o=o, w=w, pcur=pcur):
                        bld.w(s, "mm")
                        bld.w(s, "vv")
                        bld.w(s, "ss")
                        bld.inc("ss", s.activation(out=rb[:, 0:w], in_=pcur[:, 0:w], func=AF.Relu))

                    def vf(v, hc=hc, o=o, w=w):
                        bld.w(v, "ss")
                        bld.w(v, "vv")
                        bld.inc("vv", v.tensor_tensor(r2T[:, hc * NTB + o: hc * NTB + o + w],
                                                      rb[:, 0:w], rb[:, 0:w], ALU.mult))
                    bld.phase(tensor=tf, scalar=sf, vector=vf, order=("tensor", "scalar", "vector"))
                if blk == 0 and hc == 0:
                    emit_dbg(bld, dram, "dbg_r2", r2T[:, 0:NTB])

            for dc in range(DC):
                if blk == 0:
                    def syp(sy, dc=dc):
                        bld.w(sy, "mm")
                        bld.w(sy, "ss")
                        bld.w(sy, "vv")
                        bld.inc("sq", sy.dma_start(out=wpt[:, :],
                                                   in_=dram["L0_ffn_w_proj"][dc * 128:(dc + 1) * 128, :]), 16)

                    def vpp(v, dc=dc):
                        bld.w(v, "sq")
                        bld.w(v, "vv")
                        bld.inc("vv", v.tensor_reduce(red[:, :], wpt[:, :], AX.X, ALU.add,
                                                      apply_absolute_value=True))
                        bld.w(v, "vv")
                        bld.inc("vv", v.tensor_scalar(hs[:, :], red[:, :], 0.5 / HID, 0.5e-5, ALU.mult, ALU.max))
                        bld.w(v, "vv")
                        bld.inc("vv", v.tensor_scalar(scl[:, :], red[:, :], 1.0 / HID, 1e-5, ALU.mult, ALU.max))
                        bld.w(v, "vv")
                        bld.inc("vv", v.tensor_tensor(hs2[:, :], hs[:, :], hs[:, :], ALU.mult))

                    def spp(s, dc=dc):
                        bld.w(s, "sq")
                        bld.w(s, "vv")
                        bld.w(s, "ss")
                        bld.inc("ss", s.activation(out=sgn[:, :], in_=wpt[:, :], func=AF.Sign))
                        bld.w(s, "ss")
                        bld.inc("ss", s.activation(out=absq[:, :], in_=wpt[:, :], func=AF.Square))
                    bld.phase(sync=syp, vector=vpp, scalar=spp, order=("sync", "vector", "scalar"))

                    def vpp2(v, dc=dc):
                        bld.w(v, "ss")
                        bld.w(v, "vv")
                        bld.inc("vv", v.tensor_scalar(absq[:, :], absq[:, :], hs2[:, :], None, ALU.subtract))
                        bld.w(v, "vv")
                        bld.inc("vv", v.tensor_scalar(weff[:, :], absq[:, :], 0.0, None, ALU.is_ge))
                        bld.w(v, "vv")
                        bld.inc("vv", v.tensor_tensor(weff[:, :], weff[:, :], sgn[:, :], ALU.mult))
                        bld.w(v, "vv")
                        bld.inc("vv", v.tensor_scalar(weff[:, :], weff[:, :], scl[:, :], None, ALU.mult))
                    bld.phase(vector=vpp2, order=("vector",))

                    for a in range(HC):
                        def mk4(a):
                            return lambda v: bld.inc("vv", v.tensor_copy(wpstrip[:, a * 128:(a + 1) * 128],
                                                                         bld.PT[:, :]))
                        emit_small_transpose(bld, weff[:, a * 128:(a + 1) * 128], 128, 128, mk4(a))

                    def swp(sy, dc=dc):
                        bld.w(sy, "vv")
                        bld.inc("sq", sy.dma_start(
                            out=dram["wprojT"].rearrange("(d x) c -> d x c", x=128)[dc], in_=wpstrip[:, :]), 16)
                    bld.phase(sync=swp)
                    if dc == 0:
                        emit_dbg(bld, dram, "dbg_wps", wpstrip[:, :])
                else:
                    def srp(sy, dc=dc):
                        bld.w(sy, "mm")
                        bld.inc("sq", sy.dma_start(
                            out=wpstrip[:, :],
                            in_=dram["wprojT"].rearrange("(d x) c -> d x c", x=128)[dc]), 16)
                    bld.phase(sync=srp)

                for ti, (o, w) in enumerate(tch(NTB)):
                    pcur = po0 if ti % 2 == 0 else po1
                    def tp(t, dc=dc, o=o, w=w, pcur=pcur):
                        bld.w(t, "sq")
                        bld.w(t, "vv")
                        bld.w(t, "mm")
                        for a in range(HC):
                            bld.inc("mm", t.matmul(pcur[:, 0:w], wpstrip[:, a * 128:(a + 1) * 128],
                                                   r2T[:, a * NTB + o: a * NTB + o + w],
                                                   start=(a == 0), stop=(a == HC - 1)))

                    def vpj(v, dc=dc, o=o, w=w, pcur=pcur):
                        bld.w(v, "mm")
                        sl = resid[:, dc * NT + t0 + o: dc * NT + t0 + o + w]
                        bld.w(v, "vv")
                        bld.inc("vv", v.tensor_tensor(sl, sl, pcur[:, 0:w], ALU.add))
                    bld.phase(tensor=tp, vector=vpj, order=("tensor", "vector"))


def ffn_dram(nc, dram):
    dram["L0_ffn_w_fc"] = nc.dram_tensor("L0_ffn_w_fc", [HID, D], F32, kind="ExternalInput").ap()
    dram["L0_ffn_w_proj"] = nc.dram_tensor("L0_ffn_w_proj", [D, HID], F32, kind="ExternalInput").ap()
    dram["L0_ffn_u"] = nc.dram_tensor("L0_ffn_u", [HID, 16], F32, kind="ExternalInput").ap()
    dram["L0_ffn_v"] = nc.dram_tensor("L0_ffn_v", [16, D], F32, kind="ExternalInput").ap()
    dram["L0_ffn_fast_b"] = nc.dram_tensor("L0_ffn_fast_b", [16, 16], F32, kind="ExternalInput").ap()
    dram["L0_ffn_gate"] = nc.dram_tensor("L0_ffn_gate", [1], F32, kind="ExternalInput").ap()
    dram["wfcT"] = nc.dram_tensor("wfcT", [HC * 128, D], BF16, kind="Internal").ap()
    if DBG:
        dram["dbg_ln2"] = nc.dram_tensor("dbg_ln2", [128, DC * NTB], BF16, kind="ExternalOutput").ap()
        dram["dbg_s1"] = nc.dram_tensor("dbg_s1", [128, NT], BF16, kind="ExternalOutput").ap()
        dram["dbg_s2"] = nc.dram_tensor("dbg_s2", [128, NT], BF16, kind="ExternalOutput").ap()
        dram["dbg_strip"] = nc.dram_tensor("dbg_strip", [128, DC * 128], BF16, kind="ExternalOutput").ap()
        dram["dbg_r2"] = nc.dram_tensor("dbg_r2", [128, NTB], BF16, kind="ExternalOutput").ap()
        dram["dbg_wps"] = nc.dram_tensor("dbg_wps", [128, HC * 128], BF16, kind="ExternalOutput").ap()
    dram["wprojT"] = nc.dram_tensor("wprojT", [DC * 128, HC * 128], BF16, kind="Internal").ap()


def common_dram(nc, dram):
    dram["identf"] = nc.dram_tensor("identf", [128, 128], F32, kind="ExternalInput").ap()
    dram["ones_col"] = nc.dram_tensor("ones_col", [128, 1], F32, kind="ExternalInput").ap()
    dram["ones_row"] = nc.dram_tensor("ones_row", [1, 128], F32, kind="ExternalInput").ap()


def build_l0ffn_launch():
    nc = bass.Bass("TRN2", target_bir_lowering=False, debug=False)
    dram = {}
    dram["resid_in"] = nc.dram_tensor("resid_in", [DC, 128, NT], F32, kind="ExternalInput").ap()
    dram["resid_out"] = nc.dram_tensor("resid_out", [DC, 128, NT], F32, kind="ExternalOutput").ap()
    common_dram(nc, dram)
    ffn_dram(nc, dram)

    with ExitStack() as ctx:
        E = ctx.enter_context
        bld = Bld(nc, ctx)
        alloc_consts(bld, E)
        resid = E(nc.sbuf_tensor("resid", [128, DC * NT], F32))
        emit_load_consts(bld, dram)

        def ld(sy):
            for c in range(DC):
                bld.inc("sq", sy.dma_start(out=resid[:, c * NT:(c + 1) * NT], in_=dram["resid_in"][c]), 16)
        bld.phase(sync=ld)

        def bar(t):
            bld.w(t, "sq")
            bld.w(t, "gq")
        bld.phase(tensor=bar, order=("tensor",))

        emit_l0ffn(bld, resid, dram)

        def st(sy):
            bld.w(sy, "vv")
            bld.w(sy, "mm")
            bld.w(sy, "ss")
            for c in range(DC):
                bld.inc("sq", sy.dma_start(out=dram["resid_out"][c], in_=resid[:, c * NT:(c + 1) * NT]), 16)
            bld.w(sy, "sq")
        bld.phase(sync=st)
    return nc


def shard_inputs(inputs):
    x = np.ascontiguousarray(np.asarray(inputs["L0_attn_x"], np.float32))
    maps = []
    ident = np.eye(128, dtype=np.float32)
    ones_col = np.ones((128, 1), np.float32)
    ones_row = np.ones((1, 128), np.float32)
    for core in range(NCORES):
        b, half = core // 2, core % 2
        start = half * NOWN
        xr = np.zeros((NT, D), np.float32)
        lo = start - HALO
        if lo < 0:
            xr[HALO:, :] = x[b, start:start + NOWN, :]
        else:
            xr[:, :] = x[b, lo:start + NOWN, :]
        m = {
            "xrows": xr,
            "xfull": np.ascontiguousarray(x[b]),
            "identf": ident,
            "ones_col": ones_col,
            "ones_row": ones_row,
            "convmask": np.full((64, 1), 0.0 if half == 0 else 1.0, np.float32),
        }
        for k, v in inputs.items():
            if k == "L0_attn_x":
                continue
            a = np.asarray(v, np.float32)
            if a.ndim == 0:
                a = a.reshape(1)
            m[k] = np.ascontiguousarray(a)
        maps.append(m)
    return maps


_NC_CACHE = {}


def _get_nc(name, builder):
    if name not in _NC_CACHE:
        _NC_CACHE[name] = builder()
    return _NC_CACHE[name]


_ATTN_KEYS = ["xrows", "xfull", "L0_attn_Wq", "L0_attn_Wk", "L0_attn_Wv", "L0_attn_Wo",
              "L0_attn_quadrant_weights", "identf", "ones_col", "ones_row"]
_FFN_KEYS = ["L0_ffn_w_fc", "L0_ffn_w_proj", "L0_ffn_u", "L0_ffn_v", "L0_ffn_fast_b",
             "L0_ffn_gate", "identf", "ones_col", "ones_row"]
_SSM_KEYS = ["L1_attn_A", "L1_attn_Bm", "L1_attn_Cm", "L1_attn_Dm", "L1_attn_ln_w",
             "L1_attn_ln_b", "L1_attn_W_out", "L1_attn_b_out", "convmask", "blockmask8",
             "identf", "ones_col", "ones_row"]
_MOE_KEYS = ["L1_ffn_Wg", "L1_ffn_bg", "L1_ffn_bias", "L1_ffn_W1", "L1_ffn_b1",
             "L1_ffn_W2", "L1_ffn_b2", "identf", "ones_col", "ones_row"]


def kernel(**inputs):
    maps = shard_inputs(inputs)
    mask8 = np.zeros((128, 8), np.float32)
    for p in range(128):
        mask8[p, p // 16] = 1.0
    for m in maps:
        m["blockmask8"] = mask8

    cores = list(range(NCORES))

    nc_a = _get_nc("attn", build_l0attn_launch)
    res = run_bass_kernel_spmd(nc_a, [{k: m[k] for k in _ATTN_KEYS} for m in maps], core_ids=cores)
    resid = [r["resid_out"] for r in res.results]

    nc_b = _get_nc("ffn", build_l0ffn_launch)
    res = run_bass_kernel_spmd(
        nc_b, [dict({k: m[k] for k in _FFN_KEYS}, resid_in=resid[i]) for i, m in enumerate(maps)],
        core_ids=cores)
    resid = [r["resid_out"] for r in res.results]

    nc_c = _get_nc("ssm", build_l1ssm_launch)
    res = run_bass_kernel_spmd(
        nc_c, [dict({k: m[k] for k in _SSM_KEYS}, resid_in=resid[i]) for i, m in enumerate(maps)],
        core_ids=cores)
    resid = [r["resid_out"] for r in res.results]

    nc_d = _get_nc("moe", build_l1moe_launch)
    res = run_bass_kernel_spmd(
        nc_d, [dict({k: m[k] for k in _MOE_KEYS}, resid_in=resid[i]) for i, m in enumerate(maps)],
        core_ids=cores)
    resid = [r["resid_out"] for r in res.results]

    out = np.empty((4, 4096, D), np.float32)
    for core in range(NCORES):
        b, half = core // 2, core % 2
        start = half * NOWN
        r = np.asarray(resid[core], np.float32)
        blockrows = r.transpose(2, 0, 1).reshape(NT, D)
        out[b, start:start + NOWN, :] = blockrows[HALO:, :]
    return out


MOE_BLOCKS = [(0, 1152), (1152, 1024)]
NTBM = 1152
NE = 16


def emit_l1moe(bld, resid, dram):
    nc = bld.nc
    with ExitStack() as ctx:
        E = ctx.enter_context
        ln4 = E(nc.sbuf_tensor("ln4", [128, DC * NTBM], BF16))
        hT = E(nc.sbuf_tensor("hT", [128, NE * NTBM], BF16))
        W1b = E(nc.sbuf_tensor("W1b", [128, DC * 2048], BF16))
        W2d = [E(nc.sbuf_tensor("W2d0", [128, NE * 128], BF16)),
               E(nc.sbuf_tensor("W2d1", [128, NE * 128], BF16))]
        b1c = E(nc.sbuf_tensor("b1c", [128, NE * 16], F32))
        b2c = E(nc.sbuf_tensor("b2c", [128, NE * DC], F32))
        b1n = E(nc.sbuf_tensor("b1n", [16, 2048], F32))
        b2n = E(nc.sbuf_tensor("b2n", [16, 1024], F32))
        Wgb = E(nc.sbuf_tensor("Wgb", [128, DC * 16], BF16))
        bgr = E(nc.sbuf_tensor("bgr", [1, 16], F32))
        bir = E(nc.sbuf_tensor("bir", [1, 16], F32))
        onesw = E(nc.sbuf_tensor("onesw", [1, 512], F32))
        cw = E(nc.sbuf_tensor("cw", [128, 17 * 16], F32))
        cwT = E(nc.sbuf_tensor("cwT", [16, NT], BF16))
        cwrow = E(nc.sbuf_tensor("cwrow", [1, NT], BF16))
        cwbs = E(nc.sbuf_tensor("cwbs", [128, 3 * 512], BF16))
        t1b = E(nc.sbuf_tensor("t1b", [128, 512], F32))
        lgt = t1b
        tA = E(nc.sbuf_tensor("tA", [128, 16], F32))
        tB = E(nc.sbuf_tensor("tB", [128, 16], F32))
        mk1 = E(nc.sbuf_tensor("mk1", [128, 16], F32))
        mk2 = E(nc.sbuf_tensor("mk2", [128, 16], F32))
        m1 = E(nc.sbuf_tensor("m1c", [128, 1], F32))
        m2 = E(nc.sbuf_tensor("m2c", [128, 1], F32))
        e2 = E(nc.sbuf_tensor("e2c", [128, 1], F32))
        den = E(nc.sbuf_tensor("denc", [128, 1], F32))
        rec = E(nc.sbuf_tensor("recc", [128, 1], F32))
        w2c = E(nc.sbuf_tensor("w2cc", [128, 1], F32))
        lns4 = alloc_ln_scr(bld, E, "m", tmp=t1b)
        ph, ph2, po0, po1, pg = bld.P[0], bld.P[1], bld.P[2], bld.P[3], bld.P[4]

        def g0(g):
            wg = dram["L1_ffn_Wg"].rearrange("(c p) e -> p c e", p=128)
            bld.inc("gq", g.dma_start(out=Wgb[:, :].rearrange("p (c e) -> p c e", e=16), in_=wg), 16)
            bld.inc("gq", g.dma_start(out=bgr[:, :], in_=dram["L1_ffn_bg"][None, :]), 16)
            bld.inc("gq", g.dma_start(out=bir[:, :], in_=dram["L1_ffn_bias"][None, :]), 16)
            bld.inc("gq", g.dma_start(out=b1n[:, :], in_=dram["L1_ffn_b1"][:, :]), 16)
            bld.inc("gq", g.dma_start(out=b2n[:, :], in_=dram["L1_ffn_b2"][:, :]), 16)

        def v0(v):
            bld.V(v, lambda: v.memset(onesw[:, :], 1.0))
        bld.phase(gpsimd=g0, vector=v0)

        def v1(v):
            bld.w(v, "gq")
            bld.V(v, lambda: v.tensor_tensor(bgr[:, :], bgr[:, :], bir[:, :], ALU.add))
        bld.phase(vector=v1)

        for hc in range(NE):
            def tb1(t, hc=hc):
                bld.w(t, "gq")
                bld.w(t, "vv")
                bld.w(t, "mm")
                bld.inc("mm", t.matmul(bld.PTF[0:128, 0:16], b1n[:, hc * 128:(hc + 1) * 128],
                                       bld.identf[0:16, 0:16], start=True, stop=True, is_transpose=True))

            def vb1(v, hc=hc):
                bld.w(v, "mm")
                bld.V(v, lambda: v.tensor_copy(b1c[:, hc * 16:(hc + 1) * 16], bld.PTF[0:128, 0:16]))
            bld.phase(tensor=tb1, vector=vb1, order=("tensor", "vector"))
        for dc in range(DC):
            def tb2(t, dc=dc):
                bld.w(t, "gq")
                bld.w(t, "vv")
                bld.w(t, "mm")
                bld.inc("mm", t.matmul(bld.PTF[0:128, 0:16], b2n[:, dc * 128:(dc + 1) * 128],
                                       bld.identf[0:16, 0:16], start=True, stop=True, is_transpose=True))

            def vb2(v, dc=dc):
                bld.w(v, "mm")
                bld.V(v, lambda: v.tensor_copy(b2c[:, dc * 16:(dc + 1) * 16], bld.PTF[0:128, 0:16]))
            bld.phase(tensor=tb2, vector=vb2, order=("tensor", "vector"))

        for (t0, ntb) in MOE_BLOCKS:
            emit_ln_fm(bld, lns4, resid, NT, t0, ntb, ln4, NTBM, 0)
            if t0 == 0:
                emit_dbg(bld, dram, "dbg_ln4", ln4[:, :])

            for (o, w) in tch(ntb):
                def tl(t, o=o, w=w):
                    bld.w(t, "vv")
                    bld.w(t, "mm")
                    for c in range(DC):
                        bld.inc("mm", t.matmul(pg[0:16, 0:w], Wgb[:, c * 16:(c + 1) * 16],
                                               ln4[:, c * NTBM + o: c * NTBM + o + w],
                                               start=(c == 0), stop=False))
                    bld.w(t, "mm")
                    bld.inc("mm", t.matmul(pg[0:16, 0:w], bgr[:, :], onesw[:, 0:w],
                                           start=False, stop=True))

                def vl(v, o=o, w=w):
                    bld.w(v, "mm")
                    bld.V(v, lambda: v.tensor_copy(lgt[0:16, 0:w], pg[0:16, 0:w]))
                bld.phase(tensor=tl, vector=vl, order=("tensor", "vector"))
                if t0 == 0 and o == 0:
                    emit_dbg(bld, dram, "dbg_lgt", lgt[0:16, 0:512])

                for k in range(w // 128):
                    tile = (t0 + o) // 128 + k
                    def tt(t, k=k, tile=tile, w=w):
                        bld.w(t, "vv")
                        bld.w(t, "mm")
                        bld.inc("mm", t.matmul(bld.PTF[0:128, 0:16], lgt[0:16, k * 128:(k + 1) * 128],
                                               bld.identf[0:16, 0:16], start=True, stop=True,
                                               is_transpose=True))

                    def vt(v, k=k, tile=tile):
                        bld.w(v, "mm")
                        bld.V(v, lambda: v.tensor_copy(cw[:, tile * 16:(tile + 1) * 16], bld.PTF[0:128, 0:16]))
                    bld.phase(tensor=tt, vector=vt, order=("tensor", "vector"))

            nt_tiles = ntb // 128
            for k in range(nt_tiles):
                tile = t0 // 128 + k
                cwt = cw[:, tile * 16:(tile + 1) * 16]
                def vtop(v, cwt=cwt):
                    bld.V(v, lambda: v.tensor_reduce(m1[:, :], cwt, AX.X, ALU.max))
                    bld.V(v, lambda: v.tensor_scalar(tA[:, :], cwt, m1[:, :], None, ALU.subtract))
                    bld.V(v, lambda: v.tensor_scalar(mk1[:, :], tA[:, :], 0.0, None, ALU.is_ge))
                    bld.V(v, lambda: v.tensor_scalar(tB[:, :], mk1[:, :], -1e30, None, ALU.mult))
                    bld.V(v, lambda: v.tensor_tensor(tB[:, :], tB[:, :], cwt, ALU.add))
                    bld.V(v, lambda: v.tensor_reduce(m2[:, :], tB[:, :], AX.X, ALU.max))
                    bld.V(v, lambda: v.tensor_scalar(tB[:, :], tB[:, :], m2[:, :], None, ALU.subtract))
                    bld.V(v, lambda: v.tensor_scalar(mk2[:, :], tB[:, :], 0.0, None, ALU.is_ge))
                    bld.V(v, lambda: v.tensor_tensor(e2[:, :], m2[:, :], m1[:, :], ALU.subtract))

                def stop_(s):
                    bld.w(s, "vv")
                    bld.S(s, lambda: s.activation(out=e2[:, :], in_=e2[:, :], func=AF.Exp))

                def vtop2(v, cwt=cwt):
                    bld.w(v, "ss")
                    bld.V(v, lambda: v.tensor_scalar(den[:, :], e2[:, :], 1.0, None, ALU.add))
                    bld.V(v, lambda: v.reciprocal(rec[:, :], den[:, :]))
                    bld.V(v, lambda: v.tensor_tensor(w2c[:, :], e2[:, :], rec[:, :], ALU.mult))
                    bld.V(v, lambda: v.tensor_scalar(mk1[:, :], mk1[:, :], rec[:, :], None, ALU.mult))
                    bld.V(v, lambda: v.scalar_tensor_tensor(cwt, mk2[:, :], w2c[:, :], mk1[:, :],
                                                            ALU.mult, ALU.add))
                bld.phase(vector=vtop, order=("vector",))
                bld.phase(scalar=stop_, order=("scalar",))
                bld.phase(vector=vtop2, order=("vector",))

                def tct(t, tile=tile):
                    bld.w(t, "vv")
                    bld.w(t, "mm")
                    bld.inc("mm", t.matmul(bld.PTF[0:16, 0:128], cw[:, tile * 16:(tile + 1) * 16],
                                           bld.identf[:, :], start=True, stop=True, is_transpose=True))

                def vct(v, tile=tile):
                    bld.w(v, "mm")
                    bld.V(v, lambda: v.tensor_copy(cwT[0:16, tile * 128:(tile + 1) * 128], bld.PTF[0:16, 0:128]))
                bld.phase(tensor=tct, vector=vct, order=("tensor", "vector"))

            def emit_dma_w1(e):
                def g(g, e=e):
                    bld.w(g, "mm")
                    bld.w(g, "vv")
                    for dcn in range(DC):
                        bld.inc("gq", g.dma_start(out=W1b[:, dcn * 2048:(dcn + 1) * 2048],
                                                  in_=dram["L1_ffn_W1"][e, dcn * 128:(dcn + 1) * 128, :]), 16)
                bld.phase(gpsimd=g)
                bld.gq_marks[("w1", e)] = bld.cnt["gq"]

            def emit_dma_w2(e, dc):
                buf = W2d[dc % 2]
                def g(g, e=e, dc=dc, buf=buf):
                    bld.w(g, "mm")
                    src = dram["L1_ffn_W2"][e].rearrange("(hcn p) c -> p hcn c", p=128)[:, :, dc * 128:(dc + 1) * 128]
                    dst = buf[:, :].rearrange("p (hcn c) -> p hcn c", c=128)
                    bld.inc("gq", g.dma_start(out=dst[:, 0:8, :], in_=src[:, 0:8, :]), 16)
                    bld.inc("gq", g.dma_start(out=dst[:, 8:16, :], in_=src[:, 8:16, :]), 16)
                bld.phase(gpsimd=g)
                bld.gq_marks[("w2", e, dc)] = bld.cnt["gq"]

            if t0 == 0:
                emit_dbg(bld, dram, "dbg_cw", cw[:, :])
                emit_dbg(bld, dram, "dbg_cwT", cwT[:, :])
            emit_dma_w1(0)
            for e in range(NE):
                for hc in range(NE):
                    for ti, (o, w) in enumerate(tch(ntb)):
                        pcur = ph if ti % 2 == 0 else ph2
                        def tf(t, e=e, hc=hc, o=o, w=w, pcur=pcur):
                            t.wait_ge(bld.sem["gq"], bld.gq_marks[("w1", e)])
                            bld.w(t, "vv")
                            bld.w(t, "ss")
                            bld.w(t, "mm")
                            for c in range(DC):
                                bld.inc("mm", t.matmul(pcur[:, 0:w],
                                                       W1b[:, c * 2048 + hc * 128: c * 2048 + (hc + 1) * 128],
                                                       ln4[:, c * NTBM + o: c * NTBM + o + w],
                                                       start=(c == 0), stop=(c == DC - 1)))

                        def sf(s, e=e, hc=hc, o=o, w=w, pcur=pcur):
                            bld.w(s, "mm")
                            bld.w(s, "vv")
                            bld.S(s, lambda: s.activation(out=hT[:, hc * NTBM + o: hc * NTBM + o + w],
                                                          in_=pcur[:, 0:w], func=AF.Relu,
                                                          bias=b1c[:, hc * 16 + e: hc * 16 + e + 1]))
                        bld.phase(tensor=tf, scalar=sf, order=("tensor", "scalar"))
                if t0 == 0 and e == 0:
                    emit_dbg(bld, dram, "dbg_hT", hT[:, 0:NTBM])
                if e + 1 < NE:
                    emit_dma_w1(e + 1)

                def gcw(g, e=e):
                    bld.w(g, "mm")
                    bld.inc("gq", g.dma_start(out=cwrow[:, :], in_=cwT[e:e + 1, :]), 16)
                bld.phase(gpsimd=gcw)
                bld.gq_marks[("cwr", e, t0)] = bld.cnt["gq"]
                for ti, (o, w) in enumerate(tch(ntb)):
                    def tb_(t, e=e, ti=ti, o=o, w=w):
                        t.wait_ge(bld.sem["gq"], bld.gq_marks[("cwr", e, t0)])
                        bld.w(t, "vv")
                        bld.w(t, "mm")
                        bld.inc("mm", t.matmul(pg[:, 0:w], bld.onesrow_b[:, :],
                                               cwrow[:, t0 + o: t0 + o + w], start=True, stop=True))

                    def vb(v, ti=ti, o=o, w=w):
                        bld.w(v, "mm")
                        bld.V(v, lambda: v.tensor_copy(cwbs[:, ti * 512: ti * 512 + w], pg[:, 0:w]))
                    bld.phase(tensor=tb_, vector=vb, order=("tensor", "vector"))

                emit_dma_w2(e, 0)
                emit_dma_w2(e, 1)
                for dc in range(DC):
                    buf = W2d[dc % 2]
                    for ti, (o, w) in enumerate(tch(ntb)):
                        pcur = po0 if ti % 2 == 0 else po1
                        def tw2(t, e=e, dc=dc, ti=ti, o=o, w=w, pcur=pcur, buf=buf):
                            t.wait_ge(bld.sem["gq"], bld.gq_marks[("w2", e, dc)])
                            bld.w(t, "vv")
                            bld.w(t, "ss")
                            bld.w(t, "mm")
                            for hcn in range(NE):
                                bld.inc("mm", t.matmul(pcur[:, 0:w],
                                                       buf[:, hcn * 128:(hcn + 1) * 128],
                                                       hT[:, hcn * NTBM + o: hcn * NTBM + o + w],
                                                       start=(hcn == 0), stop=(hcn == NE - 1)))

                        def vw2(v, e=e, dc=dc, ti=ti, o=o, w=w, pcur=pcur):
                            bld.w(v, "mm")
                            bld.V(v, lambda: v.scalar_tensor_tensor(
                                t1b[:, 0:w], pcur[:, 0:w], b2c[:, dc * 16 + e: dc * 16 + e + 1],
                                cwbs[:, ti * 512: ti * 512 + w], ALU.add, ALU.mult))
                            sl = resid[:, dc * NT + t0 + o: dc * NT + t0 + o + w]
                            bld.V(v, lambda: v.tensor_tensor(sl, sl, t1b[:, 0:w], ALU.add))
                        bld.phase(tensor=tw2, vector=vw2, order=("tensor", "vector"))
                    if dc + 2 < DC:
                        emit_dma_w2(e, dc + 2)
                    elif e + 1 < NE:
                        emit_dma_w2(e + 1, dc - 6)
                if t0 == 0 and e == 0:
                    emit_dbg(bld, dram, "dbg_res0", resid[:, 0:512])
                    emit_dbg(bld, dram, "dbg_cwbs", cwbs[:, 0:512])


def moe_dram(nc, dram):
    dram["L1_ffn_Wg"] = nc.dram_tensor("L1_ffn_Wg", [D, 16], F32, kind="ExternalInput").ap()
    dram["L1_ffn_bg"] = nc.dram_tensor("L1_ffn_bg", [16], F32, kind="ExternalInput").ap()
    dram["L1_ffn_bias"] = nc.dram_tensor("L1_ffn_bias", [16], F32, kind="ExternalInput").ap()
    dram["L1_ffn_W1"] = nc.dram_tensor("L1_ffn_W1", [16, D, 2048], F32, kind="ExternalInput").ap()
    dram["L1_ffn_b1"] = nc.dram_tensor("L1_ffn_b1", [16, 2048], F32, kind="ExternalInput").ap()
    dram["L1_ffn_W2"] = nc.dram_tensor("L1_ffn_W2", [16, 2048, D], F32, kind="ExternalInput").ap()
    dram["L1_ffn_b2"] = nc.dram_tensor("L1_ffn_b2", [16, D], F32, kind="ExternalInput").ap()


def build_l1moe_launch():
    nc = bass.Bass("TRN2", target_bir_lowering=False, debug=False)
    dram = {}
    dram["resid_in"] = nc.dram_tensor("resid_in", [DC, 128, NT], F32, kind="ExternalInput").ap()
    dram["resid_out"] = nc.dram_tensor("resid_out", [DC, 128, NT], F32, kind="ExternalOutput").ap()
    common_dram(nc, dram)
    moe_dram(nc, dram)
    if DBG:
        dram["dbg_ln4"] = nc.dram_tensor("dbg_ln4", [128, DC * NTBM], BF16, kind="ExternalOutput").ap()
        dram["dbg_cw"] = nc.dram_tensor("dbg_cw", [128, 17 * 16], F32, kind="ExternalOutput").ap()
        dram["dbg_cwT"] = nc.dram_tensor("dbg_cwT", [16, NT], BF16, kind="ExternalOutput").ap()
        dram["dbg_hT"] = nc.dram_tensor("dbg_hT", [128, NTBM], BF16, kind="ExternalOutput").ap()
        dram["dbg_lgt"] = nc.dram_tensor("dbg_lgt", [16, 512], F32, kind="ExternalOutput").ap()
        dram["dbg_res0"] = nc.dram_tensor("dbg_res0", [128, 512], F32, kind="ExternalOutput").ap()
        dram["dbg_cwbs"] = nc.dram_tensor("dbg_cwbs", [128, 512], BF16, kind="ExternalOutput").ap()
    with ExitStack() as ctx:
        E = ctx.enter_context
        bld = Bld(nc, ctx)
        alloc_consts(bld, E)
        resid = E(nc.sbuf_tensor("resid", [128, DC * NT], F32))
        emit_load_consts(bld, dram)

        def ld(sy):
            for c in range(DC):
                bld.inc("sq", sy.dma_start(out=resid[:, c * NT:(c + 1) * NT], in_=dram["resid_in"][c]), 16)
        bld.phase(sync=ld)

        def bar(t):
            bld.w(t, "sq")
            bld.w(t, "gq")
        bld.phase(tensor=bar, order=("tensor",))

        emit_l1moe(bld, resid, dram)

        def st(sy):
            bld.w(sy, "vv")
            bld.w(sy, "mm")
            bld.w(sy, "ss")
            for c in range(DC):
                bld.inc("sq", sy.dma_start(out=dram["resid_out"][c], in_=resid[:, c * NT:(c + 1) * NT]), 16)
            bld.w(sy, "sq")
        bld.phase(sync=st)
    return nc


NKT = 4096


ATTN_PAIRS = 8
ATTN_QS = None


def emit_l0attn(bld, resid, QTall, dram):
    nc = bld.nc
    with ExitStack() as ctx:
        E = ctx.enter_context
        wbc = E(nc.sbuf_tensor("wbc", [128, 4], F32))
        q4 = E(nc.sbuf_tensor("q4", [1, 4], F32))
        qm = E(nc.sbuf_tensor("qm", [1, 1], F32))
        Wqs = E(nc.sbuf_tensor("Wqs", [128, DC * 128], BF16))
        Wks = E(nc.sbuf_tensor("Wks", [128, DC * 128], BF16))
        Wke = E(nc.sbuf_tensor("Wke", [128, DC * 128], BF16))
        Wvs = E(nc.sbuf_tensor("Wvs", [128, DC * 128], BF16))
        Wos = E(nc.sbuf_tensor("Wos", [128, DC * 128], BF16))
        tm32 = E(nc.sbuf_tensor("tm32", [128, 32], F32))
        lnsl = [E(nc.sbuf_tensor("lnsl0", [128, DC * 512], BF16)),
                E(nc.sbuf_tensor("lnsl1", [128, DC * 512], BF16))]
        KT = E(nc.sbuf_tensor("KT", [128, NKT], BF16))
        Vp = E(nc.sbuf_tensor("Vp", [128, 32 * 130], BF16))
        PA = E(nc.sbuf_tensor("PA", [128, 32 * 128], BF16))
        PB = E(nc.sbuf_tensor("PB", [128, 32 * 128], BF16))
        OT = E(nc.sbuf_tensor("OT", [128, NT], BF16))
        recA = E(nc.sbuf_tensor("recA", [128, 1], F32))
        recB = E(nc.sbuf_tensor("recB", [128, 1], F32))
        psSA = bld.PW0
        psSB = bld.PW1
        pOA, pOB, pKV = bld.P[0], bld.P[1], bld.P[2]
        pKV2 = bld.P[3]

        def g0(g):
            bld.inc("gq", g.dma_start(out=q4[:, :], in_=dram["L0_attn_quadrant_weights"][None, :]), 16)

        def v0(v):
            bld.w(v, "gq")
            bld.V(v, lambda: v.tensor_reduce(qm[:, :], q4[:, :], AX.X, ALU.max))
            bld.V(v, lambda: v.tensor_scalar(q4[:, :], q4[:, :], qm[:, :], None, ALU.subtract))
        bld.phase(gpsimd=g0, vector=v0)

        def s0(s):
            bld.w(s, "vv")
            bld.S(s, lambda: s.activation(out=q4[:, :], in_=q4[:, :], func=AF.Exp))
        bld.phase(scalar=s0)

        def v1(v):
            bld.w(v, "ss")
            bld.V(v, lambda: v.tensor_reduce(qm[:, :], q4[:, :], AX.X, ALU.add))
            bld.V(v, lambda: v.reciprocal(qm[:, :], qm[:, :]))
            bld.V(v, lambda: v.tensor_scalar(q4[:, :], q4[:, :], qm[:, :], None, ALU.mult))
            bld.V(v, lambda: v.tensor_scalar(q4[:, :], q4[:, :], float(1.0 / np.sqrt(32.0)), None, ALU.mult))
        bld.phase(vector=v1)

        def t1(t):
            bld.w(t, "vv")
            bld.w(t, "mm")
            bld.inc("mm", t.matmul(bld.PTF[:, 0:4], bld.onesrow[:, :], q4[:, :], start=True, stop=True))

        def v2(v):
            bld.w(v, "mm")
            bld.V(v, lambda: v.tensor_copy(wbc[:, :], bld.PTF[:, 0:4]))
        bld.phase(tensor=t1, vector=v2, order=("tensor", "vector"))

        for p in range(ATTN_PAIRS):
            pc = slice(p * 128, (p + 1) * 128)

            def gw(g, p=p):
                bld.w(g, "mm")
                bld.w(g, "vv")
                for name, dst in [("L0_attn_Wq", Wqs), ("L0_attn_Wk", Wks), ("L0_attn_Wv", Wvs)]:
                    for c in range(DC):
                        bld.inc("gq", g.dma_start(out=dst[:, c * 128:(c + 1) * 128],
                                                  in_=dram[name][c * 128:(c + 1) * 128, p * 128:(p + 1) * 128]), 16)
                bld.inc("gq", g.dma_start(out=Wos[:, :], in_=dram["L0_attn_Wo"][p * 128:(p + 1) * 128, :]), 16)
            bld.phase(gpsimd=gw)

            def vmix(v, p=p):
                bld.w(v, "gq")
                for c in range(DC):
                    for h2 in range(2):
                        base = c * 128 + h2 * 64
                        k1 = Wks[:, base:base + 32]
                        k2 = Wks[:, base + 32:base + 64]
                        bld.V(v, lambda k2=k2: v.tensor_scalar(tm32[:, :], k2, wbc[:, 1:2], None, ALU.mult))
                        bld.V(v, lambda k1=k1, base=base: v.scalar_tensor_tensor(
                            Wke[:, base:base + 32], k1, wbc[:, 0:1], tm32[:, :], ALU.mult, ALU.add))
                        bld.V(v, lambda k2=k2: v.tensor_scalar(tm32[:, :], k2, wbc[:, 3:4], None, ALU.mult))
                        bld.V(v, lambda k1=k1, base=base: v.scalar_tensor_tensor(
                            Wke[:, base + 32:base + 64], k1, wbc[:, 2:3], tm32[:, :], ALU.mult, ALU.add))
            bld.phase(vector=vmix)

            for tc in range(8):
                buf = lnsl[tc % 2]
                def gl(g, tc=tc, buf=buf):
                    bld.w(g, "mm")
                    for c in range(DC):
                        bld.inc("gq", g.dma_start(
                            out=buf[:, c * 512:(c + 1) * 512],
                            in_=dram["ln1s"].rearrange("(c x) t -> c x t", x=128)[c, :, tc * 512:(tc + 1) * 512]), 16)
                bld.phase(gpsimd=gl)
                gqm = bld.cnt["gq"]

                vvb = bld.cnt["vv"]
                def tkv(t, p=p, tc=tc, buf=buf, gqm=gqm, vvb=vvb):
                    t.wait_ge(bld.sem["gq"], gqm)
                    bld.w(t, "vv")
                    bld.w(t, "mm")
                    for c in range(DC):
                        bld.inc("mm", t.matmul(pKV[:, 0:512], Wke[:, c * 128:(c + 1) * 128],
                                               buf[:, c * 512:(c + 1) * 512],
                                               start=(c == 0), stop=(c == DC - 1)))
                    for tt_ in range(4):
                        if tt_ > 0:
                            t.wait_ge(bld.sem["vv"], vvb + 1 + 2 * tt_)
                        for c in range(DC):
                            bld.inc("mm", t.matmul(pKV2[:, 0:128],
                                                   buf[:, c * 512 + tt_ * 128: c * 512 + tt_ * 128 + 128],
                                                   Wvs[:, c * 128:(c + 1) * 128],
                                                   start=(c == 0), stop=(c == DC - 1)))

                def vkv(v, p=p, tc=tc):
                    mmb = bld.cnt["mm"] - 40
                    v.wait_ge(bld.sem["mm"], mmb + 8)
                    bld.inc("vv", v.tensor_copy(KT[:, tc * 512:(tc + 1) * 512], pKV[:, 0:512]))
                    for tt_ in range(4):
                        v.wait_ge(bld.sem["mm"], mmb + 8 + (tt_ + 1) * 8)
                        tile = tc * 4 + tt_
                        bld.w(v, "vv")
                        bld.inc("vv", v.tensor_copy(Vp[:, tile * 130: tile * 130 + 64], pKV2[0:128, 0:64]))
                        bld.w(v, "vv")
                        bld.inc("vv", v.tensor_copy(Vp[:, tile * 130 + 65: tile * 130 + 129], pKV2[0:128, 64:128]))
                bld.phase(tensor=tkv, vector=vkv, order=("tensor", "vector"))

            def vones(v, p=p):
                bld.w(v, "mm")
                bld.V(v, lambda: v.memset(Vp[:, :].rearrange("x (a b) -> x a b", b=65)[:, :, 64], 1.0))
            if p == 0:
                bld.phase(vector=vones)

            for qs in range(NT // 128 if ATTN_QS is None else ATTN_QS):
                for g4 in range(4):
                    def tS(t, p=p, qs=qs, g4=g4):
                        if g4 == 0:
                            bld.w(t, "mm")
                        bld.w(t, "ss")
                        for kk in range(8):
                            kc = g4 * 8 + kk
                            bld.inc("mm", t.matmul(psSA[:, kk * 128:(kk + 1) * 128],
                                                   KT[0:64, kc * 128:(kc + 1) * 128],
                                                   QTall[0:64, p * NT + qs * 128: p * NT + (qs + 1) * 128],
                                                   start=True, stop=True, tile_position=(0, 0)))
                            bld.inc("mm", t.matmul(psSB[:, kk * 128:(kk + 1) * 128],
                                                   KT[64:128, kc * 128:(kc + 1) * 128],
                                                   QTall[64:128, p * NT + qs * 128: p * NT + (qs + 1) * 128],
                                                   start=True, stop=True, tile_position=(64, 0)))

                    def sS(s, p=p, qs=qs, g4=g4):
                        bld.w(s, "mm")
                        bld.S(s, lambda: s.activation(out=PA[:, g4 * 1024:(g4 + 1) * 1024],
                                                      in_=psSA[:, :], func=AF.Exp))
                        bld.S(s, lambda: s.activation(out=PB[:, g4 * 1024:(g4 + 1) * 1024],
                                                      in_=psSB[:, :], func=AF.Exp))
                    bld.phase(tensor=tS, scalar=sS, order=("tensor", "scalar"))

                def tAV(t, p=p, qs=qs):
                    bld.w(t, "mm")
                    bld.w(t, "ss")
                    bld.w(t, "vv")
                    for kc in range(32):
                        bld.inc("mm", t.matmul(pOA[:, 0:65], PA[:, kc * 128:(kc + 1) * 128],
                                               Vp[:, kc * 130: kc * 130 + 65],
                                               start=(kc == 0), stop=(kc == 31)))
                        bld.inc("mm", t.matmul(pOB[:, 0:65], PB[:, kc * 128:(kc + 1) * 128],
                                               Vp[:, kc * 130 + 65: kc * 130 + 130],
                                               start=(kc == 0), stop=(kc == 31)))

                def vAV(v, p=p, qs=qs):
                    bld.w(v, "mm")
                    bld.V(v, lambda: v.reciprocal(recA[:, :], pOA[:, 64:65]))
                    bld.V(v, lambda: v.reciprocal(recB[:, :], pOB[:, 64:65]))
                    bld.V(v, lambda: v.tensor_scalar(pOA[:, 0:64], pOA[:, 0:64], recA[:, :], None, ALU.mult))
                    bld.V(v, lambda: v.tensor_scalar(pOB[:, 0:64], pOB[:, 0:64], recB[:, :], None, ALU.mult))
                bld.phase(tensor=tAV, vector=vAV, order=("tensor", "vector"))

                def vO(v, p=p, qs=qs):
                    bld.w(v, "mm")
                    bld.V(v, lambda: v.tensor_copy(lnsl[0][:, 0:64], pOA[:, 0:64]))
                    bld.V(v, lambda: v.tensor_copy(lnsl[0][:, 64:128], pOB[:, 0:64]))
                bld.phase(vector=vO)

                for h2 in range(2):
                    def tT(t, h2=h2, qs=qs):
                        bld.w(t, "vv")
                        bld.w(t, "mm")
                        bld.inc("mm", t.matmul(bld.PT[0:64, 0:128], lnsl[0][:, h2 * 64:(h2 + 1) * 64],
                                               bld.identb[:, :], start=True, stop=True, is_transpose=True))

                    def vT(v, h2=h2, qs=qs):
                        bld.w(v, "mm")
                        bld.V(v, lambda: v.tensor_copy(OT[h2 * 64:(h2 + 1) * 64, qs * 128:(qs + 1) * 128],
                                                       bld.PT[0:64, 0:128]))
                    bld.phase(tensor=tT, vector=vT, order=("tensor", "vector"))

            for dc in range(DC):
                for ti, (o, w) in enumerate(tch(NT)):
                    pcur = bld.P[4] if ti % 2 == 0 else bld.P[5]
                    def tP(t, p=p, dc=dc, o=o, w=w, pcur=pcur):
                        bld.w(t, "vv")
                        bld.w(t, "mm")
                        bld.inc("mm", t.matmul(pcur[:, 0:w], Wos[:, dc * 128:(dc + 1) * 128],
                                               OT[:, o:o + w], start=True, stop=True))

                    def vP(v, p=p, dc=dc, o=o, w=w, pcur=pcur):
                        bld.w(v, "mm")
                        sl = resid[:, dc * NT + o: dc * NT + o + w]
                        bld.V(v, lambda: v.tensor_tensor(sl, sl, pcur[:, 0:w], ALU.add))
                    bld.phase(tensor=tP, vector=vP, order=("tensor", "vector"))


def emit_p0(bld, resid, QTall, dram):
    nc = bld.nc
    with ExitStack() as ctx:
        E = ctx.enter_context
        xt = [E(nc.sbuf_tensor("xt0", [128, 1024], F32)), E(nc.sbuf_tensor("xt1", [128, 1024], F32))]
        xc = E(nc.sbuf_tensor("xc", [128, 1024], F32))
        lnb = E(nc.sbuf_tensor("lnb", [128, 1024], BF16))
        lnf = E(nc.sbuf_tensor("lnf", [128, DC * 128], BF16))
        mcol = E(nc.sbuf_tensor("mcol", [128, 1], F32))
        scol2 = E(nc.sbuf_tensor("scol2", [128, 1], F32))
        sqd = E(nc.sbuf_tensor("sqd", [128, 1024], BF16))


        def ln_tile(src_ap, ti, into_resid, into_qt, into_scratch):
            buf = xt[ti % 2]
            def gld(g):
                bld.w(g, "mm")
                bld.w(g, "vv")
                bld.w(g, "ss")
                bld.inc("gq", g.dma_start(out=buf[:, :], in_=src_ap), 16)
            bld.phase(gpsimd=gld)
            gqm = bld.cnt["gq"]

            def vln(v):
                v.wait_ge(bld.sem["gq"], gqm)
                bld.V(v, lambda: v.tensor_reduce(mcol[:, :], buf[:, :], AX.X, ALU.add))
                bld.V(v, lambda: v.tensor_scalar(mcol[:, :], mcol[:, :], 1.0 / D, None, ALU.mult))
                bld.V(v, lambda: v.tensor_scalar(xc[:, :], buf[:, :], mcol[:, :], None, ALU.subtract))

            def sln(s):
                bld.w(s, "vv")
                bld.S(s, lambda: s.activation(out=sqd[:, :], in_=xc[:, :], func=AF.Square,
                                              accum_out=scol2[:, :]))
            bld.phase(vector=vln, scalar=sln, order=("vector", "scalar"))

            def vln2(v):
                bld.w(v, "ss")
                bld.V(v, lambda: v.tensor_scalar(scol2[:, :], scol2[:, :], 1.0 / D, EPS, ALU.mult, ALU.add))

            def sln2(s):
                bld.w(s, "vv")
                bld.S(s, lambda: s.activation(out=scol2[:, :], in_=scol2[:, :], func=AF.Sqrt))
            bld.phase(vector=vln2, order=("vector",))
            bld.phase(scalar=sln2, order=("scalar",))

            def vln3(v):
                bld.w(v, "ss")
                bld.V(v, lambda: v.reciprocal(scol2[:, :], scol2[:, :]))
                bld.V(v, lambda: v.tensor_scalar(lnb[:, :], xc[:, :], scol2[:, :], None, ALU.mult))
            bld.phase(vector=vln3, order=("vector",))

            for c in range(DC):
                def tt(t, c=c):
                    bld.w(t, "vv")
                    bld.w(t, "mm")
                    bld.inc("mm", t.matmul(bld.PT[:, :], lnb[:, c * 128:(c + 1) * 128],
                                           bld.identb[:, :], start=True, stop=True, is_transpose=True))

                def vv(v, c=c):
                    bld.w(v, "mm")
                    bld.V(v, lambda: v.tensor_copy(lnf[:, c * 128:(c + 1) * 128], bld.PT[:, :]))
                bld.phase(tensor=tt, vector=vv, order=("tensor", "vector"))

            if into_resid is not None:
                for c in range(DC):
                    def tt2(t, c=c):
                        bld.w(t, "vv")
                        bld.w(t, "mm")
                        bld.inc("mm", t.matmul(bld.PTF[:, :], buf[:, c * 128:(c + 1) * 128],
                                               bld.identf[:, :], start=True, stop=True, is_transpose=True))

                    def vv2(v, c=c, into_resid=into_resid):
                        bld.w(v, "mm")
                        bld.V(v, lambda: v.tensor_copy(into_resid[c], bld.PTF[:, :]))
                    bld.phase(tensor=tt2, vector=vv2, order=("tensor", "vector"))

            if into_scratch is not None:
                def sst(sy):
                    bld.w(sy, "vv")
                    for c in range(DC):
                        bld.inc("sq", sy.dma_start(out=into_scratch[c], in_=lnf[:, c * 128:(c + 1) * 128]), 16)
                bld.phase(sync=sst)


        QWALL = E(nc.sbuf_tensor("QWALL", [128, 64 * 128], BF16))

        def gq0(g):
            for c in range(DC):
                bld.inc("gq", g.dma_start(out=QWALL[:, c * 1024:(c + 1) * 1024],
                                          in_=dram["L0_attn_Wq"][c * 128:(c + 1) * 128, :]), 16)
        bld.phase(gpsimd=gq0)

        for ti in range(NT // 128):
            ln_tile(dram["xrows"][ti * 128:(ti + 1) * 128, :], ti,
                    [resid[:, c * NT + ti * 128: c * NT + (ti + 1) * 128] for c in range(DC)],
                    False, None)
            for p in range(8):
                def tq(t, p=p, ti=ti):
                    bld.w(t, "vv")
                    bld.w(t, "mm")
                    bld.w(t, "gq")
                    for c in range(DC):
                        bld.inc("mm", t.matmul(bld.P[0][:, 0:128],
                                               QWALL[:, c * 1024 + p * 128: c * 1024 + (p + 1) * 128],
                                               lnf[:, c * 128:(c + 1) * 128],
                                               start=(c == 0), stop=(c == DC - 1)))

                def vq(v, p=p, ti=ti):
                    bld.w(v, "mm")
                    bld.V(v, lambda: v.tensor_copy(QTall[:, p * NT + ti * 128:(p * NT + (ti + 1) * 128)],
                                                   bld.P[0][:, 0:128]))
                bld.phase(tensor=tq, vector=vq, order=("tensor", "vector"))

        for ti in range(NKT // 128):
            dst = dram["ln1s"].rearrange("(c x) t -> c x t", x=128)[:, :, ti * 128:(ti + 1) * 128]
            ln_tile(dram["xfull"][ti * 128:(ti + 1) * 128, :], ti, None, False, dst)


def attn_dram(nc, dram):
    dram["L0_attn_Wq"] = nc.dram_tensor("L0_attn_Wq", [D, D], F32, kind="ExternalInput").ap()
    dram["L0_attn_Wk"] = nc.dram_tensor("L0_attn_Wk", [D, D], F32, kind="ExternalInput").ap()
    dram["L0_attn_Wv"] = nc.dram_tensor("L0_attn_Wv", [D, D], F32, kind="ExternalInput").ap()
    dram["L0_attn_Wo"] = nc.dram_tensor("L0_attn_Wo", [D, D], F32, kind="ExternalInput").ap()
    dram["L0_attn_quadrant_weights"] = nc.dram_tensor("L0_attn_quadrant_weights", [4], F32,
                                                      kind="ExternalInput").ap()
    dram["xrows"] = nc.dram_tensor("xrows", [NT, D], F32, kind="ExternalInput").ap()
    dram["xfull"] = nc.dram_tensor("xfull", [NKT, D], F32, kind="ExternalInput").ap()
    dram["ln1s"] = nc.dram_tensor("ln1s", [DC * 128, NKT], BF16, kind="Internal").ap()


def build_l0attn_launch():
    nc = bass.Bass("TRN2", target_bir_lowering=False, debug=False)
    dram = {}
    dram["resid_out"] = nc.dram_tensor("resid_out", [DC, 128, NT], F32, kind="ExternalOutput").ap()
    common_dram(nc, dram)
    attn_dram(nc, dram)
    with ExitStack() as ctx:
        E = ctx.enter_context
        bld = Bld(nc, ctx)
        alloc_consts(bld, E)
        resid = E(nc.sbuf_tensor("resid", [128, DC * NT], F32))
        QTall = E(nc.sbuf_tensor("QTall", [128, 8 * NT], BF16))
        emit_load_consts(bld, dram)
        emit_p0(bld, resid, QTall, dram)
        emit_l0attn(bld, resid, QTall, dram)

        def st(sy):
            bld.w(sy, "vv")
            bld.w(sy, "mm")
            bld.w(sy, "ss")
            for c in range(DC):
                bld.inc("sq", sy.dma_start(out=dram["resid_out"][c], in_=resid[:, c * NT:(c + 1) * NT]), 16)
            bld.w(sy, "sq")
        bld.phase(sync=st)
    return nc


def emit_l1ssm(bld, resid, dram):
    nc = bld.nc
    NTILES = 128
    with ExitStack() as ctx:
        E = ctx.enter_context
        Astr = E(nc.sbuf_tensor("Astr", [128, NTILES * 16], BF16))
        Vall = E(nc.sbuf_tensor("Vall", [128, NTILES * 128], BF16))
        Tall = E(nc.sbuf_tensor("Tall", [128, NTILES * 128], BF16))
        Pall = E(nc.sbuf_tensor("Pall", [128, NTILES * 128], BF16))
        Cblk = E(nc.sbuf_tensor("Cblk", [128, NTILES * 8], BF16))
        bca = E(nc.sbuf_tensor("bca", [128, 128], F32))
        cca = E(nc.sbuf_tensor("cca", [128, 128], F32))
        dca = E(nc.sbuf_tensor("dca", [128, 8], F32))
        bcaT = E(nc.sbuf_tensor("bcaT", [128, 128], F32))
        ccaT = E(nc.sbuf_tensor("ccaT", [128, 128], F32))
        dcaT = E(nc.sbuf_tensor("dcaT", [128, 128], F32))
        gst = E(nc.sbuf_tensor("gst", [128, 512], F32))
        grev = E(nc.sbuf_tensor("grev", [128, 512], BF16))
        gTsb = E(nc.sbuf_tensor("gTsb", [128, 1024], BF16))
        zb = E(nc.sbuf_tensor("zb", [128, 1024], BF16))

        def g0(g):
            for dloc in range(8):
                for th in range(2):
                    src = bass.AP(dram["L1_attn_A"].tensor, (th * 64 * 8 + dloc) * 256,
                                  [[16, 16], [8 * 256, 64], [1, 16]])
                    dst = Astr[dloc * 16:(dloc + 1) * 16, th * 64 * 16:(th + 1) * 64 * 16] \
                        .rearrange("n (t m) -> n t m", m=16)
                    bld.inc("gq", g.dma_start(out=dst, in_=src), 16)
            bld.inc("gq", g.dma_start(out=bca[:, :],
                                      in_=dram["L1_attn_Bm"].rearrange("(t a) n -> t (a n)", a=8)), 16)
            bld.inc("gq", g.dma_start(out=cca[:, :],
                                      in_=dram["L1_attn_Cm"].rearrange("(t a) n -> t (a n)", a=8)), 16)
            bld.inc("gq", g.dma_start(out=dca[:, :],
                                      in_=dram["L1_attn_Dm"].rearrange("(t p) -> t p", p=8)), 16)

        def v0(v):
            bld.V(v, lambda: v.memset(zb[0:64, :], 0.0))
        bld.phase(gpsimd=g0, vector=v0)

        for (srcT, dstT, rows) in [(bca, bcaT, 128), (cca, ccaT, 128), (dca, dcaT, 8)]:
            def tt(t, srcT=srcT, rows=rows):
                bld.w(t, "gq")
                bld.w(t, "vv")
                bld.w(t, "mm")
                bld.inc("mm", t.matmul(bld.PTF[0:rows, 0:128], srcT[:, 0:rows], bld.identf[:, :],
                                       start=True, stop=True, is_transpose=True))

            def vv(v, dstT=dstT, rows=rows):
                bld.w(v, "mm")
                bld.V(v, lambda: v.tensor_copy(dstT[0:rows, 0:128], bld.PTF[0:rows, 0:128]))
            bld.phase(tensor=tt, vector=vv, order=("tensor", "vector"))

        def vA(v):
            bld.w(v, "gq")
            for dl in range(8):
                def op(dl=dl):
                    o = Pall[:, :].rearrange("p (t f) -> p t f", f=128)[:, :, dl * 16:(dl + 1) * 16]
                    i0 = Astr[:, :].rearrange("p (t m) -> p t m", m=16)
                    i1 = dram
                    return v.scalar_tensor_tensor(o, i0, bld.msk8[:, dl:dl + 1], i0, ALU.bypass, ALU.mult)
                o = Pall[:, :].rearrange("p (t f) -> p t f", f=128)[:, :, dl * 16:(dl + 1) * 16]
                i0 = Astr[:, :].rearrange("p (t m) -> p t m", m=16)
                bld.V(v, lambda o=o, i0=i0, dl=dl: v.tensor_scalar(o, i0, bld.msk8[:, dl:dl + 1],
                                                                   None, ALU.mult))
            o = Vall[:, :].rearrange("p (t f) -> p t f", f=128)[:, :, 127]
            bld.V(v, lambda o=o: v.tensor_copy(o, bcaT[:, :]))
            for dl in range(8):
                o = Cblk[:, :].rearrange("p (t f) -> p t f", f=8)[:, :, dl]
                bld.V(v, lambda o=o, dl=dl: v.tensor_scalar(o, ccaT[:, :], bld.msk8[:, dl:dl + 1],
                                                            None, ALU.mult))
        bld.phase(vector=vA)

        for t_ in range(NTILES):
            def ttr(t, t_=t_):
                bld.w(t, "vv")
                bld.w(t, "mm")
                bld.inc("mm", t.matmul(bld.PT[:, :], Pall[:, t_ * 128:(t_ + 1) * 128],
                                       bld.identb[:, :], start=True, stop=True, is_transpose=True))

            def vtr(v, t_=t_):
                bld.w(v, "mm")
                bld.V(v, lambda: v.tensor_copy(Tall[:, t_ * 128:(t_ + 1) * 128], bld.PT[:, :]))
            bld.phase(tensor=ttr, vector=vtr, order=("tensor", "vector"))

        for k in range(7):
            wk = 1 << k
            per = max(1, 512 // (wk if wk > 0 else 1)) if wk else 8
            per = min(8, 512 // wk)
            grp = 0
            t_ = 0
            while t_ < NTILES:
                n = min(per, NTILES - t_)
                pb = bld.P[0] if grp % 2 == 0 else bld.P[1]
                def tmv(t, t_=t_, n=n, wk=wk, pb=pb):
                    bld.w(t, "vv")
                    bld.w(t, "ss")
                    bld.w(t, "mm")
                    for j in range(n):
                        bld.inc("mm", t.matmul(pb[:, j * wk:(j + 1) * wk],
                                               Tall[:, (t_ + j) * 128:(t_ + j + 1) * 128],
                                               Vall[:, (t_ + j) * 128 + 128 - wk: (t_ + j) * 128 + 128],
                                               start=True, stop=True))

                def smv(s, t_=t_, n=n, wk=wk, pb=pb):
                    bld.w(s, "mm")
                    for j in range(n):
                        bld.S(s, lambda j=j: s.copy(
                            Vall[:, (t_ + j) * 128 + 128 - 2 * wk: (t_ + j) * 128 + 128 - wk],
                            pb[:, j * wk:(j + 1) * wk]))
                bld.phase(tensor=tmv, scalar=smv, order=("tensor", "scalar"))
                t_ += n
                grp += 1
            if k == 6:
                break
            t_ = 0
            grp = 0
            while t_ < NTILES:
                n = min(4, NTILES - t_)
                pb = bld.PW0 if grp % 2 == 0 else bld.PW1
                pb2 = bld.P[0] if grp % 2 == 0 else bld.P[1]
                def tsq(t, t_=t_, n=n, pb=pb, pb2=pb2):
                    bld.w(t, "vv")
                    bld.w(t, "ss")
                    bld.w(t, "mm")
                    for j in range(n):
                        sl = slice((t_ + j) * 128, (t_ + j + 1) * 128)
                        bld.inc("mm", t.matmul(pb[:, j * 128:(j + 1) * 128] if j < 8 else None,
                                               Tall[:, sl], Pall[:, sl], start=True, stop=True))
                    for j in range(n):
                        sl = slice((t_ + j) * 128, (t_ + j + 1) * 128)
                        bld.inc("mm", t.matmul(pb2[:, j * 128:(j + 1) * 128],
                                               Pall[:, sl], Tall[:, sl], start=True, stop=True))

                def ssq(s, t_=t_, n=n, pb=pb, pb2=pb2):
                    bld.w(s, "mm")
                    for j in range(n):
                        sl = slice((t_ + j) * 128, (t_ + j + 1) * 128)
                        bld.S(s, lambda sl=sl, j=j: s.copy(Pall[:, sl], pb[:, j * 128:(j + 1) * 128]))

                def vsq(v, t_=t_, n=n, pb=pb, pb2=pb2):
                    bld.w(v, "mm")
                    for j in range(n):
                        sl = slice((t_ + j) * 128, (t_ + j + 1) * 128)
                        bld.V(v, lambda sl=sl, j=j: v.tensor_copy(Tall[:, sl],
                                                                  pb2[:, j * 128:(j + 1) * 128]))
                bld.phase(tensor=tsq, scalar=ssq, vector=vsq, order=("tensor", "scalar", "vector"))
                t_ += n
                grp += 1

        for t0_ in range(0, NTILES, 4):
            pb = bld.P[0]
            def tge(t, t0_=t0_, pb=pb):
                bld.w(t, "vv")
                bld.w(t, "ss")
                bld.w(t, "mm")
                for j in range(4):
                    sl = slice((t0_ + j) * 128, (t0_ + j + 1) * 128)
                    bld.inc("mm", t.matmul(pb[0:8, j * 128:(j + 1) * 128],
                                           Cblk[:, (t0_ + j) * 8:(t0_ + j + 1) * 8],
                                           Vall[:, sl], start=True, stop=True))

            def vge(v, t0_=t0_, pb=pb):
                bld.w(v, "mm")
                bld.V(v, lambda: v.tensor_copy(gst[0:8, :], pb[0:8, 0:512]))
                bld.V(v, lambda: v.tensor_copy(grev[0:8, :], gst[0:8, :]))
                o = grev[0:8, :].rearrange("p (t j) -> p t j", j=128)[:, :, 127]
                bld.V(v, lambda o=o: v.tensor_tensor(o, o, dcaT[0:8, t0_:t0_ + 4], ALU.add))
            bld.phase(tensor=tge, vector=vge, order=("tensor", "vector"))

            for j in range(4):
                def ttg(t, t0_=t0_, j=j):
                    bld.w(t, "vv")
                    bld.w(t, "mm")
                    bld.inc("mm", t.matmul(bld.PT[0:128, j * 8:(j + 1) * 8],
                                           grev[0:8, j * 128:(j + 1) * 128],
                                           bld.identb[0:8, 0:8], start=True, stop=True,
                                           is_transpose=True))
                bld.phase(tensor=ttg)

            def vgt(v, t0_=t0_):
                bld.w(v, "mm")
                bld.V(v, lambda: v.tensor_copy(gTsb[:, t0_ * 8:(t0_ + 4) * 8], bld.PT[0:128, 0:32]))
            bld.phase(vector=vgt)

        def sw(sy):
            bld.w(sy, "vv")
            bld.inc("sq", sy.dma_start(out=dram["gTdR"][0:128, :], in_=gTsb[:, :]), 16)
            bld.inc("sq", sy.dma_start(out=dram["gTdR"][128:192, :], in_=zb[0:64, :]), 16)
            bld.w(sy, "sq")
        bld.phase(sync=sw)
        emit_dbg(bld, dram, "dbg_g", gTsb[:, :])
        emit_dbg(bld, dram, "dbg_V", Vall[:, 0:2048])

    with ExitStack() as ctx:
        E = ctx.enter_context
        ln3f = E(nc.sbuf_tensor("ln3f", [128, DC * NT], BF16))
        y64 = E(nc.sbuf_tensor("y64", [128, 16 * 1024], BF16))
        lnwc = E(nc.sbuf_tensor("lnwc", [128, 8], F32))
        lnbc = E(nc.sbuf_tensor("lnbc", [128, 8], F32))
        boc = E(nc.sbuf_tensor("boc", [128, 8], F32))
        cmk = E(nc.sbuf_tensor("cmk", [128, 1], F32))
        with ExitStack() as ctxl:
            lns3 = alloc_ln_scr(bld, ctxl.enter_context, "s")
            emit_ln_fm(bld, lns3, resid, NT, 0, NT, ln3f, NT, 0)

        with ExitStack() as ctxw:
            tmpW = ctxw.enter_context(nc.sbuf_tensor("tmpW", [128, DC * 1024], BF16))
            WoT = ctxw.enter_context(nc.sbuf_tensor("WoTs", [128, DC * 1024], BF16))

            def gwo(g):
                bld.inc("gq", g.dma_start(out=cmk[0:64, :], in_=dram["convmask"][:, :]), 16)
                for dc in range(DC):
                    bld.inc("gq", g.dma_start(out=tmpW[:, dc * 1024:(dc + 1) * 1024],
                                              in_=dram["L1_attn_W_out"][dc * 128:(dc + 1) * 128, :]), 16)
            bld.phase(gpsimd=gwo)
            for dc in range(DC):
                for c in range(DC):
                    def twt(t, dc=dc, c=c):
                        bld.w(t, "gq")
                        bld.w(t, "vv")
                        bld.w(t, "mm")
                        bld.inc("mm", t.matmul(bld.PT[:, :],
                                               tmpW[:, dc * 1024 + c * 128: dc * 1024 + (c + 1) * 128],
                                               bld.identb[:, :], start=True, stop=True, is_transpose=True))

                    def vwt(v, dc=dc, c=c):
                        bld.w(v, "mm")
                        bld.V(v, lambda: v.tensor_copy(WoT[:, c * 1024 + dc * 128: c * 1024 + (dc + 1) * 128],
                                                       bld.PT[:, :]))
                    bld.phase(tensor=twt, vector=vwt, order=("tensor", "vector"))

            def swot(sy):
                bld.w(sy, "vv")
                bld.inc("sq", sy.dma_start(out=dram["wotT"][0:64, :], in_=WoT[0:64, 0:8192]), 16)
                bld.inc("sq", sy.dma_start(out=dram["wotT"][64:128, :], in_=WoT[64:128, 0:8192]), 16)
                bld.w(sy, "sq")
            bld.phase(sync=swot)

        with ExitStack() as ctxr:
            lrow = ctxr.enter_context(nc.sbuf_tensor("lrow", [1, 1024], F32))
            for (name, dstc) in [("L1_attn_ln_w", lnwc), ("L1_attn_ln_b", lnbc), ("L1_attn_b_out", boc)]:
                def gl(g, name=name):
                    bld.w(g, "vv")
                    bld.inc("gq", g.dma_start(out=lrow[:, :], in_=dram[name][None, :]), 16)
                bld.phase(gpsimd=gl)
                for c in range(DC):
                    def tl2(t, c=c):
                        bld.w(t, "gq")
                        bld.w(t, "vv")
                        bld.w(t, "mm")
                        bld.inc("mm", t.matmul(bld.PTF[0:128, 0:1], lrow[:, c * 128:(c + 1) * 128],
                                               bld.identf[0:1, 0:1], start=True, stop=True, is_transpose=True))

                    def vl2(v, c=c, dstc=dstc):
                        bld.w(v, "mm")
                        bld.V(v, lambda: v.tensor_copy(dstc[:, c:c + 1], bld.PTF[0:128, 0:1]))
                    bld.phase(tensor=tl2, vector=vl2, order=("tensor", "vector"))

        for (hb, (t0, n)) in enumerate([(128, 1024), (1152, 1024)]):
            nblk = n // 64
            b0 = t0 // 64
            wb0 = b0 - 1
            with ExitStack() as ctx1:
                E1 = ctx1.enter_context
                x64 = E1(nc.sbuf_tensor(f"x64_{hb}", [128, 17 * 1024], BF16))
                Gt = E1(nc.sbuf_tensor(f"Gt_{hb}", [128, 64 * 256], BF16))
                for wb in range(17):
                    blkt = wb0 + wb
                    for c in range(DC):
                        def tx(t, blkt=blkt, wb=wb, c=c):
                            bld.w(t, "vv")
                            bld.w(t, "mm")
                            bld.inc("mm", t.matmul(bld.PT[0:64, 0:128],
                                                   ln3f[:, c * NT + blkt * 64: c * NT + (blkt + 1) * 64],
                                                   bld.identb[:, :], start=True, stop=True,
                                                   is_transpose=True))

                        def vx(v, blkt=blkt, wb=wb, c=c):
                            bld.w(v, "mm")
                            dst = x64[0:64, wb * 1024 + c * 128: wb * 1024 + (c + 1) * 128]
                            if blkt < 2:
                                bld.V(v, lambda: v.tensor_scalar(dst, bld.PT[0:64, 0:128], cmk[0:64, :],
                                                                 None, ALU.mult))
                            else:
                                bld.V(v, lambda: v.tensor_copy(dst, bld.PT[0:64, 0:128]))
                        bld.phase(tensor=tx, vector=vx, order=("tensor", "vector"))

                for ct in range(8):
                    def ga(sy, ct=ct):
                        bld.w(sy, "mm")
                        for i in range(64):
                            src = bass.AP(dram["gTdR"].tensor, (63 - i) * 1024 + ct * 128,
                                          [[1024, 64], [64 * 1024, 2], [1, 128]])
                            dst = Gt[0:64, i * 256:(i + 1) * 256].rearrange("w (h c) -> w h c", c=128)
                            bld.inc("sq", sy.dma_start(out=dst, in_=src), 16)
                    bld.phase(sync=ga)
                    sqm = bld.cnt["sq"]

                    ssb = bld.cnt["ss"]
                    mmb = bld.cnt["mm"]
                    for g4_ in range(32):
                        pb = bld.P[0] if g4_ % 2 == 0 else bld.P[1]
                        def tcv(t, ct=ct, g4_=g4_, pb=pb, sqm=sqm, ssb=ssb):
                            t.wait_ge(bld.sem["sq"], sqm)
                            if g4_ == 0:
                                bld.w(t, "ss")
                                bld.w(t, "mm")
                            elif g4_ > 1:
                                t.wait_ge(bld.sem["ss"], ssb + g4_ - 1)
                            for cj in range(4):
                                ch = g4_ * 4 + cj
                                lh = Gt[0:64, :].rearrange("w (i h c) -> w i h c", h=2, c=128)
                                rh = x64[0:64, :].rearrange("w (b f) -> w b f", f=1024)
                                bld.inc("mm", t.matmul(pb[0:64, cj * 16:(cj + 1) * 16],
                                                       lh[:, :, 0, ch],
                                                       rh[:, 0:16, ct * 128 + ch], start=True, stop=False))
                                bld.inc("mm", t.matmul(pb[0:64, cj * 16:(cj + 1) * 16],
                                                       lh[:, :, 1, ch],
                                                       rh[:, 1:17, ct * 128 + ch], start=False, stop=True))

                        def scv(s, ct=ct, g4_=g4_, pb=pb, mmb=mmb):
                            s.wait_ge(bld.sem["mm"], mmb + 8 * (g4_ + 1))
                            o = y64[0:64, :].rearrange("w (b f) -> w b f", f=1024) \
                                [:, :, ct * 128 + g4_ * 4: ct * 128 + (g4_ + 1) * 4]
                            bld.inc("ss", s.copy(o, pb[0:64, 0:64].rearrange("w (c b) -> w b c", b=16)))
                        bld.phase(tensor=tcv, scalar=scv, order=("tensor", "scalar"))

            if hb == 0:
                emit_dbg(bld, dram, "dbg_x64", x64[0:64, :])
                emit_dbg(bld, dram, "dbg_y64", y64[0:64, :])
            with ExitStack() as ctx2:
                E2 = ctx2.enter_context
                m64 = E2(nc.sbuf_tensor(f"m64_{hb}", [128, 1], F32))
                s64 = E2(nc.sbuf_tensor(f"s64_{hb}", [128, 1], F32))
                sq64 = E2(nc.sbuf_tensor(f"sq64_{hb}", [128, 1024], BF16))
                yT = E2(nc.sbuf_tensor(f"yT_{hb}", [128, DC * 1024], BF16))
                vwtmp = E2(nc.sbuf_tensor(f"vwtmp_{hb}", [128, 512], F32))
                WoT = E2(nc.sbuf_tensor(f"WoT2_{hb}", [128, DC * 1024], BF16))
                dbgpb = E2(nc.sbuf_tensor(f"dbgpb_{hb}", [128, 512], F32))
                vwtmp2 = E2(nc.sbuf_tensor(f"vwtmp2_{hb}", [128, 512], F32))

                def gwot(g):
                    bld.w(g, "mm")
                    bld.w(g, "vv")
                    bld.inc("gq", g.dma_start(out=WoT[:, :], in_=dram["wotT"][:, :]), 16)
                bld.phase(gpsimd=gwot)
                gq_wot = bld.cnt["gq"]
                if hb == 0:
                    def wotbar(t, gq_wot=gq_wot):
                        t.wait_ge(bld.sem["gq"], gq_wot)
                    bld.phase(tensor=wotbar, order=("tensor",))
                    emit_dbg(bld, dram, "dbg_wot", WoT[:, :])
                for b in range(16):
                    def vst(v, b=b):
                        bld.w(v, "ss")
                        bld.V(v, lambda: v.tensor_reduce(m64[0:64, :], y64[0:64, b * 1024:(b + 1) * 1024],
                                                         AX.X, ALU.add))
                        bld.V(v, lambda: v.tensor_scalar(m64[0:64, :], m64[0:64, :], 1.0 / D, None, ALU.mult))
                        bld.V(v, lambda: v.tensor_scalar(sq64[0:64, :], y64[0:64, b * 1024:(b + 1) * 1024],
                                                         m64[0:64, :], None, ALU.subtract))

                    def sst2(s, b=b):
                        bld.w(s, "vv")
                        bld.S(s, lambda: s.activation(out=y64[0:64, b * 1024:(b + 1) * 1024],
                                                      in_=sq64[0:64, :], func=AF.Square,
                                                      accum_out=s64[0:64, :]))
                    bld.phase(vector=vst, scalar=sst2, order=("vector", "scalar"))

                    def vst2(v, b=b):
                        bld.w(v, "ss")
                        bld.V(v, lambda: v.tensor_scalar(s64[0:64, :], s64[0:64, :], 1.0 / D, EPS, ALU.mult, ALU.add))
                    bld.phase(vector=vst2)

                    def sst3(s, b=b):
                        bld.w(s, "vv")
                        bld.S(s, lambda: s.activation(out=s64[0:64, :], in_=s64[0:64, :], func=AF.Sqrt))
                    bld.phase(scalar=sst3)

                    def vst3(v, b=b):
                        bld.w(v, "ss")
                        bld.V(v, lambda: v.reciprocal(s64[0:64, :], s64[0:64, :]))
                        bld.V(v, lambda: v.tensor_scalar(y64[0:64, b * 1024:(b + 1) * 1024],
                                                         sq64[0:64, :], s64[0:64, :], None, ALU.mult))
                    bld.phase(vector=vst3)
                    if hb == 0 and b == 0:
                        emit_dbg(bld, dram, "dbg_m64", m64[0:64, :])
                        emit_dbg(bld, dram, "dbg_s64", s64[0:64, :])
                        emit_dbg(bld, dram, "dbg_yb0", y64[0:64, 0:1024])
                        emit_dbg(bld, dram, "dbg_sq0", sq64[0:64, :])

                for b in range(16):
                    for c in range(DC):
                        def ty(t, b=b, c=c):
                            bld.w(t, "vv")
                            bld.w(t, "mm")
                            bld.inc("mm", t.matmul(bld.PT[0:128, 0:64],
                                                   y64[0:64, b * 1024 + c * 128: b * 1024 + (c + 1) * 128],
                                                   bld.identb[0:64, 0:64], start=True, stop=True,
                                                   is_transpose=True))

                        def vy(v, b=b, c=c):
                            bld.w(v, "mm")
                            bld.V(v, lambda: v.tensor_scalar(
                                yT[:, c * 1024 + b * 64: c * 1024 + (b + 1) * 64],
                                bld.PT[0:128, 0:64], lnwc[:, c:c + 1], lnbc[:, c:c + 1],
                                ALU.mult, ALU.add))
                        bld.phase(tensor=ty, vector=vy, order=("tensor", "vector"))

                if hb == 0:
                    emit_dbg(bld, dram, "dbg_y64b", y64[0:64, :])
                    emit_dbg(bld, dram, "dbg_z", yT[:, :])

                def sg2(s):
                    bld.w(s, "vv")
                    for c in range(DC):
                        bld.S(s, lambda c=c: s.activation(out=yT[:, c * 1024:(c + 1) * 1024],
                                                          in_=yT[:, c * 1024:(c + 1) * 1024], func=AF.Gelu))
                bld.phase(scalar=sg2)
                if hb == 0:
                    emit_dbg(bld, dram, "dbg_gel", yT[:, :])

                for dc in range(DC):
                    pbA, pbB = bld.P[3], bld.P[5]
                    def tw(t, dc=dc, gq_wot=gq_wot):
                        t.wait_ge(bld.sem["gq"], gq_wot)
                        bld.w(t, "gq")
                        bld.w(t, "sq")
                        bld.w(t, "vv")
                        bld.w(t, "ss")
                        bld.w(t, "mm")
                        for c in range(DC):
                            bld.inc("mm", t.matmul(pbA[:, 0:512],
                                                   WoT[:, c * 1024 + dc * 128: c * 1024 + (dc + 1) * 128],
                                                   yT[:, c * 1024: c * 1024 + 512],
                                                   start=(c == 0), stop=(c == DC - 1)))
                            bld.inc("mm", t.matmul(pbB[:, 0:512],
                                                   WoT[:, c * 1024 + dc * 128: c * 1024 + (dc + 1) * 128],
                                                   yT[:, c * 1024 + 512: c * 1024 + 1024],
                                                   start=(c == 0), stop=(c == DC - 1)))

                    def sw2(s, dc=dc):
                        bld.w(s, "mm")
                        bld.w(s, "vv")
                        bld.S(s, lambda: s.copy(vwtmp[:, 0:512], pbA[:, 0:512]))
                        bld.S(s, lambda: s.copy(vwtmp2[:, 0:512], pbB[:, 0:512]))

                    def vw(v, dc=dc, t0=t0):
                        bld.w(v, "ss")
                        for (o, buf) in [(0, vwtmp), (512, vwtmp2)]:
                            bld.V(v, lambda o=o, buf=buf: v.scalar_tensor_tensor(
                                buf[:, 0:512], buf[:, 0:512], boc[:, dc:dc + 1],
                                ln3f[:, dc * NT + t0 + o: dc * NT + t0 + o + 512], ALU.add, ALU.add))
                            sl = resid[:, dc * NT + t0 + o: dc * NT + t0 + o + 512]
                            bld.V(v, lambda sl=sl, buf=buf: v.tensor_tensor(sl, sl, buf[:, 0:512], ALU.add))
                    bld.phase(tensor=tw, scalar=sw2, vector=vw, order=("tensor", "scalar", "vector"))
                if hb == 0:
                    emit_dbg(bld, dram, "dbg_res0c", resid[:, 0:2176])
                    emit_dbg(bld, dram, "dbg_pb", dbgpb[:, :])


def ssm_dram(nc, dram):
    dram["L1_attn_A"] = nc.dram_tensor("L1_attn_A", [D, 16, 16], F32, kind="ExternalInput").ap()
    dram["L1_attn_Bm"] = nc.dram_tensor("L1_attn_Bm", [D, 16], F32, kind="ExternalInput").ap()
    dram["L1_attn_Cm"] = nc.dram_tensor("L1_attn_Cm", [D, 16], F32, kind="ExternalInput").ap()
    dram["L1_attn_Dm"] = nc.dram_tensor("L1_attn_Dm", [D], F32, kind="ExternalInput").ap()
    dram["L1_attn_ln_w"] = nc.dram_tensor("L1_attn_ln_w", [D], F32, kind="ExternalInput").ap()
    dram["L1_attn_ln_b"] = nc.dram_tensor("L1_attn_ln_b", [D], F32, kind="ExternalInput").ap()
    dram["L1_attn_W_out"] = nc.dram_tensor("L1_attn_W_out", [D, D], F32, kind="ExternalInput").ap()
    dram["L1_attn_b_out"] = nc.dram_tensor("L1_attn_b_out", [D], F32, kind="ExternalInput").ap()
    dram["convmask"] = nc.dram_tensor("convmask", [64, 1], F32, kind="ExternalInput").ap()
    dram["gTdR"] = nc.dram_tensor("gTdR", [192, D], BF16, kind="Internal").ap()
    dram["wotT"] = nc.dram_tensor("wotT", [128, 8192], BF16, kind="Internal").ap()
    if DBG:
        dram["dbg_g"] = nc.dram_tensor("dbg_g", [128, 1024], BF16, kind="ExternalOutput").ap()
        dram["dbg_V"] = nc.dram_tensor("dbg_V", [128, 2048], BF16, kind="ExternalOutput").ap()
        dram["dbg_x64"] = nc.dram_tensor("dbg_x64", [64, 17 * 1024], BF16, kind="ExternalOutput").ap()
        dram["dbg_y64"] = nc.dram_tensor("dbg_y64", [64, 16 * 1024], BF16, kind="ExternalOutput").ap()
        dram["dbg_z"] = nc.dram_tensor("dbg_z", [128, 8192], BF16, kind="ExternalOutput").ap()
        dram["dbg_gel"] = nc.dram_tensor("dbg_gel", [128, 8192], BF16, kind="ExternalOutput").ap()
        dram["dbg_y64b"] = nc.dram_tensor("dbg_y64b", [64, 16384], BF16, kind="ExternalOutput").ap()
        dram["dbg_m64"] = nc.dram_tensor("dbg_m64", [64, 1], F32, kind="ExternalOutput").ap()
        dram["dbg_s64"] = nc.dram_tensor("dbg_s64", [64, 1], F32, kind="ExternalOutput").ap()
        dram["dbg_yb0"] = nc.dram_tensor("dbg_yb0", [64, 1024], BF16, kind="ExternalOutput").ap()
        dram["dbg_sq0"] = nc.dram_tensor("dbg_sq0", [64, 1024], BF16, kind="ExternalOutput").ap()
        dram["dbg_wot"] = nc.dram_tensor("dbg_wot", [128, 8192], BF16, kind="ExternalOutput").ap()
        dram["dbg_res0c"] = nc.dram_tensor("dbg_res0c", [128, 2176], F32, kind="ExternalOutput").ap()
        dram["dbg_vwt"] = nc.dram_tensor("dbg_vwt", [128, 512], F32, kind="ExternalOutput").ap()
        dram["dbg_pb"] = nc.dram_tensor("dbg_pb", [128, 512], F32, kind="ExternalOutput").ap()


def build_l1ssm_launch():
    nc = bass.Bass("TRN2", target_bir_lowering=False, debug=False)
    dram = {}
    dram["resid_in"] = nc.dram_tensor("resid_in", [DC, 128, NT], F32, kind="ExternalInput").ap()
    dram["resid_out"] = nc.dram_tensor("resid_out", [DC, 128, NT], F32, kind="ExternalOutput").ap()
    common_dram(nc, dram)
    ssm_dram(nc, dram)
    with ExitStack() as ctx:
        E = ctx.enter_context
        bld = Bld(nc, ctx)
        alloc_consts(bld, E)
        bld.msk8 = E(nc.sbuf_tensor("msk8", [128, 8], F32))
        resid = E(nc.sbuf_tensor("resid", [128, DC * NT], F32))
        dram["blockmask8"] = nc.dram_tensor("blockmask8", [128, 8], F32, kind="ExternalInput").ap()
        emit_load_consts(bld, dram)

        def ld(sy):
            for c in range(DC):
                bld.inc("sq", sy.dma_start(out=resid[:, c * NT:(c + 1) * NT], in_=dram["resid_in"][c]), 16)
        bld.phase(sync=ld)

        def g1(g):
            bld.inc("gq", g.dma_start(out=bld.msk8[:, :], in_=dram["blockmask8"][:, :]), 16)
        bld.phase(gpsimd=g1)

        def bar(t):
            bld.w(t, "sq")
            bld.w(t, "gq")
        bld.phase(tensor=bar, order=("tensor",))

        emit_l1ssm(bld, resid, dram)

        def st(sy):
            bld.w(sy, "vv")
            bld.w(sy, "mm")
            bld.w(sy, "ss")
            for c in range(DC):
                bld.inc("sq", sy.dma_start(out=dram["resid_out"][c], in_=resid[:, c * NT:(c + 1) * NT]), 16)
            bld.w(sy, "sq")
        bld.phase(sync=st)
    return nc


# version 04
# speedup vs baseline: 1.0615x; 1.0615x over previous
import numpy as np
import ml_dtypes
from contextlib import ExitStack

import concourse.bass as bass
import concourse.mybir as mybir
from concourse.bass_utils import run_bass_kernel_spmd

F32 = mybir.dt.float32
BF16 = mybir.dt.bfloat16
AF = mybir.ActivationFunctionType
ALU = mybir.AluOpType
AX = mybir.AxisListType

NT = 2176
NOWN = 2048
HALO = 128
D = 1024
DC = 8
EPS = 1e-5
NCORES = 8
NTB = 544
HID = 4096
HC = 32


def tch(n, w=512):
    out = []
    o = 0
    while o < n:
        c = min(w, n - o)
        out.append((o, c))
        o += c
    return out


class Bld:

    def __init__(self, nc, ctx):
        self.nc = nc
        self.ctx = ctx
        self.sem = {}
        self.cnt = {}
        for name in ("sq", "gq", "mm", "vv", "ss", "gc"):
            self.sem[name] = ctx.enter_context(nc.semaphore(name))
            self.cnt[name] = 0
        self._P0 = ctx.enter_context(nc.psum_tensor("P0", [128, 512], F32))
        self._P1 = ctx.enter_context(nc.psum_tensor("P1", [128, 512], F32))
        self.PW0 = ctx.enter_context(nc.psum_tensor("PW0", [128, 1024], F32))
        self.PW1 = ctx.enter_context(nc.psum_tensor("PW1", [128, 1024], F32))
        self.P = [self._P0[:, :], self._P1[:, :], self.PW0[:, 0:512], self.PW1[:, 0:512],
                  self.PW0[:, 512:1024], self.PW1[:, 512:1024]]
        self.PT = ctx.enter_context(nc.psum_tensor("PT", [128, 128], BF16))
        self.PT2 = ctx.enter_context(nc.psum_tensor("PT2", [128, 128], BF16))
        self.PTF = self.PW0[:, 0:128]
        self.PTF2 = self.PW1[:, 0:128]
        self.gq_marks = {}

    def inc(self, name, inst, k=1):
        inst.then_inc(self.sem[name], k)
        self.cnt[name] += k
        return self.cnt[name]

    def V(self, v, f):
        self.w(v, "vv")
        return self.inc("vv", f())

    def S(self, s, f):
        self.w(s, "ss")
        return self.inc("ss", f())

    def w(self, eng, name, val=None):
        v = self.cnt[name] if val is None else val
        if v > 0:
            eng.wait_ge(self.sem[name], v)

    def phase(self, **closures):
        order = closures.pop("order", ("sync", "gpsimd", "tensor", "scalar", "vector"))
        with self.nc.Block() as blk:
            for name in order:
                fn = closures.get(name)
                if fn is not None:
                    getattr(blk, name)(fn)


def alloc_consts(bld, E):
    nc = bld.nc
    bld.identf = E(nc.sbuf_tensor("c_identf", [128, 128], F32))
    bld.identb = E(nc.sbuf_tensor("c_identb", [128, 128], BF16))
    bld.onescol = E(nc.sbuf_tensor("c_onescol", [128, 1], F32))
    bld.onescol_b = E(nc.sbuf_tensor("c_onescol_b", [128, 1], BF16))
    bld.onesrow = E(nc.sbuf_tensor("c_onesrow", [1, 128], F32))
    bld.onesrow_b = E(nc.sbuf_tensor("c_onesrow_b", [1, 128], BF16))


def emit_load_consts(bld, dram):
    def g(g):
        for dst, src in [(bld.identf, dram["identf"]), (bld.identb, dram["identf"]),
                         (bld.onescol, dram["ones_col"]), (bld.onescol_b, dram["ones_col"]),
                         (bld.onesrow, dram["ones_row"]), (bld.onesrow_b, dram["ones_row"])]:
            bld.inc("gq", g.dma_start(out=dst[:, :], in_=src[:, :]), 16)
    bld.phase(gpsimd=g)


def emit_ln_fm(bld, scr, src, src_stride, t0, n, dst, dst_stride, dt0):
    pst1, pst2, pmb, prb = bld.P[4], bld.P[5], bld.P[2], bld.P[3]
    for (o, w) in tch(n):
        def vv(v, o=o, w=w):
            ss_base = bld.cnt["ss"]
            mm_base = bld.cnt["mm"]
            for c in range(DC):
                if c == 0:
                    bld.w(v, "ss")
                    bld.w(v, "mm")
                else:
                    v.wait_ge(bld.sem["ss"], ss_base + c)
                    v.wait_ge(bld.sem["mm"], mm_base + 2 * c)
                bld.w(v, "vv")
                bld.inc("vv", v.tensor_copy(
                    scr["xb"][:, 0:w], src[:, c * src_stride + t0 + o: c * src_stride + t0 + o + w]))

        def ss(s, o=o, w=w):
            vv_base = bld.cnt["vv"] - DC
            mm_base = bld.cnt["mm"]
            for c in range(DC):
                s.wait_ge(bld.sem["vv"], vv_base + c + 1)
                if c > 0:
                    s.wait_ge(bld.sem["mm"], mm_base + 2 * c)
                bld.w(s, "ss")
                bld.inc("ss", s.activation(out=scr["sq"][:, 0:w], in_=scr["xb"][:, 0:w], func=AF.Square))

        def tt(t, o=o, w=w):
            vv_base = bld.cnt["vv"] - DC
            ss_base = bld.cnt["ss"] - DC
            bld.w(t, "mm")
            for c in range(DC):
                t.wait_ge(bld.sem["vv"], vv_base + c + 1)
                t.wait_ge(bld.sem["ss"], ss_base + c + 1)
                bld.inc("mm", t.matmul(pst1[0:1, 0:w], bld.onescol_b[:, :], scr["xb"][:, 0:w],
                                       start=(c == 0), stop=(c == DC - 1)))
                bld.inc("mm", t.matmul(pst2[0:1, 0:w], bld.onescol_b[:, :], scr["sq"][:, 0:w],
                                       start=(c == 0), stop=(c == DC - 1)))
        bld.phase(vector=vv, scalar=ss, tensor=tt, order=("vector", "scalar", "tensor"))

        def vv2(v, o=o, w=w):
            bld.w(v, "mm")
            bld.w(v, "vv")
            bld.inc("vv", v.tensor_scalar(scr["m_r"][:, 0:w], pst1[0:1, 0:w], 1.0 / D, None, ALU.mult))
            bld.w(v, "vv")
            bld.inc("vv", v.tensor_scalar(scr["a_r"][:, 0:w], pst2[0:1, 0:w], 1.0 / D, None, ALU.mult))
            bld.w(v, "vv")
            bld.inc("vv", v.tensor_tensor(scr["b_r"][:, 0:w], scr["m_r"][:, 0:w], scr["m_r"][:, 0:w], ALU.mult))
            bld.w(v, "vv")
            bld.inc("vv", v.tensor_tensor(scr["a_r"][:, 0:w], scr["a_r"][:, 0:w], scr["b_r"][:, 0:w], ALU.subtract))
            bld.w(v, "vv")
            bld.inc("vv", v.tensor_scalar(scr["a_r"][:, 0:w], scr["a_r"][:, 0:w], EPS, None, ALU.add))
        bld.phase(vector=vv2, order=("vector",))

        def ss2(s, o=o, w=w):
            bld.w(s, "vv")
            bld.w(s, "ss")
            bld.inc("ss", s.activation(out=scr["b_r"][:, 0:w], in_=scr["a_r"][:, 0:w], func=AF.Sqrt))
        bld.phase(scalar=ss2, order=("scalar",))

        def vv3(v, o=o, w=w):
            bld.w(v, "ss")
            bld.w(v, "vv")
            bld.inc("vv", v.reciprocal(scr["a_r"][:, 0:w], scr["b_r"][:, 0:w]))
        bld.phase(vector=vv3, order=("vector",))

        def tt4(t, o=o, w=w):
            bld.w(t, "vv")
            bld.w(t, "mm")
            bld.inc("mm", t.matmul(pmb[:, 0:w], bld.onesrow[:, :], scr["m_r"][:, 0:w], start=True, stop=True))
            bld.inc("mm", t.matmul(prb[:, 0:w], bld.onesrow[:, :], scr["a_r"][:, 0:w], start=True, stop=True))

        def vv4(v, o=o, w=w):
            bld.w(v, "mm")
            for c in range(DC):
                bld.w(v, "vv")
                bld.inc("vv", v.tensor_tensor(scr["tmp"][:, 0:w],
                                              src[:, c * src_stride + t0 + o: c * src_stride + t0 + o + w],
                                              pmb[:, 0:w], ALU.subtract))
                bld.w(v, "vv")
                bld.inc("vv", v.tensor_tensor(dst[:, c * dst_stride + dt0 + o: c * dst_stride + dt0 + o + w],
                                              scr["tmp"][:, 0:w], prb[:, 0:w], ALU.mult))
        bld.phase(tensor=tt4, vector=vv4, order=("tensor", "vector"))


def alloc_ln_scr(bld, E, tag="", tmp=None):
    nc = bld.nc
    return {
        "xb": E(nc.sbuf_tensor(f"lnxb{tag}", [128, 512], BF16)),
        "sq": E(nc.sbuf_tensor(f"lnsq{tag}", [128, 512], BF16)),
        "m_r": E(nc.sbuf_tensor(f"lnm{tag}", [1, 512], F32)),
        "a_r": E(nc.sbuf_tensor(f"lna{tag}", [1, 512], F32)),
        "b_r": E(nc.sbuf_tensor(f"lnb{tag}", [1, 512], F32)),
        "tmp": tmp if tmp is not None else E(nc.sbuf_tensor(f"lntmp{tag}", [128, 512], F32)),
    }


def emit_tpose_pipe(bld, items, f32=False):
    pts = [bld.PTF, bld.PTF2] if f32 else [bld.PT[:, :], bld.PT2[:, :]]
    ident = bld.identf if f32 else bld.identb
    mm0 = bld.cnt["mm"]
    vv0 = bld.cnt["vv"]
    n = len(items)

    def tt(t):
        bld.w(t, "gq")
        bld.w(t, "sq")
        bld.w(t, "vv")
        bld.w(t, "mm")
        for i, (src, rows, cols, _) in enumerate(items):
            if i >= 2:
                t.wait_ge(bld.sem["vv"], vv0 + i - 1)
            pt = pts[i % 2]
            bld.inc("mm", t.matmul(pt[0:cols, 0:rows], src, ident[0:rows, 0:rows],
                                   start=True, stop=True, is_transpose=True))

    def vv(v):
        for i, (src, rows, cols, evac) in enumerate(items):
            v.wait_ge(bld.sem["mm"], mm0 + i + 1)
            pt = pts[i % 2]
            bld.inc("vv", evac(v, pt[0:cols, 0:rows]))
    bld.phase(tensor=tt, vector=vv, order=("tensor", "vector"))


def emit_small_transpose(bld, src_ap, rows, cols, evac_cb):
    def tt(t):
        bld.w(t, "gq")
        bld.w(t, "sq")
        bld.w(t, "vv")
        bld.w(t, "mm")
        bld.inc("mm", t.matmul(bld.PT[0:cols, 0:rows], src_ap, bld.identb[0:rows, 0:rows],
                               start=True, stop=True, is_transpose=True))

    def vv(v):
        bld.w(v, "mm")
        bld.w(v, "sq")
        evac_cb(v)
    bld.phase(tensor=tt, vector=vv, order=("tensor", "vector"))


DBG = False


def emit_dbg(bld, dram, key, src_ap):
    if not DBG or key not in dram:
        return
    def sy(sy):
        bld.w(sy, "vv"); bld.w(sy, "mm"); bld.w(sy, "ss")
        bld.inc("sq", sy.dma_start(out=dram[key][:, :], in_=src_ap), 16)
        bld.w(sy, "sq")
    bld.phase(sync=sy)


def emit_l0ffn(bld, resid, dram):
    nc = bld.nc
    with ExitStack() as ctx:
        E = ctx.enter_context
        ln2 = E(nc.sbuf_tensor("ln2", [128, DC * NTB], BF16))
        r2T = E(nc.sbuf_tensor("r2T", [128, HC * NTB], BF16))
        vT = E(nc.sbuf_tensor("vT", [128, DC * 16], BF16))
        fbT = E(nc.sbuf_tensor("fbT", [128, 16], BF16))
        uTg = E(nc.sbuf_tensor("uTg", [128, HID], BF16))
        s1p = E(nc.sbuf_tensor("s1p", [128, NT], BF16))
        s2p = E(nc.sbuf_tensor("s2p", [128, NT], BF16))
        v16b = E(nc.sbuf_tensor("v16b", [16, D], BF16))
        fbb = E(nc.sbuf_tensor("fbb", [16, 16], BF16))
        ub = E(nc.sbuf_tensor("ub", [128, 32 * 16], BF16))
        gcol = E(nc.sbuf_tensor("gcol", [16, 1], F32))
        gsb = E(nc.sbuf_tensor("gsb", [1, 1], F32))
        wt = E(nc.sbuf_tensor("wt", [128, 1024], F32))
        wpt = E(nc.sbuf_tensor("wpt", [128, HID], F32))
        red = E(nc.sbuf_tensor("red", [128, 1], F32))
        hs = E(nc.sbuf_tensor("hs", [128, 1], F32))
        scl = E(nc.sbuf_tensor("scl", [128, 1], F32))
        sgn = E(nc.sbuf_tensor("sgn", [128, HID], BF16))
        absq = E(nc.sbuf_tensor("absq", [128, HID], F32))
        hs2 = E(nc.sbuf_tensor("hs2", [128, 1], F32))
        weff = E(nc.sbuf_tensor("weff", [128, HID], BF16))
        strip = E(nc.sbuf_tensor("strip", [128, DC * 128], BF16))
        wpstrip = E(nc.sbuf_tensor("wpstrip", [128, HC * 128], BF16))
        rb = E(nc.sbuf_tensor("rbuf", [128, 512], BF16))
        lns = alloc_ln_scr(bld, E, "f", tmp=absq)
        ph, ph2, po0, po1 = bld.P[0], bld.P[1], bld.P[2], bld.P[3]
        ps16 = bld.P[4]

        def g0(g):
            bld.inc("gq", g.dma_start(out=v16b[:, :], in_=dram["L0_ffn_v"][:, :]), 16)
            bld.inc("gq", g.dma_start(out=fbb[:, :], in_=dram["L0_ffn_fast_b"][:, :]), 16)
            usrc = dram["L0_ffn_u"].rearrange("(a p) j -> p a j", p=128)
            bld.inc("gq", g.dma_start(out=ub[:, :].rearrange("p (a j) -> p a j", j=16), in_=usrc), 16)
            bld.inc("gq", g.dma_start(out=gsb[:, :], in_=dram["L0_ffn_gate"][None, :]), 16)

        def v0(v):
            bld.w(v, "vv")
            bld.inc("vv", v.memset(uTg[:, :], 0.0))
            bld.w(v, "vv")
            bld.inc("vv", v.memset(fbT[:, :], 0.0))
            bld.w(v, "vv")
            bld.inc("vv", v.memset(s1p[:, :], 0.0))
            bld.w(v, "vv")
            bld.inc("vv", v.memset(s2p[:, :], 0.0))
        bld.phase(gpsimd=g0, vector=v0)

        for c in range(DC):
            def mk(c):
                return lambda v: bld.inc("vv", v.tensor_copy(vT[:, c * 16:(c + 1) * 16], bld.PT[0:128, 0:16]))
            emit_small_transpose(bld, v16b[:, c * 128:(c + 1) * 128], 16, 128, mk(c))
        emit_small_transpose(bld, fbb[:, :], 16, 16,
                             lambda v: bld.inc("vv", v.tensor_copy(fbT[0:16, 0:16], bld.PT[0:16, 0:16])))
        for a in range(32):
            def mk2(a):
                return lambda v: bld.inc("vv", v.tensor_copy(uTg[0:16, a * 128:(a + 1) * 128], bld.PT[0:16, 0:128]))
            emit_small_transpose(bld, ub[:, a * 16:(a + 1) * 16], 128, 16, mk2(a))

        def tg(t):
            bld.w(t, "gq")
            bld.w(t, "vv")
            bld.w(t, "mm")
            bld.inc("mm", t.matmul(ps16[0:16, 0:1], bld.onesrow[:, 0:16], gsb[:, :], start=True, stop=True))

        def vg(v):
            bld.w(v, "mm")
            bld.w(v, "vv")
            bld.inc("vv", v.tensor_copy(gcol[:, :], ps16[0:16, 0:1]))
            bld.w(v, "vv")
            bld.inc("vv", v.tensor_scalar(uTg[0:16, :], uTg[0:16, :], gcol[0:16, :], None, ALU.mult))
        bld.phase(tensor=tg, vector=vg, order=("tensor", "vector"))

        for blk in range(NT // NTB):
            t0 = blk * NTB
            emit_ln_fm(bld, lns, resid, NT, t0, NTB, ln2, NTB, 0)
            if blk == 0:
                emit_dbg(bld, dram, "dbg_ln2", ln2[:, :])

            for (o, w) in tch(NTB):
                def ts1(t, o=o, w=w):
                    bld.w(t, "vv")
                    bld.w(t, "mm")
                    for c in range(DC):
                        bld.inc("mm", t.matmul(ps16[0:16, 0:w], vT[:, c * 16:(c + 1) * 16],
                                               ln2[:, c * NTB + o: c * NTB + o + w],
                                               start=(c == 0), stop=(c == DC - 1)))

                def vs1(v, o=o, w=w):
                    bld.w(v, "mm")
                    bld.w(v, "vv")
                    bld.inc("vv", v.tensor_copy(s1p[0:16, t0 + o: t0 + o + w], ps16[0:16, 0:w]))
                bld.phase(tensor=ts1, vector=vs1, order=("tensor", "vector"))

                def ts2(t, o=o, w=w):
                    bld.w(t, "vv")
                    bld.inc("mm", t.matmul(ps16[0:16, 0:w], fbT[:, 0:16], s1p[:, t0 + o: t0 + o + w],
                                           start=True, stop=True))

                def vs2(v, o=o, w=w):
                    bld.w(v, "mm")
                    bld.w(v, "vv")
                    bld.inc("vv", v.tensor_copy(s2p[0:16, t0 + o: t0 + o + w], ps16[0:16, 0:w]))
                bld.phase(tensor=ts2, vector=vs2, order=("tensor", "vector"))
            if blk == 0:
                emit_dbg(bld, dram, "dbg_s1", s1p[:, :])
                emit_dbg(bld, dram, "dbg_s2", s2p[:, :])

            for hc in range(HC):
                if blk == 0:
                    def sy(sy, hc=hc):
                        bld.w(sy, "mm")
                        bld.w(sy, "ss")
                        bld.w(sy, "vv")
                        bld.inc("sq", sy.dma_start(out=wt[:, :],
                                                   in_=dram["L0_ffn_w_fc"][hc * 128:(hc + 1) * 128, :]), 16)

                    def vp(v, hc=hc):
                        bld.w(v, "sq")
                        bld.w(v, "vv")
                        bld.inc("vv", v.tensor_reduce(red[:, :], wt[:, :], AX.X, ALU.add,
                                                      apply_absolute_value=True))
                        bld.w(v, "vv")
                        bld.inc("vv", v.tensor_scalar(hs[:, :], red[:, :], 0.5 / D, 0.5e-5, ALU.mult, ALU.max))
                        bld.w(v, "vv")
                        bld.inc("vv", v.tensor_scalar(scl[:, :], red[:, :], 1.0 / D, 1e-5, ALU.mult, ALU.max))
                        bld.w(v, "vv")
                        bld.inc("vv", v.tensor_tensor(hs2[:, :], hs[:, :], hs[:, :], ALU.mult))

                    def sp(s, hc=hc):
                        bld.w(s, "sq")
                        bld.w(s, "vv")
                        bld.w(s, "ss")
                        bld.inc("ss", s.activation(out=sgn[:, 0:1024], in_=wt[:, :], func=AF.Sign))
                        bld.w(s, "ss")
                        bld.inc("ss", s.activation(out=absq[:, 0:1024], in_=wt[:, :], func=AF.Square))
                    bld.phase(sync=sy, vector=vp, scalar=sp, order=("sync", "vector", "scalar"))

                    def vp2(v, hc=hc):
                        bld.w(v, "ss")
                        bld.w(v, "vv")
                        bld.inc("vv", v.tensor_scalar(absq[:, 0:1024], absq[:, 0:1024], hs2[:, :], None, ALU.subtract))
                        bld.w(v, "vv")
                        bld.inc("vv", v.tensor_scalar(weff[:, 0:1024], absq[:, 0:1024], 0.0, None, ALU.is_ge))
                        bld.w(v, "vv")
                        bld.inc("vv", v.tensor_tensor(weff[:, 0:1024], weff[:, 0:1024], sgn[:, 0:1024], ALU.mult))
                        bld.w(v, "vv")
                        bld.inc("vv", v.tensor_scalar(weff[:, 0:1024], weff[:, 0:1024], scl[:, :], None, ALU.mult))
                    bld.phase(vector=vp2, order=("vector",))

                    items = []
                    for dc in range(DC):
                        src = weff[:, dc * 128:(dc + 1) * 128]
                        dst = strip[:, dc * 128:(dc + 1) * 128]
                        ev = (lambda dst: (lambda v, pt: v.tensor_copy(dst, pt)))(dst)
                        items.append((src, 128, 128, ev))
                    emit_tpose_pipe(bld, items)

                    def sw(sy, hc=hc):
                        bld.w(sy, "vv")
                        bld.inc("sq", sy.dma_start(
                            out=dram["wfcT"].rearrange("(h x) c -> h x c", x=128)[hc], in_=strip[:, :]), 16)
                    bld.phase(sync=sw)
                    if hc == 0:
                        emit_dbg(bld, dram, "dbg_strip", strip[:, :])
                else:
                    def sr(sy, hc=hc):
                        bld.w(sy, "mm")
                        bld.inc("sq", sy.dma_start(
                            out=strip[:, :],
                            in_=dram["wfcT"].rearrange("(h x) c -> h x c", x=128)[hc]), 16)
                    bld.phase(sync=sr)

                for ti, (o, w) in enumerate(tch(NTB)):
                    pcur = ph if ti % 2 == 0 else ph2
                    def tf(t, hc=hc, o=o, w=w, pcur=pcur):
                        bld.w(t, "sq")
                        bld.w(t, "vv")
                        bld.w(t, "ss")
                        bld.w(t, "mm")
                        for c in range(DC):
                            bld.inc("mm", t.matmul(pcur[:, 0:w], strip[:, c * 128:(c + 1) * 128],
                                                   ln2[:, c * NTB + o: c * NTB + o + w],
                                                   start=(c == 0), stop=False))
                        bld.inc("mm", t.matmul(pcur[:, 0:w], uTg[:, hc * 128:(hc + 1) * 128],
                                               s2p[:, t0 + o: t0 + o + w], start=False, stop=True))

                    def sf(s, hc=hc, o=o, w=w, pcur=pcur):
                        bld.w(s, "mm")
                        bld.w(s, "vv")
                        bld.w(s, "ss")
                        bld.inc("ss", s.activation(out=rb[:, 0:w], in_=pcur[:, 0:w], func=AF.Relu))

                    def vf(v, hc=hc, o=o, w=w):
                        bld.w(v, "ss")
                        bld.w(v, "vv")
                        bld.inc("vv", v.tensor_tensor(r2T[:, hc * NTB + o: hc * NTB + o + w],
                                                      rb[:, 0:w], rb[:, 0:w], ALU.mult))
                    bld.phase(tensor=tf, scalar=sf, vector=vf, order=("tensor", "scalar", "vector"))
                if blk == 0 and hc == 0:
                    emit_dbg(bld, dram, "dbg_r2", r2T[:, 0:NTB])

            for dc in range(DC):
                if blk == 0:
                    def syp(sy, dc=dc):
                        bld.w(sy, "mm")
                        bld.w(sy, "ss")
                        bld.w(sy, "vv")
                        bld.inc("sq", sy.dma_start(out=wpt[:, :],
                                                   in_=dram["L0_ffn_w_proj"][dc * 128:(dc + 1) * 128, :]), 16)

                    def vpp(v, dc=dc):
                        bld.w(v, "sq")
                        bld.w(v, "vv")
                        bld.inc("vv", v.tensor_reduce(red[:, :], wpt[:, :], AX.X, ALU.add,
                                                      apply_absolute_value=True))
                        bld.w(v, "vv")
                        bld.inc("vv", v.tensor_scalar(hs[:, :], red[:, :], 0.5 / HID, 0.5e-5, ALU.mult, ALU.max))
                        bld.w(v, "vv")
                        bld.inc("vv", v.tensor_scalar(scl[:, :], red[:, :], 1.0 / HID, 1e-5, ALU.mult, ALU.max))
                        bld.w(v, "vv")
                        bld.inc("vv", v.tensor_tensor(hs2[:, :], hs[:, :], hs[:, :], ALU.mult))

                    def spp(s, dc=dc):
                        bld.w(s, "sq")
                        bld.w(s, "vv")
                        bld.w(s, "ss")
                        bld.inc("ss", s.activation(out=sgn[:, :], in_=wpt[:, :], func=AF.Sign))
                        bld.w(s, "ss")
                        bld.inc("ss", s.activation(out=absq[:, :], in_=wpt[:, :], func=AF.Square))
                    bld.phase(sync=syp, vector=vpp, scalar=spp, order=("sync", "vector", "scalar"))

                    def vpp2(v, dc=dc):
                        bld.w(v, "ss")
                        bld.w(v, "vv")
                        bld.inc("vv", v.tensor_scalar(absq[:, :], absq[:, :], hs2[:, :], None, ALU.subtract))
                        bld.w(v, "vv")
                        bld.inc("vv", v.tensor_scalar(weff[:, :], absq[:, :], 0.0, None, ALU.is_ge))
                        bld.w(v, "vv")
                        bld.inc("vv", v.tensor_tensor(weff[:, :], weff[:, :], sgn[:, :], ALU.mult))
                        bld.w(v, "vv")
                        bld.inc("vv", v.tensor_scalar(weff[:, :], weff[:, :], scl[:, :], None, ALU.mult))
                    bld.phase(vector=vpp2, order=("vector",))

                    items = []
                    for a in range(HC):
                        src = weff[:, a * 128:(a + 1) * 128]
                        dst = wpstrip[:, a * 128:(a + 1) * 128]
                        ev = (lambda dst: (lambda v, pt: v.tensor_copy(dst, pt)))(dst)
                        items.append((src, 128, 128, ev))
                    emit_tpose_pipe(bld, items)

                    def swp(sy, dc=dc):
                        bld.w(sy, "vv")
                        bld.inc("sq", sy.dma_start(
                            out=dram["wprojT"].rearrange("(d x) c -> d x c", x=128)[dc], in_=wpstrip[:, :]), 16)
                    bld.phase(sync=swp)
                    if dc == 0:
                        emit_dbg(bld, dram, "dbg_wps", wpstrip[:, :])
                else:
                    def srp(sy, dc=dc):
                        bld.w(sy, "mm")
                        bld.inc("sq", sy.dma_start(
                            out=wpstrip[:, :],
                            in_=dram["wprojT"].rearrange("(d x) c -> d x c", x=128)[dc]), 16)
                    bld.phase(sync=srp)

                for ti, (o, w) in enumerate(tch(NTB)):
                    npj = dc * len(tch(NTB)) + ti
                    pcur = po0 if npj % 2 == 0 else po1
                    if npj == 0:
                        pj_vv0 = bld.cnt["vv"]
                    pjm = pj_vv0
                    def tp(t, dc=dc, o=o, w=w, pcur=pcur, npj=npj, pjm=pjm):
                        bld.w(t, "sq")
                        if npj == 0:
                            bld.w(t, "vv")
                            bld.w(t, "mm")
                        elif npj >= 2:
                            t.wait_ge(bld.sem["vv"], pjm + npj - 1)
                        for a in range(HC):
                            bld.inc("mm", t.matmul(pcur[:, 0:w], wpstrip[:, a * 128:(a + 1) * 128],
                                                   r2T[:, a * NTB + o: a * NTB + o + w],
                                                   start=(a == 0), stop=(a == HC - 1)))

                    def vpj(v, dc=dc, o=o, w=w, pcur=pcur):
                        bld.w(v, "mm")
                        sl = resid[:, dc * NT + t0 + o: dc * NT + t0 + o + w]
                        bld.w(v, "vv")
                        bld.inc("vv", v.tensor_tensor(sl, sl, pcur[:, 0:w], ALU.add))
                    bld.phase(tensor=tp, vector=vpj, order=("tensor", "vector"))


def ffn_dram(nc, dram):
    dram["L0_ffn_w_fc"] = nc.dram_tensor("L0_ffn_w_fc", [HID, D], F32, kind="ExternalInput").ap()
    dram["L0_ffn_w_proj"] = nc.dram_tensor("L0_ffn_w_proj", [D, HID], F32, kind="ExternalInput").ap()
    dram["L0_ffn_u"] = nc.dram_tensor("L0_ffn_u", [HID, 16], F32, kind="ExternalInput").ap()
    dram["L0_ffn_v"] = nc.dram_tensor("L0_ffn_v", [16, D], F32, kind="ExternalInput").ap()
    dram["L0_ffn_fast_b"] = nc.dram_tensor("L0_ffn_fast_b", [16, 16], F32, kind="ExternalInput").ap()
    dram["L0_ffn_gate"] = nc.dram_tensor("L0_ffn_gate", [1], F32, kind="ExternalInput").ap()
    dram["wfcT"] = nc.dram_tensor("wfcT", [HC * 128, D], BF16, kind="Internal").ap()
    if DBG:
        dram["dbg_ln2"] = nc.dram_tensor("dbg_ln2", [128, DC * NTB], BF16, kind="ExternalOutput").ap()
        dram["dbg_s1"] = nc.dram_tensor("dbg_s1", [128, NT], BF16, kind="ExternalOutput").ap()
        dram["dbg_s2"] = nc.dram_tensor("dbg_s2", [128, NT], BF16, kind="ExternalOutput").ap()
        dram["dbg_strip"] = nc.dram_tensor("dbg_strip", [128, DC * 128], BF16, kind="ExternalOutput").ap()
        dram["dbg_r2"] = nc.dram_tensor("dbg_r2", [128, NTB], BF16, kind="ExternalOutput").ap()
        dram["dbg_wps"] = nc.dram_tensor("dbg_wps", [128, HC * 128], BF16, kind="ExternalOutput").ap()
    dram["wprojT"] = nc.dram_tensor("wprojT", [DC * 128, HC * 128], BF16, kind="Internal").ap()


def common_dram(nc, dram):
    dram["identf"] = nc.dram_tensor("identf", [128, 128], F32, kind="ExternalInput").ap()
    dram["ones_col"] = nc.dram_tensor("ones_col", [128, 1], F32, kind="ExternalInput").ap()
    dram["ones_row"] = nc.dram_tensor("ones_row", [1, 128], F32, kind="ExternalInput").ap()


def build_l0ffn_launch():
    nc = bass.Bass("TRN2", target_bir_lowering=False, debug=False)
    dram = {}
    dram["resid_in"] = nc.dram_tensor("resid_in", [DC, 128, NT], F32, kind="ExternalInput").ap()
    dram["resid_out"] = nc.dram_tensor("resid_out", [DC, 128, NT], F32, kind="ExternalOutput").ap()
    common_dram(nc, dram)
    ffn_dram(nc, dram)

    with ExitStack() as ctx:
        E = ctx.enter_context
        bld = Bld(nc, ctx)
        alloc_consts(bld, E)
        resid = E(nc.sbuf_tensor("resid", [128, DC * NT], F32))
        emit_load_consts(bld, dram)

        def ld(sy):
            for c in range(DC):
                bld.inc("sq", sy.dma_start(out=resid[:, c * NT:(c + 1) * NT], in_=dram["resid_in"][c]), 16)
        bld.phase(sync=ld)

        def bar(t):
            bld.w(t, "sq")
            bld.w(t, "gq")
        bld.phase(tensor=bar, order=("tensor",))

        emit_l0ffn(bld, resid, dram)

        def st(sy):
            bld.w(sy, "vv")
            bld.w(sy, "mm")
            bld.w(sy, "ss")
            for c in range(DC):
                bld.inc("sq", sy.dma_start(out=dram["resid_out"][c], in_=resid[:, c * NT:(c + 1) * NT]), 16)
            bld.w(sy, "sq")
        bld.phase(sync=st)
    return nc


def shard_inputs(inputs):
    x = np.ascontiguousarray(np.asarray(inputs["L0_attn_x"], np.float32))
    maps = []
    ident = np.eye(128, dtype=np.float32)
    ones_col = np.ones((128, 1), np.float32)
    ones_row = np.ones((1, 128), np.float32)
    for core in range(NCORES):
        b, half = core // 2, core % 2
        start = half * NOWN
        xr = np.zeros((NT, D), np.float32)
        lo = start - HALO
        if lo < 0:
            xr[HALO:, :] = x[b, start:start + NOWN, :]
        else:
            xr[:, :] = x[b, lo:start + NOWN, :]
        m = {
            "xrows": xr,
            "xfull": np.ascontiguousarray(x[b]),
            "identf": ident,
            "ones_col": ones_col,
            "ones_row": ones_row,
            "convmask": np.full((64, 1), 0.0 if half == 0 else 1.0, np.float32),
        }
        for k, v in inputs.items():
            if k == "L0_attn_x":
                continue
            a = np.asarray(v, np.float32)
            if a.ndim == 0:
                a = a.reshape(1)
            m[k] = np.ascontiguousarray(a)
        maps.append(m)
    return maps


def build_fused_launch():
    nc = bass.Bass("TRN2", target_bir_lowering=False, debug=False)
    dram = {}
    dram["resid_out"] = nc.dram_tensor("resid_out", [DC, 128, NT], F32, kind="ExternalOutput").ap()
    common_dram(nc, dram)
    attn_dram(nc, dram)
    ffn_dram(nc, dram)
    ssm_dram(nc, dram)
    moe_dram(nc, dram)
    dram["blockmask8"] = nc.dram_tensor("blockmask8", [128, 8], F32, kind="ExternalInput").ap()
    with ExitStack() as ctx:
        E = ctx.enter_context
        bld = Bld(nc, ctx)
        alloc_consts(bld, E)
        bld.msk8 = E(nc.sbuf_tensor("msk8", [128, 8], F32))
        resid = E(nc.sbuf_tensor("resid", [128, DC * NT], F32))
        emit_load_consts(bld, dram)

        def g1(g):
            bld.inc("gq", g.dma_start(out=bld.msk8[:, :], in_=dram["blockmask8"][:, :]), 16)
        bld.phase(gpsimd=g1)

        def bar(t):
            bld.w(t, "gq")
        bld.phase(tensor=bar, order=("tensor",))

        with ExitStack() as ctxa:
            QTall = ctxa.enter_context(nc.sbuf_tensor("QTall", [128, 8 * NT], BF16))
            emit_p0(bld, resid, QTall, dram)
            emit_l0attn(bld, resid, QTall, dram)
        emit_l0ffn(bld, resid, dram)
        emit_l1ssm(bld, resid, dram)
        emit_l1moe(bld, resid, dram)

        def st(sy):
            bld.w(sy, "vv")
            bld.w(sy, "mm")
            bld.w(sy, "ss")
            for c in range(DC):
                bld.inc("sq", sy.dma_start(out=dram["resid_out"][c], in_=resid[:, c * NT:(c + 1) * NT]), 16)
            bld.w(sy, "sq")
        bld.phase(sync=st)
    return nc


FUSED = True
_NC_CACHE = {}


def _get_nc(name, builder):
    if name not in _NC_CACHE:
        _NC_CACHE[name] = builder()
    return _NC_CACHE[name]


_ATTN_KEYS = ["xrows", "xfull", "L0_attn_Wq", "L0_attn_Wk", "L0_attn_Wv", "L0_attn_Wo",
              "L0_attn_quadrant_weights", "identf", "ones_col", "ones_row"]
_FFN_KEYS = ["L0_ffn_w_fc", "L0_ffn_w_proj", "L0_ffn_u", "L0_ffn_v", "L0_ffn_fast_b",
             "L0_ffn_gate", "identf", "ones_col", "ones_row"]
_SSM_KEYS = ["L1_attn_A", "L1_attn_Bm", "L1_attn_Cm", "L1_attn_Dm", "L1_attn_ln_w",
             "L1_attn_ln_b", "L1_attn_W_out", "L1_attn_b_out", "convmask", "blockmask8",
             "identf", "ones_col", "ones_row"]
_MOE_KEYS = ["L1_ffn_Wg", "L1_ffn_bg", "L1_ffn_bias", "L1_ffn_W1", "L1_ffn_b1",
             "L1_ffn_W2", "L1_ffn_b2", "identf", "ones_col", "ones_row"]


def kernel(**inputs):
    maps = shard_inputs(inputs)
    mask8 = np.zeros((128, 8), np.float32)
    for p in range(128):
        mask8[p, p // 16] = 1.0
    for m in maps:
        m["blockmask8"] = mask8

    cores = list(range(NCORES))

    if FUSED:
        keys = sorted(set(_ATTN_KEYS + _FFN_KEYS + _SSM_KEYS + _MOE_KEYS + ["blockmask8"]))
        nc = _get_nc("fused", build_fused_launch)
        res = run_bass_kernel_spmd(nc, [{k: m[k] for k in keys} for m in maps], core_ids=cores)
        resid = [r["resid_out"] for r in res.results]
        out = np.empty((4, 4096, D), np.float32)
        for core in range(NCORES):
            b, half = core // 2, core % 2
            start = half * NOWN
            r = np.asarray(resid[core], np.float32)
            blockrows = r.transpose(2, 0, 1).reshape(NT, D)
            out[b, start:start + NOWN, :] = blockrows[HALO:, :]
        return out

    nc_a = _get_nc("attn", build_l0attn_launch)
    res = run_bass_kernel_spmd(nc_a, [{k: m[k] for k in _ATTN_KEYS} for m in maps], core_ids=cores)
    resid = [r["resid_out"] for r in res.results]

    nc_b = _get_nc("ffn", build_l0ffn_launch)
    res = run_bass_kernel_spmd(
        nc_b, [dict({k: m[k] for k in _FFN_KEYS}, resid_in=resid[i]) for i, m in enumerate(maps)],
        core_ids=cores)
    resid = [r["resid_out"] for r in res.results]

    nc_c = _get_nc("ssm", build_l1ssm_launch)
    res = run_bass_kernel_spmd(
        nc_c, [dict({k: m[k] for k in _SSM_KEYS}, resid_in=resid[i]) for i, m in enumerate(maps)],
        core_ids=cores)
    resid = [r["resid_out"] for r in res.results]

    nc_d = _get_nc("moe", build_l1moe_launch)
    res = run_bass_kernel_spmd(
        nc_d, [dict({k: m[k] for k in _MOE_KEYS}, resid_in=resid[i]) for i, m in enumerate(maps)],
        core_ids=cores)
    resid = [r["resid_out"] for r in res.results]

    out = np.empty((4, 4096, D), np.float32)
    for core in range(NCORES):
        b, half = core // 2, core % 2
        start = half * NOWN
        r = np.asarray(resid[core], np.float32)
        blockrows = r.transpose(2, 0, 1).reshape(NT, D)
        out[b, start:start + NOWN, :] = blockrows[HALO:, :]
    return out


MOE_BLOCKS = [(0, 1152), (1152, 1024)]
NTBM = 1152
NE = 16


def emit_l1moe(bld, resid, dram):
    nc = bld.nc
    with ExitStack() as ctx:
        E = ctx.enter_context
        ln4 = E(nc.sbuf_tensor("ln4", [128, DC * NTBM], BF16))
        hT = E(nc.sbuf_tensor("hT", [128, NE * NTBM], BF16))
        W1b = E(nc.sbuf_tensor("W1b", [128, DC * 2048], BF16))
        W2d = [E(nc.sbuf_tensor("W2d0", [128, NE * 128], BF16)),
               E(nc.sbuf_tensor("W2d1", [128, NE * 128], BF16))]
        b1c = E(nc.sbuf_tensor("b1c", [128, NE * 16], F32))
        b2c = E(nc.sbuf_tensor("b2c", [128, NE * DC], F32))
        b1n = E(nc.sbuf_tensor("b1n", [16, 2048], F32))
        b2n = E(nc.sbuf_tensor("b2n", [16, 1024], F32))
        Wgb = E(nc.sbuf_tensor("Wgb", [128, DC * 16], BF16))
        bgr = E(nc.sbuf_tensor("bgr", [1, 16], F32))
        bir = E(nc.sbuf_tensor("bir", [1, 16], F32))
        onesw = E(nc.sbuf_tensor("onesw", [1, 512], F32))
        cw = E(nc.sbuf_tensor("cw", [128, 17 * 16], F32))
        cwT = E(nc.sbuf_tensor("cwT", [16, NT], BF16))
        cwrow = E(nc.sbuf_tensor("cwrow", [1, NT], BF16))
        cwbs = E(nc.sbuf_tensor("cwbs", [128, 3 * 512], BF16))
        t1b = E(nc.sbuf_tensor("t1b", [128, 512], F32))
        lgt = t1b
        tA = E(nc.sbuf_tensor("tA", [128, 16], F32))
        tB = E(nc.sbuf_tensor("tB", [128, 16], F32))
        mk1 = E(nc.sbuf_tensor("mk1", [128, 16], F32))
        mk2 = E(nc.sbuf_tensor("mk2", [128, 16], F32))
        m1 = E(nc.sbuf_tensor("m1c", [128, 1], F32))
        m2 = E(nc.sbuf_tensor("m2c", [128, 1], F32))
        e2 = E(nc.sbuf_tensor("e2c", [128, 1], F32))
        den = E(nc.sbuf_tensor("denc", [128, 1], F32))
        rec = E(nc.sbuf_tensor("recc", [128, 1], F32))
        w2c = E(nc.sbuf_tensor("w2cc", [128, 1], F32))
        lns4 = alloc_ln_scr(bld, E, "m", tmp=t1b)
        ph, ph2, po0, po1, pg = bld.P[0], bld.P[1], bld.P[2], bld.P[3], bld.P[4]

        def g0(g):
            wg = dram["L1_ffn_Wg"].rearrange("(c p) e -> p c e", p=128)
            bld.inc("gq", g.dma_start(out=Wgb[:, :].rearrange("p (c e) -> p c e", e=16), in_=wg), 16)
            bld.inc("gq", g.dma_start(out=bgr[:, :], in_=dram["L1_ffn_bg"][None, :]), 16)
            bld.inc("gq", g.dma_start(out=bir[:, :], in_=dram["L1_ffn_bias"][None, :]), 16)
            bld.inc("gq", g.dma_start(out=b1n[:, :], in_=dram["L1_ffn_b1"][:, :]), 16)
            bld.inc("gq", g.dma_start(out=b2n[:, :], in_=dram["L1_ffn_b2"][:, :]), 16)

        def v0(v):
            bld.V(v, lambda: v.memset(onesw[:, :], 1.0))
        bld.phase(gpsimd=g0, vector=v0)

        def v1(v):
            bld.w(v, "gq")
            bld.V(v, lambda: v.tensor_tensor(bgr[:, :], bgr[:, :], bir[:, :], ALU.add))
        bld.phase(vector=v1)

        for hc in range(NE):
            def tb1(t, hc=hc):
                bld.w(t, "gq")
                bld.w(t, "vv")
                bld.w(t, "mm")
                bld.inc("mm", t.matmul(bld.PTF[0:128, 0:16], b1n[:, hc * 128:(hc + 1) * 128],
                                       bld.identf[0:16, 0:16], start=True, stop=True, is_transpose=True))

            def vb1(v, hc=hc):
                bld.w(v, "mm")
                bld.V(v, lambda: v.tensor_copy(b1c[:, hc * 16:(hc + 1) * 16], bld.PTF[0:128, 0:16]))
            bld.phase(tensor=tb1, vector=vb1, order=("tensor", "vector"))
        for dc in range(DC):
            def tb2(t, dc=dc):
                bld.w(t, "gq")
                bld.w(t, "vv")
                bld.w(t, "mm")
                bld.inc("mm", t.matmul(bld.PTF[0:128, 0:16], b2n[:, dc * 128:(dc + 1) * 128],
                                       bld.identf[0:16, 0:16], start=True, stop=True, is_transpose=True))

            def vb2(v, dc=dc):
                bld.w(v, "mm")
                bld.V(v, lambda: v.tensor_copy(b2c[:, dc * 16:(dc + 1) * 16], bld.PTF[0:128, 0:16]))
            bld.phase(tensor=tb2, vector=vb2, order=("tensor", "vector"))

        for (t0, ntb) in MOE_BLOCKS:
            emit_ln_fm(bld, lns4, resid, NT, t0, ntb, ln4, NTBM, 0)
            if t0 == 0:
                emit_dbg(bld, dram, "dbg_ln4", ln4[:, :])

            for (o, w) in tch(ntb):
                def tl(t, o=o, w=w):
                    bld.w(t, "vv")
                    bld.w(t, "mm")
                    for c in range(DC):
                        bld.inc("mm", t.matmul(pg[0:16, 0:w], Wgb[:, c * 16:(c + 1) * 16],
                                               ln4[:, c * NTBM + o: c * NTBM + o + w],
                                               start=(c == 0), stop=False))
                    bld.w(t, "mm")
                    bld.inc("mm", t.matmul(pg[0:16, 0:w], bgr[:, :], onesw[:, 0:w],
                                           start=False, stop=True))

                def vl(v, o=o, w=w):
                    bld.w(v, "mm")
                    bld.V(v, lambda: v.tensor_copy(lgt[0:16, 0:w], pg[0:16, 0:w]))
                bld.phase(tensor=tl, vector=vl, order=("tensor", "vector"))
                if t0 == 0 and o == 0:
                    emit_dbg(bld, dram, "dbg_lgt", lgt[0:16, 0:512])

                for k in range(w // 128):
                    tile = (t0 + o) // 128 + k
                    def tt(t, k=k, tile=tile, w=w):
                        bld.w(t, "vv")
                        bld.w(t, "mm")
                        bld.inc("mm", t.matmul(bld.PTF[0:128, 0:16], lgt[0:16, k * 128:(k + 1) * 128],
                                               bld.identf[0:16, 0:16], start=True, stop=True,
                                               is_transpose=True))

                    def vt(v, k=k, tile=tile):
                        bld.w(v, "mm")
                        bld.V(v, lambda: v.tensor_copy(cw[:, tile * 16:(tile + 1) * 16], bld.PTF[0:128, 0:16]))
                    bld.phase(tensor=tt, vector=vt, order=("tensor", "vector"))

            nt_tiles = ntb // 128
            for k in range(nt_tiles):
                tile = t0 // 128 + k
                cwt = cw[:, tile * 16:(tile + 1) * 16]
                def vtop(v, cwt=cwt):
                    bld.V(v, lambda: v.tensor_reduce(m1[:, :], cwt, AX.X, ALU.max))
                    bld.V(v, lambda: v.tensor_scalar(tA[:, :], cwt, m1[:, :], None, ALU.subtract))
                    bld.V(v, lambda: v.tensor_scalar(mk1[:, :], tA[:, :], 0.0, None, ALU.is_ge))
                    bld.V(v, lambda: v.tensor_scalar(tB[:, :], mk1[:, :], -1e30, None, ALU.mult))
                    bld.V(v, lambda: v.tensor_tensor(tB[:, :], tB[:, :], cwt, ALU.add))
                    bld.V(v, lambda: v.tensor_reduce(m2[:, :], tB[:, :], AX.X, ALU.max))
                    bld.V(v, lambda: v.tensor_scalar(tB[:, :], tB[:, :], m2[:, :], None, ALU.subtract))
                    bld.V(v, lambda: v.tensor_scalar(mk2[:, :], tB[:, :], 0.0, None, ALU.is_ge))
                    bld.V(v, lambda: v.tensor_tensor(e2[:, :], m2[:, :], m1[:, :], ALU.subtract))

                def stop_(s):
                    bld.w(s, "vv")
                    bld.S(s, lambda: s.activation(out=e2[:, :], in_=e2[:, :], func=AF.Exp))

                def vtop2(v, cwt=cwt):
                    bld.w(v, "ss")
                    bld.V(v, lambda: v.tensor_scalar(den[:, :], e2[:, :], 1.0, None, ALU.add))
                    bld.V(v, lambda: v.reciprocal(rec[:, :], den[:, :]))
                    bld.V(v, lambda: v.tensor_tensor(w2c[:, :], e2[:, :], rec[:, :], ALU.mult))
                    bld.V(v, lambda: v.tensor_scalar(mk1[:, :], mk1[:, :], rec[:, :], None, ALU.mult))
                    bld.V(v, lambda: v.scalar_tensor_tensor(cwt, mk2[:, :], w2c[:, :], mk1[:, :],
                                                            ALU.mult, ALU.add))
                bld.phase(vector=vtop, order=("vector",))
                bld.phase(scalar=stop_, order=("scalar",))
                bld.phase(vector=vtop2, order=("vector",))

                def tct(t, tile=tile):
                    bld.w(t, "vv")
                    bld.w(t, "mm")
                    bld.inc("mm", t.matmul(bld.PTF[0:16, 0:128], cw[:, tile * 16:(tile + 1) * 16],
                                           bld.identf[:, :], start=True, stop=True, is_transpose=True))

                def vct(v, tile=tile):
                    bld.w(v, "mm")
                    bld.V(v, lambda: v.tensor_copy(cwT[0:16, tile * 128:(tile + 1) * 128], bld.PTF[0:16, 0:128]))
                bld.phase(tensor=tct, vector=vct, order=("tensor", "vector"))

            def emit_dma_w1(e):
                def g(g, e=e):
                    bld.w(g, "mm")
                    bld.w(g, "vv")
                    for dcn in range(DC):
                        bld.inc("gq", g.dma_start(out=W1b[:, dcn * 2048:(dcn + 1) * 2048],
                                                  in_=dram["L1_ffn_W1"][e, dcn * 128:(dcn + 1) * 128, :]), 16)
                bld.phase(gpsimd=g)
                bld.gq_marks[("w1", e)] = bld.cnt["gq"]

            def emit_dma_w2(e, dc):
                buf = W2d[dc % 2]
                def g(g, e=e, dc=dc, buf=buf):
                    bld.w(g, "mm")
                    src = dram["L1_ffn_W2"][e].rearrange("(hcn p) c -> p hcn c", p=128)[:, :, dc * 128:(dc + 1) * 128]
                    dst = buf[:, :].rearrange("p (hcn c) -> p hcn c", c=128)
                    bld.inc("gq", g.dma_start(out=dst[:, 0:8, :], in_=src[:, 0:8, :]), 16)
                    bld.inc("gq", g.dma_start(out=dst[:, 8:16, :], in_=src[:, 8:16, :]), 16)
                bld.phase(gpsimd=g)
                bld.gq_marks[("w2", e, dc)] = bld.cnt["gq"]

            if t0 == 0:
                emit_dbg(bld, dram, "dbg_cw", cw[:, :])
                emit_dbg(bld, dram, "dbg_cwT", cwT[:, :])
            emit_dma_w1(0)
            for e in range(NE):
                for hc in range(NE):
                    for ti, (o, w) in enumerate(tch(ntb)):
                        pcur = ph if ti % 2 == 0 else ph2
                        def tf(t, e=e, hc=hc, o=o, w=w, pcur=pcur):
                            t.wait_ge(bld.sem["gq"], bld.gq_marks[("w1", e)])
                            bld.w(t, "vv")
                            bld.w(t, "ss")
                            bld.w(t, "mm")
                            for c in range(DC):
                                bld.inc("mm", t.matmul(pcur[:, 0:w],
                                                       W1b[:, c * 2048 + hc * 128: c * 2048 + (hc + 1) * 128],
                                                       ln4[:, c * NTBM + o: c * NTBM + o + w],
                                                       start=(c == 0), stop=(c == DC - 1)))

                        def sf(s, e=e, hc=hc, o=o, w=w, pcur=pcur):
                            bld.w(s, "mm")
                            bld.w(s, "vv")
                            bld.S(s, lambda: s.activation(out=hT[:, hc * NTBM + o: hc * NTBM + o + w],
                                                          in_=pcur[:, 0:w], func=AF.Relu,
                                                          bias=b1c[:, hc * 16 + e: hc * 16 + e + 1]))
                        bld.phase(tensor=tf, scalar=sf, order=("tensor", "scalar"))
                if t0 == 0 and e == 0:
                    emit_dbg(bld, dram, "dbg_hT", hT[:, 0:NTBM])
                if e + 1 < NE:
                    emit_dma_w1(e + 1)

                def gcw(g, e=e):
                    bld.w(g, "mm")
                    bld.inc("gq", g.dma_start(out=cwrow[:, :], in_=cwT[e:e + 1, :]), 16)
                bld.phase(gpsimd=gcw)
                bld.gq_marks[("cwr", e, t0)] = bld.cnt["gq"]
                for ti, (o, w) in enumerate(tch(ntb)):
                    def tb_(t, e=e, ti=ti, o=o, w=w):
                        t.wait_ge(bld.sem["gq"], bld.gq_marks[("cwr", e, t0)])
                        bld.w(t, "vv")
                        bld.w(t, "mm")
                        bld.inc("mm", t.matmul(pg[:, 0:w], bld.onesrow_b[:, :],
                                               cwrow[:, t0 + o: t0 + o + w], start=True, stop=True))

                    def vb(v, ti=ti, o=o, w=w):
                        bld.w(v, "mm")
                        bld.V(v, lambda: v.tensor_copy(cwbs[:, ti * 512: ti * 512 + w], pg[:, 0:w]))
                    bld.phase(tensor=tb_, vector=vb, order=("tensor", "vector"))

                emit_dma_w2(e, 0)
                emit_dma_w2(e, 1)
                for dc in range(DC):
                    buf = W2d[dc % 2]
                    for ti, (o, w) in enumerate(tch(ntb)):
                        pcur = po0 if ti % 2 == 0 else po1
                        def tw2(t, e=e, dc=dc, ti=ti, o=o, w=w, pcur=pcur, buf=buf):
                            t.wait_ge(bld.sem["gq"], bld.gq_marks[("w2", e, dc)])
                            bld.w(t, "vv")
                            bld.w(t, "ss")
                            bld.w(t, "mm")
                            for hcn in range(NE):
                                bld.inc("mm", t.matmul(pcur[:, 0:w],
                                                       buf[:, hcn * 128:(hcn + 1) * 128],
                                                       hT[:, hcn * NTBM + o: hcn * NTBM + o + w],
                                                       start=(hcn == 0), stop=(hcn == NE - 1)))

                        def vw2(v, e=e, dc=dc, ti=ti, o=o, w=w, pcur=pcur):
                            bld.w(v, "mm")
                            bld.V(v, lambda: v.scalar_tensor_tensor(
                                t1b[:, 0:w], pcur[:, 0:w], b2c[:, dc * 16 + e: dc * 16 + e + 1],
                                cwbs[:, ti * 512: ti * 512 + w], ALU.add, ALU.mult))
                            sl = resid[:, dc * NT + t0 + o: dc * NT + t0 + o + w]
                            bld.V(v, lambda: v.tensor_tensor(sl, sl, t1b[:, 0:w], ALU.add))
                        bld.phase(tensor=tw2, vector=vw2, order=("tensor", "vector"))
                    if dc + 2 < DC:
                        emit_dma_w2(e, dc + 2)
                    elif e + 1 < NE:
                        emit_dma_w2(e + 1, dc - 6)
                if t0 == 0 and e == 0:
                    emit_dbg(bld, dram, "dbg_res0", resid[:, 0:512])
                    emit_dbg(bld, dram, "dbg_cwbs", cwbs[:, 0:512])


def moe_dram(nc, dram):
    dram["L1_ffn_Wg"] = nc.dram_tensor("L1_ffn_Wg", [D, 16], F32, kind="ExternalInput").ap()
    dram["L1_ffn_bg"] = nc.dram_tensor("L1_ffn_bg", [16], F32, kind="ExternalInput").ap()
    dram["L1_ffn_bias"] = nc.dram_tensor("L1_ffn_bias", [16], F32, kind="ExternalInput").ap()
    dram["L1_ffn_W1"] = nc.dram_tensor("L1_ffn_W1", [16, D, 2048], F32, kind="ExternalInput").ap()
    dram["L1_ffn_b1"] = nc.dram_tensor("L1_ffn_b1", [16, 2048], F32, kind="ExternalInput").ap()
    dram["L1_ffn_W2"] = nc.dram_tensor("L1_ffn_W2", [16, 2048, D], F32, kind="ExternalInput").ap()
    dram["L1_ffn_b2"] = nc.dram_tensor("L1_ffn_b2", [16, D], F32, kind="ExternalInput").ap()


def build_l1moe_launch():
    nc = bass.Bass("TRN2", target_bir_lowering=False, debug=False)
    dram = {}
    dram["resid_in"] = nc.dram_tensor("resid_in", [DC, 128, NT], F32, kind="ExternalInput").ap()
    dram["resid_out"] = nc.dram_tensor("resid_out", [DC, 128, NT], F32, kind="ExternalOutput").ap()
    common_dram(nc, dram)
    moe_dram(nc, dram)
    if DBG:
        dram["dbg_ln4"] = nc.dram_tensor("dbg_ln4", [128, DC * NTBM], BF16, kind="ExternalOutput").ap()
        dram["dbg_cw"] = nc.dram_tensor("dbg_cw", [128, 17 * 16], F32, kind="ExternalOutput").ap()
        dram["dbg_cwT"] = nc.dram_tensor("dbg_cwT", [16, NT], BF16, kind="ExternalOutput").ap()
        dram["dbg_hT"] = nc.dram_tensor("dbg_hT", [128, NTBM], BF16, kind="ExternalOutput").ap()
        dram["dbg_lgt"] = nc.dram_tensor("dbg_lgt", [16, 512], F32, kind="ExternalOutput").ap()
        dram["dbg_res0"] = nc.dram_tensor("dbg_res0", [128, 512], F32, kind="ExternalOutput").ap()
        dram["dbg_cwbs"] = nc.dram_tensor("dbg_cwbs", [128, 512], BF16, kind="ExternalOutput").ap()
    with ExitStack() as ctx:
        E = ctx.enter_context
        bld = Bld(nc, ctx)
        alloc_consts(bld, E)
        resid = E(nc.sbuf_tensor("resid", [128, DC * NT], F32))
        emit_load_consts(bld, dram)

        def ld(sy):
            for c in range(DC):
                bld.inc("sq", sy.dma_start(out=resid[:, c * NT:(c + 1) * NT], in_=dram["resid_in"][c]), 16)
        bld.phase(sync=ld)

        def bar(t):
            bld.w(t, "sq")
            bld.w(t, "gq")
        bld.phase(tensor=bar, order=("tensor",))

        emit_l1moe(bld, resid, dram)

        def st(sy):
            bld.w(sy, "vv")
            bld.w(sy, "mm")
            bld.w(sy, "ss")
            for c in range(DC):
                bld.inc("sq", sy.dma_start(out=dram["resid_out"][c], in_=resid[:, c * NT:(c + 1) * NT]), 16)
            bld.w(sy, "sq")
        bld.phase(sync=st)
    return nc


NKT = 4096


ATTN_PAIRS = 8
ATTN_QS = None


def emit_l0attn(bld, resid, QTall, dram):
    nc = bld.nc
    with ExitStack() as ctx:
        E = ctx.enter_context
        wbc = E(nc.sbuf_tensor("wbc", [128, 4], F32))
        q4 = E(nc.sbuf_tensor("q4", [1, 4], F32))
        qm = E(nc.sbuf_tensor("qm", [1, 1], F32))
        Wqs = E(nc.sbuf_tensor("Wqs", [128, DC * 128], BF16))
        Wks = E(nc.sbuf_tensor("Wks", [128, DC * 128], BF16))
        Wke = E(nc.sbuf_tensor("Wke", [128, DC * 128], BF16))
        Wvs = E(nc.sbuf_tensor("Wvs", [128, DC * 128], BF16))
        Wos = E(nc.sbuf_tensor("Wos", [128, DC * 128], BF16))
        tm32 = E(nc.sbuf_tensor("tm32", [128, 32], F32))
        lnsl = [E(nc.sbuf_tensor("lnsl0", [128, DC * 512], BF16)),
                E(nc.sbuf_tensor("lnsl1", [128, DC * 512], BF16))]
        KT = E(nc.sbuf_tensor("KT", [128, NKT], BF16))
        Vp = E(nc.sbuf_tensor("Vp", [128, 32 * 130], BF16))
        PA = E(nc.sbuf_tensor("PA", [128, 32 * 128], BF16))
        PB = E(nc.sbuf_tensor("PB", [128, 32 * 128], BF16))
        OT = E(nc.sbuf_tensor("OT", [128, NT], BF16))
        recA = E(nc.sbuf_tensor("recA", [128, 1], F32))
        recB = E(nc.sbuf_tensor("recB", [128, 1], F32))
        psSA = bld.PW0
        psSB = bld.PW1
        pOA, pOB, pKV = bld.P[0], bld.P[1], bld.P[2]
        pKV2 = bld.P[3]

        def g0(g):
            bld.inc("gq", g.dma_start(out=q4[:, :], in_=dram["L0_attn_quadrant_weights"][None, :]), 16)

        def v0(v):
            bld.w(v, "gq")
            bld.V(v, lambda: v.tensor_reduce(qm[:, :], q4[:, :], AX.X, ALU.max))
            bld.V(v, lambda: v.tensor_scalar(q4[:, :], q4[:, :], qm[:, :], None, ALU.subtract))
        bld.phase(gpsimd=g0, vector=v0)

        def s0(s):
            bld.w(s, "vv")
            bld.S(s, lambda: s.activation(out=q4[:, :], in_=q4[:, :], func=AF.Exp))
        bld.phase(scalar=s0)

        def v1(v):
            bld.w(v, "ss")
            bld.V(v, lambda: v.tensor_reduce(qm[:, :], q4[:, :], AX.X, ALU.add))
            bld.V(v, lambda: v.reciprocal(qm[:, :], qm[:, :]))
            bld.V(v, lambda: v.tensor_scalar(q4[:, :], q4[:, :], qm[:, :], None, ALU.mult))
            bld.V(v, lambda: v.tensor_scalar(q4[:, :], q4[:, :], float(1.0 / np.sqrt(32.0)), None, ALU.mult))
        bld.phase(vector=v1)

        def t1(t):
            bld.w(t, "vv")
            bld.w(t, "mm")
            bld.inc("mm", t.matmul(bld.PTF[:, 0:4], bld.onesrow[:, :], q4[:, :], start=True, stop=True))

        def v2(v):
            bld.w(v, "mm")
            bld.V(v, lambda: v.tensor_copy(wbc[:, :], bld.PTF[:, 0:4]))
        bld.phase(tensor=t1, vector=v2, order=("tensor", "vector"))

        for p in range(ATTN_PAIRS):
            pc = slice(p * 128, (p + 1) * 128)

            def gw(g, p=p):
                bld.w(g, "mm")
                bld.w(g, "vv")
                for name, dst in [("L0_attn_Wq", Wqs), ("L0_attn_Wk", Wks), ("L0_attn_Wv", Wvs)]:
                    for c in range(DC):
                        bld.inc("gq", g.dma_start(out=dst[:, c * 128:(c + 1) * 128],
                                                  in_=dram[name][c * 128:(c + 1) * 128, p * 128:(p + 1) * 128]), 16)
                bld.inc("gq", g.dma_start(out=Wos[:, :], in_=dram["L0_attn_Wo"][p * 128:(p + 1) * 128, :]), 16)
            bld.phase(gpsimd=gw)

            def vmix(v, p=p):
                bld.w(v, "gq")
                for c in range(DC):
                    for h2 in range(2):
                        base = c * 128 + h2 * 64
                        k1 = Wks[:, base:base + 32]
                        k2 = Wks[:, base + 32:base + 64]
                        bld.V(v, lambda k2=k2: v.tensor_scalar(tm32[:, :], k2, wbc[:, 1:2], None, ALU.mult))
                        bld.V(v, lambda k1=k1, base=base: v.scalar_tensor_tensor(
                            Wke[:, base:base + 32], k1, wbc[:, 0:1], tm32[:, :], ALU.mult, ALU.add))
                        bld.V(v, lambda k2=k2: v.tensor_scalar(tm32[:, :], k2, wbc[:, 3:4], None, ALU.mult))
                        bld.V(v, lambda k1=k1, base=base: v.scalar_tensor_tensor(
                            Wke[:, base + 32:base + 64], k1, wbc[:, 2:3], tm32[:, :], ALU.mult, ALU.add))
            bld.phase(vector=vmix)

            for tc in range(8):
                buf = lnsl[tc % 2]
                def gl(g, tc=tc, buf=buf):
                    bld.w(g, "mm")
                    for c in range(DC):
                        bld.inc("gq", g.dma_start(
                            out=buf[:, c * 512:(c + 1) * 512],
                            in_=dram["ln1s"].rearrange("(c x) t -> c x t", x=128)[c, :, tc * 512:(tc + 1) * 512]), 16)
                bld.phase(gpsimd=gl)
                gqm = bld.cnt["gq"]

                vvb = bld.cnt["vv"]
                def tkv(t, p=p, tc=tc, buf=buf, gqm=gqm, vvb=vvb):
                    t.wait_ge(bld.sem["gq"], gqm)
                    bld.w(t, "vv")
                    bld.w(t, "mm")
                    for c in range(DC):
                        bld.inc("mm", t.matmul(pKV[:, 0:512], Wke[:, c * 128:(c + 1) * 128],
                                               buf[:, c * 512:(c + 1) * 512],
                                               start=(c == 0), stop=(c == DC - 1)))
                    for tt_ in range(4):
                        if tt_ > 0:
                            t.wait_ge(bld.sem["vv"], vvb + 1 + 2 * tt_)
                        for c in range(DC):
                            bld.inc("mm", t.matmul(pKV2[:, 0:128],
                                                   buf[:, c * 512 + tt_ * 128: c * 512 + tt_ * 128 + 128],
                                                   Wvs[:, c * 128:(c + 1) * 128],
                                                   start=(c == 0), stop=(c == DC - 1)))

                def vkv(v, p=p, tc=tc):
                    mmb = bld.cnt["mm"] - 40
                    v.wait_ge(bld.sem["mm"], mmb + 8)
                    bld.inc("vv", v.tensor_copy(KT[:, tc * 512:(tc + 1) * 512], pKV[:, 0:512]))
                    for tt_ in range(4):
                        v.wait_ge(bld.sem["mm"], mmb + 8 + (tt_ + 1) * 8)
                        tile = tc * 4 + tt_
                        bld.w(v, "vv")
                        bld.inc("vv", v.tensor_copy(Vp[:, tile * 130: tile * 130 + 64], pKV2[0:128, 0:64]))
                        bld.w(v, "vv")
                        bld.inc("vv", v.tensor_copy(Vp[:, tile * 130 + 65: tile * 130 + 129], pKV2[0:128, 64:128]))
                bld.phase(tensor=tkv, vector=vkv, order=("tensor", "vector"))

            def vones(v, p=p):
                bld.w(v, "mm")
                bld.V(v, lambda: v.memset(Vp[:, :].rearrange("x (a b) -> x a b", b=65)[:, :, 64], 1.0))
            if p == 0:
                bld.phase(vector=vones)

            for qs in range(NT // 128 if ATTN_QS is None else ATTN_QS):
                for g4 in range(4):
                    def tS(t, p=p, qs=qs, g4=g4):
                        if g4 == 0:
                            bld.w(t, "mm")
                        bld.w(t, "ss")
                        for kk in range(8):
                            kc = g4 * 8 + kk
                            bld.inc("mm", t.matmul(psSA[:, kk * 128:(kk + 1) * 128],
                                                   KT[0:64, kc * 128:(kc + 1) * 128],
                                                   QTall[0:64, p * NT + qs * 128: p * NT + (qs + 1) * 128],
                                                   start=True, stop=True, tile_position=(0, 0)))
                            bld.inc("mm", t.matmul(psSB[:, kk * 128:(kk + 1) * 128],
                                                   KT[64:128, kc * 128:(kc + 1) * 128],
                                                   QTall[64:128, p * NT + qs * 128: p * NT + (qs + 1) * 128],
                                                   start=True, stop=True, tile_position=(64, 0)))

                    def sS(s, p=p, qs=qs, g4=g4):
                        bld.w(s, "mm")
                        bld.S(s, lambda: s.activation(out=PA[:, g4 * 1024:(g4 + 1) * 1024],
                                                      in_=psSA[:, :], func=AF.Exp))
                        bld.S(s, lambda: s.activation(out=PB[:, g4 * 1024:(g4 + 1) * 1024],
                                                      in_=psSB[:, :], func=AF.Exp))
                    bld.phase(tensor=tS, scalar=sS, order=("tensor", "scalar"))

                def tAV(t, p=p, qs=qs):
                    bld.w(t, "mm")
                    bld.w(t, "ss")
                    bld.w(t, "vv")
                    for kc in range(32):
                        bld.inc("mm", t.matmul(pOA[:, 0:65], PA[:, kc * 128:(kc + 1) * 128],
                                               Vp[:, kc * 130: kc * 130 + 65],
                                               start=(kc == 0), stop=(kc == 31)))
                        bld.inc("mm", t.matmul(pOB[:, 0:65], PB[:, kc * 128:(kc + 1) * 128],
                                               Vp[:, kc * 130 + 65: kc * 130 + 130],
                                               start=(kc == 0), stop=(kc == 31)))

                def vAV(v, p=p, qs=qs):
                    bld.w(v, "mm")
                    bld.V(v, lambda: v.reciprocal(recA[:, :], pOA[:, 64:65]))
                    bld.V(v, lambda: v.reciprocal(recB[:, :], pOB[:, 64:65]))
                    bld.V(v, lambda: v.tensor_scalar(pOA[:, 0:64], pOA[:, 0:64], recA[:, :], None, ALU.mult))
                    bld.V(v, lambda: v.tensor_scalar(pOB[:, 0:64], pOB[:, 0:64], recB[:, :], None, ALU.mult))
                bld.phase(tensor=tAV, vector=vAV, order=("tensor", "vector"))

                def vO(v, p=p, qs=qs):
                    bld.w(v, "mm")
                    bld.V(v, lambda: v.tensor_copy(lnsl[0][:, 0:64], pOA[:, 0:64]))
                    bld.V(v, lambda: v.tensor_copy(lnsl[0][:, 64:128], pOB[:, 0:64]))
                bld.phase(vector=vO)

                for h2 in range(2):
                    def tT(t, h2=h2, qs=qs):
                        bld.w(t, "vv")
                        bld.w(t, "mm")
                        bld.inc("mm", t.matmul(bld.PT[0:64, 0:128], lnsl[0][:, h2 * 64:(h2 + 1) * 64],
                                               bld.identb[:, :], start=True, stop=True, is_transpose=True))

                    def vT(v, h2=h2, qs=qs):
                        bld.w(v, "mm")
                        bld.V(v, lambda: v.tensor_copy(OT[h2 * 64:(h2 + 1) * 64, qs * 128:(qs + 1) * 128],
                                                       bld.PT[0:64, 0:128]))
                    bld.phase(tensor=tT, vector=vT, order=("tensor", "vector"))

            for dc in range(DC):
                for ti, (o, w) in enumerate(tch(NT)):
                    pcur = bld.P[4] if ti % 2 == 0 else bld.P[5]
                    def tP(t, p=p, dc=dc, o=o, w=w, pcur=pcur):
                        bld.w(t, "vv")
                        bld.w(t, "mm")
                        bld.inc("mm", t.matmul(pcur[:, 0:w], Wos[:, dc * 128:(dc + 1) * 128],
                                               OT[:, o:o + w], start=True, stop=True))

                    def vP(v, p=p, dc=dc, o=o, w=w, pcur=pcur):
                        bld.w(v, "mm")
                        sl = resid[:, dc * NT + o: dc * NT + o + w]
                        bld.V(v, lambda: v.tensor_tensor(sl, sl, pcur[:, 0:w], ALU.add))
                    bld.phase(tensor=tP, vector=vP, order=("tensor", "vector"))


def emit_p0(bld, resid, QTall, dram):
    nc = bld.nc
    with ExitStack() as ctx:
        E = ctx.enter_context
        xt = [E(nc.sbuf_tensor("xt0", [128, 1024], F32)), E(nc.sbuf_tensor("xt1", [128, 1024], F32))]
        xc = E(nc.sbuf_tensor("xc", [128, 1024], F32))
        lnb = E(nc.sbuf_tensor("lnb", [128, 1024], BF16))
        lnf = E(nc.sbuf_tensor("lnf", [128, DC * 128], BF16))
        mcol = E(nc.sbuf_tensor("mcol", [128, 1], F32))
        scol2 = E(nc.sbuf_tensor("scol2", [128, 1], F32))
        sqd = E(nc.sbuf_tensor("sqd", [128, 1024], BF16))


        def ln_tile(src_ap, ti, into_resid, into_qt, into_scratch):
            buf = xt[ti % 2]
            def gld(g):
                bld.w(g, "mm")
                bld.w(g, "vv")
                bld.w(g, "ss")
                bld.inc("gq", g.dma_start(out=buf[:, :], in_=src_ap), 16)
            bld.phase(gpsimd=gld)
            gqm = bld.cnt["gq"]

            def vln(v):
                v.wait_ge(bld.sem["gq"], gqm)
                bld.V(v, lambda: v.tensor_reduce(mcol[:, :], buf[:, :], AX.X, ALU.add))
                bld.V(v, lambda: v.tensor_scalar(mcol[:, :], mcol[:, :], 1.0 / D, None, ALU.mult))
                bld.V(v, lambda: v.tensor_scalar(xc[:, :], buf[:, :], mcol[:, :], None, ALU.subtract))

            def sln(s):
                bld.w(s, "vv")
                bld.S(s, lambda: s.activation(out=sqd[:, :], in_=xc[:, :], func=AF.Square,
                                              accum_out=scol2[:, :]))
            bld.phase(vector=vln, scalar=sln, order=("vector", "scalar"))

            def vln2(v):
                bld.w(v, "ss")
                bld.V(v, lambda: v.tensor_scalar(scol2[:, :], scol2[:, :], 1.0 / D, EPS, ALU.mult, ALU.add))

            def sln2(s):
                bld.w(s, "vv")
                bld.S(s, lambda: s.activation(out=scol2[:, :], in_=scol2[:, :], func=AF.Ln))
                bld.S(s, lambda: s.activation(out=scol2[:, :], in_=scol2[:, :], func=AF.Exp, scale=-0.5))
            bld.phase(vector=vln2, order=("vector",))
            bld.phase(scalar=sln2, order=("scalar",))

            def vln3(v):
                bld.w(v, "ss")
                bld.V(v, lambda: v.tensor_scalar(lnb[:, :], xc[:, :], scol2[:, :], None, ALU.mult))
            bld.phase(vector=vln3, order=("vector",))

            items = []
            for c in range(DC):
                src = lnb[:, c * 128:(c + 1) * 128]
                dst = lnf[:, c * 128:(c + 1) * 128]
                ev = (lambda dst: (lambda v, pt: v.tensor_copy(dst, pt)))(dst)
                items.append((src, 128, 128, ev))
            emit_tpose_pipe(bld, items)

            if into_resid is not None:
                items = []
                for c in range(DC):
                    src = buf[:, c * 128:(c + 1) * 128]
                    dst = into_resid[c]
                    ev = (lambda dst: (lambda v, pt: v.tensor_copy(dst, pt)))(dst)
                    items.append((src, 128, 128, ev))
                emit_tpose_pipe(bld, items, f32=True)

            if into_scratch is not None:
                def sst(sy):
                    bld.w(sy, "vv")
                    for c in range(DC):
                        bld.inc("sq", sy.dma_start(out=into_scratch[c], in_=lnf[:, c * 128:(c + 1) * 128]), 16)
                bld.phase(sync=sst)


        QWALL = E(nc.sbuf_tensor("QWALL", [128, 64 * 128], BF16))

        def gq0(g):
            for c in range(DC):
                bld.inc("gq", g.dma_start(out=QWALL[:, c * 1024:(c + 1) * 1024],
                                          in_=dram["L0_attn_Wq"][c * 128:(c + 1) * 128, :]), 16)
        bld.phase(gpsimd=gq0)

        for ti in range(NT // 128):
            ln_tile(dram["xrows"][ti * 128:(ti + 1) * 128, :], ti,
                    [resid[:, c * NT + ti * 128: c * NT + (ti + 1) * 128] for c in range(DC)],
                    False, None)
            for p in range(8):
                def tq(t, p=p, ti=ti):
                    bld.w(t, "vv")
                    bld.w(t, "mm")
                    bld.w(t, "gq")
                    for c in range(DC):
                        bld.inc("mm", t.matmul(bld.P[0][:, 0:128],
                                               QWALL[:, c * 1024 + p * 128: c * 1024 + (p + 1) * 128],
                                               lnf[:, c * 128:(c + 1) * 128],
                                               start=(c == 0), stop=(c == DC - 1)))

                def vq(v, p=p, ti=ti):
                    bld.w(v, "mm")
                    bld.V(v, lambda: v.tensor_copy(QTall[:, p * NT + ti * 128:(p * NT + (ti + 1) * 128)],
                                                   bld.P[0][:, 0:128]))
                bld.phase(tensor=tq, vector=vq, order=("tensor", "vector"))

        for ti in range(NKT // 128):
            dst = dram["ln1s"].rearrange("(c x) t -> c x t", x=128)[:, :, ti * 128:(ti + 1) * 128]
            ln_tile(dram["xfull"][ti * 128:(ti + 1) * 128, :], ti, None, False, dst)


def attn_dram(nc, dram):
    dram["L0_attn_Wq"] = nc.dram_tensor("L0_attn_Wq", [D, D], F32, kind="ExternalInput").ap()
    dram["L0_attn_Wk"] = nc.dram_tensor("L0_attn_Wk", [D, D], F32, kind="ExternalInput").ap()
    dram["L0_attn_Wv"] = nc.dram_tensor("L0_attn_Wv", [D, D], F32, kind="ExternalInput").ap()
    dram["L0_attn_Wo"] = nc.dram_tensor("L0_attn_Wo", [D, D], F32, kind="ExternalInput").ap()
    dram["L0_attn_quadrant_weights"] = nc.dram_tensor("L0_attn_quadrant_weights", [4], F32,
                                                      kind="ExternalInput").ap()
    dram["xrows"] = nc.dram_tensor("xrows", [NT, D], F32, kind="ExternalInput").ap()
    dram["xfull"] = nc.dram_tensor("xfull", [NKT, D], F32, kind="ExternalInput").ap()
    dram["ln1s"] = nc.dram_tensor("ln1s", [DC * 128, NKT], BF16, kind="Internal").ap()


def build_l0attn_launch():
    nc = bass.Bass("TRN2", target_bir_lowering=False, debug=False)
    dram = {}
    dram["resid_out"] = nc.dram_tensor("resid_out", [DC, 128, NT], F32, kind="ExternalOutput").ap()
    common_dram(nc, dram)
    attn_dram(nc, dram)
    with ExitStack() as ctx:
        E = ctx.enter_context
        bld = Bld(nc, ctx)
        alloc_consts(bld, E)
        resid = E(nc.sbuf_tensor("resid", [128, DC * NT], F32))
        QTall = E(nc.sbuf_tensor("QTall", [128, 8 * NT], BF16))
        emit_load_consts(bld, dram)
        emit_p0(bld, resid, QTall, dram)
        emit_l0attn(bld, resid, QTall, dram)

        def st(sy):
            bld.w(sy, "vv")
            bld.w(sy, "mm")
            bld.w(sy, "ss")
            for c in range(DC):
                bld.inc("sq", sy.dma_start(out=dram["resid_out"][c], in_=resid[:, c * NT:(c + 1) * NT]), 16)
            bld.w(sy, "sq")
        bld.phase(sync=st)
    return nc


def emit_l1ssm(bld, resid, dram):
    nc = bld.nc
    NTILES = 128
    with ExitStack() as ctx:
        E = ctx.enter_context
        Astr = E(nc.sbuf_tensor("Astr", [128, NTILES * 16], BF16))
        Vall = E(nc.sbuf_tensor("Vall", [128, NTILES * 128], BF16))
        Tall = E(nc.sbuf_tensor("Tall", [128, NTILES * 128], BF16))
        Pall = E(nc.sbuf_tensor("Pall", [128, NTILES * 128], BF16))
        Cblk = E(nc.sbuf_tensor("Cblk", [128, NTILES * 8], BF16))
        bca = E(nc.sbuf_tensor("bca", [128, 128], F32))
        cca = E(nc.sbuf_tensor("cca", [128, 128], F32))
        dca = E(nc.sbuf_tensor("dca", [128, 8], F32))
        bcaT = E(nc.sbuf_tensor("bcaT", [128, 128], F32))
        ccaT = E(nc.sbuf_tensor("ccaT", [128, 128], F32))
        dcaT = E(nc.sbuf_tensor("dcaT", [128, 128], F32))
        gst = E(nc.sbuf_tensor("gst", [128, 512], F32))
        grev = E(nc.sbuf_tensor("grev", [128, 512], BF16))
        gTsb = E(nc.sbuf_tensor("gTsb", [128, 1024], BF16))
        zb = E(nc.sbuf_tensor("zb", [128, 1024], BF16))

        def g0(g):
            for dloc in range(8):
                for th in range(2):
                    src = bass.AP(dram["L1_attn_A"].tensor, (th * 64 * 8 + dloc) * 256,
                                  [[16, 16], [8 * 256, 64], [1, 16]])
                    dst = Astr[dloc * 16:(dloc + 1) * 16, th * 64 * 16:(th + 1) * 64 * 16] \
                        .rearrange("n (t m) -> n t m", m=16)
                    bld.inc("gq", g.dma_start(out=dst, in_=src), 16)
            bld.inc("gq", g.dma_start(out=bca[:, :],
                                      in_=dram["L1_attn_Bm"].rearrange("(t a) n -> t (a n)", a=8)), 16)
            bld.inc("gq", g.dma_start(out=cca[:, :],
                                      in_=dram["L1_attn_Cm"].rearrange("(t a) n -> t (a n)", a=8)), 16)
            bld.inc("gq", g.dma_start(out=dca[:, :],
                                      in_=dram["L1_attn_Dm"].rearrange("(t p) -> t p", p=8)), 16)

        def v0(v):
            bld.V(v, lambda: v.memset(zb[0:64, :], 0.0))
        bld.phase(gpsimd=g0, vector=v0)

        for (srcT, dstT, rows) in [(bca, bcaT, 128), (cca, ccaT, 128), (dca, dcaT, 8)]:
            def tt(t, srcT=srcT, rows=rows):
                bld.w(t, "gq")
                bld.w(t, "vv")
                bld.w(t, "mm")
                bld.inc("mm", t.matmul(bld.PTF[0:rows, 0:128], srcT[:, 0:rows], bld.identf[:, :],
                                       start=True, stop=True, is_transpose=True))

            def vv(v, dstT=dstT, rows=rows):
                bld.w(v, "mm")
                bld.V(v, lambda: v.tensor_copy(dstT[0:rows, 0:128], bld.PTF[0:rows, 0:128]))
            bld.phase(tensor=tt, vector=vv, order=("tensor", "vector"))

        def vA(v):
            bld.w(v, "gq")
            for dl in range(8):
                def op(dl=dl):
                    o = Pall[:, :].rearrange("p (t f) -> p t f", f=128)[:, :, dl * 16:(dl + 1) * 16]
                    i0 = Astr[:, :].rearrange("p (t m) -> p t m", m=16)
                    i1 = dram
                    return v.scalar_tensor_tensor(o, i0, bld.msk8[:, dl:dl + 1], i0, ALU.bypass, ALU.mult)
                o = Pall[:, :].rearrange("p (t f) -> p t f", f=128)[:, :, dl * 16:(dl + 1) * 16]
                i0 = Astr[:, :].rearrange("p (t m) -> p t m", m=16)
                bld.V(v, lambda o=o, i0=i0, dl=dl: v.tensor_scalar(o, i0, bld.msk8[:, dl:dl + 1],
                                                                   None, ALU.mult))
            o = Vall[:, :].rearrange("p (t f) -> p t f", f=128)[:, :, 127]
            bld.V(v, lambda o=o: v.tensor_copy(o, bcaT[:, :]))
            for dl in range(8):
                o = Cblk[:, :].rearrange("p (t f) -> p t f", f=8)[:, :, dl]
                bld.V(v, lambda o=o, dl=dl: v.tensor_scalar(o, ccaT[:, :], bld.msk8[:, dl:dl + 1],
                                                            None, ALU.mult))
        bld.phase(vector=vA)

        for t_ in range(NTILES):
            def ttr(t, t_=t_):
                bld.w(t, "vv")
                bld.w(t, "mm")
                bld.inc("mm", t.matmul(bld.PT[:, :], Pall[:, t_ * 128:(t_ + 1) * 128],
                                       bld.identb[:, :], start=True, stop=True, is_transpose=True))

            def vtr(v, t_=t_):
                bld.w(v, "mm")
                bld.V(v, lambda: v.tensor_copy(Tall[:, t_ * 128:(t_ + 1) * 128], bld.PT[:, :]))
            bld.phase(tensor=ttr, vector=vtr, order=("tensor", "vector"))

        for k in range(7):
            wk = 1 << k
            per = max(1, 512 // (wk if wk > 0 else 1)) if wk else 8
            per = min(8, 512 // wk)
            grp = 0
            t_ = 0
            while t_ < NTILES:
                n = min(per, NTILES - t_)
                pb = bld.P[0] if grp % 2 == 0 else bld.P[1]
                def tmv(t, t_=t_, n=n, wk=wk, pb=pb):
                    bld.w(t, "vv")
                    bld.w(t, "ss")
                    bld.w(t, "mm")
                    for j in range(n):
                        bld.inc("mm", t.matmul(pb[:, j * wk:(j + 1) * wk],
                                               Tall[:, (t_ + j) * 128:(t_ + j + 1) * 128],
                                               Vall[:, (t_ + j) * 128 + 128 - wk: (t_ + j) * 128 + 128],
                                               start=True, stop=True))

                def smv(s, t_=t_, n=n, wk=wk, pb=pb):
                    bld.w(s, "mm")
                    for j in range(n):
                        bld.S(s, lambda j=j: s.copy(
                            Vall[:, (t_ + j) * 128 + 128 - 2 * wk: (t_ + j) * 128 + 128 - wk],
                            pb[:, j * wk:(j + 1) * wk]))
                bld.phase(tensor=tmv, scalar=smv, order=("tensor", "scalar"))
                t_ += n
                grp += 1
            if k == 6:
                break
            t_ = 0
            grp = 0
            while t_ < NTILES:
                n = min(4, NTILES - t_)
                pb = bld.PW0 if grp % 2 == 0 else bld.PW1
                pb2 = bld.P[0] if grp % 2 == 0 else bld.P[1]
                def tsq(t, t_=t_, n=n, pb=pb, pb2=pb2):
                    bld.w(t, "vv")
                    bld.w(t, "ss")
                    bld.w(t, "mm")
                    for j in range(n):
                        sl = slice((t_ + j) * 128, (t_ + j + 1) * 128)
                        bld.inc("mm", t.matmul(pb[:, j * 128:(j + 1) * 128] if j < 8 else None,
                                               Tall[:, sl], Pall[:, sl], start=True, stop=True))
                    for j in range(n):
                        sl = slice((t_ + j) * 128, (t_ + j + 1) * 128)
                        bld.inc("mm", t.matmul(pb2[:, j * 128:(j + 1) * 128],
                                               Pall[:, sl], Tall[:, sl], start=True, stop=True))

                def ssq(s, t_=t_, n=n, pb=pb, pb2=pb2):
                    bld.w(s, "mm")
                    for j in range(n):
                        sl = slice((t_ + j) * 128, (t_ + j + 1) * 128)
                        bld.S(s, lambda sl=sl, j=j: s.copy(Pall[:, sl], pb[:, j * 128:(j + 1) * 128]))

                def vsq(v, t_=t_, n=n, pb=pb, pb2=pb2):
                    bld.w(v, "mm")
                    for j in range(n):
                        sl = slice((t_ + j) * 128, (t_ + j + 1) * 128)
                        bld.V(v, lambda sl=sl, j=j: v.tensor_copy(Tall[:, sl],
                                                                  pb2[:, j * 128:(j + 1) * 128]))
                bld.phase(tensor=tsq, scalar=ssq, vector=vsq, order=("tensor", "scalar", "vector"))
                t_ += n
                grp += 1

        for t0_ in range(0, NTILES, 4):
            pb = bld.P[0]
            def tge(t, t0_=t0_, pb=pb):
                bld.w(t, "vv")
                bld.w(t, "ss")
                bld.w(t, "mm")
                for j in range(4):
                    sl = slice((t0_ + j) * 128, (t0_ + j + 1) * 128)
                    bld.inc("mm", t.matmul(pb[0:8, j * 128:(j + 1) * 128],
                                           Cblk[:, (t0_ + j) * 8:(t0_ + j + 1) * 8],
                                           Vall[:, sl], start=True, stop=True))

            def vge(v, t0_=t0_, pb=pb):
                bld.w(v, "mm")
                bld.V(v, lambda: v.tensor_copy(gst[0:8, :], pb[0:8, 0:512]))
                bld.V(v, lambda: v.tensor_copy(grev[0:8, :], gst[0:8, :]))
                o = grev[0:8, :].rearrange("p (t j) -> p t j", j=128)[:, :, 127]
                bld.V(v, lambda o=o: v.tensor_tensor(o, o, dcaT[0:8, t0_:t0_ + 4], ALU.add))
            bld.phase(tensor=tge, vector=vge, order=("tensor", "vector"))

            for j in range(4):
                def ttg(t, t0_=t0_, j=j):
                    bld.w(t, "vv")
                    bld.w(t, "mm")
                    bld.inc("mm", t.matmul(bld.PT[0:128, j * 8:(j + 1) * 8],
                                           grev[0:8, j * 128:(j + 1) * 128],
                                           bld.identb[0:8, 0:8], start=True, stop=True,
                                           is_transpose=True))
                bld.phase(tensor=ttg)

            def vgt(v, t0_=t0_):
                bld.w(v, "mm")
                bld.V(v, lambda: v.tensor_copy(gTsb[:, t0_ * 8:(t0_ + 4) * 8], bld.PT[0:128, 0:32]))
            bld.phase(vector=vgt)

        def sw(sy):
            bld.w(sy, "vv")
            bld.inc("sq", sy.dma_start(out=dram["gTdR"][0:128, :], in_=gTsb[:, :]), 16)
            bld.inc("sq", sy.dma_start(out=dram["gTdR"][128:192, :], in_=zb[0:64, :]), 16)
            bld.w(sy, "sq")
        bld.phase(sync=sw)
        emit_dbg(bld, dram, "dbg_g", gTsb[:, :])
        emit_dbg(bld, dram, "dbg_V", Vall[:, 0:2048])

    with ExitStack() as ctx:
        E = ctx.enter_context
        ln3f = E(nc.sbuf_tensor("ln3f", [128, DC * NT], BF16))
        y64 = E(nc.sbuf_tensor("y64", [128, 16 * 1024], BF16))
        lnwc = E(nc.sbuf_tensor("lnwc", [128, 8], F32))
        lnbc = E(nc.sbuf_tensor("lnbc", [128, 8], F32))
        boc = E(nc.sbuf_tensor("boc", [128, 8], F32))
        cmk = E(nc.sbuf_tensor("cmk", [128, 1], F32))
        with ExitStack() as ctxl:
            lns3 = alloc_ln_scr(bld, ctxl.enter_context, "s")
            emit_ln_fm(bld, lns3, resid, NT, 0, NT, ln3f, NT, 0)

        with ExitStack() as ctxw:
            tmpW = ctxw.enter_context(nc.sbuf_tensor("tmpW", [128, DC * 1024], BF16))
            WoT = ctxw.enter_context(nc.sbuf_tensor("WoTs", [128, DC * 1024], BF16))

            def gwo(g):
                bld.inc("gq", g.dma_start(out=cmk[0:64, :], in_=dram["convmask"][:, :]), 16)
                for dc in range(DC):
                    bld.inc("gq", g.dma_start(out=tmpW[:, dc * 1024:(dc + 1) * 1024],
                                              in_=dram["L1_attn_W_out"][dc * 128:(dc + 1) * 128, :]), 16)
            bld.phase(gpsimd=gwo)
            items = []
            for dc in range(DC):
                for c in range(DC):
                    src = tmpW[:, dc * 1024 + c * 128: dc * 1024 + (c + 1) * 128]
                    dst = WoT[:, c * 1024 + dc * 128: c * 1024 + (dc + 1) * 128]
                    ev = (lambda dst: (lambda v, pt: v.tensor_copy(dst, pt)))(dst)
                    items.append((src, 128, 128, ev))
            emit_tpose_pipe(bld, items)

            def swot(sy):
                bld.w(sy, "vv")
                bld.inc("sq", sy.dma_start(out=dram["wotT"][0:64, :], in_=WoT[0:64, 0:8192]), 16)
                bld.inc("sq", sy.dma_start(out=dram["wotT"][64:128, :], in_=WoT[64:128, 0:8192]), 16)
                bld.w(sy, "sq")
            bld.phase(sync=swot)

        with ExitStack() as ctxr:
            lrow = ctxr.enter_context(nc.sbuf_tensor("lrow", [1, 1024], F32))
            for (name, dstc) in [("L1_attn_ln_w", lnwc), ("L1_attn_ln_b", lnbc), ("L1_attn_b_out", boc)]:
                def gl(g, name=name):
                    bld.w(g, "vv")
                    bld.inc("gq", g.dma_start(out=lrow[:, :], in_=dram[name][None, :]), 16)
                bld.phase(gpsimd=gl)
                for c in range(DC):
                    def tl2(t, c=c):
                        bld.w(t, "gq")
                        bld.w(t, "vv")
                        bld.w(t, "mm")
                        bld.inc("mm", t.matmul(bld.PTF[0:128, 0:1], lrow[:, c * 128:(c + 1) * 128],
                                               bld.identf[0:1, 0:1], start=True, stop=True, is_transpose=True))

                    def vl2(v, c=c, dstc=dstc):
                        bld.w(v, "mm")
                        bld.V(v, lambda: v.tensor_copy(dstc[:, c:c + 1], bld.PTF[0:128, 0:1]))
                    bld.phase(tensor=tl2, vector=vl2, order=("tensor", "vector"))

        for (hb, (t0, n)) in enumerate([(128, 1024), (1152, 1024)]):
            nblk = n // 64
            b0 = t0 // 64
            wb0 = b0 - 1
            with ExitStack() as ctx1:
                E1 = ctx1.enter_context
                x64 = E1(nc.sbuf_tensor(f"x64_{hb}", [128, 17 * 1024], BF16))
                Gt = E1(nc.sbuf_tensor(f"Gt_{hb}", [128, 64 * 256], BF16))
                items = []
                for wb in range(17):
                    blkt = wb0 + wb
                    for c in range(DC):
                        dst = x64[0:64, wb * 1024 + c * 128: wb * 1024 + (c + 1) * 128]
                        src = ln3f[:, c * NT + blkt * 64: c * NT + (blkt + 1) * 64]
                        if blkt < 2:
                            ev = (lambda dst: (lambda v, pt: v.tensor_scalar(
                                dst, pt, cmk[0:64, :], None, ALU.mult)))(dst)
                        else:
                            ev = (lambda dst: (lambda v, pt: v.tensor_copy(dst, pt)))(dst)
                        items.append((src, 128, 64, ev))
                emit_tpose_pipe(bld, items)

                for ct in range(8):
                    def ga(sy, ct=ct):
                        bld.w(sy, "mm")
                        for i in range(64):
                            src = bass.AP(dram["gTdR"].tensor, (63 - i) * 1024 + ct * 128,
                                          [[1024, 64], [64 * 1024, 2], [1, 128]])
                            dst = Gt[0:64, i * 256:(i + 1) * 256].rearrange("w (h c) -> w h c", c=128)
                            bld.inc("sq", sy.dma_start(out=dst, in_=src), 16)
                    bld.phase(sync=ga)
                    sqm = bld.cnt["sq"]

                    ssb = bld.cnt["ss"]
                    mmb = bld.cnt["mm"]
                    for g4_ in range(32):
                        pb = bld.P[0] if g4_ % 2 == 0 else bld.P[1]
                        def tcv(t, ct=ct, g4_=g4_, pb=pb, sqm=sqm, ssb=ssb):
                            t.wait_ge(bld.sem["sq"], sqm)
                            if g4_ == 0:
                                bld.w(t, "ss")
                                bld.w(t, "mm")
                            elif g4_ > 1:
                                t.wait_ge(bld.sem["ss"], ssb + g4_ - 1)
                            for cj in range(4):
                                ch = g4_ * 4 + cj
                                lh = Gt[0:64, :].rearrange("w (i h c) -> w i h c", h=2, c=128)
                                rh = x64[0:64, :].rearrange("w (b f) -> w b f", f=1024)
                                bld.inc("mm", t.matmul(pb[0:64, cj * 16:(cj + 1) * 16],
                                                       lh[:, :, 0, ch],
                                                       rh[:, 0:16, ct * 128 + ch], start=True, stop=False))
                                bld.inc("mm", t.matmul(pb[0:64, cj * 16:(cj + 1) * 16],
                                                       lh[:, :, 1, ch],
                                                       rh[:, 1:17, ct * 128 + ch], start=False, stop=True))

                        def scv(s, ct=ct, g4_=g4_, pb=pb, mmb=mmb):
                            s.wait_ge(bld.sem["mm"], mmb + 8 * (g4_ + 1))
                            o = y64[0:64, :].rearrange("w (b f) -> w b f", f=1024) \
                                [:, :, ct * 128 + g4_ * 4: ct * 128 + (g4_ + 1) * 4]
                            bld.inc("ss", s.copy(o, pb[0:64, 0:64].rearrange("w (c b) -> w b c", b=16)))
                        bld.phase(tensor=tcv, scalar=scv, order=("tensor", "scalar"))

            if hb == 0:
                emit_dbg(bld, dram, "dbg_x64", x64[0:64, :])
                emit_dbg(bld, dram, "dbg_y64", y64[0:64, :])
            with ExitStack() as ctx2:
                E2 = ctx2.enter_context
                m64 = E2(nc.sbuf_tensor(f"m64_{hb}", [128, 1], F32))
                s64 = E2(nc.sbuf_tensor(f"s64_{hb}", [128, 1], F32))
                sq64 = E2(nc.sbuf_tensor(f"sq64_{hb}", [128, 1024], BF16))
                yT = E2(nc.sbuf_tensor(f"yT_{hb}", [128, DC * 1024], BF16))
                vwtmp = E2(nc.sbuf_tensor(f"vwtmp_{hb}", [128, 512], F32))
                WoT = E2(nc.sbuf_tensor(f"WoT2_{hb}", [128, DC * 1024], BF16))
                dbgpb = E2(nc.sbuf_tensor(f"dbgpb_{hb}", [128, 512], F32))
                vwtmp2 = E2(nc.sbuf_tensor(f"vwtmp2_{hb}", [128, 512], F32))

                def gwot(g):
                    bld.w(g, "mm")
                    bld.w(g, "vv")
                    bld.inc("gq", g.dma_start(out=WoT[:, :], in_=dram["wotT"][:, :]), 16)
                bld.phase(gpsimd=gwot)
                gq_wot = bld.cnt["gq"]
                if hb == 0:
                    def wotbar(t, gq_wot=gq_wot):
                        t.wait_ge(bld.sem["gq"], gq_wot)
                    bld.phase(tensor=wotbar, order=("tensor",))
                    emit_dbg(bld, dram, "dbg_wot", WoT[:, :])
                for b in range(16):
                    def vst(v, b=b):
                        bld.w(v, "ss")
                        bld.V(v, lambda: v.tensor_reduce(m64[0:64, :], y64[0:64, b * 1024:(b + 1) * 1024],
                                                         AX.X, ALU.add))
                        bld.V(v, lambda: v.tensor_scalar(m64[0:64, :], m64[0:64, :], 1.0 / D, None, ALU.mult))
                        bld.V(v, lambda: v.tensor_scalar(sq64[0:64, :], y64[0:64, b * 1024:(b + 1) * 1024],
                                                         m64[0:64, :], None, ALU.subtract))

                    def sst2(s, b=b):
                        bld.w(s, "vv")
                        bld.S(s, lambda: s.activation(out=y64[0:64, b * 1024:(b + 1) * 1024],
                                                      in_=sq64[0:64, :], func=AF.Square,
                                                      accum_out=s64[0:64, :]))
                    bld.phase(vector=vst, scalar=sst2, order=("vector", "scalar"))

                    def vst2(v, b=b):
                        bld.w(v, "ss")
                        bld.V(v, lambda: v.tensor_scalar(s64[0:64, :], s64[0:64, :], 1.0 / D, EPS, ALU.mult, ALU.add))
                    bld.phase(vector=vst2)

                    def sst3(s, b=b):
                        bld.w(s, "vv")
                        bld.S(s, lambda: s.activation(out=s64[0:64, :], in_=s64[0:64, :], func=AF.Ln))
                        bld.S(s, lambda: s.activation(out=s64[0:64, :], in_=s64[0:64, :], func=AF.Exp,
                                                      scale=-0.5))
                    bld.phase(scalar=sst3)

                    def vst3(v, b=b):
                        bld.w(v, "ss")
                        bld.V(v, lambda: v.tensor_scalar(y64[0:64, b * 1024:(b + 1) * 1024],
                                                         sq64[0:64, :], s64[0:64, :], None, ALU.mult))
                    bld.phase(vector=vst3)
                    if hb == 0 and b == 0:
                        emit_dbg(bld, dram, "dbg_m64", m64[0:64, :])
                        emit_dbg(bld, dram, "dbg_s64", s64[0:64, :])
                        emit_dbg(bld, dram, "dbg_yb0", y64[0:64, 0:1024])
                        emit_dbg(bld, dram, "dbg_sq0", sq64[0:64, :])

                items = []
                for b in range(16):
                    for c in range(DC):
                        src = y64[0:64, b * 1024 + c * 128: b * 1024 + (c + 1) * 128]
                        dst = yT[:, c * 1024 + b * 64: c * 1024 + (b + 1) * 64]
                        ev = (lambda dst, c: (lambda v, pt: v.tensor_scalar(
                            dst, pt, lnwc[:, c:c + 1], lnbc[:, c:c + 1], ALU.mult, ALU.add)))(dst, c)
                        items.append((src, 64, 128, ev))
                emit_tpose_pipe(bld, items)

                if hb == 0:
                    emit_dbg(bld, dram, "dbg_y64b", y64[0:64, :])
                    emit_dbg(bld, dram, "dbg_z", yT[:, :])

                def sg2(s):
                    bld.w(s, "vv")
                    for c in range(DC):
                        bld.S(s, lambda c=c: s.activation(out=yT[:, c * 1024:(c + 1) * 1024],
                                                          in_=yT[:, c * 1024:(c + 1) * 1024], func=AF.Gelu))
                bld.phase(scalar=sg2)
                if hb == 0:
                    emit_dbg(bld, dram, "dbg_gel", yT[:, :])

                for dc in range(DC):
                    pbA, pbB = bld.P[3], bld.P[5]
                    def tw(t, dc=dc, gq_wot=gq_wot):
                        t.wait_ge(bld.sem["gq"], gq_wot)
                        bld.w(t, "gq")
                        bld.w(t, "sq")
                        bld.w(t, "vv")
                        bld.w(t, "ss")
                        bld.w(t, "mm")
                        for c in range(DC):
                            bld.inc("mm", t.matmul(pbA[:, 0:512],
                                                   WoT[:, c * 1024 + dc * 128: c * 1024 + (dc + 1) * 128],
                                                   yT[:, c * 1024: c * 1024 + 512],
                                                   start=(c == 0), stop=(c == DC - 1)))
                            bld.inc("mm", t.matmul(pbB[:, 0:512],
                                                   WoT[:, c * 1024 + dc * 128: c * 1024 + (dc + 1) * 128],
                                                   yT[:, c * 1024 + 512: c * 1024 + 1024],
                                                   start=(c == 0), stop=(c == DC - 1)))

                    def sw2(s, dc=dc):
                        bld.w(s, "mm")
                        bld.w(s, "vv")
                        bld.S(s, lambda: s.copy(vwtmp[:, 0:512], pbA[:, 0:512]))
                        bld.S(s, lambda: s.copy(vwtmp2[:, 0:512], pbB[:, 0:512]))

                    def vw(v, dc=dc, t0=t0):
                        bld.w(v, "ss")
                        for (o, buf) in [(0, vwtmp), (512, vwtmp2)]:
                            bld.V(v, lambda o=o, buf=buf: v.scalar_tensor_tensor(
                                buf[:, 0:512], buf[:, 0:512], boc[:, dc:dc + 1],
                                ln3f[:, dc * NT + t0 + o: dc * NT + t0 + o + 512], ALU.add, ALU.add))
                            sl = resid[:, dc * NT + t0 + o: dc * NT + t0 + o + 512]
                            bld.V(v, lambda sl=sl, buf=buf: v.tensor_tensor(sl, sl, buf[:, 0:512], ALU.add))
                    bld.phase(tensor=tw, scalar=sw2, vector=vw, order=("tensor", "scalar", "vector"))
                if hb == 0:
                    emit_dbg(bld, dram, "dbg_res0c", resid[:, 0:2176])
                    emit_dbg(bld, dram, "dbg_pb", dbgpb[:, :])


def ssm_dram(nc, dram):
    dram["L1_attn_A"] = nc.dram_tensor("L1_attn_A", [D, 16, 16], F32, kind="ExternalInput").ap()
    dram["L1_attn_Bm"] = nc.dram_tensor("L1_attn_Bm", [D, 16], F32, kind="ExternalInput").ap()
    dram["L1_attn_Cm"] = nc.dram_tensor("L1_attn_Cm", [D, 16], F32, kind="ExternalInput").ap()
    dram["L1_attn_Dm"] = nc.dram_tensor("L1_attn_Dm", [D], F32, kind="ExternalInput").ap()
    dram["L1_attn_ln_w"] = nc.dram_tensor("L1_attn_ln_w", [D], F32, kind="ExternalInput").ap()
    dram["L1_attn_ln_b"] = nc.dram_tensor("L1_attn_ln_b", [D], F32, kind="ExternalInput").ap()
    dram["L1_attn_W_out"] = nc.dram_tensor("L1_attn_W_out", [D, D], F32, kind="ExternalInput").ap()
    dram["L1_attn_b_out"] = nc.dram_tensor("L1_attn_b_out", [D], F32, kind="ExternalInput").ap()
    dram["convmask"] = nc.dram_tensor("convmask", [64, 1], F32, kind="ExternalInput").ap()
    dram["gTdR"] = nc.dram_tensor("gTdR", [192, D], BF16, kind="Internal").ap()
    dram["wotT"] = nc.dram_tensor("wotT", [128, 8192], BF16, kind="Internal").ap()
    if DBG:
        dram["dbg_g"] = nc.dram_tensor("dbg_g", [128, 1024], BF16, kind="ExternalOutput").ap()
        dram["dbg_V"] = nc.dram_tensor("dbg_V", [128, 2048], BF16, kind="ExternalOutput").ap()
        dram["dbg_x64"] = nc.dram_tensor("dbg_x64", [64, 17 * 1024], BF16, kind="ExternalOutput").ap()
        dram["dbg_y64"] = nc.dram_tensor("dbg_y64", [64, 16 * 1024], BF16, kind="ExternalOutput").ap()
        dram["dbg_z"] = nc.dram_tensor("dbg_z", [128, 8192], BF16, kind="ExternalOutput").ap()
        dram["dbg_gel"] = nc.dram_tensor("dbg_gel", [128, 8192], BF16, kind="ExternalOutput").ap()
        dram["dbg_y64b"] = nc.dram_tensor("dbg_y64b", [64, 16384], BF16, kind="ExternalOutput").ap()
        dram["dbg_m64"] = nc.dram_tensor("dbg_m64", [64, 1], F32, kind="ExternalOutput").ap()
        dram["dbg_s64"] = nc.dram_tensor("dbg_s64", [64, 1], F32, kind="ExternalOutput").ap()
        dram["dbg_yb0"] = nc.dram_tensor("dbg_yb0", [64, 1024], BF16, kind="ExternalOutput").ap()
        dram["dbg_sq0"] = nc.dram_tensor("dbg_sq0", [64, 1024], BF16, kind="ExternalOutput").ap()
        dram["dbg_wot"] = nc.dram_tensor("dbg_wot", [128, 8192], BF16, kind="ExternalOutput").ap()
        dram["dbg_res0c"] = nc.dram_tensor("dbg_res0c", [128, 2176], F32, kind="ExternalOutput").ap()
        dram["dbg_vwt"] = nc.dram_tensor("dbg_vwt", [128, 512], F32, kind="ExternalOutput").ap()
        dram["dbg_pb"] = nc.dram_tensor("dbg_pb", [128, 512], F32, kind="ExternalOutput").ap()


def build_l1ssm_launch():
    nc = bass.Bass("TRN2", target_bir_lowering=False, debug=False)
    dram = {}
    dram["resid_in"] = nc.dram_tensor("resid_in", [DC, 128, NT], F32, kind="ExternalInput").ap()
    dram["resid_out"] = nc.dram_tensor("resid_out", [DC, 128, NT], F32, kind="ExternalOutput").ap()
    common_dram(nc, dram)
    ssm_dram(nc, dram)
    with ExitStack() as ctx:
        E = ctx.enter_context
        bld = Bld(nc, ctx)
        alloc_consts(bld, E)
        bld.msk8 = E(nc.sbuf_tensor("msk8", [128, 8], F32))
        resid = E(nc.sbuf_tensor("resid", [128, DC * NT], F32))
        dram["blockmask8"] = nc.dram_tensor("blockmask8", [128, 8], F32, kind="ExternalInput").ap()
        emit_load_consts(bld, dram)

        def ld(sy):
            for c in range(DC):
                bld.inc("sq", sy.dma_start(out=resid[:, c * NT:(c + 1) * NT], in_=dram["resid_in"][c]), 16)
        bld.phase(sync=ld)

        def g1(g):
            bld.inc("gq", g.dma_start(out=bld.msk8[:, :], in_=dram["blockmask8"][:, :]), 16)
        bld.phase(gpsimd=g1)

        def bar(t):
            bld.w(t, "sq")
            bld.w(t, "gq")
        bld.phase(tensor=bar, order=("tensor",))

        emit_l1ssm(bld, resid, dram)

        def st(sy):
            bld.w(sy, "vv")
            bld.w(sy, "mm")
            bld.w(sy, "ss")
            for c in range(DC):
                bld.inc("sq", sy.dma_start(out=dram["resid_out"][c], in_=resid[:, c * NT:(c + 1) * NT]), 16)
            bld.w(sy, "sq")
        bld.phase(sync=st)
    return nc


# version 05
# speedup vs baseline: 1.2153x; 1.1449x over previous
import numpy as np
import ml_dtypes
from contextlib import ExitStack

import concourse.bass as bass
import concourse.mybir as mybir
from concourse.bass_utils import run_bass_kernel_spmd

F32 = mybir.dt.float32
BF16 = mybir.dt.bfloat16
AF = mybir.ActivationFunctionType
ALU = mybir.AluOpType
AX = mybir.AxisListType

NT = 2176
NOWN = 2048
HALO = 128
D = 1024
DC = 8
EPS = 1e-5
NCORES = 8
NTB = 544
HID = 4096
HC = 32


def tch(n, w=512):
    out = []
    o = 0
    while o < n:
        c = min(w, n - o)
        out.append((o, c))
        o += c
    return out


class Bld:

    def __init__(self, nc, ctx):
        self.nc = nc
        self.ctx = ctx
        self.sem = {}
        self.cnt = {}
        for name in ("sq", "gq", "mm", "vv", "ss", "gc"):
            self.sem[name] = ctx.enter_context(nc.semaphore(name))
            self.cnt[name] = 0
        self._P0 = ctx.enter_context(nc.psum_tensor("P0", [128, 512], F32))
        self._P1 = ctx.enter_context(nc.psum_tensor("P1", [128, 512], F32))
        self.PW0 = ctx.enter_context(nc.psum_tensor("PW0", [128, 1024], F32))
        self.PW1 = ctx.enter_context(nc.psum_tensor("PW1", [128, 1024], F32))
        self.P = [self._P0[:, :], self._P1[:, :], self.PW0[:, 0:512], self.PW1[:, 0:512],
                  self.PW0[:, 512:1024], self.PW1[:, 512:1024]]
        self.PT = ctx.enter_context(nc.psum_tensor("PT", [128, 128], BF16))
        self.PT2 = ctx.enter_context(nc.psum_tensor("PT2", [128, 128], BF16))
        self.PTF = self.PW0[:, 0:128]
        self.PTF2 = self.PW1[:, 0:128]
        self.gq_marks = {}

    def inc(self, name, inst, k=1):
        inst.then_inc(self.sem[name], k)
        self.cnt[name] += k
        return self.cnt[name]

    def V(self, v, f):
        self.w(v, "vv")
        return self.inc("vv", f())

    def S(self, s, f):
        self.w(s, "ss")
        return self.inc("ss", f())

    def w(self, eng, name, val=None):
        v = self.cnt[name] if val is None else val
        if v > 0:
            eng.wait_ge(self.sem[name], v)

    def phase(self, **closures):
        order = closures.pop("order", ("sync", "gpsimd", "tensor", "scalar", "vector"))
        with self.nc.Block() as blk:
            for name in order:
                fn = closures.get(name)
                if fn is not None:
                    getattr(blk, name)(fn)


def alloc_consts(bld, E):
    nc = bld.nc
    bld.identf = E(nc.sbuf_tensor("c_identf", [128, 128], F32))
    bld.identb = E(nc.sbuf_tensor("c_identb", [128, 128], BF16))
    bld.onescol = E(nc.sbuf_tensor("c_onescol", [128, 1], F32))
    bld.onescol_b = E(nc.sbuf_tensor("c_onescol_b", [128, 1], BF16))
    bld.onesrow = E(nc.sbuf_tensor("c_onesrow", [1, 128], F32))
    bld.onesrow_b = E(nc.sbuf_tensor("c_onesrow_b", [1, 128], BF16))


def emit_load_consts(bld, dram):
    def g(g):
        for dst, src in [(bld.identf, dram["identf"]), (bld.identb, dram["identf"]),
                         (bld.onescol, dram["ones_col"]), (bld.onescol_b, dram["ones_col"]),
                         (bld.onesrow, dram["ones_row"]), (bld.onesrow_b, dram["ones_row"])]:
            bld.inc("gq", g.dma_start(out=dst[:, :], in_=src[:, :]), 16)
    bld.phase(gpsimd=g)


def emit_ln_fm(bld, scr, src, src_stride, t0, n, dst, dst_stride, dt0):
    pst1, pst2, pmb, prb = bld.P[4], bld.P[5], bld.P[2], bld.P[3]
    for (o, w) in tch(n):
        def vv(v, o=o, w=w):
            ss_base = bld.cnt["ss"]
            mm_base = bld.cnt["mm"]
            for c in range(DC):
                if c == 0:
                    bld.w(v, "ss")
                    bld.w(v, "mm")
                else:
                    v.wait_ge(bld.sem["ss"], ss_base + c)
                    v.wait_ge(bld.sem["mm"], mm_base + 2 * c)
                bld.w(v, "vv")
                bld.inc("vv", v.tensor_copy(
                    scr["xb"][:, 0:w], src[:, c * src_stride + t0 + o: c * src_stride + t0 + o + w]))

        def ss(s, o=o, w=w):
            vv_base = bld.cnt["vv"] - DC
            mm_base = bld.cnt["mm"]
            for c in range(DC):
                s.wait_ge(bld.sem["vv"], vv_base + c + 1)
                if c > 0:
                    s.wait_ge(bld.sem["mm"], mm_base + 2 * c)
                bld.w(s, "ss")
                bld.inc("ss", s.activation(out=scr["sq"][:, 0:w], in_=scr["xb"][:, 0:w], func=AF.Square))

        def tt(t, o=o, w=w):
            vv_base = bld.cnt["vv"] - DC
            ss_base = bld.cnt["ss"] - DC
            bld.w(t, "mm")
            for c in range(DC):
                t.wait_ge(bld.sem["vv"], vv_base + c + 1)
                t.wait_ge(bld.sem["ss"], ss_base + c + 1)
                bld.inc("mm", t.matmul(pst1[0:1, 0:w], bld.onescol_b[:, :], scr["xb"][:, 0:w],
                                       start=(c == 0), stop=(c == DC - 1)))
                bld.inc("mm", t.matmul(pst2[0:1, 0:w], bld.onescol_b[:, :], scr["sq"][:, 0:w],
                                       start=(c == 0), stop=(c == DC - 1)))
        bld.phase(vector=vv, scalar=ss, tensor=tt, order=("vector", "scalar", "tensor"))

        def vv2(v, o=o, w=w):
            bld.w(v, "mm")
            bld.w(v, "vv")
            bld.inc("vv", v.tensor_scalar(scr["m_r"][:, 0:w], pst1[0:1, 0:w], 1.0 / D, None, ALU.mult))
            bld.w(v, "vv")
            bld.inc("vv", v.tensor_scalar(scr["a_r"][:, 0:w], pst2[0:1, 0:w], 1.0 / D, None, ALU.mult))
            bld.w(v, "vv")
            bld.inc("vv", v.tensor_tensor(scr["b_r"][:, 0:w], scr["m_r"][:, 0:w], scr["m_r"][:, 0:w], ALU.mult))
            bld.w(v, "vv")
            bld.inc("vv", v.tensor_tensor(scr["a_r"][:, 0:w], scr["a_r"][:, 0:w], scr["b_r"][:, 0:w], ALU.subtract))
            bld.w(v, "vv")
            bld.inc("vv", v.tensor_scalar(scr["a_r"][:, 0:w], scr["a_r"][:, 0:w], EPS, None, ALU.add))
        bld.phase(vector=vv2, order=("vector",))

        def ss2(s, o=o, w=w):
            bld.w(s, "vv")
            bld.w(s, "ss")
            bld.inc("ss", s.activation(out=scr["b_r"][:, 0:w], in_=scr["a_r"][:, 0:w], func=AF.Sqrt))
        bld.phase(scalar=ss2, order=("scalar",))

        def vv3(v, o=o, w=w):
            bld.w(v, "ss")
            bld.w(v, "vv")
            bld.inc("vv", v.reciprocal(scr["a_r"][:, 0:w], scr["b_r"][:, 0:w]))
        bld.phase(vector=vv3, order=("vector",))

        def tt4(t, o=o, w=w):
            bld.w(t, "vv")
            bld.w(t, "mm")
            bld.inc("mm", t.matmul(pmb[:, 0:w], bld.onesrow[:, :], scr["m_r"][:, 0:w], start=True, stop=True))
            bld.inc("mm", t.matmul(prb[:, 0:w], bld.onesrow[:, :], scr["a_r"][:, 0:w], start=True, stop=True))

        def vv4(v, o=o, w=w):
            bld.w(v, "mm")
            for c in range(DC):
                bld.w(v, "vv")
                bld.inc("vv", v.tensor_tensor(scr["tmp"][:, 0:w],
                                              src[:, c * src_stride + t0 + o: c * src_stride + t0 + o + w],
                                              pmb[:, 0:w], ALU.subtract))
                bld.w(v, "vv")
                bld.inc("vv", v.tensor_tensor(dst[:, c * dst_stride + dt0 + o: c * dst_stride + dt0 + o + w],
                                              scr["tmp"][:, 0:w], prb[:, 0:w], ALU.mult))
        bld.phase(tensor=tt4, vector=vv4, order=("tensor", "vector"))


def alloc_ln_scr(bld, E, tag="", tmp=None):
    nc = bld.nc
    return {
        "xb": E(nc.sbuf_tensor(f"lnxb{tag}", [128, 512], BF16)),
        "sq": E(nc.sbuf_tensor(f"lnsq{tag}", [128, 512], BF16)),
        "m_r": E(nc.sbuf_tensor(f"lnm{tag}", [1, 512], F32)),
        "a_r": E(nc.sbuf_tensor(f"lna{tag}", [1, 512], F32)),
        "b_r": E(nc.sbuf_tensor(f"lnb{tag}", [1, 512], F32)),
        "tmp": tmp if tmp is not None else E(nc.sbuf_tensor(f"lntmp{tag}", [128, 512], F32)),
    }


def emit_tpose_pipe(bld, items, f32=False):
    pts = [bld.PTF, bld.PTF2] if f32 else [bld.PT[:, :], bld.PT2[:, :]]
    ident = bld.identf if f32 else bld.identb
    mm0 = bld.cnt["mm"]
    vv0 = bld.cnt["vv"]
    n = len(items)

    def tt(t):
        bld.w(t, "gq")
        bld.w(t, "sq")
        bld.w(t, "vv")
        bld.w(t, "mm")
        for i, (src, rows, cols, _) in enumerate(items):
            if i >= 2:
                t.wait_ge(bld.sem["vv"], vv0 + i - 1)
            pt = pts[i % 2]
            bld.inc("mm", t.matmul(pt[0:cols, 0:rows], src, ident[0:rows, 0:rows],
                                   start=True, stop=True, is_transpose=True))

    def vv(v):
        for i, (src, rows, cols, evac) in enumerate(items):
            v.wait_ge(bld.sem["mm"], mm0 + i + 1)
            pt = pts[i % 2]
            bld.inc("vv", evac(v, pt[0:cols, 0:rows]))
    bld.phase(tensor=tt, vector=vv, order=("tensor", "vector"))


def emit_small_transpose(bld, src_ap, rows, cols, evac_cb):
    def tt(t):
        bld.w(t, "gq")
        bld.w(t, "sq")
        bld.w(t, "vv")
        bld.w(t, "mm")
        bld.inc("mm", t.matmul(bld.PT[0:cols, 0:rows], src_ap, bld.identb[0:rows, 0:rows],
                               start=True, stop=True, is_transpose=True))

    def vv(v):
        bld.w(v, "mm")
        bld.w(v, "sq")
        evac_cb(v)
    bld.phase(tensor=tt, vector=vv, order=("tensor", "vector"))


DBG = False


def emit_dbg(bld, dram, key, src_ap):
    if not DBG or key not in dram:
        return
    def sy(sy):
        bld.w(sy, "vv"); bld.w(sy, "mm"); bld.w(sy, "ss")
        bld.inc("sq", sy.dma_start(out=dram[key][:, :], in_=src_ap), 16)
        bld.w(sy, "sq")
    bld.phase(sync=sy)


def emit_l0ffn(bld, resid, dram):
    nc = bld.nc
    with ExitStack() as ctx:
        E = ctx.enter_context
        ln2 = E(nc.sbuf_tensor("ln2", [128, DC * NTB], BF16))
        r2T = E(nc.sbuf_tensor("r2T", [128, HC * NTB], BF16))
        vT = E(nc.sbuf_tensor("vT", [128, DC * 16], BF16))
        fbT = E(nc.sbuf_tensor("fbT", [128, 16], BF16))
        uTg = E(nc.sbuf_tensor("uTg", [128, HID], BF16))
        s1p = E(nc.sbuf_tensor("s1p", [128, NT], BF16))
        s2p = E(nc.sbuf_tensor("s2p", [128, NT], BF16))
        v16b = E(nc.sbuf_tensor("v16b", [16, D], BF16))
        fbb = E(nc.sbuf_tensor("fbb", [16, 16], BF16))
        ub = E(nc.sbuf_tensor("ub", [128, 32 * 16], BF16))
        gcol = E(nc.sbuf_tensor("gcol", [16, 1], F32))
        gsb = E(nc.sbuf_tensor("gsb", [1, 1], F32))
        wt = E(nc.sbuf_tensor("wt", [128, 1024], F32))
        wpt = E(nc.sbuf_tensor("wpt", [128, HID], F32))
        red = E(nc.sbuf_tensor("red", [128, 1], F32))
        hs = E(nc.sbuf_tensor("hs", [128, 1], F32))
        scl = E(nc.sbuf_tensor("scl", [128, 1], F32))
        sgn = E(nc.sbuf_tensor("sgn", [128, HID], BF16))
        absq = E(nc.sbuf_tensor("absq", [128, HID], F32))
        hs2 = E(nc.sbuf_tensor("hs2", [128, 1], F32))
        weff = E(nc.sbuf_tensor("weff", [128, HID], BF16))
        strip = E(nc.sbuf_tensor("strip", [128, DC * 128], BF16))
        wpstrip = E(nc.sbuf_tensor("wpstrip", [128, HC * 128], BF16))
        rb = E(nc.sbuf_tensor("rbuf", [128, 512], BF16))
        lns = alloc_ln_scr(bld, E, "f", tmp=absq)
        ph, ph2, po0, po1 = bld.P[0], bld.P[1], bld.P[2], bld.P[3]
        ps16 = bld.P[4]

        def g0(g):
            bld.inc("gq", g.dma_start(out=v16b[:, :], in_=dram["L0_ffn_v"][:, :]), 16)
            bld.inc("gq", g.dma_start(out=fbb[:, :], in_=dram["L0_ffn_fast_b"][:, :]), 16)
            usrc = dram["L0_ffn_u"].rearrange("(a p) j -> p a j", p=128)
            bld.inc("gq", g.dma_start(out=ub[:, :].rearrange("p (a j) -> p a j", j=16), in_=usrc), 16)
            bld.inc("gq", g.dma_start(out=gsb[:, :], in_=dram["L0_ffn_gate"][None, :]), 16)

        def v0(v):
            bld.w(v, "vv")
            bld.inc("vv", v.memset(uTg[:, :], 0.0))
            bld.w(v, "vv")
            bld.inc("vv", v.memset(fbT[:, :], 0.0))
            bld.w(v, "vv")
            bld.inc("vv", v.memset(s1p[:, :], 0.0))
            bld.w(v, "vv")
            bld.inc("vv", v.memset(s2p[:, :], 0.0))
        bld.phase(gpsimd=g0, vector=v0)

        for c in range(DC):
            def mk(c):
                return lambda v: bld.inc("vv", v.tensor_copy(vT[:, c * 16:(c + 1) * 16], bld.PT[0:128, 0:16]))
            emit_small_transpose(bld, v16b[:, c * 128:(c + 1) * 128], 16, 128, mk(c))
        emit_small_transpose(bld, fbb[:, :], 16, 16,
                             lambda v: bld.inc("vv", v.tensor_copy(fbT[0:16, 0:16], bld.PT[0:16, 0:16])))
        for a in range(32):
            def mk2(a):
                return lambda v: bld.inc("vv", v.tensor_copy(uTg[0:16, a * 128:(a + 1) * 128], bld.PT[0:16, 0:128]))
            emit_small_transpose(bld, ub[:, a * 16:(a + 1) * 16], 128, 16, mk2(a))

        def tg(t):
            bld.w(t, "gq")
            bld.w(t, "vv")
            bld.w(t, "mm")
            bld.inc("mm", t.matmul(ps16[0:16, 0:1], bld.onesrow[:, 0:16], gsb[:, :], start=True, stop=True))

        def vg(v):
            bld.w(v, "mm")
            bld.w(v, "vv")
            bld.inc("vv", v.tensor_copy(gcol[:, :], ps16[0:16, 0:1]))
            bld.w(v, "vv")
            bld.inc("vv", v.tensor_scalar(uTg[0:16, :], uTg[0:16, :], gcol[0:16, :], None, ALU.mult))
        bld.phase(tensor=tg, vector=vg, order=("tensor", "vector"))

        for blk in range(NT // NTB):
            t0 = blk * NTB
            emit_ln_fm(bld, lns, resid, NT, t0, NTB, ln2, NTB, 0)
            if blk == 0:
                emit_dbg(bld, dram, "dbg_ln2", ln2[:, :])

            for (o, w) in tch(NTB):
                def ts1(t, o=o, w=w):
                    bld.w(t, "vv")
                    bld.w(t, "mm")
                    for c in range(DC):
                        bld.inc("mm", t.matmul(ps16[0:16, 0:w], vT[:, c * 16:(c + 1) * 16],
                                               ln2[:, c * NTB + o: c * NTB + o + w],
                                               start=(c == 0), stop=(c == DC - 1)))

                def vs1(v, o=o, w=w):
                    bld.w(v, "mm")
                    bld.w(v, "vv")
                    bld.inc("vv", v.tensor_copy(s1p[0:16, t0 + o: t0 + o + w], ps16[0:16, 0:w]))
                bld.phase(tensor=ts1, vector=vs1, order=("tensor", "vector"))

                def ts2(t, o=o, w=w):
                    bld.w(t, "vv")
                    bld.inc("mm", t.matmul(ps16[0:16, 0:w], fbT[:, 0:16], s1p[:, t0 + o: t0 + o + w],
                                           start=True, stop=True))

                def vs2(v, o=o, w=w):
                    bld.w(v, "mm")
                    bld.w(v, "vv")
                    bld.inc("vv", v.tensor_copy(s2p[0:16, t0 + o: t0 + o + w], ps16[0:16, 0:w]))
                bld.phase(tensor=ts2, vector=vs2, order=("tensor", "vector"))
            if blk == 0:
                emit_dbg(bld, dram, "dbg_s1", s1p[:, :])
                emit_dbg(bld, dram, "dbg_s2", s2p[:, :])

            for hc in range(HC):
                if blk == 0:
                    def sy(sy, hc=hc):
                        bld.w(sy, "mm")
                        bld.w(sy, "ss")
                        bld.w(sy, "vv")
                        bld.inc("sq", sy.dma_start(out=wt[:, :],
                                                   in_=dram["L0_ffn_w_fc"][hc * 128:(hc + 1) * 128, :]), 16)

                    def vp(v, hc=hc):
                        bld.w(v, "sq")
                        bld.w(v, "vv")
                        bld.inc("vv", v.tensor_reduce(red[:, :], wt[:, :], AX.X, ALU.add,
                                                      apply_absolute_value=True))
                        bld.w(v, "vv")
                        bld.inc("vv", v.tensor_scalar(hs[:, :], red[:, :], 0.5 / D, 0.5e-5, ALU.mult, ALU.max))
                        bld.w(v, "vv")
                        bld.inc("vv", v.tensor_scalar(scl[:, :], red[:, :], 1.0 / D, 1e-5, ALU.mult, ALU.max))
                        bld.w(v, "vv")
                        bld.inc("vv", v.tensor_tensor(hs2[:, :], hs[:, :], hs[:, :], ALU.mult))

                    def sp(s, hc=hc):
                        bld.w(s, "sq")
                        bld.w(s, "vv")
                        bld.w(s, "ss")
                        bld.inc("ss", s.activation(out=sgn[:, 0:1024], in_=wt[:, :], func=AF.Sign))
                        bld.w(s, "ss")
                        bld.inc("ss", s.activation(out=absq[:, 0:1024], in_=wt[:, :], func=AF.Square))
                    bld.phase(sync=sy, vector=vp, scalar=sp, order=("sync", "vector", "scalar"))

                    def vp2(v, hc=hc):
                        bld.w(v, "ss")
                        bld.w(v, "vv")
                        bld.inc("vv", v.tensor_scalar(absq[:, 0:1024], absq[:, 0:1024], hs2[:, :], None, ALU.subtract))
                        bld.w(v, "vv")
                        bld.inc("vv", v.tensor_scalar(weff[:, 0:1024], absq[:, 0:1024], 0.0, None, ALU.is_ge))
                        bld.w(v, "vv")
                        bld.inc("vv", v.tensor_tensor(weff[:, 0:1024], weff[:, 0:1024], sgn[:, 0:1024], ALU.mult))
                        bld.w(v, "vv")
                        bld.inc("vv", v.tensor_scalar(weff[:, 0:1024], weff[:, 0:1024], scl[:, :], None, ALU.mult))
                    bld.phase(vector=vp2, order=("vector",))

                    items = []
                    for dc in range(DC):
                        src = weff[:, dc * 128:(dc + 1) * 128]
                        dst = strip[:, dc * 128:(dc + 1) * 128]
                        ev = (lambda dst: (lambda v, pt: v.tensor_copy(dst, pt)))(dst)
                        items.append((src, 128, 128, ev))
                    emit_tpose_pipe(bld, items)

                    def sw(sy, hc=hc):
                        bld.w(sy, "vv")
                        bld.inc("sq", sy.dma_start(
                            out=dram["wfcT"].rearrange("(h x) c -> h x c", x=128)[hc], in_=strip[:, :]), 16)
                    bld.phase(sync=sw)
                    if hc == 0:
                        emit_dbg(bld, dram, "dbg_strip", strip[:, :])
                else:
                    def sr(sy, hc=hc):
                        bld.w(sy, "mm")
                        bld.inc("sq", sy.dma_start(
                            out=strip[:, :],
                            in_=dram["wfcT"].rearrange("(h x) c -> h x c", x=128)[hc]), 16)
                    bld.phase(sync=sr)

                for ti, (o, w) in enumerate(tch(NTB)):
                    pcur = ph if ti % 2 == 0 else ph2
                    def tf(t, hc=hc, o=o, w=w, pcur=pcur):
                        bld.w(t, "sq")
                        bld.w(t, "vv")
                        bld.w(t, "ss")
                        bld.w(t, "mm")
                        for c in range(DC):
                            bld.inc("mm", t.matmul(pcur[:, 0:w], strip[:, c * 128:(c + 1) * 128],
                                                   ln2[:, c * NTB + o: c * NTB + o + w],
                                                   start=(c == 0), stop=False))
                        bld.inc("mm", t.matmul(pcur[:, 0:w], uTg[:, hc * 128:(hc + 1) * 128],
                                               s2p[:, t0 + o: t0 + o + w], start=False, stop=True))

                    def sf(s, hc=hc, o=o, w=w, pcur=pcur):
                        bld.w(s, "mm")
                        bld.w(s, "vv")
                        bld.w(s, "ss")
                        bld.inc("ss", s.activation(out=rb[:, 0:w], in_=pcur[:, 0:w], func=AF.Relu))

                    def vf(v, hc=hc, o=o, w=w):
                        bld.w(v, "ss")
                        bld.w(v, "vv")
                        bld.inc("vv", v.tensor_tensor(r2T[:, hc * NTB + o: hc * NTB + o + w],
                                                      rb[:, 0:w], rb[:, 0:w], ALU.mult))
                    bld.phase(tensor=tf, scalar=sf, vector=vf, order=("tensor", "scalar", "vector"))
                if blk == 0 and hc == 0:
                    emit_dbg(bld, dram, "dbg_r2", r2T[:, 0:NTB])

            for dc in range(DC):
                if blk == 0:
                    def syp(sy, dc=dc):
                        bld.w(sy, "mm")
                        bld.w(sy, "ss")
                        bld.w(sy, "vv")
                        bld.inc("sq", sy.dma_start(out=wpt[:, :],
                                                   in_=dram["L0_ffn_w_proj"][dc * 128:(dc + 1) * 128, :]), 16)

                    def vpp(v, dc=dc):
                        bld.w(v, "sq")
                        bld.w(v, "vv")
                        bld.inc("vv", v.tensor_reduce(red[:, :], wpt[:, :], AX.X, ALU.add,
                                                      apply_absolute_value=True))
                        bld.w(v, "vv")
                        bld.inc("vv", v.tensor_scalar(hs[:, :], red[:, :], 0.5 / HID, 0.5e-5, ALU.mult, ALU.max))
                        bld.w(v, "vv")
                        bld.inc("vv", v.tensor_scalar(scl[:, :], red[:, :], 1.0 / HID, 1e-5, ALU.mult, ALU.max))
                        bld.w(v, "vv")
                        bld.inc("vv", v.tensor_tensor(hs2[:, :], hs[:, :], hs[:, :], ALU.mult))

                    def spp(s, dc=dc):
                        bld.w(s, "sq")
                        bld.w(s, "vv")
                        bld.w(s, "ss")
                        bld.inc("ss", s.activation(out=sgn[:, :], in_=wpt[:, :], func=AF.Sign))
                        bld.w(s, "ss")
                        bld.inc("ss", s.activation(out=absq[:, :], in_=wpt[:, :], func=AF.Square))
                    bld.phase(sync=syp, vector=vpp, scalar=spp, order=("sync", "vector", "scalar"))

                    def vpp2(v, dc=dc):
                        bld.w(v, "ss")
                        bld.w(v, "vv")
                        bld.inc("vv", v.tensor_scalar(absq[:, :], absq[:, :], hs2[:, :], None, ALU.subtract))
                        bld.w(v, "vv")
                        bld.inc("vv", v.tensor_scalar(weff[:, :], absq[:, :], 0.0, None, ALU.is_ge))
                        bld.w(v, "vv")
                        bld.inc("vv", v.tensor_tensor(weff[:, :], weff[:, :], sgn[:, :], ALU.mult))
                        bld.w(v, "vv")
                        bld.inc("vv", v.tensor_scalar(weff[:, :], weff[:, :], scl[:, :], None, ALU.mult))
                    bld.phase(vector=vpp2, order=("vector",))

                    items = []
                    for a in range(HC):
                        src = weff[:, a * 128:(a + 1) * 128]
                        dst = wpstrip[:, a * 128:(a + 1) * 128]
                        ev = (lambda dst: (lambda v, pt: v.tensor_copy(dst, pt)))(dst)
                        items.append((src, 128, 128, ev))
                    emit_tpose_pipe(bld, items)

                    def swp(sy, dc=dc):
                        bld.w(sy, "vv")
                        bld.inc("sq", sy.dma_start(
                            out=dram["wprojT"].rearrange("(d x) c -> d x c", x=128)[dc], in_=wpstrip[:, :]), 16)
                    bld.phase(sync=swp)
                    if dc == 0:
                        emit_dbg(bld, dram, "dbg_wps", wpstrip[:, :])
                else:
                    def srp(sy, dc=dc):
                        bld.w(sy, "mm")
                        bld.inc("sq", sy.dma_start(
                            out=wpstrip[:, :],
                            in_=dram["wprojT"].rearrange("(d x) c -> d x c", x=128)[dc]), 16)
                    bld.phase(sync=srp)

                for ti, (o, w) in enumerate(tch(NTB)):
                    npj = dc * len(tch(NTB)) + ti
                    pcur = po0 if npj % 2 == 0 else po1
                    if npj == 0:
                        pj_vv0 = bld.cnt["vv"]
                    pjm = pj_vv0
                    def tp(t, dc=dc, o=o, w=w, pcur=pcur, npj=npj, pjm=pjm):
                        bld.w(t, "sq")
                        if npj == 0:
                            bld.w(t, "vv")
                            bld.w(t, "mm")
                        elif npj >= 2:
                            t.wait_ge(bld.sem["vv"], pjm + npj - 1)
                        for a in range(HC):
                            bld.inc("mm", t.matmul(pcur[:, 0:w], wpstrip[:, a * 128:(a + 1) * 128],
                                                   r2T[:, a * NTB + o: a * NTB + o + w],
                                                   start=(a == 0), stop=(a == HC - 1)))

                    def vpj(v, dc=dc, o=o, w=w, pcur=pcur):
                        bld.w(v, "mm")
                        sl = resid[:, dc * NT + t0 + o: dc * NT + t0 + o + w]
                        bld.w(v, "vv")
                        bld.inc("vv", v.tensor_tensor(sl, sl, pcur[:, 0:w], ALU.add))
                    bld.phase(tensor=tp, vector=vpj, order=("tensor", "vector"))


def ffn_dram(nc, dram):
    dram["L0_ffn_w_fc"] = nc.dram_tensor("L0_ffn_w_fc", [HID, D], F32, kind="ExternalInput").ap()
    dram["L0_ffn_w_proj"] = nc.dram_tensor("L0_ffn_w_proj", [D, HID], F32, kind="ExternalInput").ap()
    dram["L0_ffn_u"] = nc.dram_tensor("L0_ffn_u", [HID, 16], F32, kind="ExternalInput").ap()
    dram["L0_ffn_v"] = nc.dram_tensor("L0_ffn_v", [16, D], F32, kind="ExternalInput").ap()
    dram["L0_ffn_fast_b"] = nc.dram_tensor("L0_ffn_fast_b", [16, 16], F32, kind="ExternalInput").ap()
    dram["L0_ffn_gate"] = nc.dram_tensor("L0_ffn_gate", [1], F32, kind="ExternalInput").ap()
    dram["wfcT"] = nc.dram_tensor("wfcT", [HC * 128, D], BF16, kind="Internal").ap()
    if DBG:
        dram["dbg_ln2"] = nc.dram_tensor("dbg_ln2", [128, DC * NTB], BF16, kind="ExternalOutput").ap()
        dram["dbg_s1"] = nc.dram_tensor("dbg_s1", [128, NT], BF16, kind="ExternalOutput").ap()
        dram["dbg_s2"] = nc.dram_tensor("dbg_s2", [128, NT], BF16, kind="ExternalOutput").ap()
        dram["dbg_strip"] = nc.dram_tensor("dbg_strip", [128, DC * 128], BF16, kind="ExternalOutput").ap()
        dram["dbg_r2"] = nc.dram_tensor("dbg_r2", [128, NTB], BF16, kind="ExternalOutput").ap()
        dram["dbg_wps"] = nc.dram_tensor("dbg_wps", [128, HC * 128], BF16, kind="ExternalOutput").ap()
    dram["wprojT"] = nc.dram_tensor("wprojT", [DC * 128, HC * 128], BF16, kind="Internal").ap()


def common_dram(nc, dram):
    dram["identf"] = nc.dram_tensor("identf", [128, 128], F32, kind="ExternalInput").ap()
    dram["ones_col"] = nc.dram_tensor("ones_col", [128, 1], F32, kind="ExternalInput").ap()
    dram["ones_row"] = nc.dram_tensor("ones_row", [1, 128], F32, kind="ExternalInput").ap()


def build_l0ffn_launch():
    nc = bass.Bass("TRN2", target_bir_lowering=False, debug=False)
    dram = {}
    dram["resid_in"] = nc.dram_tensor("resid_in", [DC, 128, NT], F32, kind="ExternalInput").ap()
    dram["resid_out"] = nc.dram_tensor("resid_out", [DC, 128, NT], F32, kind="ExternalOutput").ap()
    common_dram(nc, dram)
    ffn_dram(nc, dram)

    with ExitStack() as ctx:
        E = ctx.enter_context
        bld = Bld(nc, ctx)
        alloc_consts(bld, E)
        resid = E(nc.sbuf_tensor("resid", [128, DC * NT], F32))
        emit_load_consts(bld, dram)

        def ld(sy):
            for c in range(DC):
                bld.inc("sq", sy.dma_start(out=resid[:, c * NT:(c + 1) * NT], in_=dram["resid_in"][c]), 16)
        bld.phase(sync=ld)

        def bar(t):
            bld.w(t, "sq")
            bld.w(t, "gq")
        bld.phase(tensor=bar, order=("tensor",))

        emit_l0ffn(bld, resid, dram)

        def st(sy):
            bld.w(sy, "vv")
            bld.w(sy, "mm")
            bld.w(sy, "ss")
            for c in range(DC):
                bld.inc("sq", sy.dma_start(out=dram["resid_out"][c], in_=resid[:, c * NT:(c + 1) * NT]), 16)
            bld.w(sy, "sq")
        bld.phase(sync=st)
    return nc


def shard_inputs(inputs):
    x = np.ascontiguousarray(np.asarray(inputs["L0_attn_x"], np.float32))
    maps = []
    ident = np.eye(128, dtype=np.float32)
    ones_col = np.ones((128, 1), np.float32)
    ones_row = np.ones((1, 128), np.float32)
    for core in range(NCORES):
        b, half = core // 2, core % 2
        start = half * NOWN
        xr = np.zeros((NT, D), np.float32)
        lo = start - HALO
        if lo < 0:
            xr[HALO:, :] = x[b, start:start + NOWN, :]
        else:
            xr[:, :] = x[b, lo:start + NOWN, :]
        m = {
            "xrows": xr,
            "xfull": np.ascontiguousarray(x[b]),
            "identf": ident,
            "ones_col": ones_col,
            "ones_row": ones_row,
            "convmask": np.full((64, 1), 0.0 if half == 0 else 1.0, np.float32),
        }
        for k, v in inputs.items():
            if k == "L0_attn_x":
                continue
            a = np.asarray(v, np.float32)
            if a.ndim == 0:
                a = a.reshape(1)
            m[k] = np.ascontiguousarray(a)
        maps.append(m)
    return maps


def build_fused_launch():
    nc = bass.Bass("TRN2", target_bir_lowering=False, debug=False)
    dram = {}
    dram["resid_out"] = nc.dram_tensor("resid_out", [DC, 128, NT], F32, kind="ExternalOutput").ap()
    common_dram(nc, dram)
    attn_dram(nc, dram)
    ffn_dram(nc, dram)
    ssm_dram(nc, dram)
    moe_dram(nc, dram)
    dram["blockmask8"] = nc.dram_tensor("blockmask8", [128, 8], F32, kind="ExternalInput").ap()
    with ExitStack() as ctx:
        E = ctx.enter_context
        bld = Bld(nc, ctx)
        alloc_consts(bld, E)
        bld.msk8 = E(nc.sbuf_tensor("msk8", [128, 8], F32))
        resid = E(nc.sbuf_tensor("resid", [128, DC * NT], F32))
        emit_load_consts(bld, dram)

        def g1(g):
            bld.inc("gq", g.dma_start(out=bld.msk8[:, :], in_=dram["blockmask8"][:, :]), 16)
        bld.phase(gpsimd=g1)

        def bar(t):
            bld.w(t, "gq")
        bld.phase(tensor=bar, order=("tensor",))

        with ExitStack() as ctxa:
            QTall = ctxa.enter_context(nc.sbuf_tensor("QTall", [128, 8 * NT], BF16))
            emit_p0(bld, resid, QTall, dram)
            emit_l0attn(bld, resid, QTall, dram)
        emit_l0ffn(bld, resid, dram)
        emit_l1ssm(bld, resid, dram)
        emit_l1moe(bld, resid, dram)

        def st(sy):
            bld.w(sy, "vv")
            bld.w(sy, "mm")
            bld.w(sy, "ss")
            for c in range(DC):
                bld.inc("sq", sy.dma_start(out=dram["resid_out"][c], in_=resid[:, c * NT:(c + 1) * NT]), 16)
            bld.w(sy, "sq")
        bld.phase(sync=st)
    return nc


FUSED = True
_NC_CACHE = {}


def _get_nc(name, builder):
    if name not in _NC_CACHE:
        _NC_CACHE[name] = builder()
    return _NC_CACHE[name]


_ATTN_KEYS = ["xrows", "xfull", "L0_attn_Wq", "L0_attn_Wk", "L0_attn_Wv", "L0_attn_Wo",
              "L0_attn_quadrant_weights", "identf", "ones_col", "ones_row"]
_FFN_KEYS = ["L0_ffn_w_fc", "L0_ffn_w_proj", "L0_ffn_u", "L0_ffn_v", "L0_ffn_fast_b",
             "L0_ffn_gate", "identf", "ones_col", "ones_row"]
_SSM_KEYS = ["L1_attn_A", "L1_attn_Bm", "L1_attn_Cm", "L1_attn_Dm", "L1_attn_ln_w",
             "L1_attn_ln_b", "L1_attn_W_out", "L1_attn_b_out", "convmask", "blockmask8",
             "identf", "ones_col", "ones_row"]
_MOE_KEYS = ["L1_ffn_Wg", "L1_ffn_bg", "L1_ffn_bias", "L1_ffn_W1", "L1_ffn_b1",
             "L1_ffn_W2", "L1_ffn_b2", "identf", "ones_col", "ones_row"]


def kernel(**inputs):
    maps = shard_inputs(inputs)
    mask8 = np.zeros((128, 8), np.float32)
    for p in range(128):
        mask8[p, p // 16] = 1.0
    for m in maps:
        m["blockmask8"] = mask8

    cores = list(range(NCORES))

    if FUSED:
        keys = sorted(set(_ATTN_KEYS + _FFN_KEYS + _SSM_KEYS + _MOE_KEYS + ["blockmask8"]))
        nc = _get_nc("fused", build_fused_launch)
        res = run_bass_kernel_spmd(nc, [{k: m[k] for k in keys} for m in maps], core_ids=cores)
        resid = [r["resid_out"] for r in res.results]
        out = np.empty((4, 4096, D), np.float32)
        for core in range(NCORES):
            b, half = core // 2, core % 2
            start = half * NOWN
            r = np.asarray(resid[core], np.float32)
            blockrows = r.transpose(2, 0, 1).reshape(NT, D)
            out[b, start:start + NOWN, :] = blockrows[HALO:, :]
        return out

    nc_a = _get_nc("attn", build_l0attn_launch)
    res = run_bass_kernel_spmd(nc_a, [{k: m[k] for k in _ATTN_KEYS} for m in maps], core_ids=cores)
    resid = [r["resid_out"] for r in res.results]

    nc_b = _get_nc("ffn", build_l0ffn_launch)
    res = run_bass_kernel_spmd(
        nc_b, [dict({k: m[k] for k in _FFN_KEYS}, resid_in=resid[i]) for i, m in enumerate(maps)],
        core_ids=cores)
    resid = [r["resid_out"] for r in res.results]

    nc_c = _get_nc("ssm", build_l1ssm_launch)
    res = run_bass_kernel_spmd(
        nc_c, [dict({k: m[k] for k in _SSM_KEYS}, resid_in=resid[i]) for i, m in enumerate(maps)],
        core_ids=cores)
    resid = [r["resid_out"] for r in res.results]

    nc_d = _get_nc("moe", build_l1moe_launch)
    res = run_bass_kernel_spmd(
        nc_d, [dict({k: m[k] for k in _MOE_KEYS}, resid_in=resid[i]) for i, m in enumerate(maps)],
        core_ids=cores)
    resid = [r["resid_out"] for r in res.results]

    out = np.empty((4, 4096, D), np.float32)
    for core in range(NCORES):
        b, half = core // 2, core % 2
        start = half * NOWN
        r = np.asarray(resid[core], np.float32)
        blockrows = r.transpose(2, 0, 1).reshape(NT, D)
        out[b, start:start + NOWN, :] = blockrows[HALO:, :]
    return out


MOE_BLOCKS = [(0, 1152), (1152, 1024)]
NTBM = 1152
NE = 16


def emit_l1moe(bld, resid, dram):
    nc = bld.nc
    with ExitStack() as ctx:
        E = ctx.enter_context
        ln4 = E(nc.sbuf_tensor("ln4", [128, DC * NTBM], BF16))
        hT = E(nc.sbuf_tensor("hT", [128, NE * NTBM], BF16))
        W1b = E(nc.sbuf_tensor("W1b", [128, DC * 2048], BF16))
        W2d = [E(nc.sbuf_tensor("W2d0", [128, NE * 128], BF16)),
               E(nc.sbuf_tensor("W2d1", [128, NE * 128], BF16))]
        b1c = E(nc.sbuf_tensor("b1c", [128, NE * 16], F32))
        b2c = E(nc.sbuf_tensor("b2c", [128, NE * DC], F32))
        b1n = E(nc.sbuf_tensor("b1n", [16, 2048], F32))
        b2n = E(nc.sbuf_tensor("b2n", [16, 1024], F32))
        Wgb = E(nc.sbuf_tensor("Wgb", [128, DC * 16], BF16))
        bgr = E(nc.sbuf_tensor("bgr", [1, 16], F32))
        bir = E(nc.sbuf_tensor("bir", [1, 16], F32))
        onesw = E(nc.sbuf_tensor("onesw", [1, 512], F32))
        cw = E(nc.sbuf_tensor("cw", [128, 17 * 16], F32))
        cwT = E(nc.sbuf_tensor("cwT", [16, NT], BF16))
        cwrow = E(nc.sbuf_tensor("cwrow", [1, NT], BF16))
        cwbs = E(nc.sbuf_tensor("cwbs", [128, 3 * 512], BF16))
        t1b = E(nc.sbuf_tensor("t1b", [128, 1024], F32))
        lgt = t1b
        tA = E(nc.sbuf_tensor("tA", [128, 16], F32))
        tB = E(nc.sbuf_tensor("tB", [128, 16], F32))
        mk1 = E(nc.sbuf_tensor("mk1", [128, 16], F32))
        mk2 = E(nc.sbuf_tensor("mk2", [128, 16], F32))
        m1 = E(nc.sbuf_tensor("m1c", [128, 1], F32))
        m2 = E(nc.sbuf_tensor("m2c", [128, 1], F32))
        e2 = E(nc.sbuf_tensor("e2c", [128, 1], F32))
        den = E(nc.sbuf_tensor("denc", [128, 1], F32))
        rec = E(nc.sbuf_tensor("recc", [128, 1], F32))
        w2c = E(nc.sbuf_tensor("w2cc", [128, 1], F32))
        lns4 = alloc_ln_scr(bld, E, "m", tmp=t1b)
        ph, ph2, po0, po1, pg = bld.P[0], bld.P[1], bld.P[2], bld.P[3], bld.P[4]

        def g0(g):
            wg = dram["L1_ffn_Wg"].rearrange("(c p) e -> p c e", p=128)
            bld.inc("gq", g.dma_start(out=Wgb[:, :].rearrange("p (c e) -> p c e", e=16), in_=wg), 16)
            bld.inc("gq", g.dma_start(out=bgr[:, :], in_=dram["L1_ffn_bg"][None, :]), 16)
            bld.inc("gq", g.dma_start(out=bir[:, :], in_=dram["L1_ffn_bias"][None, :]), 16)
            bld.inc("gq", g.dma_start(out=b1n[:, :], in_=dram["L1_ffn_b1"][:, :]), 16)
            bld.inc("gq", g.dma_start(out=b2n[:, :], in_=dram["L1_ffn_b2"][:, :]), 16)

        def v0(v):
            bld.V(v, lambda: v.memset(onesw[:, :], 1.0))
        bld.phase(gpsimd=g0, vector=v0)

        def v1(v):
            bld.w(v, "gq")
            bld.V(v, lambda: v.tensor_tensor(bgr[:, :], bgr[:, :], bir[:, :], ALU.add))
        bld.phase(vector=v1)

        for hc in range(NE):
            def tb1(t, hc=hc):
                bld.w(t, "gq")
                bld.w(t, "vv")
                bld.w(t, "mm")
                bld.inc("mm", t.matmul(bld.PTF[0:128, 0:16], b1n[:, hc * 128:(hc + 1) * 128],
                                       bld.identf[0:16, 0:16], start=True, stop=True, is_transpose=True))

            def vb1(v, hc=hc):
                bld.w(v, "mm")
                bld.V(v, lambda: v.tensor_copy(b1c[:, hc * 16:(hc + 1) * 16], bld.PTF[0:128, 0:16]))
            bld.phase(tensor=tb1, vector=vb1, order=("tensor", "vector"))
        for dc in range(DC):
            def tb2(t, dc=dc):
                bld.w(t, "gq")
                bld.w(t, "vv")
                bld.w(t, "mm")
                bld.inc("mm", t.matmul(bld.PTF[0:128, 0:16], b2n[:, dc * 128:(dc + 1) * 128],
                                       bld.identf[0:16, 0:16], start=True, stop=True, is_transpose=True))

            def vb2(v, dc=dc):
                bld.w(v, "mm")
                bld.V(v, lambda: v.tensor_copy(b2c[:, dc * 16:(dc + 1) * 16], bld.PTF[0:128, 0:16]))
            bld.phase(tensor=tb2, vector=vb2, order=("tensor", "vector"))

        for (t0, ntb) in MOE_BLOCKS:
            emit_ln_fm(bld, lns4, resid, NT, t0, ntb, ln4, NTBM, 0)
            if t0 == 0:
                emit_dbg(bld, dram, "dbg_ln4", ln4[:, :])

            for (o, w) in tch(ntb):
                def tl(t, o=o, w=w):
                    bld.w(t, "vv")
                    bld.w(t, "mm")
                    for c in range(DC):
                        bld.inc("mm", t.matmul(pg[0:16, 0:w], Wgb[:, c * 16:(c + 1) * 16],
                                               ln4[:, c * NTBM + o: c * NTBM + o + w],
                                               start=(c == 0), stop=False))
                    bld.w(t, "mm")
                    bld.inc("mm", t.matmul(pg[0:16, 0:w], bgr[:, :], onesw[:, 0:w],
                                           start=False, stop=True))

                def vl(v, o=o, w=w):
                    bld.w(v, "mm")
                    bld.V(v, lambda: v.tensor_copy(lgt[0:16, 0:w], pg[0:16, 0:w]))
                bld.phase(tensor=tl, vector=vl, order=("tensor", "vector"))
                if t0 == 0 and o == 0:
                    emit_dbg(bld, dram, "dbg_lgt", lgt[0:16, 0:512])

                for k in range(w // 128):
                    tile = (t0 + o) // 128 + k
                    def tt(t, k=k, tile=tile, w=w):
                        bld.w(t, "vv")
                        bld.w(t, "mm")
                        bld.inc("mm", t.matmul(bld.PTF[0:128, 0:16], lgt[0:16, k * 128:(k + 1) * 128],
                                               bld.identf[0:16, 0:16], start=True, stop=True,
                                               is_transpose=True))

                    def vt(v, k=k, tile=tile):
                        bld.w(v, "mm")
                        bld.V(v, lambda: v.tensor_copy(cw[:, tile * 16:(tile + 1) * 16], bld.PTF[0:128, 0:16]))
                    bld.phase(tensor=tt, vector=vt, order=("tensor", "vector"))

            nt_tiles = ntb // 128
            for k in range(nt_tiles):
                tile = t0 // 128 + k
                cwt = cw[:, tile * 16:(tile + 1) * 16]
                def vtop(v, cwt=cwt):
                    bld.V(v, lambda: v.tensor_reduce(m1[:, :], cwt, AX.X, ALU.max))
                    bld.V(v, lambda: v.tensor_scalar(tA[:, :], cwt, m1[:, :], None, ALU.subtract))
                    bld.V(v, lambda: v.tensor_scalar(mk1[:, :], tA[:, :], 0.0, None, ALU.is_ge))
                    bld.V(v, lambda: v.tensor_scalar(tB[:, :], mk1[:, :], -1e30, None, ALU.mult))
                    bld.V(v, lambda: v.tensor_tensor(tB[:, :], tB[:, :], cwt, ALU.add))
                    bld.V(v, lambda: v.tensor_reduce(m2[:, :], tB[:, :], AX.X, ALU.max))
                    bld.V(v, lambda: v.tensor_scalar(tB[:, :], tB[:, :], m2[:, :], None, ALU.subtract))
                    bld.V(v, lambda: v.tensor_scalar(mk2[:, :], tB[:, :], 0.0, None, ALU.is_ge))
                    bld.V(v, lambda: v.tensor_tensor(e2[:, :], m2[:, :], m1[:, :], ALU.subtract))

                def stop_(s):
                    bld.w(s, "vv")
                    bld.S(s, lambda: s.activation(out=e2[:, :], in_=e2[:, :], func=AF.Exp))

                def vtop2(v, cwt=cwt):
                    bld.w(v, "ss")
                    bld.V(v, lambda: v.tensor_scalar(den[:, :], e2[:, :], 1.0, None, ALU.add))
                    bld.V(v, lambda: v.reciprocal(rec[:, :], den[:, :]))
                    bld.V(v, lambda: v.tensor_tensor(w2c[:, :], e2[:, :], rec[:, :], ALU.mult))
                    bld.V(v, lambda: v.tensor_scalar(mk1[:, :], mk1[:, :], rec[:, :], None, ALU.mult))
                    bld.V(v, lambda: v.scalar_tensor_tensor(cwt, mk2[:, :], w2c[:, :], mk1[:, :],
                                                            ALU.mult, ALU.add))
                bld.phase(vector=vtop, order=("vector",))
                bld.phase(scalar=stop_, order=("scalar",))
                bld.phase(vector=vtop2, order=("vector",))

                def tct(t, tile=tile):
                    bld.w(t, "vv")
                    bld.w(t, "mm")
                    bld.inc("mm", t.matmul(bld.PTF[0:16, 0:128], cw[:, tile * 16:(tile + 1) * 16],
                                           bld.identf[:, :], start=True, stop=True, is_transpose=True))

                def vct(v, tile=tile):
                    bld.w(v, "mm")
                    bld.V(v, lambda: v.tensor_copy(cwT[0:16, tile * 128:(tile + 1) * 128], bld.PTF[0:16, 0:128]))
                bld.phase(tensor=tct, vector=vct, order=("tensor", "vector"))

            def emit_dma_w1(e):
                def g(g, e=e):
                    bld.w(g, "mm")
                    bld.w(g, "vv")
                    for dcn in range(DC):
                        bld.inc("gq", g.dma_start(out=W1b[:, dcn * 2048:(dcn + 1) * 2048],
                                                  in_=dram["L1_ffn_W1"][e, dcn * 128:(dcn + 1) * 128, :]), 16)
                bld.phase(gpsimd=g)
                bld.gq_marks[("w1", e)] = bld.cnt["gq"]

            def emit_dma_w2(e, dc):
                buf = W2d[dc % 2]
                def g(g, e=e, dc=dc, buf=buf):
                    bld.w(g, "mm")
                    src = dram["L1_ffn_W2"][e].rearrange("(hcn p) c -> p hcn c", p=128)[:, :, dc * 128:(dc + 1) * 128]
                    dst = buf[:, :].rearrange("p (hcn c) -> p hcn c", c=128)
                    bld.inc("gq", g.dma_start(out=dst[:, 0:8, :], in_=src[:, 0:8, :]), 16)
                    bld.inc("gq", g.dma_start(out=dst[:, 8:16, :], in_=src[:, 8:16, :]), 16)
                bld.phase(gpsimd=g)
                bld.gq_marks[("w2", e, dc)] = bld.cnt["gq"]

            if t0 == 0:
                emit_dbg(bld, dram, "dbg_cw", cw[:, :])
                emit_dbg(bld, dram, "dbg_cwT", cwT[:, :])
            emit_dma_w1(0)
            for e in range(NE):
                ss0 = bld.cnt["ss"]
                chunks = tch(ntb)
                nev = 2 if ntb > 1024 else 1
                for hc in range(NE):
                    big = bld.PW0 if hc % 2 == 0 else bld.PW1
                    tail = bld.P[0] if hc % 2 == 0 else bld.P[1]
                    def tf(t, e=e, hc=hc, big=big, tail=tail, ss0=ss0, chunks=chunks, nev=nev):
                        t.wait_ge(bld.sem["gq"], bld.gq_marks[("w1", e)])
                        if hc == 0:
                            bld.w(t, "vv")
                            bld.w(t, "ss")
                            bld.w(t, "mm")
                        elif hc == 1:
                            bld.w(t, "vv")
                        else:
                            t.wait_ge(bld.sem["ss"], ss0 + nev * (hc - 1))
                        for ti, (o, w) in enumerate(chunks):
                            dst = big[:, o:o + w] if o < 1024 else tail[:, 0:w]
                            for c in range(DC):
                                bld.inc("mm", t.matmul(dst,
                                                       W1b[:, c * 2048 + hc * 128: c * 2048 + (hc + 1) * 128],
                                                       ln4[:, c * NTBM + o: c * NTBM + o + w],
                                                       start=(c == 0), stop=(c == DC - 1)))

                    def sf(s, e=e, hc=hc, big=big, tail=tail, ntb=ntb):
                        bld.w(s, "mm")
                        wbig = min(1024, ntb)
                        bld.S(s, lambda: s.activation(out=hT[:, hc * NTBM: hc * NTBM + wbig],
                                                      in_=big[:, 0:wbig], func=AF.Relu,
                                                      bias=b1c[:, hc * 16 + e: hc * 16 + e + 1]))
                        if ntb > 1024:
                            bld.S(s, lambda: s.activation(out=hT[:, hc * NTBM + 1024: hc * NTBM + ntb],
                                                          in_=tail[:, 0:ntb - 1024], func=AF.Relu,
                                                          bias=b1c[:, hc * 16 + e: hc * 16 + e + 1]))
                    bld.phase(tensor=tf, scalar=sf, order=("tensor", "scalar"))
                if t0 == 0 and e == 0:
                    emit_dbg(bld, dram, "dbg_hT", hT[:, 0:NTBM])
                if e + 1 < NE:
                    emit_dma_w1(e + 1)

                def gcw(g, e=e):
                    bld.w(g, "mm")
                    bld.inc("gq", g.dma_start(out=cwrow[:, :], in_=cwT[e:e + 1, :]), 16)
                bld.phase(gpsimd=gcw)
                bld.gq_marks[("cwr", e, t0)] = bld.cnt["gq"]
                for ti, (o, w) in enumerate(tch(ntb)):
                    def tb_(t, e=e, ti=ti, o=o, w=w):
                        t.wait_ge(bld.sem["gq"], bld.gq_marks[("cwr", e, t0)])
                        bld.w(t, "vv")
                        bld.w(t, "mm")
                        bld.inc("mm", t.matmul(pg[:, 0:w], bld.onesrow_b[:, :],
                                               cwrow[:, t0 + o: t0 + o + w], start=True, stop=True))

                    def vb(v, ti=ti, o=o, w=w):
                        bld.w(v, "mm")
                        bld.V(v, lambda: v.tensor_copy(cwbs[:, ti * 512: ti * 512 + w], pg[:, 0:w]))
                    bld.phase(tensor=tb_, vector=vb, order=("tensor", "vector"))

                emit_dma_w2(e, 0)
                emit_dma_w2(e, 1)
                relu_done = bld.cnt["ss"]
                vv0 = bld.cnt["vv"]
                nops = 4 if ntb > 1024 else 2
                for dc in range(DC):
                    buf = W2d[dc % 2]
                    big2 = bld.PW0 if dc % 2 == 0 else bld.PW1
                    tail2 = bld.P[0] if dc % 2 == 0 else bld.P[1]
                    def tw2(t, e=e, dc=dc, buf=buf, big2=big2, tail2=tail2,
                            relu_done=relu_done, vv0=vv0, nops=nops, chunks=chunks):
                        t.wait_ge(bld.sem["gq"], bld.gq_marks[("w2", e, dc)])
                        t.wait_ge(bld.sem["ss"], relu_done)
                        if dc == 0:
                            bld.w(t, "vv")
                            bld.w(t, "mm")
                        elif dc >= 2:
                            t.wait_ge(bld.sem["vv"], vv0 + nops * (dc - 1))
                        for ti, (o, w) in enumerate(chunks):
                            dst = big2[:, o:o + w] if o < 1024 else tail2[:, 0:w]
                            for hcn in range(NE):
                                bld.inc("mm", t.matmul(dst,
                                                       buf[:, hcn * 128:(hcn + 1) * 128],
                                                       hT[:, hcn * NTBM + o: hcn * NTBM + o + w],
                                                       start=(hcn == 0), stop=(hcn == NE - 1)))

                    def vw2(v, e=e, dc=dc, big2=big2, tail2=tail2, ntb=ntb):
                        bld.w(v, "mm")
                        wbig = min(1024, ntb)
                        bld.V(v, lambda: v.scalar_tensor_tensor(
                            t1b[:, 0:wbig], big2[:, 0:wbig], b2c[:, dc * 16 + e: dc * 16 + e + 1],
                            cwbs[:, 0:wbig], ALU.add, ALU.mult))
                        sl = resid[:, dc * NT + t0: dc * NT + t0 + wbig]
                        bld.V(v, lambda sl=sl: v.tensor_tensor(sl, sl, t1b[:, 0:wbig], ALU.add))
                        if ntb > 1024:
                            wt_ = ntb - 1024
                            bld.V(v, lambda: v.scalar_tensor_tensor(
                                t1b[:, 0:wt_], tail2[:, 0:wt_], b2c[:, dc * 16 + e: dc * 16 + e + 1],
                                cwbs[:, 1024:1024 + wt_], ALU.add, ALU.mult))
                            sl2 = resid[:, dc * NT + t0 + 1024: dc * NT + t0 + ntb]
                            bld.V(v, lambda sl2=sl2: v.tensor_tensor(sl2, sl2, t1b[:, 0:wt_], ALU.add))
                    bld.phase(tensor=tw2, vector=vw2, order=("tensor", "vector"))
                    if dc + 2 < DC:
                        emit_dma_w2(e, dc + 2)
                    elif e + 1 < NE:
                        emit_dma_w2(e + 1, dc - 6)
                if t0 == 0 and e == 0:
                    emit_dbg(bld, dram, "dbg_res0", resid[:, 0:512])
                    emit_dbg(bld, dram, "dbg_cwbs", cwbs[:, 0:512])


def moe_dram(nc, dram):
    dram["L1_ffn_Wg"] = nc.dram_tensor("L1_ffn_Wg", [D, 16], F32, kind="ExternalInput").ap()
    dram["L1_ffn_bg"] = nc.dram_tensor("L1_ffn_bg", [16], F32, kind="ExternalInput").ap()
    dram["L1_ffn_bias"] = nc.dram_tensor("L1_ffn_bias", [16], F32, kind="ExternalInput").ap()
    dram["L1_ffn_W1"] = nc.dram_tensor("L1_ffn_W1", [16, D, 2048], F32, kind="ExternalInput").ap()
    dram["L1_ffn_b1"] = nc.dram_tensor("L1_ffn_b1", [16, 2048], F32, kind="ExternalInput").ap()
    dram["L1_ffn_W2"] = nc.dram_tensor("L1_ffn_W2", [16, 2048, D], F32, kind="ExternalInput").ap()
    dram["L1_ffn_b2"] = nc.dram_tensor("L1_ffn_b2", [16, D], F32, kind="ExternalInput").ap()


def build_l1moe_launch():
    nc = bass.Bass("TRN2", target_bir_lowering=False, debug=False)
    dram = {}
    dram["resid_in"] = nc.dram_tensor("resid_in", [DC, 128, NT], F32, kind="ExternalInput").ap()
    dram["resid_out"] = nc.dram_tensor("resid_out", [DC, 128, NT], F32, kind="ExternalOutput").ap()
    common_dram(nc, dram)
    moe_dram(nc, dram)
    if DBG:
        dram["dbg_ln4"] = nc.dram_tensor("dbg_ln4", [128, DC * NTBM], BF16, kind="ExternalOutput").ap()
        dram["dbg_cw"] = nc.dram_tensor("dbg_cw", [128, 17 * 16], F32, kind="ExternalOutput").ap()
        dram["dbg_cwT"] = nc.dram_tensor("dbg_cwT", [16, NT], BF16, kind="ExternalOutput").ap()
        dram["dbg_hT"] = nc.dram_tensor("dbg_hT", [128, NTBM], BF16, kind="ExternalOutput").ap()
        dram["dbg_lgt"] = nc.dram_tensor("dbg_lgt", [16, 512], F32, kind="ExternalOutput").ap()
        dram["dbg_res0"] = nc.dram_tensor("dbg_res0", [128, 512], F32, kind="ExternalOutput").ap()
        dram["dbg_cwbs"] = nc.dram_tensor("dbg_cwbs", [128, 512], BF16, kind="ExternalOutput").ap()
    with ExitStack() as ctx:
        E = ctx.enter_context
        bld = Bld(nc, ctx)
        alloc_consts(bld, E)
        resid = E(nc.sbuf_tensor("resid", [128, DC * NT], F32))
        emit_load_consts(bld, dram)

        def ld(sy):
            for c in range(DC):
                bld.inc("sq", sy.dma_start(out=resid[:, c * NT:(c + 1) * NT], in_=dram["resid_in"][c]), 16)
        bld.phase(sync=ld)

        def bar(t):
            bld.w(t, "sq")
            bld.w(t, "gq")
        bld.phase(tensor=bar, order=("tensor",))

        emit_l1moe(bld, resid, dram)

        def st(sy):
            bld.w(sy, "vv")
            bld.w(sy, "mm")
            bld.w(sy, "ss")
            for c in range(DC):
                bld.inc("sq", sy.dma_start(out=dram["resid_out"][c], in_=resid[:, c * NT:(c + 1) * NT]), 16)
            bld.w(sy, "sq")
        bld.phase(sync=st)
    return nc


NKT = 4096


ATTN_PAIRS = 8
ATTN_QS = None


def emit_l0attn(bld, resid, QTall, dram):
    nc = bld.nc
    with ExitStack() as ctx:
        E = ctx.enter_context
        wbc = E(nc.sbuf_tensor("wbc", [128, 4], F32))
        q4 = E(nc.sbuf_tensor("q4", [1, 4], F32))
        qm = E(nc.sbuf_tensor("qm", [1, 1], F32))
        Wqs = E(nc.sbuf_tensor("Wqs", [128, DC * 128], BF16))
        Wks = E(nc.sbuf_tensor("Wks", [128, DC * 128], BF16))
        Wke = E(nc.sbuf_tensor("Wke", [128, DC * 128], BF16))
        Wvs = E(nc.sbuf_tensor("Wvs", [128, DC * 128], BF16))
        Wos = E(nc.sbuf_tensor("Wos", [128, DC * 128], BF16))
        tm32 = E(nc.sbuf_tensor("tm32", [128, 32], F32))
        lnsl = [E(nc.sbuf_tensor("lnsl0", [128, DC * 512], BF16)),
                E(nc.sbuf_tensor("lnsl1", [128, DC * 512], BF16))]
        KT = E(nc.sbuf_tensor("KT", [128, NKT], BF16))
        Vp = E(nc.sbuf_tensor("Vp", [128, 32 * 130], BF16))
        PA = E(nc.sbuf_tensor("PA", [128, 32 * 128], BF16))
        PB = E(nc.sbuf_tensor("PB", [128, 32 * 128], BF16))
        OT = E(nc.sbuf_tensor("OT", [128, NT], BF16))
        recA = E(nc.sbuf_tensor("recA", [128, 1], F32))
        recB = E(nc.sbuf_tensor("recB", [128, 1], F32))
        psSA = bld.PW0
        psSB = bld.PW1
        pOA, pOB, pKV = bld.P[0], bld.P[1], bld.P[2]
        pKV2 = bld.P[3]

        def g0(g):
            bld.inc("gq", g.dma_start(out=q4[:, :], in_=dram["L0_attn_quadrant_weights"][None, :]), 16)

        def v0(v):
            bld.w(v, "gq")
            bld.V(v, lambda: v.tensor_reduce(qm[:, :], q4[:, :], AX.X, ALU.max))
            bld.V(v, lambda: v.tensor_scalar(q4[:, :], q4[:, :], qm[:, :], None, ALU.subtract))
        bld.phase(gpsimd=g0, vector=v0)

        def s0(s):
            bld.w(s, "vv")
            bld.S(s, lambda: s.activation(out=q4[:, :], in_=q4[:, :], func=AF.Exp))
        bld.phase(scalar=s0)

        def v1(v):
            bld.w(v, "ss")
            bld.V(v, lambda: v.tensor_reduce(qm[:, :], q4[:, :], AX.X, ALU.add))
            bld.V(v, lambda: v.reciprocal(qm[:, :], qm[:, :]))
            bld.V(v, lambda: v.tensor_scalar(q4[:, :], q4[:, :], qm[:, :], None, ALU.mult))
            bld.V(v, lambda: v.tensor_scalar(q4[:, :], q4[:, :], float(1.0 / np.sqrt(32.0)), None, ALU.mult))
        bld.phase(vector=v1)

        def t1(t):
            bld.w(t, "vv")
            bld.w(t, "mm")
            bld.inc("mm", t.matmul(bld.PTF[:, 0:4], bld.onesrow[:, :], q4[:, :], start=True, stop=True))

        def v2(v):
            bld.w(v, "mm")
            bld.V(v, lambda: v.tensor_copy(wbc[:, :], bld.PTF[:, 0:4]))
        bld.phase(tensor=t1, vector=v2, order=("tensor", "vector"))

        for p in range(ATTN_PAIRS):
            pc = slice(p * 128, (p + 1) * 128)

            def gw(g, p=p):
                bld.w(g, "mm")
                bld.w(g, "vv")
                for name, dst in [("L0_attn_Wq", Wqs), ("L0_attn_Wk", Wks), ("L0_attn_Wv", Wvs)]:
                    for c in range(DC):
                        bld.inc("gq", g.dma_start(out=dst[:, c * 128:(c + 1) * 128],
                                                  in_=dram[name][c * 128:(c + 1) * 128, p * 128:(p + 1) * 128]), 16)
                bld.inc("gq", g.dma_start(out=Wos[:, :], in_=dram["L0_attn_Wo"][p * 128:(p + 1) * 128, :]), 16)
            bld.phase(gpsimd=gw)

            def vmix(v, p=p):
                bld.w(v, "gq")
                for c in range(DC):
                    for h2 in range(2):
                        base = c * 128 + h2 * 64
                        k1 = Wks[:, base:base + 32]
                        k2 = Wks[:, base + 32:base + 64]
                        bld.V(v, lambda k2=k2: v.tensor_scalar(tm32[:, :], k2, wbc[:, 1:2], None, ALU.mult))
                        bld.V(v, lambda k1=k1, base=base: v.scalar_tensor_tensor(
                            Wke[:, base:base + 32], k1, wbc[:, 0:1], tm32[:, :], ALU.mult, ALU.add))
                        bld.V(v, lambda k2=k2: v.tensor_scalar(tm32[:, :], k2, wbc[:, 3:4], None, ALU.mult))
                        bld.V(v, lambda k1=k1, base=base: v.scalar_tensor_tensor(
                            Wke[:, base + 32:base + 64], k1, wbc[:, 2:3], tm32[:, :], ALU.mult, ALU.add))
            bld.phase(vector=vmix)

            for tc in range(8):
                buf = lnsl[tc % 2]
                def gl(g, tc=tc, buf=buf):
                    bld.w(g, "mm")
                    for c in range(DC):
                        bld.inc("gq", g.dma_start(
                            out=buf[:, c * 512:(c + 1) * 512],
                            in_=dram["ln1s"].rearrange("(c x) t -> c x t", x=128)[c, :, tc * 512:(tc + 1) * 512]), 16)
                bld.phase(gpsimd=gl)
                gqm = bld.cnt["gq"]

                vvb = bld.cnt["vv"]
                def tkv(t, p=p, tc=tc, buf=buf, gqm=gqm, vvb=vvb):
                    t.wait_ge(bld.sem["gq"], gqm)
                    bld.w(t, "vv")
                    bld.w(t, "mm")
                    for c in range(DC):
                        bld.inc("mm", t.matmul(pKV[:, 0:512], Wke[:, c * 128:(c + 1) * 128],
                                               buf[:, c * 512:(c + 1) * 512],
                                               start=(c == 0), stop=(c == DC - 1)))
                    for tt_ in range(4):
                        if tt_ > 0:
                            t.wait_ge(bld.sem["vv"], vvb + 1 + 2 * tt_)
                        for c in range(DC):
                            bld.inc("mm", t.matmul(pKV2[:, 0:128],
                                                   buf[:, c * 512 + tt_ * 128: c * 512 + tt_ * 128 + 128],
                                                   Wvs[:, c * 128:(c + 1) * 128],
                                                   start=(c == 0), stop=(c == DC - 1)))

                def vkv(v, p=p, tc=tc):
                    mmb = bld.cnt["mm"] - 40
                    v.wait_ge(bld.sem["mm"], mmb + 8)
                    bld.inc("vv", v.tensor_copy(KT[:, tc * 512:(tc + 1) * 512], pKV[:, 0:512]))
                    for tt_ in range(4):
                        v.wait_ge(bld.sem["mm"], mmb + 8 + (tt_ + 1) * 8)
                        tile = tc * 4 + tt_
                        bld.w(v, "vv")
                        bld.inc("vv", v.tensor_copy(Vp[:, tile * 130: tile * 130 + 64], pKV2[0:128, 0:64]))
                        bld.w(v, "vv")
                        bld.inc("vv", v.tensor_copy(Vp[:, tile * 130 + 65: tile * 130 + 129], pKV2[0:128, 64:128]))
                bld.phase(tensor=tkv, vector=vkv, order=("tensor", "vector"))

            def vones(v, p=p):
                bld.w(v, "mm")
                bld.V(v, lambda: v.memset(Vp[:, :].rearrange("x (a b) -> x a b", b=65)[:, :, 64], 1.0))
            if p == 0:
                bld.phase(vector=vones)

            for qs in range(NT // 128 if ATTN_QS is None else ATTN_QS):
                for g4 in range(4):
                    def tS(t, p=p, qs=qs, g4=g4):
                        if g4 == 0:
                            bld.w(t, "mm")
                        bld.w(t, "ss")
                        for kk in range(8):
                            kc = g4 * 8 + kk
                            bld.inc("mm", t.matmul(psSA[:, kk * 128:(kk + 1) * 128],
                                                   KT[0:64, kc * 128:(kc + 1) * 128],
                                                   QTall[0:64, p * NT + qs * 128: p * NT + (qs + 1) * 128],
                                                   start=True, stop=True, tile_position=(0, 0)))
                            bld.inc("mm", t.matmul(psSB[:, kk * 128:(kk + 1) * 128],
                                                   KT[64:128, kc * 128:(kc + 1) * 128],
                                                   QTall[64:128, p * NT + qs * 128: p * NT + (qs + 1) * 128],
                                                   start=True, stop=True, tile_position=(64, 0)))

                    def sS(s, p=p, qs=qs, g4=g4):
                        bld.w(s, "mm")
                        bld.S(s, lambda: s.activation(out=PA[:, g4 * 1024:(g4 + 1) * 1024],
                                                      in_=psSA[:, :], func=AF.Exp))
                        bld.S(s, lambda: s.activation(out=PB[:, g4 * 1024:(g4 + 1) * 1024],
                                                      in_=psSB[:, :], func=AF.Exp))
                    bld.phase(tensor=tS, scalar=sS, order=("tensor", "scalar"))

                def tAV(t, p=p, qs=qs):
                    bld.w(t, "mm")
                    bld.w(t, "ss")
                    bld.w(t, "vv")
                    for kc in range(32):
                        bld.inc("mm", t.matmul(pOA[:, 0:65], PA[:, kc * 128:(kc + 1) * 128],
                                               Vp[:, kc * 130: kc * 130 + 65],
                                               start=(kc == 0), stop=(kc == 31)))
                        bld.inc("mm", t.matmul(pOB[:, 0:65], PB[:, kc * 128:(kc + 1) * 128],
                                               Vp[:, kc * 130 + 65: kc * 130 + 130],
                                               start=(kc == 0), stop=(kc == 31)))

                def vAV(v, p=p, qs=qs):
                    bld.w(v, "mm")
                    bld.V(v, lambda: v.reciprocal(recA[:, :], pOA[:, 64:65]))
                    bld.V(v, lambda: v.reciprocal(recB[:, :], pOB[:, 64:65]))
                    bld.V(v, lambda: v.tensor_scalar(pOA[:, 0:64], pOA[:, 0:64], recA[:, :], None, ALU.mult))
                    bld.V(v, lambda: v.tensor_scalar(pOB[:, 0:64], pOB[:, 0:64], recB[:, :], None, ALU.mult))
                bld.phase(tensor=tAV, vector=vAV, order=("tensor", "vector"))

                def vO(v, p=p, qs=qs):
                    bld.w(v, "mm")
                    bld.V(v, lambda: v.tensor_copy(lnsl[0][:, 0:64], pOA[:, 0:64]))
                    bld.V(v, lambda: v.tensor_copy(lnsl[0][:, 64:128], pOB[:, 0:64]))
                bld.phase(vector=vO)

                for h2 in range(2):
                    def tT(t, h2=h2, qs=qs):
                        bld.w(t, "vv")
                        bld.w(t, "mm")
                        bld.inc("mm", t.matmul(bld.PT[0:64, 0:128], lnsl[0][:, h2 * 64:(h2 + 1) * 64],
                                               bld.identb[:, :], start=True, stop=True, is_transpose=True))

                    def vT(v, h2=h2, qs=qs):
                        bld.w(v, "mm")
                        bld.V(v, lambda: v.tensor_copy(OT[h2 * 64:(h2 + 1) * 64, qs * 128:(qs + 1) * 128],
                                                       bld.PT[0:64, 0:128]))
                    bld.phase(tensor=tT, vector=vT, order=("tensor", "vector"))

            for dc in range(DC):
                for ti, (o, w) in enumerate(tch(NT)):
                    pcur = bld.P[4] if ti % 2 == 0 else bld.P[5]
                    def tP(t, p=p, dc=dc, o=o, w=w, pcur=pcur):
                        bld.w(t, "vv")
                        bld.w(t, "mm")
                        bld.inc("mm", t.matmul(pcur[:, 0:w], Wos[:, dc * 128:(dc + 1) * 128],
                                               OT[:, o:o + w], start=True, stop=True))

                    def vP(v, p=p, dc=dc, o=o, w=w, pcur=pcur):
                        bld.w(v, "mm")
                        sl = resid[:, dc * NT + o: dc * NT + o + w]
                        bld.V(v, lambda: v.tensor_tensor(sl, sl, pcur[:, 0:w], ALU.add))
                    bld.phase(tensor=tP, vector=vP, order=("tensor", "vector"))


def emit_p0(bld, resid, QTall, dram):
    nc = bld.nc
    with ExitStack() as ctx:
        E = ctx.enter_context
        xt = [E(nc.sbuf_tensor("xt0", [128, 1024], F32)), E(nc.sbuf_tensor("xt1", [128, 1024], F32))]
        xc = E(nc.sbuf_tensor("xc", [128, 1024], F32))
        lnb = E(nc.sbuf_tensor("lnb", [128, 1024], BF16))
        lnf = E(nc.sbuf_tensor("lnf", [128, DC * 128], BF16))
        mcol = E(nc.sbuf_tensor("mcol", [128, 1], F32))
        scol2 = E(nc.sbuf_tensor("scol2", [128, 1], F32))
        sqd = E(nc.sbuf_tensor("sqd", [128, 1024], BF16))


        def ln_tile(src_ap, ti, into_resid, into_qt, into_scratch):
            buf = xt[ti % 2]
            def gld(g):
                bld.w(g, "mm")
                bld.w(g, "vv")
                bld.w(g, "ss")
                bld.inc("gq", g.dma_start(out=buf[:, :], in_=src_ap), 16)
            bld.phase(gpsimd=gld)
            gqm = bld.cnt["gq"]

            def vln(v):
                v.wait_ge(bld.sem["gq"], gqm)
                bld.V(v, lambda: v.tensor_reduce(mcol[:, :], buf[:, :], AX.X, ALU.add))
                bld.V(v, lambda: v.tensor_scalar(mcol[:, :], mcol[:, :], 1.0 / D, None, ALU.mult))
                bld.V(v, lambda: v.tensor_scalar(xc[:, :], buf[:, :], mcol[:, :], None, ALU.subtract))

            def sln(s):
                bld.w(s, "vv")
                bld.S(s, lambda: s.activation(out=sqd[:, :], in_=xc[:, :], func=AF.Square,
                                              accum_out=scol2[:, :]))
            bld.phase(vector=vln, scalar=sln, order=("vector", "scalar"))

            def vln2(v):
                bld.w(v, "ss")
                bld.V(v, lambda: v.tensor_scalar(scol2[:, :], scol2[:, :], 1.0 / D, EPS, ALU.mult, ALU.add))

            def sln2(s):
                bld.w(s, "vv")
                bld.S(s, lambda: s.activation(out=scol2[:, :], in_=scol2[:, :], func=AF.Ln))
                bld.S(s, lambda: s.activation(out=scol2[:, :], in_=scol2[:, :], func=AF.Exp, scale=-0.5))
            bld.phase(vector=vln2, order=("vector",))
            bld.phase(scalar=sln2, order=("scalar",))

            def vln3(v):
                bld.w(v, "ss")
                bld.V(v, lambda: v.tensor_scalar(lnb[:, :], xc[:, :], scol2[:, :], None, ALU.mult))
            bld.phase(vector=vln3, order=("vector",))

            items = []
            for c in range(DC):
                src = lnb[:, c * 128:(c + 1) * 128]
                dst = lnf[:, c * 128:(c + 1) * 128]
                ev = (lambda dst: (lambda v, pt: v.tensor_copy(dst, pt)))(dst)
                items.append((src, 128, 128, ev))
            emit_tpose_pipe(bld, items)

            if into_resid is not None:
                items = []
                for c in range(DC):
                    src = buf[:, c * 128:(c + 1) * 128]
                    dst = into_resid[c]
                    ev = (lambda dst: (lambda v, pt: v.tensor_copy(dst, pt)))(dst)
                    items.append((src, 128, 128, ev))
                emit_tpose_pipe(bld, items, f32=True)

            if into_scratch is not None:
                def sst(sy):
                    bld.w(sy, "vv")
                    for c in range(DC):
                        bld.inc("sq", sy.dma_start(out=into_scratch[c], in_=lnf[:, c * 128:(c + 1) * 128]), 16)
                bld.phase(sync=sst)


        QWALL = E(nc.sbuf_tensor("QWALL", [128, 64 * 128], BF16))

        def gq0(g):
            for c in range(DC):
                bld.inc("gq", g.dma_start(out=QWALL[:, c * 1024:(c + 1) * 1024],
                                          in_=dram["L0_attn_Wq"][c * 128:(c + 1) * 128, :]), 16)
        bld.phase(gpsimd=gq0)

        for ti in range(NT // 128):
            ln_tile(dram["xrows"][ti * 128:(ti + 1) * 128, :], ti,
                    [resid[:, c * NT + ti * 128: c * NT + (ti + 1) * 128] for c in range(DC)],
                    False, None)
            for p in range(8):
                def tq(t, p=p, ti=ti):
                    bld.w(t, "vv")
                    bld.w(t, "mm")
                    bld.w(t, "gq")
                    for c in range(DC):
                        bld.inc("mm", t.matmul(bld.P[0][:, 0:128],
                                               QWALL[:, c * 1024 + p * 128: c * 1024 + (p + 1) * 128],
                                               lnf[:, c * 128:(c + 1) * 128],
                                               start=(c == 0), stop=(c == DC - 1)))

                def vq(v, p=p, ti=ti):
                    bld.w(v, "mm")
                    bld.V(v, lambda: v.tensor_copy(QTall[:, p * NT + ti * 128:(p * NT + (ti + 1) * 128)],
                                                   bld.P[0][:, 0:128]))
                bld.phase(tensor=tq, vector=vq, order=("tensor", "vector"))

        for ti in range(NKT // 128):
            dst = dram["ln1s"].rearrange("(c x) t -> c x t", x=128)[:, :, ti * 128:(ti + 1) * 128]
            ln_tile(dram["xfull"][ti * 128:(ti + 1) * 128, :], ti, None, False, dst)


def attn_dram(nc, dram):
    dram["L0_attn_Wq"] = nc.dram_tensor("L0_attn_Wq", [D, D], F32, kind="ExternalInput").ap()
    dram["L0_attn_Wk"] = nc.dram_tensor("L0_attn_Wk", [D, D], F32, kind="ExternalInput").ap()
    dram["L0_attn_Wv"] = nc.dram_tensor("L0_attn_Wv", [D, D], F32, kind="ExternalInput").ap()
    dram["L0_attn_Wo"] = nc.dram_tensor("L0_attn_Wo", [D, D], F32, kind="ExternalInput").ap()
    dram["L0_attn_quadrant_weights"] = nc.dram_tensor("L0_attn_quadrant_weights", [4], F32,
                                                      kind="ExternalInput").ap()
    dram["xrows"] = nc.dram_tensor("xrows", [NT, D], F32, kind="ExternalInput").ap()
    dram["xfull"] = nc.dram_tensor("xfull", [NKT, D], F32, kind="ExternalInput").ap()
    dram["ln1s"] = nc.dram_tensor("ln1s", [DC * 128, NKT], BF16, kind="Internal").ap()


def build_l0attn_launch():
    nc = bass.Bass("TRN2", target_bir_lowering=False, debug=False)
    dram = {}
    dram["resid_out"] = nc.dram_tensor("resid_out", [DC, 128, NT], F32, kind="ExternalOutput").ap()
    common_dram(nc, dram)
    attn_dram(nc, dram)
    with ExitStack() as ctx:
        E = ctx.enter_context
        bld = Bld(nc, ctx)
        alloc_consts(bld, E)
        resid = E(nc.sbuf_tensor("resid", [128, DC * NT], F32))
        QTall = E(nc.sbuf_tensor("QTall", [128, 8 * NT], BF16))
        emit_load_consts(bld, dram)
        emit_p0(bld, resid, QTall, dram)
        emit_l0attn(bld, resid, QTall, dram)

        def st(sy):
            bld.w(sy, "vv")
            bld.w(sy, "mm")
            bld.w(sy, "ss")
            for c in range(DC):
                bld.inc("sq", sy.dma_start(out=dram["resid_out"][c], in_=resid[:, c * NT:(c + 1) * NT]), 16)
            bld.w(sy, "sq")
        bld.phase(sync=st)
    return nc


def emit_l1ssm(bld, resid, dram):
    nc = bld.nc
    NTILES = 128
    with ExitStack() as ctx:
        E = ctx.enter_context
        Astr = E(nc.sbuf_tensor("Astr", [128, NTILES * 16], BF16))
        Vall = E(nc.sbuf_tensor("Vall", [128, NTILES * 128], BF16))
        Tall = E(nc.sbuf_tensor("Tall", [128, NTILES * 128], BF16))
        Pall = E(nc.sbuf_tensor("Pall", [128, NTILES * 128], BF16))
        Cblk = E(nc.sbuf_tensor("Cblk", [128, NTILES * 8], BF16))
        bca = E(nc.sbuf_tensor("bca", [128, 128], F32))
        cca = E(nc.sbuf_tensor("cca", [128, 128], F32))
        dca = E(nc.sbuf_tensor("dca", [128, 8], F32))
        bcaT = E(nc.sbuf_tensor("bcaT", [128, 128], F32))
        ccaT = E(nc.sbuf_tensor("ccaT", [128, 128], F32))
        dcaT = E(nc.sbuf_tensor("dcaT", [128, 128], F32))
        gst = E(nc.sbuf_tensor("gst", [128, 512], F32))
        grev = E(nc.sbuf_tensor("grev", [128, 512], BF16))
        gTsb = E(nc.sbuf_tensor("gTsb", [128, 1024], BF16))
        zb = E(nc.sbuf_tensor("zb", [128, 1024], BF16))

        def g0(g):
            for dloc in range(8):
                for th in range(2):
                    src = bass.AP(dram["L1_attn_A"].tensor, (th * 64 * 8 + dloc) * 256,
                                  [[16, 16], [8 * 256, 64], [1, 16]])
                    dst = Astr[dloc * 16:(dloc + 1) * 16, th * 64 * 16:(th + 1) * 64 * 16] \
                        .rearrange("n (t m) -> n t m", m=16)
                    bld.inc("gq", g.dma_start(out=dst, in_=src), 16)
            bld.inc("gq", g.dma_start(out=bca[:, :],
                                      in_=dram["L1_attn_Bm"].rearrange("(t a) n -> t (a n)", a=8)), 16)
            bld.inc("gq", g.dma_start(out=cca[:, :],
                                      in_=dram["L1_attn_Cm"].rearrange("(t a) n -> t (a n)", a=8)), 16)
            bld.inc("gq", g.dma_start(out=dca[:, :],
                                      in_=dram["L1_attn_Dm"].rearrange("(t p) -> t p", p=8)), 16)

        def v0(v):
            bld.V(v, lambda: v.memset(zb[0:64, :], 0.0))
        bld.phase(gpsimd=g0, vector=v0)

        for (srcT, dstT, rows) in [(bca, bcaT, 128), (cca, ccaT, 128), (dca, dcaT, 8)]:
            def tt(t, srcT=srcT, rows=rows):
                bld.w(t, "gq")
                bld.w(t, "vv")
                bld.w(t, "mm")
                bld.inc("mm", t.matmul(bld.PTF[0:rows, 0:128], srcT[:, 0:rows], bld.identf[:, :],
                                       start=True, stop=True, is_transpose=True))

            def vv(v, dstT=dstT, rows=rows):
                bld.w(v, "mm")
                bld.V(v, lambda: v.tensor_copy(dstT[0:rows, 0:128], bld.PTF[0:rows, 0:128]))
            bld.phase(tensor=tt, vector=vv, order=("tensor", "vector"))

        def vA(v):
            bld.w(v, "gq")
            for dl in range(8):
                def op(dl=dl):
                    o = Pall[:, :].rearrange("p (t f) -> p t f", f=128)[:, :, dl * 16:(dl + 1) * 16]
                    i0 = Astr[:, :].rearrange("p (t m) -> p t m", m=16)
                    i1 = dram
                    return v.scalar_tensor_tensor(o, i0, bld.msk8[:, dl:dl + 1], i0, ALU.bypass, ALU.mult)
                o = Pall[:, :].rearrange("p (t f) -> p t f", f=128)[:, :, dl * 16:(dl + 1) * 16]
                i0 = Astr[:, :].rearrange("p (t m) -> p t m", m=16)
                bld.V(v, lambda o=o, i0=i0, dl=dl: v.tensor_scalar(o, i0, bld.msk8[:, dl:dl + 1],
                                                                   None, ALU.mult))
            o = Vall[:, :].rearrange("p (t f) -> p t f", f=128)[:, :, 127]
            bld.V(v, lambda o=o: v.tensor_copy(o, bcaT[:, :]))
            for dl in range(8):
                o = Cblk[:, :].rearrange("p (t f) -> p t f", f=8)[:, :, dl]
                bld.V(v, lambda o=o, dl=dl: v.tensor_scalar(o, ccaT[:, :], bld.msk8[:, dl:dl + 1],
                                                            None, ALU.mult))
        bld.phase(vector=vA)

        for t_ in range(NTILES):
            def ttr(t, t_=t_):
                bld.w(t, "vv")
                bld.w(t, "mm")
                bld.inc("mm", t.matmul(bld.PT[:, :], Pall[:, t_ * 128:(t_ + 1) * 128],
                                       bld.identb[:, :], start=True, stop=True, is_transpose=True))

            def vtr(v, t_=t_):
                bld.w(v, "mm")
                bld.V(v, lambda: v.tensor_copy(Tall[:, t_ * 128:(t_ + 1) * 128], bld.PT[:, :]))
            bld.phase(tensor=ttr, vector=vtr, order=("tensor", "vector"))

        for k in range(7):
            wk = 1 << k
            per = max(1, 512 // (wk if wk > 0 else 1)) if wk else 8
            per = min(8, 512 // wk)
            grp = 0
            t_ = 0
            while t_ < NTILES:
                n = min(per, NTILES - t_)
                pb = bld.P[0] if grp % 2 == 0 else bld.P[1]
                def tmv(t, t_=t_, n=n, wk=wk, pb=pb):
                    bld.w(t, "vv")
                    bld.w(t, "ss")
                    bld.w(t, "mm")
                    for j in range(n):
                        bld.inc("mm", t.matmul(pb[:, j * wk:(j + 1) * wk],
                                               Tall[:, (t_ + j) * 128:(t_ + j + 1) * 128],
                                               Vall[:, (t_ + j) * 128 + 128 - wk: (t_ + j) * 128 + 128],
                                               start=True, stop=True))

                def smv(s, t_=t_, n=n, wk=wk, pb=pb):
                    bld.w(s, "mm")
                    for j in range(n):
                        bld.S(s, lambda j=j: s.copy(
                            Vall[:, (t_ + j) * 128 + 128 - 2 * wk: (t_ + j) * 128 + 128 - wk],
                            pb[:, j * wk:(j + 1) * wk]))
                bld.phase(tensor=tmv, scalar=smv, order=("tensor", "scalar"))
                t_ += n
                grp += 1
            if k == 6:
                break
            t_ = 0
            grp = 0
            while t_ < NTILES:
                n = min(4, NTILES - t_)
                pb = bld.PW0 if grp % 2 == 0 else bld.PW1
                pb2 = bld.P[0] if grp % 2 == 0 else bld.P[1]
                def tsq(t, t_=t_, n=n, pb=pb, pb2=pb2):
                    bld.w(t, "vv")
                    bld.w(t, "ss")
                    bld.w(t, "mm")
                    for j in range(n):
                        sl = slice((t_ + j) * 128, (t_ + j + 1) * 128)
                        bld.inc("mm", t.matmul(pb[:, j * 128:(j + 1) * 128] if j < 8 else None,
                                               Tall[:, sl], Pall[:, sl], start=True, stop=True))
                    for j in range(n):
                        sl = slice((t_ + j) * 128, (t_ + j + 1) * 128)
                        bld.inc("mm", t.matmul(pb2[:, j * 128:(j + 1) * 128],
                                               Pall[:, sl], Tall[:, sl], start=True, stop=True))

                def ssq(s, t_=t_, n=n, pb=pb, pb2=pb2):
                    bld.w(s, "mm")
                    for j in range(n):
                        sl = slice((t_ + j) * 128, (t_ + j + 1) * 128)
                        bld.S(s, lambda sl=sl, j=j: s.copy(Pall[:, sl], pb[:, j * 128:(j + 1) * 128]))

                def vsq(v, t_=t_, n=n, pb=pb, pb2=pb2):
                    bld.w(v, "mm")
                    for j in range(n):
                        sl = slice((t_ + j) * 128, (t_ + j + 1) * 128)
                        bld.V(v, lambda sl=sl, j=j: v.tensor_copy(Tall[:, sl],
                                                                  pb2[:, j * 128:(j + 1) * 128]))
                bld.phase(tensor=tsq, scalar=ssq, vector=vsq, order=("tensor", "scalar", "vector"))
                t_ += n
                grp += 1

        for t0_ in range(0, NTILES, 4):
            pb = bld.P[0]
            def tge(t, t0_=t0_, pb=pb):
                bld.w(t, "vv")
                bld.w(t, "ss")
                bld.w(t, "mm")
                for j in range(4):
                    sl = slice((t0_ + j) * 128, (t0_ + j + 1) * 128)
                    bld.inc("mm", t.matmul(pb[0:8, j * 128:(j + 1) * 128],
                                           Cblk[:, (t0_ + j) * 8:(t0_ + j + 1) * 8],
                                           Vall[:, sl], start=True, stop=True))

            def vge(v, t0_=t0_, pb=pb):
                bld.w(v, "mm")
                bld.V(v, lambda: v.tensor_copy(gst[0:8, :], pb[0:8, 0:512]))
                bld.V(v, lambda: v.tensor_copy(grev[0:8, :], gst[0:8, :]))
                o = grev[0:8, :].rearrange("p (t j) -> p t j", j=128)[:, :, 127]
                bld.V(v, lambda o=o: v.tensor_tensor(o, o, dcaT[0:8, t0_:t0_ + 4], ALU.add))
            bld.phase(tensor=tge, vector=vge, order=("tensor", "vector"))

            for j in range(4):
                def ttg(t, t0_=t0_, j=j):
                    bld.w(t, "vv")
                    bld.w(t, "mm")
                    bld.inc("mm", t.matmul(bld.PT[0:128, j * 8:(j + 1) * 8],
                                           grev[0:8, j * 128:(j + 1) * 128],
                                           bld.identb[0:8, 0:8], start=True, stop=True,
                                           is_transpose=True))
                bld.phase(tensor=ttg)

            def vgt(v, t0_=t0_):
                bld.w(v, "mm")
                bld.V(v, lambda: v.tensor_copy(gTsb[:, t0_ * 8:(t0_ + 4) * 8], bld.PT[0:128, 0:32]))
            bld.phase(vector=vgt)

        def sw(sy):
            bld.w(sy, "vv")
            bld.inc("sq", sy.dma_start(out=dram["gTdR"][0:128, :], in_=gTsb[:, :]), 16)
            bld.inc("sq", sy.dma_start(out=dram["gTdR"][128:192, :], in_=zb[0:64, :]), 16)
            bld.w(sy, "sq")
        bld.phase(sync=sw)
        emit_dbg(bld, dram, "dbg_g", gTsb[:, :])
        emit_dbg(bld, dram, "dbg_V", Vall[:, 0:2048])

    with ExitStack() as ctx:
        E = ctx.enter_context
        ln3f = E(nc.sbuf_tensor("ln3f", [128, DC * NT], BF16))
        y64 = E(nc.sbuf_tensor("y64", [128, 16 * 1024], BF16))
        lnwc = E(nc.sbuf_tensor("lnwc", [128, 8], F32))
        lnbc = E(nc.sbuf_tensor("lnbc", [128, 8], F32))
        boc = E(nc.sbuf_tensor("boc", [128, 8], F32))
        cmk = E(nc.sbuf_tensor("cmk", [128, 1], F32))
        with ExitStack() as ctxl:
            lns3 = alloc_ln_scr(bld, ctxl.enter_context, "s")
            emit_ln_fm(bld, lns3, resid, NT, 0, NT, ln3f, NT, 0)

        with ExitStack() as ctxw:
            tmpW = ctxw.enter_context(nc.sbuf_tensor("tmpW", [128, DC * 1024], BF16))
            WoT = ctxw.enter_context(nc.sbuf_tensor("WoTs", [128, DC * 1024], BF16))

            def gwo(g):
                bld.inc("gq", g.dma_start(out=cmk[0:64, :], in_=dram["convmask"][:, :]), 16)
                for dc in range(DC):
                    bld.inc("gq", g.dma_start(out=tmpW[:, dc * 1024:(dc + 1) * 1024],
                                              in_=dram["L1_attn_W_out"][dc * 128:(dc + 1) * 128, :]), 16)
            bld.phase(gpsimd=gwo)
            items = []
            for dc in range(DC):
                for c in range(DC):
                    src = tmpW[:, dc * 1024 + c * 128: dc * 1024 + (c + 1) * 128]
                    dst = WoT[:, c * 1024 + dc * 128: c * 1024 + (dc + 1) * 128]
                    ev = (lambda dst: (lambda v, pt: v.tensor_copy(dst, pt)))(dst)
                    items.append((src, 128, 128, ev))
            emit_tpose_pipe(bld, items)

            def swot(sy):
                bld.w(sy, "vv")
                bld.inc("sq", sy.dma_start(out=dram["wotT"][0:64, :], in_=WoT[0:64, 0:8192]), 16)
                bld.inc("sq", sy.dma_start(out=dram["wotT"][64:128, :], in_=WoT[64:128, 0:8192]), 16)
                bld.w(sy, "sq")
            bld.phase(sync=swot)

        with ExitStack() as ctxr:
            lrow = ctxr.enter_context(nc.sbuf_tensor("lrow", [1, 1024], F32))
            for (name, dstc) in [("L1_attn_ln_w", lnwc), ("L1_attn_ln_b", lnbc), ("L1_attn_b_out", boc)]:
                def gl(g, name=name):
                    bld.w(g, "vv")
                    bld.inc("gq", g.dma_start(out=lrow[:, :], in_=dram[name][None, :]), 16)
                bld.phase(gpsimd=gl)
                for c in range(DC):
                    def tl2(t, c=c):
                        bld.w(t, "gq")
                        bld.w(t, "vv")
                        bld.w(t, "mm")
                        bld.inc("mm", t.matmul(bld.PTF[0:128, 0:1], lrow[:, c * 128:(c + 1) * 128],
                                               bld.identf[0:1, 0:1], start=True, stop=True, is_transpose=True))

                    def vl2(v, c=c, dstc=dstc):
                        bld.w(v, "mm")
                        bld.V(v, lambda: v.tensor_copy(dstc[:, c:c + 1], bld.PTF[0:128, 0:1]))
                    bld.phase(tensor=tl2, vector=vl2, order=("tensor", "vector"))

        for (hb, (t0, n)) in enumerate([(128, 1024), (1152, 1024)]):
            nblk = n // 64
            b0 = t0 // 64
            wb0 = b0 - 1
            with ExitStack() as ctx1:
                E1 = ctx1.enter_context
                x64 = E1(nc.sbuf_tensor(f"x64_{hb}", [128, 17 * 1024], BF16))
                Gt = E1(nc.sbuf_tensor(f"Gt_{hb}", [128, 64 * 256], BF16))
                items = []
                for wb in range(17):
                    blkt = wb0 + wb
                    for c in range(DC):
                        dst = x64[0:64, wb * 1024 + c * 128: wb * 1024 + (c + 1) * 128]
                        src = ln3f[:, c * NT + blkt * 64: c * NT + (blkt + 1) * 64]
                        if blkt < 2:
                            ev = (lambda dst: (lambda v, pt: v.tensor_scalar(
                                dst, pt, cmk[0:64, :], None, ALU.mult)))(dst)
                        else:
                            ev = (lambda dst: (lambda v, pt: v.tensor_copy(dst, pt)))(dst)
                        items.append((src, 128, 64, ev))
                emit_tpose_pipe(bld, items)

                for ct in range(8):
                    def ga(sy, ct=ct):
                        bld.w(sy, "mm")
                        for i in range(64):
                            src = bass.AP(dram["gTdR"].tensor, (63 - i) * 1024 + ct * 128,
                                          [[1024, 64], [64 * 1024, 2], [1, 128]])
                            dst = Gt[0:64, i * 256:(i + 1) * 256].rearrange("w (h c) -> w h c", c=128)
                            bld.inc("sq", sy.dma_start(out=dst, in_=src), 16)
                    bld.phase(sync=ga)
                    sqm = bld.cnt["sq"]

                    ssb = bld.cnt["ss"]
                    mmb = bld.cnt["mm"]
                    for g4_ in range(32):
                        pb = bld.P[0] if g4_ % 2 == 0 else bld.P[1]
                        def tcv(t, ct=ct, g4_=g4_, pb=pb, sqm=sqm, ssb=ssb):
                            t.wait_ge(bld.sem["sq"], sqm)
                            if g4_ == 0:
                                bld.w(t, "ss")
                                bld.w(t, "mm")
                            elif g4_ > 1:
                                t.wait_ge(bld.sem["ss"], ssb + g4_ - 1)
                            for cj in range(4):
                                ch = g4_ * 4 + cj
                                lh = Gt[0:64, :].rearrange("w (i h c) -> w i h c", h=2, c=128)
                                rh = x64[0:64, :].rearrange("w (b f) -> w b f", f=1024)
                                bld.inc("mm", t.matmul(pb[0:64, cj * 16:(cj + 1) * 16],
                                                       lh[:, :, 0, ch],
                                                       rh[:, 0:16, ct * 128 + ch], start=True, stop=False))
                                bld.inc("mm", t.matmul(pb[0:64, cj * 16:(cj + 1) * 16],
                                                       lh[:, :, 1, ch],
                                                       rh[:, 1:17, ct * 128 + ch], start=False, stop=True))

                        def scv(s, ct=ct, g4_=g4_, pb=pb, mmb=mmb):
                            s.wait_ge(bld.sem["mm"], mmb + 8 * (g4_ + 1))
                            o = y64[0:64, :].rearrange("w (b f) -> w b f", f=1024) \
                                [:, :, ct * 128 + g4_ * 4: ct * 128 + (g4_ + 1) * 4]
                            bld.inc("ss", s.copy(o, pb[0:64, 0:64].rearrange("w (c b) -> w b c", b=16)))
                        bld.phase(tensor=tcv, scalar=scv, order=("tensor", "scalar"))

            if hb == 0:
                emit_dbg(bld, dram, "dbg_x64", x64[0:64, :])
                emit_dbg(bld, dram, "dbg_y64", y64[0:64, :])
            with ExitStack() as ctx2:
                E2 = ctx2.enter_context
                m64 = E2(nc.sbuf_tensor(f"m64_{hb}", [128, 1], F32))
                s64 = E2(nc.sbuf_tensor(f"s64_{hb}", [128, 1], F32))
                sq64 = E2(nc.sbuf_tensor(f"sq64_{hb}", [128, 1024], BF16))
                yT = E2(nc.sbuf_tensor(f"yT_{hb}", [128, DC * 1024], BF16))
                vwtmp = E2(nc.sbuf_tensor(f"vwtmp_{hb}", [128, 512], F32))
                WoT = E2(nc.sbuf_tensor(f"WoT2_{hb}", [128, DC * 1024], BF16))
                dbgpb = E2(nc.sbuf_tensor(f"dbgpb_{hb}", [128, 512], F32))
                vwtmp2 = E2(nc.sbuf_tensor(f"vwtmp2_{hb}", [128, 512], F32))

                def gwot(g):
                    bld.w(g, "mm")
                    bld.w(g, "vv")
                    bld.inc("gq", g.dma_start(out=WoT[:, :], in_=dram["wotT"][:, :]), 16)
                bld.phase(gpsimd=gwot)
                gq_wot = bld.cnt["gq"]
                if hb == 0:
                    def wotbar(t, gq_wot=gq_wot):
                        t.wait_ge(bld.sem["gq"], gq_wot)
                    bld.phase(tensor=wotbar, order=("tensor",))
                    emit_dbg(bld, dram, "dbg_wot", WoT[:, :])
                for b in range(16):
                    def vst(v, b=b):
                        bld.w(v, "ss")
                        bld.V(v, lambda: v.tensor_reduce(m64[0:64, :], y64[0:64, b * 1024:(b + 1) * 1024],
                                                         AX.X, ALU.add))
                        bld.V(v, lambda: v.tensor_scalar(m64[0:64, :], m64[0:64, :], 1.0 / D, None, ALU.mult))
                        bld.V(v, lambda: v.tensor_scalar(sq64[0:64, :], y64[0:64, b * 1024:(b + 1) * 1024],
                                                         m64[0:64, :], None, ALU.subtract))

                    def sst2(s, b=b):
                        bld.w(s, "vv")
                        bld.S(s, lambda: s.activation(out=y64[0:64, b * 1024:(b + 1) * 1024],
                                                      in_=sq64[0:64, :], func=AF.Square,
                                                      accum_out=s64[0:64, :]))
                    bld.phase(vector=vst, scalar=sst2, order=("vector", "scalar"))

                    def vst2(v, b=b):
                        bld.w(v, "ss")
                        bld.V(v, lambda: v.tensor_scalar(s64[0:64, :], s64[0:64, :], 1.0 / D, EPS, ALU.mult, ALU.add))
                    bld.phase(vector=vst2)

                    def sst3(s, b=b):
                        bld.w(s, "vv")
                        bld.S(s, lambda: s.activation(out=s64[0:64, :], in_=s64[0:64, :], func=AF.Ln))
                        bld.S(s, lambda: s.activation(out=s64[0:64, :], in_=s64[0:64, :], func=AF.Exp,
                                                      scale=-0.5))
                    bld.phase(scalar=sst3)

                    def vst3(v, b=b):
                        bld.w(v, "ss")
                        bld.V(v, lambda: v.tensor_scalar(y64[0:64, b * 1024:(b + 1) * 1024],
                                                         sq64[0:64, :], s64[0:64, :], None, ALU.mult))
                    bld.phase(vector=vst3)
                    if hb == 0 and b == 0:
                        emit_dbg(bld, dram, "dbg_m64", m64[0:64, :])
                        emit_dbg(bld, dram, "dbg_s64", s64[0:64, :])
                        emit_dbg(bld, dram, "dbg_yb0", y64[0:64, 0:1024])
                        emit_dbg(bld, dram, "dbg_sq0", sq64[0:64, :])

                items = []
                for b in range(16):
                    for c in range(DC):
                        src = y64[0:64, b * 1024 + c * 128: b * 1024 + (c + 1) * 128]
                        dst = yT[:, c * 1024 + b * 64: c * 1024 + (b + 1) * 64]
                        ev = (lambda dst, c: (lambda v, pt: v.tensor_scalar(
                            dst, pt, lnwc[:, c:c + 1], lnbc[:, c:c + 1], ALU.mult, ALU.add)))(dst, c)
                        items.append((src, 64, 128, ev))
                emit_tpose_pipe(bld, items)

                if hb == 0:
                    emit_dbg(bld, dram, "dbg_y64b", y64[0:64, :])
                    emit_dbg(bld, dram, "dbg_z", yT[:, :])

                def sg2(s):
                    bld.w(s, "vv")
                    for c in range(DC):
                        bld.S(s, lambda c=c: s.activation(out=yT[:, c * 1024:(c + 1) * 1024],
                                                          in_=yT[:, c * 1024:(c + 1) * 1024], func=AF.Gelu))
                bld.phase(scalar=sg2)
                if hb == 0:
                    emit_dbg(bld, dram, "dbg_gel", yT[:, :])

                for dc in range(DC):
                    pbA, pbB = bld.P[3], bld.P[5]
                    def tw(t, dc=dc, gq_wot=gq_wot):
                        t.wait_ge(bld.sem["gq"], gq_wot)
                        bld.w(t, "gq")
                        bld.w(t, "sq")
                        bld.w(t, "vv")
                        bld.w(t, "ss")
                        bld.w(t, "mm")
                        for c in range(DC):
                            bld.inc("mm", t.matmul(pbA[:, 0:512],
                                                   WoT[:, c * 1024 + dc * 128: c * 1024 + (dc + 1) * 128],
                                                   yT[:, c * 1024: c * 1024 + 512],
                                                   start=(c == 0), stop=(c == DC - 1)))
                            bld.inc("mm", t.matmul(pbB[:, 0:512],
                                                   WoT[:, c * 1024 + dc * 128: c * 1024 + (dc + 1) * 128],
                                                   yT[:, c * 1024 + 512: c * 1024 + 1024],
                                                   start=(c == 0), stop=(c == DC - 1)))

                    def sw2(s, dc=dc):
                        bld.w(s, "mm")
                        bld.w(s, "vv")
                        bld.S(s, lambda: s.copy(vwtmp[:, 0:512], pbA[:, 0:512]))
                        bld.S(s, lambda: s.copy(vwtmp2[:, 0:512], pbB[:, 0:512]))

                    def vw(v, dc=dc, t0=t0):
                        bld.w(v, "ss")
                        for (o, buf) in [(0, vwtmp), (512, vwtmp2)]:
                            bld.V(v, lambda o=o, buf=buf: v.scalar_tensor_tensor(
                                buf[:, 0:512], buf[:, 0:512], boc[:, dc:dc + 1],
                                ln3f[:, dc * NT + t0 + o: dc * NT + t0 + o + 512], ALU.add, ALU.add))
                            sl = resid[:, dc * NT + t0 + o: dc * NT + t0 + o + 512]
                            bld.V(v, lambda sl=sl, buf=buf: v.tensor_tensor(sl, sl, buf[:, 0:512], ALU.add))
                    bld.phase(tensor=tw, scalar=sw2, vector=vw, order=("tensor", "scalar", "vector"))
                if hb == 0:
                    emit_dbg(bld, dram, "dbg_res0c", resid[:, 0:2176])
                    emit_dbg(bld, dram, "dbg_pb", dbgpb[:, :])


def ssm_dram(nc, dram):
    dram["L1_attn_A"] = nc.dram_tensor("L1_attn_A", [D, 16, 16], F32, kind="ExternalInput").ap()
    dram["L1_attn_Bm"] = nc.dram_tensor("L1_attn_Bm", [D, 16], F32, kind="ExternalInput").ap()
    dram["L1_attn_Cm"] = nc.dram_tensor("L1_attn_Cm", [D, 16], F32, kind="ExternalInput").ap()
    dram["L1_attn_Dm"] = nc.dram_tensor("L1_attn_Dm", [D], F32, kind="ExternalInput").ap()
    dram["L1_attn_ln_w"] = nc.dram_tensor("L1_attn_ln_w", [D], F32, kind="ExternalInput").ap()
    dram["L1_attn_ln_b"] = nc.dram_tensor("L1_attn_ln_b", [D], F32, kind="ExternalInput").ap()
    dram["L1_attn_W_out"] = nc.dram_tensor("L1_attn_W_out", [D, D], F32, kind="ExternalInput").ap()
    dram["L1_attn_b_out"] = nc.dram_tensor("L1_attn_b_out", [D], F32, kind="ExternalInput").ap()
    dram["convmask"] = nc.dram_tensor("convmask", [64, 1], F32, kind="ExternalInput").ap()
    dram["gTdR"] = nc.dram_tensor("gTdR", [192, D], BF16, kind="Internal").ap()
    dram["wotT"] = nc.dram_tensor("wotT", [128, 8192], BF16, kind="Internal").ap()
    if DBG:
        dram["dbg_g"] = nc.dram_tensor("dbg_g", [128, 1024], BF16, kind="ExternalOutput").ap()
        dram["dbg_V"] = nc.dram_tensor("dbg_V", [128, 2048], BF16, kind="ExternalOutput").ap()
        dram["dbg_x64"] = nc.dram_tensor("dbg_x64", [64, 17 * 1024], BF16, kind="ExternalOutput").ap()
        dram["dbg_y64"] = nc.dram_tensor("dbg_y64", [64, 16 * 1024], BF16, kind="ExternalOutput").ap()
        dram["dbg_z"] = nc.dram_tensor("dbg_z", [128, 8192], BF16, kind="ExternalOutput").ap()
        dram["dbg_gel"] = nc.dram_tensor("dbg_gel", [128, 8192], BF16, kind="ExternalOutput").ap()
        dram["dbg_y64b"] = nc.dram_tensor("dbg_y64b", [64, 16384], BF16, kind="ExternalOutput").ap()
        dram["dbg_m64"] = nc.dram_tensor("dbg_m64", [64, 1], F32, kind="ExternalOutput").ap()
        dram["dbg_s64"] = nc.dram_tensor("dbg_s64", [64, 1], F32, kind="ExternalOutput").ap()
        dram["dbg_yb0"] = nc.dram_tensor("dbg_yb0", [64, 1024], BF16, kind="ExternalOutput").ap()
        dram["dbg_sq0"] = nc.dram_tensor("dbg_sq0", [64, 1024], BF16, kind="ExternalOutput").ap()
        dram["dbg_wot"] = nc.dram_tensor("dbg_wot", [128, 8192], BF16, kind="ExternalOutput").ap()
        dram["dbg_res0c"] = nc.dram_tensor("dbg_res0c", [128, 2176], F32, kind="ExternalOutput").ap()
        dram["dbg_vwt"] = nc.dram_tensor("dbg_vwt", [128, 512], F32, kind="ExternalOutput").ap()
        dram["dbg_pb"] = nc.dram_tensor("dbg_pb", [128, 512], F32, kind="ExternalOutput").ap()


def build_l1ssm_launch():
    nc = bass.Bass("TRN2", target_bir_lowering=False, debug=False)
    dram = {}
    dram["resid_in"] = nc.dram_tensor("resid_in", [DC, 128, NT], F32, kind="ExternalInput").ap()
    dram["resid_out"] = nc.dram_tensor("resid_out", [DC, 128, NT], F32, kind="ExternalOutput").ap()
    common_dram(nc, dram)
    ssm_dram(nc, dram)
    with ExitStack() as ctx:
        E = ctx.enter_context
        bld = Bld(nc, ctx)
        alloc_consts(bld, E)
        bld.msk8 = E(nc.sbuf_tensor("msk8", [128, 8], F32))
        resid = E(nc.sbuf_tensor("resid", [128, DC * NT], F32))
        dram["blockmask8"] = nc.dram_tensor("blockmask8", [128, 8], F32, kind="ExternalInput").ap()
        emit_load_consts(bld, dram)

        def ld(sy):
            for c in range(DC):
                bld.inc("sq", sy.dma_start(out=resid[:, c * NT:(c + 1) * NT], in_=dram["resid_in"][c]), 16)
        bld.phase(sync=ld)

        def g1(g):
            bld.inc("gq", g.dma_start(out=bld.msk8[:, :], in_=dram["blockmask8"][:, :]), 16)
        bld.phase(gpsimd=g1)

        def bar(t):
            bld.w(t, "sq")
            bld.w(t, "gq")
        bld.phase(tensor=bar, order=("tensor",))

        emit_l1ssm(bld, resid, dram)

        def st(sy):
            bld.w(sy, "vv")
            bld.w(sy, "mm")
            bld.w(sy, "ss")
            for c in range(DC):
                bld.inc("sq", sy.dma_start(out=dram["resid_out"][c], in_=resid[:, c * NT:(c + 1) * NT]), 16)
            bld.w(sy, "sq")
        bld.phase(sync=st)
    return nc


# version 06
# speedup vs baseline: 1.2558x; 1.0334x over previous
import numpy as np
import ml_dtypes
from contextlib import ExitStack

import concourse.bass as bass
import concourse.mybir as mybir
from concourse.bass_utils import run_bass_kernel_spmd

F32 = mybir.dt.float32
BF16 = mybir.dt.bfloat16
AF = mybir.ActivationFunctionType
ALU = mybir.AluOpType
AX = mybir.AxisListType

NT = 2176
NOWN = 2048
HALO = 128
D = 1024
DC = 8
EPS = 1e-5
NCORES = 8
NTB = 544
HID = 4096
HC = 32


def tch(n, w=512):
    out = []
    o = 0
    while o < n:
        c = min(w, n - o)
        out.append((o, c))
        o += c
    return out


class Bld:

    def __init__(self, nc, ctx):
        self.nc = nc
        self.ctx = ctx
        self.sem = {}
        self.cnt = {}
        for name in ("sq", "gq", "mm", "vv", "ss", "gc"):
            self.sem[name] = ctx.enter_context(nc.semaphore(name))
            self.cnt[name] = 0
        self._P0 = ctx.enter_context(nc.psum_tensor("P0", [128, 512], F32))
        self._P1 = ctx.enter_context(nc.psum_tensor("P1", [128, 512], F32))
        self.PW0 = ctx.enter_context(nc.psum_tensor("PW0", [128, 1024], F32))
        self.PW1 = ctx.enter_context(nc.psum_tensor("PW1", [128, 1024], F32))
        self.P = [self._P0[:, :], self._P1[:, :], self.PW0[:, 0:512], self.PW1[:, 0:512],
                  self.PW0[:, 512:1024], self.PW1[:, 512:1024]]
        self.PT = ctx.enter_context(nc.psum_tensor("PT", [128, 128], BF16))
        self.PT2 = ctx.enter_context(nc.psum_tensor("PT2", [128, 128], BF16))
        self.PTF = self.PW0[:, 0:128]
        self.PTF2 = self.PW1[:, 0:128]
        self.gq_marks = {}

    def inc(self, name, inst, k=1):
        inst.then_inc(self.sem[name], k)
        self.cnt[name] += k
        return self.cnt[name]

    def V(self, v, f):
        self.w(v, "vv")
        return self.inc("vv", f())

    def S(self, s, f):
        self.w(s, "ss")
        return self.inc("ss", f())

    def w(self, eng, name, val=None):
        v = self.cnt[name] if val is None else val
        if v > 0:
            eng.wait_ge(self.sem[name], v)

    def phase(self, **closures):
        order = closures.pop("order", ("sync", "gpsimd", "tensor", "scalar", "vector"))
        with self.nc.Block() as blk:
            for name in order:
                fn = closures.get(name)
                if fn is not None:
                    getattr(blk, name)(fn)


def alloc_consts(bld, E):
    nc = bld.nc
    bld.identf = E(nc.sbuf_tensor("c_identf", [128, 128], F32))
    bld.identb = E(nc.sbuf_tensor("c_identb", [128, 128], BF16))
    bld.onescol = E(nc.sbuf_tensor("c_onescol", [128, 1], F32))
    bld.onescol_b = E(nc.sbuf_tensor("c_onescol_b", [128, 1], BF16))
    bld.onesrow = E(nc.sbuf_tensor("c_onesrow", [1, 128], F32))
    bld.onesrow_b = E(nc.sbuf_tensor("c_onesrow_b", [1, 128], BF16))


def emit_load_consts(bld, dram):
    def g(g):
        for dst, src in [(bld.identf, dram["identf"]), (bld.identb, dram["identf"]),
                         (bld.onescol, dram["ones_col"]), (bld.onescol_b, dram["ones_col"]),
                         (bld.onesrow, dram["ones_row"]), (bld.onesrow_b, dram["ones_row"])]:
            bld.inc("gq", g.dma_start(out=dst[:, :], in_=src[:, :]), 16)
    bld.phase(gpsimd=g)


def emit_ln_fm(bld, scr, src, src_stride, t0, n, dst, dst_stride, dt0):
    pst1, pst2, pmb, prb = bld.P[4], bld.P[5], bld.P[2], bld.P[3]
    for (o, w) in tch(n):
        def vv(v, o=o, w=w):
            ss_base = bld.cnt["ss"]
            mm_base = bld.cnt["mm"]
            for c in range(DC):
                if c == 0:
                    bld.w(v, "ss")
                    bld.w(v, "mm")
                else:
                    v.wait_ge(bld.sem["ss"], ss_base + c)
                    v.wait_ge(bld.sem["mm"], mm_base + 2 * c)
                bld.w(v, "vv")
                bld.inc("vv", v.tensor_copy(
                    scr["xb"][:, 0:w], src[:, c * src_stride + t0 + o: c * src_stride + t0 + o + w]))

        def ss(s, o=o, w=w):
            vv_base = bld.cnt["vv"] - DC
            mm_base = bld.cnt["mm"]
            for c in range(DC):
                s.wait_ge(bld.sem["vv"], vv_base + c + 1)
                if c > 0:
                    s.wait_ge(bld.sem["mm"], mm_base + 2 * c)
                bld.w(s, "ss")
                bld.inc("ss", s.activation(out=scr["sq"][:, 0:w], in_=scr["xb"][:, 0:w], func=AF.Square))

        def tt(t, o=o, w=w):
            vv_base = bld.cnt["vv"] - DC
            ss_base = bld.cnt["ss"] - DC
            bld.w(t, "mm")
            for c in range(DC):
                t.wait_ge(bld.sem["vv"], vv_base + c + 1)
                t.wait_ge(bld.sem["ss"], ss_base + c + 1)
                bld.inc("mm", t.matmul(pst1[0:1, 0:w], bld.onescol_b[:, :], scr["xb"][:, 0:w],
                                       start=(c == 0), stop=(c == DC - 1)))
                bld.inc("mm", t.matmul(pst2[0:1, 0:w], bld.onescol_b[:, :], scr["sq"][:, 0:w],
                                       start=(c == 0), stop=(c == DC - 1)))
        bld.phase(vector=vv, scalar=ss, tensor=tt, order=("vector", "scalar", "tensor"))

        def vv2(v, o=o, w=w):
            bld.w(v, "mm")
            bld.w(v, "vv")
            bld.inc("vv", v.tensor_scalar(scr["m_r"][:, 0:w], pst1[0:1, 0:w], 1.0 / D, None, ALU.mult))
            bld.w(v, "vv")
            bld.inc("vv", v.tensor_scalar(scr["a_r"][:, 0:w], pst2[0:1, 0:w], 1.0 / D, None, ALU.mult))
            bld.w(v, "vv")
            bld.inc("vv", v.tensor_tensor(scr["b_r"][:, 0:w], scr["m_r"][:, 0:w], scr["m_r"][:, 0:w], ALU.mult))
            bld.w(v, "vv")
            bld.inc("vv", v.tensor_tensor(scr["a_r"][:, 0:w], scr["a_r"][:, 0:w], scr["b_r"][:, 0:w], ALU.subtract))
            bld.w(v, "vv")
            bld.inc("vv", v.tensor_scalar(scr["a_r"][:, 0:w], scr["a_r"][:, 0:w], EPS, None, ALU.add))
        bld.phase(vector=vv2, order=("vector",))

        def ss2(s, o=o, w=w):
            bld.w(s, "vv")
            bld.w(s, "ss")
            bld.inc("ss", s.activation(out=scr["b_r"][:, 0:w], in_=scr["a_r"][:, 0:w], func=AF.Sqrt))
        bld.phase(scalar=ss2, order=("scalar",))

        def vv3(v, o=o, w=w):
            bld.w(v, "ss")
            bld.w(v, "vv")
            bld.inc("vv", v.reciprocal(scr["a_r"][:, 0:w], scr["b_r"][:, 0:w]))
        bld.phase(vector=vv3, order=("vector",))

        def tt4(t, o=o, w=w):
            bld.w(t, "vv")
            bld.w(t, "mm")
            bld.inc("mm", t.matmul(pmb[:, 0:w], bld.onesrow[:, :], scr["m_r"][:, 0:w], start=True, stop=True))
            bld.inc("mm", t.matmul(prb[:, 0:w], bld.onesrow[:, :], scr["a_r"][:, 0:w], start=True, stop=True))

        def vv4(v, o=o, w=w):
            bld.w(v, "mm")
            for c in range(DC):
                bld.w(v, "vv")
                bld.inc("vv", v.tensor_tensor(scr["tmp"][:, 0:w],
                                              src[:, c * src_stride + t0 + o: c * src_stride + t0 + o + w],
                                              pmb[:, 0:w], ALU.subtract))
                bld.w(v, "vv")
                bld.inc("vv", v.tensor_tensor(dst[:, c * dst_stride + dt0 + o: c * dst_stride + dt0 + o + w],
                                              scr["tmp"][:, 0:w], prb[:, 0:w], ALU.mult))
        bld.phase(tensor=tt4, vector=vv4, order=("tensor", "vector"))


def alloc_ln_scr(bld, E, tag="", tmp=None):
    nc = bld.nc
    return {
        "xb": E(nc.sbuf_tensor(f"lnxb{tag}", [128, 512], BF16)),
        "sq": E(nc.sbuf_tensor(f"lnsq{tag}", [128, 512], BF16)),
        "m_r": E(nc.sbuf_tensor(f"lnm{tag}", [1, 512], F32)),
        "a_r": E(nc.sbuf_tensor(f"lna{tag}", [1, 512], F32)),
        "b_r": E(nc.sbuf_tensor(f"lnb{tag}", [1, 512], F32)),
        "tmp": tmp if tmp is not None else E(nc.sbuf_tensor(f"lntmp{tag}", [128, 512], F32)),
    }


def emit_tpose_pipe(bld, items, f32=False):
    pts = [bld.PTF, bld.PTF2] if f32 else [bld.PT[:, :], bld.PT2[:, :]]
    ident = bld.identf if f32 else bld.identb
    mm0 = bld.cnt["mm"]
    vv0 = bld.cnt["vv"]
    n = len(items)

    def tt(t):
        bld.w(t, "gq")
        bld.w(t, "sq")
        bld.w(t, "vv")
        bld.w(t, "mm")
        for i, (src, rows, cols, _) in enumerate(items):
            if i >= 2:
                t.wait_ge(bld.sem["vv"], vv0 + i - 1)
            pt = pts[i % 2]
            bld.inc("mm", t.matmul(pt[0:cols, 0:rows], src, ident[0:rows, 0:rows],
                                   start=True, stop=True, is_transpose=True))

    def vv(v):
        for i, (src, rows, cols, evac) in enumerate(items):
            v.wait_ge(bld.sem["mm"], mm0 + i + 1)
            pt = pts[i % 2]
            bld.inc("vv", evac(v, pt[0:cols, 0:rows]))
    bld.phase(tensor=tt, vector=vv, order=("tensor", "vector"))


def emit_small_transpose(bld, src_ap, rows, cols, evac_cb):
    def tt(t):
        bld.w(t, "gq")
        bld.w(t, "sq")
        bld.w(t, "vv")
        bld.w(t, "mm")
        bld.inc("mm", t.matmul(bld.PT[0:cols, 0:rows], src_ap, bld.identb[0:rows, 0:rows],
                               start=True, stop=True, is_transpose=True))

    def vv(v):
        bld.w(v, "mm")
        bld.w(v, "sq")
        evac_cb(v)
    bld.phase(tensor=tt, vector=vv, order=("tensor", "vector"))


DBG = False


def emit_dbg(bld, dram, key, src_ap):
    if not DBG or key not in dram:
        return
    def sy(sy):
        bld.w(sy, "vv"); bld.w(sy, "mm"); bld.w(sy, "ss")
        bld.inc("sq", sy.dma_start(out=dram[key][:, :], in_=src_ap), 16)
        bld.w(sy, "sq")
    bld.phase(sync=sy)


def emit_l0ffn(bld, resid, dram):
    nc = bld.nc
    with ExitStack() as ctx:
        E = ctx.enter_context
        ln2 = E(nc.sbuf_tensor("ln2", [128, DC * NTB], BF16))
        r2T = E(nc.sbuf_tensor("r2T", [128, HC * NTB], BF16))
        vT = E(nc.sbuf_tensor("vT", [128, DC * 16], BF16))
        fbT = E(nc.sbuf_tensor("fbT", [128, 16], BF16))
        uTg = E(nc.sbuf_tensor("uTg", [128, HID], BF16))
        s1p = E(nc.sbuf_tensor("s1p", [128, NT], BF16))
        s2p = E(nc.sbuf_tensor("s2p", [128, NT], BF16))
        v16b = E(nc.sbuf_tensor("v16b", [16, D], BF16))
        fbb = E(nc.sbuf_tensor("fbb", [16, 16], BF16))
        ub = E(nc.sbuf_tensor("ub", [128, 32 * 16], BF16))
        gcol = E(nc.sbuf_tensor("gcol", [16, 1], F32))
        gsb = E(nc.sbuf_tensor("gsb", [1, 1], F32))
        wt = E(nc.sbuf_tensor("wt", [128, HID], F32))
        red = E(nc.sbuf_tensor("red", [128, 1], F32))
        hs = E(nc.sbuf_tensor("hs", [128, 1], F32))
        scl = E(nc.sbuf_tensor("scl", [128, 1], F32))
        sgn = E(nc.sbuf_tensor("sgn", [128, 2048], BF16))
        absq = E(nc.sbuf_tensor("absq", [128, 2048], F32))
        hs2 = E(nc.sbuf_tensor("hs2", [128, 1], F32))
        weff = E(nc.sbuf_tensor("weff", [128, HID], BF16))
        stripD = E(nc.sbuf_tensor("stripD", [128, 2 * 1024], BF16))
        wpsD = E(nc.sbuf_tensor("wpsD", [128, 2 * HID], BF16))
        rbD = E(nc.sbuf_tensor("rbD", [128, 2 * NTB], BF16))
        lns = alloc_ln_scr(bld, E, "f", tmp=absq)
        wsrc_fc = dram["wfcT"].rearrange("(h x) c -> h x c", x=128)
        wsrc_pj = dram["wprojT"].rearrange("(d x) c -> d x c", x=128)

        def g0(g):
            bld.inc("gq", g.dma_start(out=v16b[:, :], in_=dram["L0_ffn_v"][:, :]), 16)
            bld.inc("gq", g.dma_start(out=fbb[:, :], in_=dram["L0_ffn_fast_b"][:, :]), 16)
            usrc = dram["L0_ffn_u"].rearrange("(a p) j -> p a j", p=128)
            bld.inc("gq", g.dma_start(out=ub[:, :].rearrange("p (a j) -> p a j", j=16), in_=usrc), 16)
            bld.inc("gq", g.dma_start(out=gsb[:, :], in_=dram["L0_ffn_gate"][None, :]), 16)

        def v0(v):
            bld.w(v, "vv")
            bld.inc("vv", v.memset(uTg[:, :], 0.0))
            bld.w(v, "vv")
            bld.inc("vv", v.memset(fbT[:, :], 0.0))
            bld.w(v, "vv")
            bld.inc("vv", v.memset(s1p[:, :], 0.0))
            bld.w(v, "vv")
            bld.inc("vv", v.memset(s2p[:, :], 0.0))
        bld.phase(gpsimd=g0, vector=v0)

        for c in range(DC):
            def mk(c):
                return lambda v: bld.inc("vv", v.tensor_copy(vT[:, c * 16:(c + 1) * 16], bld.PT[0:128, 0:16]))
            emit_small_transpose(bld, v16b[:, c * 128:(c + 1) * 128], 16, 128, mk(c))
        emit_small_transpose(bld, fbb[:, :], 16, 16,
                             lambda v: bld.inc("vv", v.tensor_copy(fbT[0:16, 0:16], bld.PT[0:16, 0:16])))
        for a in range(32):
            def mk2(a):
                return lambda v: bld.inc("vv", v.tensor_copy(uTg[0:16, a * 128:(a + 1) * 128], bld.PT[0:16, 0:128]))
            emit_small_transpose(bld, ub[:, a * 16:(a + 1) * 16], 128, 16, mk2(a))

        def tg(t):
            bld.w(t, "gq")
            bld.w(t, "vv")
            bld.w(t, "mm")
            bld.inc("mm", t.matmul(bld.P[4][0:16, 0:1], bld.onesrow[:, 0:16], gsb[:, :], start=True, stop=True))

        def vg(v):
            bld.w(v, "mm")
            bld.w(v, "vv")
            bld.inc("vv", v.tensor_copy(gcol[:, :], bld.P[4][0:16, 0:1]))
            bld.w(v, "vv")
            bld.inc("vv", v.tensor_scalar(uTg[0:16, :], uTg[0:16, :], gcol[0:16, :], None, ALU.mult))
        bld.phase(tensor=tg, vector=vg, order=("tensor", "vector"))

        def prep_chunk(src_ap, wview, width, denom, dst_sb, dst_dram):
            def sy(sy):
                bld.w(sy, "mm")
                bld.w(sy, "ss")
                bld.w(sy, "vv")
                bld.inc("sq", sy.dma_start(out=wview, in_=src_ap), 16)

            def vp(v):
                bld.w(v, "sq")
                bld.w(v, "vv")
                bld.inc("vv", v.tensor_reduce(red[:, :], wview, AX.X, ALU.add,
                                              apply_absolute_value=True))
                bld.w(v, "vv")
                bld.inc("vv", v.tensor_scalar(hs[:, :], red[:, :], 0.5 / denom, 0.5e-5, ALU.mult, ALU.max))
                bld.w(v, "vv")
                bld.inc("vv", v.tensor_scalar(scl[:, :], red[:, :], 1.0 / denom, 1e-5, ALU.mult, ALU.max))
                bld.w(v, "vv")
                bld.inc("vv", v.tensor_tensor(hs2[:, :], hs[:, :], hs[:, :], ALU.mult))

            bld.phase(sync=sy, vector=vp, order=("sync", "vector"))

            for (so, sw) in tch(width, 2048):
                def sp(s, so=so, sw=sw):
                    bld.w(s, "sq")
                    bld.w(s, "vv")
                    bld.w(s, "ss")
                    bld.inc("ss", s.activation(out=sgn[:, 0:sw], in_=wview[:, so:so + sw], func=AF.Sign))
                    bld.w(s, "ss")
                    bld.inc("ss", s.activation(out=absq[:, 0:sw], in_=wview[:, so:so + sw], func=AF.Square))
                bld.phase(scalar=sp, order=("scalar",))

                def vp2(v, so=so, sw=sw):
                    bld.w(v, "ss")
                    bld.w(v, "vv")
                    bld.inc("vv", v.tensor_scalar(absq[:, 0:sw], absq[:, 0:sw], hs2[:, :], None, ALU.subtract))
                    bld.w(v, "vv")
                    bld.inc("vv", v.tensor_scalar(weff[:, so:so + sw], absq[:, 0:sw], 0.0, None, ALU.is_ge))
                    bld.w(v, "vv")
                    bld.inc("vv", v.tensor_tensor(weff[:, so:so + sw], weff[:, so:so + sw], sgn[:, 0:sw], ALU.mult))
                    bld.w(v, "vv")
                    bld.inc("vv", v.tensor_scalar(weff[:, so:so + sw], weff[:, so:so + sw], scl[:, :], None, ALU.mult))
                bld.phase(vector=vp2, order=("vector",))

            items = []
            for a in range(width // 128):
                src = weff[:, a * 128:(a + 1) * 128]
                dst = dst_sb[:, a * 128:(a + 1) * 128]
                ev = (lambda dst: (lambda v, pt: v.tensor_copy(dst, pt)))(dst)
                items.append((src, 128, 128, ev))
            emit_tpose_pipe(bld, items)

            def sw(sy):
                bld.w(sy, "vv")
                bld.inc("sq", sy.dma_start(out=dst_dram, in_=dst_sb), 16)
            bld.phase(sync=sw)

        for hc in range(HC):
            prep_chunk(dram["L0_ffn_w_fc"][hc * 128:(hc + 1) * 128, :], wt[:, 0:1024],
                       1024, D, stripD[:, 0:1024], wsrc_fc[hc])
        for dc in range(DC):
            prep_chunk(dram["L0_ffn_w_proj"][dc * 128:(dc + 1) * 128, :], wt[:, :],
                       HID, HID, wpsD[:, 0:HID], wsrc_pj[dc])

        fc_sq = {}
        fc_mm = {}
        fc_ssA = {}
        fc_ssB = {}
        fc_vv = {}
        pj_sq = {}
        pj_mm = {}
        pj_vvA = {}
        pj_vvB = {}
        for blk in range(NT // NTB):
            t0 = blk * NTB
            emit_ln_fm(bld, lns, resid, NT, t0, NTB, ln2, NTB, 0)

            ps16 = bld.P[4]
            for (o, w) in tch(NTB):
                def ts1(t, o=o, w=w):
                    bld.w(t, "vv")
                    bld.w(t, "mm")
                    for c in range(DC):
                        bld.inc("mm", t.matmul(ps16[0:16, 0:w], vT[:, c * 16:(c + 1) * 16],
                                               ln2[:, c * NTB + o: c * NTB + o + w],
                                               start=(c == 0), stop=(c == DC - 1)))

                def vs1(v, o=o, w=w):
                    bld.w(v, "mm")
                    bld.w(v, "vv")
                    bld.inc("vv", v.tensor_copy(s1p[0:16, t0 + o: t0 + o + w], ps16[0:16, 0:w]))
                bld.phase(tensor=ts1, vector=vs1, order=("tensor", "vector"))

                def ts2(t, o=o, w=w):
                    bld.w(t, "vv")
                    bld.inc("mm", t.matmul(ps16[0:16, 0:w], fbT[:, 0:16], s1p[:, t0 + o: t0 + o + w],
                                           start=True, stop=True))

                def vs2(v, o=o, w=w):
                    bld.w(v, "mm")
                    bld.w(v, "vv")
                    bld.inc("vv", v.tensor_copy(s2p[0:16, t0 + o: t0 + o + w], ps16[0:16, 0:w]))
                bld.phase(tensor=ts2, vector=vs2, order=("tensor", "vector"))

            def bsy(sy):
                bld.w(sy, "mm")
                bld.w(sy, "sq")
                for hc in range(2):
                    bld.inc("sq", sy.dma_start(
                        out=stripD[:, hc * 1024:(hc + 1) * 1024], in_=wsrc_fc[hc]), 16)
                    fc_sq[hc] = bld.cnt["sq"]
            bld.phase(sync=bsy)

            for hc in range(HC):
                sb = stripD[:, (hc % 2) * 1024:(hc % 2 + 1) * 1024]
                pa = bld.P[hc % 2]
                pb = bld.P[2 + hc % 2]
                rb = rbD[:, (hc % 2) * NTB:(hc % 2 + 1) * NTB]

                def tf(t, hc=hc, sb=sb, pa=pa, pb=pb, t0=t0):
                    if hc == 0:
                        bld.w(t, "vv")
                        bld.w(t, "mm")
                    t.wait_ge(bld.sem["sq"], fc_sq[hc])
                    if hc >= 2:
                        t.wait_ge(bld.sem["ss"], fc_ssB[hc - 2])
                    for c in range(DC):
                        bld.inc("mm", t.matmul(pa[:, 0:512], sb[:, c * 128:(c + 1) * 128],
                                               ln2[:, c * NTB: c * NTB + 512],
                                               start=(c == 0), stop=False))
                    bld.inc("mm", t.matmul(pa[:, 0:512], uTg[:, hc * 128:(hc + 1) * 128],
                                           s2p[:, t0: t0 + 512], start=False, stop=True))
                    for c in range(DC):
                        bld.inc("mm", t.matmul(pb[:, 0:32], sb[:, c * 128:(c + 1) * 128],
                                               ln2[:, c * NTB + 512: c * NTB + 544],
                                               start=(c == 0), stop=False))
                    bld.inc("mm", t.matmul(pb[:, 0:32], uTg[:, hc * 128:(hc + 1) * 128],
                                           s2p[:, t0 + 512: t0 + 544], start=False, stop=True))
                    fc_mm[hc] = bld.cnt["mm"]

                def sf(s, hc=hc, pa=pa, pb=pb, rb=rb):
                    s.wait_ge(bld.sem["mm"], fc_mm[hc])
                    if hc >= 2:
                        s.wait_ge(bld.sem["vv"], fc_vv[hc - 2])
                    bld.inc("ss", s.activation(out=rb[:, 0:512], in_=pa[:, 0:512], func=AF.Relu))
                    fc_ssA[hc] = bld.cnt["ss"]
                    bld.inc("ss", s.activation(out=rb[:, 512:544], in_=pb[:, 0:32], func=AF.Relu))
                    fc_ssB[hc] = bld.cnt["ss"]

                def vf(v, hc=hc, rb=rb):
                    v.wait_ge(bld.sem["ss"], fc_ssB[hc])
                    bld.inc("vv", v.tensor_tensor(r2T[:, hc * NTB:(hc + 1) * NTB],
                                                  rb[:, :], rb[:, :], ALU.mult))
                    fc_vv[hc] = bld.cnt["vv"]

                def syf(sy, hc=hc):
                    nh = hc + 2
                    if nh < HC:
                        sy.wait_ge(bld.sem["mm"], fc_mm[hc])
                        bld.inc("sq", sy.dma_start(
                            out=stripD[:, (nh % 2) * 1024:(nh % 2 + 1) * 1024], in_=wsrc_fc[nh]), 16)
                        fc_sq[nh] = bld.cnt["sq"]
                    elif nh == HC:
                        if 6 in pj_mm:
                            sy.wait_ge(bld.sem["mm"], pj_mm[6])
                        bld.inc("sq", sy.dma_start(out=wpsD[:, 0:HID], in_=wsrc_pj[0]), 16)
                        pj_sq[0] = bld.cnt["sq"]
                        if 7 in pj_mm:
                            sy.wait_ge(bld.sem["mm"], pj_mm[7])
                        bld.inc("sq", sy.dma_start(out=wpsD[:, HID:2 * HID], in_=wsrc_pj[1]), 16)
                        pj_sq[1] = bld.cnt["sq"]
                bld.phase(tensor=tf, scalar=sf, vector=vf, sync=syf,
                          order=("tensor", "scalar", "vector", "sync"))

            for dc in range(DC):
                wps = wpsD[:, (dc % 2) * HID:(dc % 2 + 1) * HID]
                pa = bld.P[dc % 2]
                pb = bld.P[2 + dc % 2]

                def tp(t, dc=dc, wps=wps, pa=pa, pb=pb):
                    if dc == 0:
                        t.wait_ge(bld.sem["vv"], fc_vv[HC - 1])
                    t.wait_ge(bld.sem["sq"], pj_sq[dc])
                    if dc >= 2:
                        t.wait_ge(bld.sem["vv"], pj_vvA[dc - 2])
                        t.wait_ge(bld.sem["vv"], pj_vvB[dc - 2])
                    for a in range(HC):
                        bld.inc("mm", t.matmul(pa[:, 0:512], wps[:, a * 128:(a + 1) * 128],
                                               r2T[:, a * NTB: a * NTB + 512],
                                               start=(a == 0), stop=(a == HC - 1)))
                    for a in range(HC):
                        bld.inc("mm", t.matmul(pb[:, 0:32], wps[:, a * 128:(a + 1) * 128],
                                               r2T[:, a * NTB + 512: a * NTB + 544],
                                               start=(a == 0), stop=(a == HC - 1)))
                    pj_mm[dc] = bld.cnt["mm"]

                def vp(v, dc=dc, pa=pa, pb=pb, t0=t0):
                    v.wait_ge(bld.sem["mm"], pj_mm[dc])
                    sl = resid[:, dc * NT + t0: dc * NT + t0 + 512]
                    bld.inc("vv", v.tensor_tensor(sl, sl, pa[:, 0:512], ALU.add))
                    pj_vvA[dc] = bld.cnt["vv"]
                    sl2 = resid[:, dc * NT + t0 + 512: dc * NT + t0 + 544]
                    bld.inc("vv", v.tensor_tensor(sl2, sl2, pb[:, 0:32], ALU.add))
                    pj_vvB[dc] = bld.cnt["vv"]

                def syp(sy, dc=dc):
                    nd = dc + 2
                    if nd < DC:
                        sy.wait_ge(bld.sem["mm"], pj_mm[dc])
                        bld.inc("sq", sy.dma_start(
                            out=wpsD[:, (nd % 2) * HID:(nd % 2 + 1) * HID], in_=wsrc_pj[nd]), 16)
                        pj_sq[nd] = bld.cnt["sq"]
                bld.phase(tensor=tp, vector=vp, sync=syp, order=("tensor", "vector", "sync"))


def ffn_dram(nc, dram):
    dram["L0_ffn_w_fc"] = nc.dram_tensor("L0_ffn_w_fc", [HID, D], F32, kind="ExternalInput").ap()
    dram["L0_ffn_w_proj"] = nc.dram_tensor("L0_ffn_w_proj", [D, HID], F32, kind="ExternalInput").ap()
    dram["L0_ffn_u"] = nc.dram_tensor("L0_ffn_u", [HID, 16], F32, kind="ExternalInput").ap()
    dram["L0_ffn_v"] = nc.dram_tensor("L0_ffn_v", [16, D], F32, kind="ExternalInput").ap()
    dram["L0_ffn_fast_b"] = nc.dram_tensor("L0_ffn_fast_b", [16, 16], F32, kind="ExternalInput").ap()
    dram["L0_ffn_gate"] = nc.dram_tensor("L0_ffn_gate", [1], F32, kind="ExternalInput").ap()
    dram["wfcT"] = nc.dram_tensor("wfcT", [HC * 128, D], BF16, kind="Internal").ap()
    if DBG:
        dram["dbg_ln2"] = nc.dram_tensor("dbg_ln2", [128, DC * NTB], BF16, kind="ExternalOutput").ap()
        dram["dbg_s1"] = nc.dram_tensor("dbg_s1", [128, NT], BF16, kind="ExternalOutput").ap()
        dram["dbg_s2"] = nc.dram_tensor("dbg_s2", [128, NT], BF16, kind="ExternalOutput").ap()
        dram["dbg_strip"] = nc.dram_tensor("dbg_strip", [128, DC * 128], BF16, kind="ExternalOutput").ap()
        dram["dbg_r2"] = nc.dram_tensor("dbg_r2", [128, NTB], BF16, kind="ExternalOutput").ap()
        dram["dbg_wps"] = nc.dram_tensor("dbg_wps", [128, HC * 128], BF16, kind="ExternalOutput").ap()
    dram["wprojT"] = nc.dram_tensor("wprojT", [DC * 128, HC * 128], BF16, kind="Internal").ap()


def common_dram(nc, dram):
    dram["identf"] = nc.dram_tensor("identf", [128, 128], F32, kind="ExternalInput").ap()
    dram["ones_col"] = nc.dram_tensor("ones_col", [128, 1], F32, kind="ExternalInput").ap()
    dram["ones_row"] = nc.dram_tensor("ones_row", [1, 128], F32, kind="ExternalInput").ap()


def build_l0ffn_launch():
    nc = bass.Bass("TRN2", target_bir_lowering=False, debug=False)
    dram = {}
    dram["resid_in"] = nc.dram_tensor("resid_in", [DC, 128, NT], F32, kind="ExternalInput").ap()
    dram["resid_out"] = nc.dram_tensor("resid_out", [DC, 128, NT], F32, kind="ExternalOutput").ap()
    common_dram(nc, dram)
    ffn_dram(nc, dram)

    with ExitStack() as ctx:
        E = ctx.enter_context
        bld = Bld(nc, ctx)
        alloc_consts(bld, E)
        resid = E(nc.sbuf_tensor("resid", [128, DC * NT], F32))
        emit_load_consts(bld, dram)

        def ld(sy):
            for c in range(DC):
                bld.inc("sq", sy.dma_start(out=resid[:, c * NT:(c + 1) * NT], in_=dram["resid_in"][c]), 16)
        bld.phase(sync=ld)

        def bar(t):
            bld.w(t, "sq")
            bld.w(t, "gq")
        bld.phase(tensor=bar, order=("tensor",))

        emit_l0ffn(bld, resid, dram)

        def st(sy):
            bld.w(sy, "vv")
            bld.w(sy, "mm")
            bld.w(sy, "ss")
            for c in range(DC):
                bld.inc("sq", sy.dma_start(out=dram["resid_out"][c], in_=resid[:, c * NT:(c + 1) * NT]), 16)
            bld.w(sy, "sq")
        bld.phase(sync=st)
    return nc


def shard_inputs(inputs):
    x = np.ascontiguousarray(np.asarray(inputs["L0_attn_x"], np.float32))
    maps = []
    ident = np.eye(128, dtype=np.float32)
    ones_col = np.ones((128, 1), np.float32)
    ones_row = np.ones((1, 128), np.float32)
    for core in range(NCORES):
        b, half = core // 2, core % 2
        start = half * NOWN
        xr = np.zeros((NT, D), np.float32)
        lo = start - HALO
        if lo < 0:
            xr[HALO:, :] = x[b, start:start + NOWN, :]
        else:
            xr[:, :] = x[b, lo:start + NOWN, :]
        m = {
            "xrows": xr,
            "xfull": np.ascontiguousarray(x[b]),
            "identf": ident,
            "ones_col": ones_col,
            "ones_row": ones_row,
            "convmask": np.full((64, 1), 0.0 if half == 0 else 1.0, np.float32),
        }
        for k, v in inputs.items():
            if k == "L0_attn_x":
                continue
            a = np.asarray(v, np.float32)
            if a.ndim == 0:
                a = a.reshape(1)
            m[k] = np.ascontiguousarray(a)
        maps.append(m)
    return maps


def build_fused_launch():
    nc = bass.Bass("TRN2", target_bir_lowering=False, debug=False)
    dram = {}
    dram["resid_out"] = nc.dram_tensor("resid_out", [DC, 128, NT], F32, kind="ExternalOutput").ap()
    common_dram(nc, dram)
    attn_dram(nc, dram)
    ffn_dram(nc, dram)
    ssm_dram(nc, dram)
    moe_dram(nc, dram)
    dram["blockmask8"] = nc.dram_tensor("blockmask8", [128, 8], F32, kind="ExternalInput").ap()
    with ExitStack() as ctx:
        E = ctx.enter_context
        bld = Bld(nc, ctx)
        alloc_consts(bld, E)
        bld.msk8 = E(nc.sbuf_tensor("msk8", [128, 8], F32))
        resid = E(nc.sbuf_tensor("resid", [128, DC * NT], F32))
        emit_load_consts(bld, dram)

        def g1(g):
            bld.inc("gq", g.dma_start(out=bld.msk8[:, :], in_=dram["blockmask8"][:, :]), 16)
        bld.phase(gpsimd=g1)

        def bar(t):
            bld.w(t, "gq")
        bld.phase(tensor=bar, order=("tensor",))

        with ExitStack() as ctxa:
            QTall = ctxa.enter_context(nc.sbuf_tensor("QTall", [128, 8 * NT], BF16))
            emit_p0(bld, resid, QTall, dram)
            emit_l0attn(bld, resid, QTall, dram)
        emit_l0ffn(bld, resid, dram)
        emit_l1ssm(bld, resid, dram)
        emit_l1moe(bld, resid, dram)

        def st(sy):
            bld.w(sy, "vv")
            bld.w(sy, "mm")
            bld.w(sy, "ss")
            for c in range(DC):
                bld.inc("sq", sy.dma_start(out=dram["resid_out"][c], in_=resid[:, c * NT:(c + 1) * NT]), 16)
            bld.w(sy, "sq")
        bld.phase(sync=st)
    return nc


FUSED = True
_NC_CACHE = {}


def _get_nc(name, builder):
    if name not in _NC_CACHE:
        _NC_CACHE[name] = builder()
    return _NC_CACHE[name]


_ATTN_KEYS = ["xrows", "xfull", "L0_attn_Wq", "L0_attn_Wk", "L0_attn_Wv", "L0_attn_Wo",
              "L0_attn_quadrant_weights", "identf", "ones_col", "ones_row"]
_FFN_KEYS = ["L0_ffn_w_fc", "L0_ffn_w_proj", "L0_ffn_u", "L0_ffn_v", "L0_ffn_fast_b",
             "L0_ffn_gate", "identf", "ones_col", "ones_row"]
_SSM_KEYS = ["L1_attn_A", "L1_attn_Bm", "L1_attn_Cm", "L1_attn_Dm", "L1_attn_ln_w",
             "L1_attn_ln_b", "L1_attn_W_out", "L1_attn_b_out", "convmask", "blockmask8",
             "identf", "ones_col", "ones_row"]
_MOE_KEYS = ["L1_ffn_Wg", "L1_ffn_bg", "L1_ffn_bias", "L1_ffn_W1", "L1_ffn_b1",
             "L1_ffn_W2", "L1_ffn_b2", "identf", "ones_col", "ones_row"]


def kernel(**inputs):
    maps = shard_inputs(inputs)
    mask8 = np.zeros((128, 8), np.float32)
    for p in range(128):
        mask8[p, p // 16] = 1.0
    for m in maps:
        m["blockmask8"] = mask8

    cores = list(range(NCORES))

    if FUSED:
        keys = sorted(set(_ATTN_KEYS + _FFN_KEYS + _SSM_KEYS + _MOE_KEYS + ["blockmask8"]))
        nc = _get_nc("fused", build_fused_launch)
        res = run_bass_kernel_spmd(nc, [{k: m[k] for k in keys} for m in maps], core_ids=cores)
        resid = [r["resid_out"] for r in res.results]
        out = np.empty((4, 4096, D), np.float32)
        for core in range(NCORES):
            b, half = core // 2, core % 2
            start = half * NOWN
            r = np.asarray(resid[core], np.float32)
            blockrows = r.transpose(2, 0, 1).reshape(NT, D)
            out[b, start:start + NOWN, :] = blockrows[HALO:, :]
        return out

    nc_a = _get_nc("attn", build_l0attn_launch)
    res = run_bass_kernel_spmd(nc_a, [{k: m[k] for k in _ATTN_KEYS} for m in maps], core_ids=cores)
    resid = [r["resid_out"] for r in res.results]

    nc_b = _get_nc("ffn", build_l0ffn_launch)
    res = run_bass_kernel_spmd(
        nc_b, [dict({k: m[k] for k in _FFN_KEYS}, resid_in=resid[i]) for i, m in enumerate(maps)],
        core_ids=cores)
    resid = [r["resid_out"] for r in res.results]

    nc_c = _get_nc("ssm", build_l1ssm_launch)
    res = run_bass_kernel_spmd(
        nc_c, [dict({k: m[k] for k in _SSM_KEYS}, resid_in=resid[i]) for i, m in enumerate(maps)],
        core_ids=cores)
    resid = [r["resid_out"] for r in res.results]

    nc_d = _get_nc("moe", build_l1moe_launch)
    res = run_bass_kernel_spmd(
        nc_d, [dict({k: m[k] for k in _MOE_KEYS}, resid_in=resid[i]) for i, m in enumerate(maps)],
        core_ids=cores)
    resid = [r["resid_out"] for r in res.results]

    out = np.empty((4, 4096, D), np.float32)
    for core in range(NCORES):
        b, half = core // 2, core % 2
        start = half * NOWN
        r = np.asarray(resid[core], np.float32)
        blockrows = r.transpose(2, 0, 1).reshape(NT, D)
        out[b, start:start + NOWN, :] = blockrows[HALO:, :]
    return out


MOE_BLOCKS = [(0, 1152), (1152, 1024)]
NTBM = 1152
NE = 16


def emit_l1moe(bld, resid, dram):
    nc = bld.nc
    with ExitStack() as ctx:
        E = ctx.enter_context
        ln4 = E(nc.sbuf_tensor("ln4", [128, DC * NTBM], BF16))
        hT = E(nc.sbuf_tensor("hT", [128, NE * NTBM], BF16))
        W1b = E(nc.sbuf_tensor("W1b", [128, DC * 2048], BF16))
        W2d = [E(nc.sbuf_tensor("W2d0", [128, NE * 128], BF16)),
               E(nc.sbuf_tensor("W2d1", [128, NE * 128], BF16))]
        b1c = E(nc.sbuf_tensor("b1c", [128, NE * 16], F32))
        b2c = E(nc.sbuf_tensor("b2c", [128, NE * DC], F32))
        b1n = E(nc.sbuf_tensor("b1n", [16, 2048], F32))
        b2n = E(nc.sbuf_tensor("b2n", [16, 1024], F32))
        Wgb = E(nc.sbuf_tensor("Wgb", [128, DC * 16], BF16))
        bgr = E(nc.sbuf_tensor("bgr", [1, 16], F32))
        bir = E(nc.sbuf_tensor("bir", [1, 16], F32))
        onesw = E(nc.sbuf_tensor("onesw", [1, 512], F32))
        cw = E(nc.sbuf_tensor("cw", [128, 17 * 16], F32))
        cwT = E(nc.sbuf_tensor("cwT", [16, NT], BF16))
        cwrow = E(nc.sbuf_tensor("cwrow", [1, NT], BF16))
        cwbs = E(nc.sbuf_tensor("cwbs", [128, 3 * 512], BF16))
        t1b = E(nc.sbuf_tensor("t1b", [128, 1024], F32))
        lgt = t1b
        tA = E(nc.sbuf_tensor("tA", [128, 16], F32))
        tB = E(nc.sbuf_tensor("tB", [128, 16], F32))
        mk1 = E(nc.sbuf_tensor("mk1", [128, 16], F32))
        mk2 = E(nc.sbuf_tensor("mk2", [128, 16], F32))
        m1 = E(nc.sbuf_tensor("m1c", [128, 1], F32))
        m2 = E(nc.sbuf_tensor("m2c", [128, 1], F32))
        e2 = E(nc.sbuf_tensor("e2c", [128, 1], F32))
        den = E(nc.sbuf_tensor("denc", [128, 1], F32))
        rec = E(nc.sbuf_tensor("recc", [128, 1], F32))
        w2c = E(nc.sbuf_tensor("w2cc", [128, 1], F32))
        lns4 = alloc_ln_scr(bld, E, "m", tmp=t1b)
        ph, ph2, po0, po1, pg = bld.P[0], bld.P[1], bld.P[2], bld.P[3], bld.P[4]

        def g0(g):
            wg = dram["L1_ffn_Wg"].rearrange("(c p) e -> p c e", p=128)
            bld.inc("gq", g.dma_start(out=Wgb[:, :].rearrange("p (c e) -> p c e", e=16), in_=wg), 16)
            bld.inc("gq", g.dma_start(out=bgr[:, :], in_=dram["L1_ffn_bg"][None, :]), 16)
            bld.inc("gq", g.dma_start(out=bir[:, :], in_=dram["L1_ffn_bias"][None, :]), 16)
            bld.inc("gq", g.dma_start(out=b1n[:, :], in_=dram["L1_ffn_b1"][:, :]), 16)
            bld.inc("gq", g.dma_start(out=b2n[:, :], in_=dram["L1_ffn_b2"][:, :]), 16)

        def v0(v):
            bld.V(v, lambda: v.memset(onesw[:, :], 1.0))
        bld.phase(gpsimd=g0, vector=v0)

        def v1(v):
            bld.w(v, "gq")
            bld.V(v, lambda: v.tensor_tensor(bgr[:, :], bgr[:, :], bir[:, :], ALU.add))
        bld.phase(vector=v1)

        for hc in range(NE):
            def tb1(t, hc=hc):
                bld.w(t, "gq")
                bld.w(t, "vv")
                bld.w(t, "mm")
                bld.inc("mm", t.matmul(bld.PTF[0:128, 0:16], b1n[:, hc * 128:(hc + 1) * 128],
                                       bld.identf[0:16, 0:16], start=True, stop=True, is_transpose=True))

            def vb1(v, hc=hc):
                bld.w(v, "mm")
                bld.V(v, lambda: v.tensor_copy(b1c[:, hc * 16:(hc + 1) * 16], bld.PTF[0:128, 0:16]))
            bld.phase(tensor=tb1, vector=vb1, order=("tensor", "vector"))
        for dc in range(DC):
            def tb2(t, dc=dc):
                bld.w(t, "gq")
                bld.w(t, "vv")
                bld.w(t, "mm")
                bld.inc("mm", t.matmul(bld.PTF[0:128, 0:16], b2n[:, dc * 128:(dc + 1) * 128],
                                       bld.identf[0:16, 0:16], start=True, stop=True, is_transpose=True))

            def vb2(v, dc=dc):
                bld.w(v, "mm")
                bld.V(v, lambda: v.tensor_copy(b2c[:, dc * 16:(dc + 1) * 16], bld.PTF[0:128, 0:16]))
            bld.phase(tensor=tb2, vector=vb2, order=("tensor", "vector"))

        for (t0, ntb) in MOE_BLOCKS:
            emit_ln_fm(bld, lns4, resid, NT, t0, ntb, ln4, NTBM, 0)
            if t0 == 0:
                emit_dbg(bld, dram, "dbg_ln4", ln4[:, :])

            for (o, w) in tch(ntb):
                def tl(t, o=o, w=w):
                    bld.w(t, "vv")
                    bld.w(t, "mm")
                    for c in range(DC):
                        bld.inc("mm", t.matmul(pg[0:16, 0:w], Wgb[:, c * 16:(c + 1) * 16],
                                               ln4[:, c * NTBM + o: c * NTBM + o + w],
                                               start=(c == 0), stop=False))
                    bld.w(t, "mm")
                    bld.inc("mm", t.matmul(pg[0:16, 0:w], bgr[:, :], onesw[:, 0:w],
                                           start=False, stop=True))

                def vl(v, o=o, w=w):
                    bld.w(v, "mm")
                    bld.V(v, lambda: v.tensor_copy(lgt[0:16, 0:w], pg[0:16, 0:w]))
                bld.phase(tensor=tl, vector=vl, order=("tensor", "vector"))
                if t0 == 0 and o == 0:
                    emit_dbg(bld, dram, "dbg_lgt", lgt[0:16, 0:512])

                for k in range(w // 128):
                    tile = (t0 + o) // 128 + k
                    def tt(t, k=k, tile=tile, w=w):
                        bld.w(t, "vv")
                        bld.w(t, "mm")
                        bld.inc("mm", t.matmul(bld.PTF[0:128, 0:16], lgt[0:16, k * 128:(k + 1) * 128],
                                               bld.identf[0:16, 0:16], start=True, stop=True,
                                               is_transpose=True))

                    def vt(v, k=k, tile=tile):
                        bld.w(v, "mm")
                        bld.V(v, lambda: v.tensor_copy(cw[:, tile * 16:(tile + 1) * 16], bld.PTF[0:128, 0:16]))
                    bld.phase(tensor=tt, vector=vt, order=("tensor", "vector"))

            nt_tiles = ntb // 128
            for k in range(nt_tiles):
                tile = t0 // 128 + k
                cwt = cw[:, tile * 16:(tile + 1) * 16]
                def vtop(v, cwt=cwt):
                    bld.V(v, lambda: v.tensor_reduce(m1[:, :], cwt, AX.X, ALU.max))
                    bld.V(v, lambda: v.tensor_scalar(tA[:, :], cwt, m1[:, :], None, ALU.subtract))
                    bld.V(v, lambda: v.tensor_scalar(mk1[:, :], tA[:, :], 0.0, None, ALU.is_ge))
                    bld.V(v, lambda: v.tensor_scalar(tB[:, :], mk1[:, :], -1e30, None, ALU.mult))
                    bld.V(v, lambda: v.tensor_tensor(tB[:, :], tB[:, :], cwt, ALU.add))
                    bld.V(v, lambda: v.tensor_reduce(m2[:, :], tB[:, :], AX.X, ALU.max))
                    bld.V(v, lambda: v.tensor_scalar(tB[:, :], tB[:, :], m2[:, :], None, ALU.subtract))
                    bld.V(v, lambda: v.tensor_scalar(mk2[:, :], tB[:, :], 0.0, None, ALU.is_ge))
                    bld.V(v, lambda: v.tensor_tensor(e2[:, :], m2[:, :], m1[:, :], ALU.subtract))

                def stop_(s):
                    bld.w(s, "vv")
                    bld.S(s, lambda: s.activation(out=e2[:, :], in_=e2[:, :], func=AF.Exp))

                def vtop2(v, cwt=cwt):
                    bld.w(v, "ss")
                    bld.V(v, lambda: v.tensor_scalar(den[:, :], e2[:, :], 1.0, None, ALU.add))
                    bld.V(v, lambda: v.reciprocal(rec[:, :], den[:, :]))
                    bld.V(v, lambda: v.tensor_tensor(w2c[:, :], e2[:, :], rec[:, :], ALU.mult))
                    bld.V(v, lambda: v.tensor_scalar(mk1[:, :], mk1[:, :], rec[:, :], None, ALU.mult))
                    bld.V(v, lambda: v.scalar_tensor_tensor(cwt, mk2[:, :], w2c[:, :], mk1[:, :],
                                                            ALU.mult, ALU.add))
                bld.phase(vector=vtop, order=("vector",))
                bld.phase(scalar=stop_, order=("scalar",))
                bld.phase(vector=vtop2, order=("vector",))

                def tct(t, tile=tile):
                    bld.w(t, "vv")
                    bld.w(t, "mm")
                    bld.inc("mm", t.matmul(bld.PTF[0:16, 0:128], cw[:, tile * 16:(tile + 1) * 16],
                                           bld.identf[:, :], start=True, stop=True, is_transpose=True))

                def vct(v, tile=tile):
                    bld.w(v, "mm")
                    bld.V(v, lambda: v.tensor_copy(cwT[0:16, tile * 128:(tile + 1) * 128], bld.PTF[0:16, 0:128]))
                bld.phase(tensor=tct, vector=vct, order=("tensor", "vector"))

            def emit_dma_w1(e):
                def g(g, e=e):
                    bld.w(g, "mm")
                    bld.w(g, "vv")
                    for dcn in range(DC):
                        bld.inc("gq", g.dma_start(out=W1b[:, dcn * 2048:(dcn + 1) * 2048],
                                                  in_=dram["L1_ffn_W1"][e, dcn * 128:(dcn + 1) * 128, :]), 16)
                bld.phase(gpsimd=g)
                bld.gq_marks[("w1", e)] = bld.cnt["gq"]

            def emit_dma_w2(e, dc):
                buf = W2d[dc % 2]
                def g(g, e=e, dc=dc, buf=buf):
                    bld.w(g, "mm")
                    src = dram["L1_ffn_W2"][e].rearrange("(hcn p) c -> p hcn c", p=128)[:, :, dc * 128:(dc + 1) * 128]
                    dst = buf[:, :].rearrange("p (hcn c) -> p hcn c", c=128)
                    bld.inc("gq", g.dma_start(out=dst[:, 0:8, :], in_=src[:, 0:8, :]), 16)
                    bld.inc("gq", g.dma_start(out=dst[:, 8:16, :], in_=src[:, 8:16, :]), 16)
                bld.phase(gpsimd=g)
                bld.gq_marks[("w2", e, dc)] = bld.cnt["gq"]

            if t0 == 0:
                emit_dbg(bld, dram, "dbg_cw", cw[:, :])
                emit_dbg(bld, dram, "dbg_cwT", cwT[:, :])
            emit_dma_w1(0)
            for e in range(NE):
                ss0 = bld.cnt["ss"]
                chunks = tch(ntb)
                nev = 2 if ntb > 1024 else 1
                for hc in range(NE):
                    big = bld.PW0 if hc % 2 == 0 else bld.PW1
                    tail = bld.P[0] if hc % 2 == 0 else bld.P[1]
                    def tf(t, e=e, hc=hc, big=big, tail=tail, ss0=ss0, chunks=chunks, nev=nev):
                        t.wait_ge(bld.sem["gq"], bld.gq_marks[("w1", e)])
                        if hc == 0:
                            bld.w(t, "vv")
                            bld.w(t, "ss")
                            bld.w(t, "mm")
                        elif hc == 1:
                            bld.w(t, "vv")
                        else:
                            t.wait_ge(bld.sem["ss"], ss0 + nev * (hc - 1))
                        for ti, (o, w) in enumerate(chunks):
                            dst = big[:, o:o + w] if o < 1024 else tail[:, 0:w]
                            for c in range(DC):
                                bld.inc("mm", t.matmul(dst,
                                                       W1b[:, c * 2048 + hc * 128: c * 2048 + (hc + 1) * 128],
                                                       ln4[:, c * NTBM + o: c * NTBM + o + w],
                                                       start=(c == 0), stop=(c == DC - 1)))

                    def sf(s, e=e, hc=hc, big=big, tail=tail, ntb=ntb):
                        bld.w(s, "mm")
                        wbig = min(1024, ntb)
                        bld.S(s, lambda: s.activation(out=hT[:, hc * NTBM: hc * NTBM + wbig],
                                                      in_=big[:, 0:wbig], func=AF.Relu,
                                                      bias=b1c[:, hc * 16 + e: hc * 16 + e + 1]))
                        if ntb > 1024:
                            bld.S(s, lambda: s.activation(out=hT[:, hc * NTBM + 1024: hc * NTBM + ntb],
                                                          in_=tail[:, 0:ntb - 1024], func=AF.Relu,
                                                          bias=b1c[:, hc * 16 + e: hc * 16 + e + 1]))
                    bld.phase(tensor=tf, scalar=sf, order=("tensor", "scalar"))
                if t0 == 0 and e == 0:
                    emit_dbg(bld, dram, "dbg_hT", hT[:, 0:NTBM])
                if e + 1 < NE:
                    emit_dma_w1(e + 1)

                def gcw(g, e=e):
                    bld.w(g, "mm")
                    bld.inc("gq", g.dma_start(out=cwrow[:, :], in_=cwT[e:e + 1, :]), 16)
                bld.phase(gpsimd=gcw)
                bld.gq_marks[("cwr", e, t0)] = bld.cnt["gq"]
                for ti, (o, w) in enumerate(tch(ntb)):
                    def tb_(t, e=e, ti=ti, o=o, w=w):
                        t.wait_ge(bld.sem["gq"], bld.gq_marks[("cwr", e, t0)])
                        bld.w(t, "vv")
                        bld.w(t, "mm")
                        bld.inc("mm", t.matmul(pg[:, 0:w], bld.onesrow_b[:, :],
                                               cwrow[:, t0 + o: t0 + o + w], start=True, stop=True))

                    def vb(v, ti=ti, o=o, w=w):
                        bld.w(v, "mm")
                        bld.V(v, lambda: v.tensor_copy(cwbs[:, ti * 512: ti * 512 + w], pg[:, 0:w]))
                    bld.phase(tensor=tb_, vector=vb, order=("tensor", "vector"))

                emit_dma_w2(e, 0)
                emit_dma_w2(e, 1)
                relu_done = bld.cnt["ss"]
                vv0 = bld.cnt["vv"]
                nops = 4 if ntb > 1024 else 2
                for dc in range(DC):
                    buf = W2d[dc % 2]
                    big2 = bld.PW0 if dc % 2 == 0 else bld.PW1
                    tail2 = bld.P[0] if dc % 2 == 0 else bld.P[1]
                    def tw2(t, e=e, dc=dc, buf=buf, big2=big2, tail2=tail2,
                            relu_done=relu_done, vv0=vv0, nops=nops, chunks=chunks):
                        t.wait_ge(bld.sem["gq"], bld.gq_marks[("w2", e, dc)])
                        t.wait_ge(bld.sem["ss"], relu_done)
                        if dc == 0:
                            bld.w(t, "vv")
                            bld.w(t, "mm")
                        elif dc >= 2:
                            t.wait_ge(bld.sem["vv"], vv0 + nops * (dc - 1))
                        for ti, (o, w) in enumerate(chunks):
                            dst = big2[:, o:o + w] if o < 1024 else tail2[:, 0:w]
                            for hcn in range(NE):
                                bld.inc("mm", t.matmul(dst,
                                                       buf[:, hcn * 128:(hcn + 1) * 128],
                                                       hT[:, hcn * NTBM + o: hcn * NTBM + o + w],
                                                       start=(hcn == 0), stop=(hcn == NE - 1)))

                    def vw2(v, e=e, dc=dc, big2=big2, tail2=tail2, ntb=ntb):
                        bld.w(v, "mm")
                        wbig = min(1024, ntb)
                        bld.V(v, lambda: v.scalar_tensor_tensor(
                            t1b[:, 0:wbig], big2[:, 0:wbig], b2c[:, dc * 16 + e: dc * 16 + e + 1],
                            cwbs[:, 0:wbig], ALU.add, ALU.mult))
                        sl = resid[:, dc * NT + t0: dc * NT + t0 + wbig]
                        bld.V(v, lambda sl=sl: v.tensor_tensor(sl, sl, t1b[:, 0:wbig], ALU.add))
                        if ntb > 1024:
                            wt_ = ntb - 1024
                            bld.V(v, lambda: v.scalar_tensor_tensor(
                                t1b[:, 0:wt_], tail2[:, 0:wt_], b2c[:, dc * 16 + e: dc * 16 + e + 1],
                                cwbs[:, 1024:1024 + wt_], ALU.add, ALU.mult))
                            sl2 = resid[:, dc * NT + t0 + 1024: dc * NT + t0 + ntb]
                            bld.V(v, lambda sl2=sl2: v.tensor_tensor(sl2, sl2, t1b[:, 0:wt_], ALU.add))
                    bld.phase(tensor=tw2, vector=vw2, order=("tensor", "vector"))
                    if dc + 2 < DC:
                        emit_dma_w2(e, dc + 2)
                    elif e + 1 < NE:
                        emit_dma_w2(e + 1, dc - 6)
                if t0 == 0 and e == 0:
                    emit_dbg(bld, dram, "dbg_res0", resid[:, 0:512])
                    emit_dbg(bld, dram, "dbg_cwbs", cwbs[:, 0:512])


def moe_dram(nc, dram):
    dram["L1_ffn_Wg"] = nc.dram_tensor("L1_ffn_Wg", [D, 16], F32, kind="ExternalInput").ap()
    dram["L1_ffn_bg"] = nc.dram_tensor("L1_ffn_bg", [16], F32, kind="ExternalInput").ap()
    dram["L1_ffn_bias"] = nc.dram_tensor("L1_ffn_bias", [16], F32, kind="ExternalInput").ap()
    dram["L1_ffn_W1"] = nc.dram_tensor("L1_ffn_W1", [16, D, 2048], F32, kind="ExternalInput").ap()
    dram["L1_ffn_b1"] = nc.dram_tensor("L1_ffn_b1", [16, 2048], F32, kind="ExternalInput").ap()
    dram["L1_ffn_W2"] = nc.dram_tensor("L1_ffn_W2", [16, 2048, D], F32, kind="ExternalInput").ap()
    dram["L1_ffn_b2"] = nc.dram_tensor("L1_ffn_b2", [16, D], F32, kind="ExternalInput").ap()


def build_l1moe_launch():
    nc = bass.Bass("TRN2", target_bir_lowering=False, debug=False)
    dram = {}
    dram["resid_in"] = nc.dram_tensor("resid_in", [DC, 128, NT], F32, kind="ExternalInput").ap()
    dram["resid_out"] = nc.dram_tensor("resid_out", [DC, 128, NT], F32, kind="ExternalOutput").ap()
    common_dram(nc, dram)
    moe_dram(nc, dram)
    if DBG:
        dram["dbg_ln4"] = nc.dram_tensor("dbg_ln4", [128, DC * NTBM], BF16, kind="ExternalOutput").ap()
        dram["dbg_cw"] = nc.dram_tensor("dbg_cw", [128, 17 * 16], F32, kind="ExternalOutput").ap()
        dram["dbg_cwT"] = nc.dram_tensor("dbg_cwT", [16, NT], BF16, kind="ExternalOutput").ap()
        dram["dbg_hT"] = nc.dram_tensor("dbg_hT", [128, NTBM], BF16, kind="ExternalOutput").ap()
        dram["dbg_lgt"] = nc.dram_tensor("dbg_lgt", [16, 512], F32, kind="ExternalOutput").ap()
        dram["dbg_res0"] = nc.dram_tensor("dbg_res0", [128, 512], F32, kind="ExternalOutput").ap()
        dram["dbg_cwbs"] = nc.dram_tensor("dbg_cwbs", [128, 512], BF16, kind="ExternalOutput").ap()
    with ExitStack() as ctx:
        E = ctx.enter_context
        bld = Bld(nc, ctx)
        alloc_consts(bld, E)
        resid = E(nc.sbuf_tensor("resid", [128, DC * NT], F32))
        emit_load_consts(bld, dram)

        def ld(sy):
            for c in range(DC):
                bld.inc("sq", sy.dma_start(out=resid[:, c * NT:(c + 1) * NT], in_=dram["resid_in"][c]), 16)
        bld.phase(sync=ld)

        def bar(t):
            bld.w(t, "sq")
            bld.w(t, "gq")
        bld.phase(tensor=bar, order=("tensor",))

        emit_l1moe(bld, resid, dram)

        def st(sy):
            bld.w(sy, "vv")
            bld.w(sy, "mm")
            bld.w(sy, "ss")
            for c in range(DC):
                bld.inc("sq", sy.dma_start(out=dram["resid_out"][c], in_=resid[:, c * NT:(c + 1) * NT]), 16)
            bld.w(sy, "sq")
        bld.phase(sync=st)
    return nc


NKT = 4096


ATTN_PAIRS = 8
ATTN_QS = None


def emit_l0attn(bld, resid, QTall, dram):
    nc = bld.nc
    with ExitStack() as ctx:
        E = ctx.enter_context
        wbc = E(nc.sbuf_tensor("wbc", [128, 4], F32))
        q4 = E(nc.sbuf_tensor("q4", [1, 4], F32))
        qm = E(nc.sbuf_tensor("qm", [1, 1], F32))
        Wqs = E(nc.sbuf_tensor("Wqs", [128, DC * 128], BF16))
        Wks = E(nc.sbuf_tensor("Wks", [128, DC * 128], BF16))
        Wke = E(nc.sbuf_tensor("Wke", [128, DC * 128], BF16))
        Wvs = E(nc.sbuf_tensor("Wvs", [128, DC * 128], BF16))
        Wos = E(nc.sbuf_tensor("Wos", [128, DC * 128], BF16))
        tm32 = E(nc.sbuf_tensor("tm32", [128, 32], F32))
        lnsl = [E(nc.sbuf_tensor("lnsl0", [128, DC * 512], BF16)),
                E(nc.sbuf_tensor("lnsl1", [128, DC * 512], BF16))]
        KT = E(nc.sbuf_tensor("KT", [128, NKT], BF16))
        Vp = E(nc.sbuf_tensor("Vp", [128, 32 * 130], BF16))
        PA = E(nc.sbuf_tensor("PA", [128, 32 * 128], BF16))
        PB = E(nc.sbuf_tensor("PB", [128, 32 * 128], BF16))
        OT = E(nc.sbuf_tensor("OT", [128, NT], BF16))
        recA = E(nc.sbuf_tensor("recA", [128, 1], F32))
        recB = E(nc.sbuf_tensor("recB", [128, 1], F32))
        psSA = bld.PW0
        psSB = bld.PW1
        pOA, pOB, pKV = bld.P[0], bld.P[1], bld.P[2]
        pKV2 = bld.P[3]

        def g0(g):
            bld.inc("gq", g.dma_start(out=q4[:, :], in_=dram["L0_attn_quadrant_weights"][None, :]), 16)

        def v0(v):
            bld.w(v, "gq")
            bld.V(v, lambda: v.tensor_reduce(qm[:, :], q4[:, :], AX.X, ALU.max))
            bld.V(v, lambda: v.tensor_scalar(q4[:, :], q4[:, :], qm[:, :], None, ALU.subtract))
        bld.phase(gpsimd=g0, vector=v0)

        def s0(s):
            bld.w(s, "vv")
            bld.S(s, lambda: s.activation(out=q4[:, :], in_=q4[:, :], func=AF.Exp))
        bld.phase(scalar=s0)

        def v1(v):
            bld.w(v, "ss")
            bld.V(v, lambda: v.tensor_reduce(qm[:, :], q4[:, :], AX.X, ALU.add))
            bld.V(v, lambda: v.reciprocal(qm[:, :], qm[:, :]))
            bld.V(v, lambda: v.tensor_scalar(q4[:, :], q4[:, :], qm[:, :], None, ALU.mult))
            bld.V(v, lambda: v.tensor_scalar(q4[:, :], q4[:, :], float(1.0 / np.sqrt(32.0)), None, ALU.mult))
        bld.phase(vector=v1)

        def t1(t):
            bld.w(t, "vv")
            bld.w(t, "mm")
            bld.inc("mm", t.matmul(bld.PTF[:, 0:4], bld.onesrow[:, :], q4[:, :], start=True, stop=True))

        def v2(v):
            bld.w(v, "mm")
            bld.V(v, lambda: v.tensor_copy(wbc[:, :], bld.PTF[:, 0:4]))
        bld.phase(tensor=t1, vector=v2, order=("tensor", "vector"))

        for p in range(ATTN_PAIRS):
            pc = slice(p * 128, (p + 1) * 128)

            def gw(g, p=p):
                bld.w(g, "mm")
                bld.w(g, "vv")
                for name, dst in [("L0_attn_Wq", Wqs), ("L0_attn_Wk", Wks), ("L0_attn_Wv", Wvs)]:
                    for c in range(DC):
                        bld.inc("gq", g.dma_start(out=dst[:, c * 128:(c + 1) * 128],
                                                  in_=dram[name][c * 128:(c + 1) * 128, p * 128:(p + 1) * 128]), 16)
                bld.inc("gq", g.dma_start(out=Wos[:, :], in_=dram["L0_attn_Wo"][p * 128:(p + 1) * 128, :]), 16)
            bld.phase(gpsimd=gw)

            def vmix(v, p=p):
                bld.w(v, "gq")
                for c in range(DC):
                    for h2 in range(2):
                        base = c * 128 + h2 * 64
                        k1 = Wks[:, base:base + 32]
                        k2 = Wks[:, base + 32:base + 64]
                        bld.V(v, lambda k2=k2: v.tensor_scalar(tm32[:, :], k2, wbc[:, 1:2], None, ALU.mult))
                        bld.V(v, lambda k1=k1, base=base: v.scalar_tensor_tensor(
                            Wke[:, base:base + 32], k1, wbc[:, 0:1], tm32[:, :], ALU.mult, ALU.add))
                        bld.V(v, lambda k2=k2: v.tensor_scalar(tm32[:, :], k2, wbc[:, 3:4], None, ALU.mult))
                        bld.V(v, lambda k1=k1, base=base: v.scalar_tensor_tensor(
                            Wke[:, base + 32:base + 64], k1, wbc[:, 2:3], tm32[:, :], ALU.mult, ALU.add))
            bld.phase(vector=vmix)

            for tc in range(8):
                buf = lnsl[tc % 2]
                def gl(g, tc=tc, buf=buf):
                    bld.w(g, "mm")
                    for c in range(DC):
                        bld.inc("gq", g.dma_start(
                            out=buf[:, c * 512:(c + 1) * 512],
                            in_=dram["ln1s"].rearrange("(c x) t -> c x t", x=128)[c, :, tc * 512:(tc + 1) * 512]), 16)
                bld.phase(gpsimd=gl)
                gqm = bld.cnt["gq"]

                vvb = bld.cnt["vv"]
                def tkv(t, p=p, tc=tc, buf=buf, gqm=gqm, vvb=vvb):
                    t.wait_ge(bld.sem["gq"], gqm)
                    bld.w(t, "vv")
                    bld.w(t, "mm")
                    for c in range(DC):
                        bld.inc("mm", t.matmul(pKV[:, 0:512], Wke[:, c * 128:(c + 1) * 128],
                                               buf[:, c * 512:(c + 1) * 512],
                                               start=(c == 0), stop=(c == DC - 1)))
                    for tt_ in range(4):
                        if tt_ > 0:
                            t.wait_ge(bld.sem["vv"], vvb + 1 + 2 * tt_)
                        for c in range(DC):
                            bld.inc("mm", t.matmul(pKV2[:, 0:128],
                                                   buf[:, c * 512 + tt_ * 128: c * 512 + tt_ * 128 + 128],
                                                   Wvs[:, c * 128:(c + 1) * 128],
                                                   start=(c == 0), stop=(c == DC - 1)))

                def vkv(v, p=p, tc=tc):
                    mmb = bld.cnt["mm"] - 40
                    v.wait_ge(bld.sem["mm"], mmb + 8)
                    bld.inc("vv", v.tensor_copy(KT[:, tc * 512:(tc + 1) * 512], pKV[:, 0:512]))
                    for tt_ in range(4):
                        v.wait_ge(bld.sem["mm"], mmb + 8 + (tt_ + 1) * 8)
                        tile = tc * 4 + tt_
                        bld.w(v, "vv")
                        bld.inc("vv", v.tensor_copy(Vp[:, tile * 130: tile * 130 + 64], pKV2[0:128, 0:64]))
                        bld.w(v, "vv")
                        bld.inc("vv", v.tensor_copy(Vp[:, tile * 130 + 65: tile * 130 + 129], pKV2[0:128, 64:128]))
                bld.phase(tensor=tkv, vector=vkv, order=("tensor", "vector"))

            def vones(v, p=p):
                bld.w(v, "mm")
                bld.V(v, lambda: v.memset(Vp[:, :].rearrange("x (a b) -> x a b", b=65)[:, :, 64], 1.0))
            if p == 0:
                bld.phase(vector=vones)

            for qs in range(NT // 128 if ATTN_QS is None else ATTN_QS):
                for g4 in range(4):
                    def tS(t, p=p, qs=qs, g4=g4):
                        if g4 == 0:
                            bld.w(t, "mm")
                        bld.w(t, "ss")
                        for kk in range(8):
                            kc = g4 * 8 + kk
                            bld.inc("mm", t.matmul(psSA[:, kk * 128:(kk + 1) * 128],
                                                   KT[0:64, kc * 128:(kc + 1) * 128],
                                                   QTall[0:64, p * NT + qs * 128: p * NT + (qs + 1) * 128],
                                                   start=True, stop=True, tile_position=(0, 0)))
                            bld.inc("mm", t.matmul(psSB[:, kk * 128:(kk + 1) * 128],
                                                   KT[64:128, kc * 128:(kc + 1) * 128],
                                                   QTall[64:128, p * NT + qs * 128: p * NT + (qs + 1) * 128],
                                                   start=True, stop=True, tile_position=(64, 0)))

                    def sS(s, p=p, qs=qs, g4=g4):
                        bld.w(s, "mm")
                        bld.S(s, lambda: s.activation(out=PA[:, g4 * 1024:(g4 + 1) * 1024],
                                                      in_=psSA[:, :], func=AF.Exp))
                        bld.S(s, lambda: s.activation(out=PB[:, g4 * 1024:(g4 + 1) * 1024],
                                                      in_=psSB[:, :], func=AF.Exp))
                    bld.phase(tensor=tS, scalar=sS, order=("tensor", "scalar"))

                def tAV(t, p=p, qs=qs):
                    bld.w(t, "mm")
                    bld.w(t, "ss")
                    bld.w(t, "vv")
                    for kc in range(32):
                        bld.inc("mm", t.matmul(pOA[:, 0:65], PA[:, kc * 128:(kc + 1) * 128],
                                               Vp[:, kc * 130: kc * 130 + 65],
                                               start=(kc == 0), stop=(kc == 31)))
                        bld.inc("mm", t.matmul(pOB[:, 0:65], PB[:, kc * 128:(kc + 1) * 128],
                                               Vp[:, kc * 130 + 65: kc * 130 + 130],
                                               start=(kc == 0), stop=(kc == 31)))

                def vAV(v, p=p, qs=qs):
                    bld.w(v, "mm")
                    bld.V(v, lambda: v.reciprocal(recA[:, :], pOA[:, 64:65]))
                    bld.V(v, lambda: v.reciprocal(recB[:, :], pOB[:, 64:65]))
                    bld.V(v, lambda: v.tensor_scalar(pOA[:, 0:64], pOA[:, 0:64], recA[:, :], None, ALU.mult))
                    bld.V(v, lambda: v.tensor_scalar(pOB[:, 0:64], pOB[:, 0:64], recB[:, :], None, ALU.mult))
                bld.phase(tensor=tAV, vector=vAV, order=("tensor", "vector"))

                def vO(v, p=p, qs=qs):
                    bld.w(v, "mm")
                    bld.V(v, lambda: v.tensor_copy(lnsl[0][:, 0:64], pOA[:, 0:64]))
                    bld.V(v, lambda: v.tensor_copy(lnsl[0][:, 64:128], pOB[:, 0:64]))
                bld.phase(vector=vO)

                for h2 in range(2):
                    def tT(t, h2=h2, qs=qs):
                        bld.w(t, "vv")
                        bld.w(t, "mm")
                        bld.inc("mm", t.matmul(bld.PT[0:64, 0:128], lnsl[0][:, h2 * 64:(h2 + 1) * 64],
                                               bld.identb[:, :], start=True, stop=True, is_transpose=True))

                    def vT(v, h2=h2, qs=qs):
                        bld.w(v, "mm")
                        bld.V(v, lambda: v.tensor_copy(OT[h2 * 64:(h2 + 1) * 64, qs * 128:(qs + 1) * 128],
                                                       bld.PT[0:64, 0:128]))
                    bld.phase(tensor=tT, vector=vT, order=("tensor", "vector"))

            for dc in range(DC):
                for ti, (o, w) in enumerate(tch(NT)):
                    pcur = bld.P[4] if ti % 2 == 0 else bld.P[5]
                    def tP(t, p=p, dc=dc, o=o, w=w, pcur=pcur):
                        bld.w(t, "vv")
                        bld.w(t, "mm")
                        bld.inc("mm", t.matmul(pcur[:, 0:w], Wos[:, dc * 128:(dc + 1) * 128],
                                               OT[:, o:o + w], start=True, stop=True))

                    def vP(v, p=p, dc=dc, o=o, w=w, pcur=pcur):
                        bld.w(v, "mm")
                        sl = resid[:, dc * NT + o: dc * NT + o + w]
                        bld.V(v, lambda: v.tensor_tensor(sl, sl, pcur[:, 0:w], ALU.add))
                    bld.phase(tensor=tP, vector=vP, order=("tensor", "vector"))


def emit_p0(bld, resid, QTall, dram):
    nc = bld.nc
    with ExitStack() as ctx:
        E = ctx.enter_context
        xt = [E(nc.sbuf_tensor("xt0", [128, 1024], F32)), E(nc.sbuf_tensor("xt1", [128, 1024], F32))]
        xc = E(nc.sbuf_tensor("xc", [128, 1024], F32))
        lnb = E(nc.sbuf_tensor("lnb", [128, 1024], BF16))
        lnf = E(nc.sbuf_tensor("lnf", [128, DC * 128], BF16))
        mcol = E(nc.sbuf_tensor("mcol", [128, 1], F32))
        scol2 = E(nc.sbuf_tensor("scol2", [128, 1], F32))
        sqd = E(nc.sbuf_tensor("sqd", [128, 1024], BF16))


        def ln_tile(src_ap, ti, into_resid, into_qt, into_scratch):
            buf = xt[ti % 2]
            def gld(g):
                bld.w(g, "mm")
                bld.w(g, "vv")
                bld.w(g, "ss")
                bld.inc("gq", g.dma_start(out=buf[:, :], in_=src_ap), 16)
            bld.phase(gpsimd=gld)
            gqm = bld.cnt["gq"]

            def vln(v):
                v.wait_ge(bld.sem["gq"], gqm)
                bld.V(v, lambda: v.tensor_reduce(mcol[:, :], buf[:, :], AX.X, ALU.add))
                bld.V(v, lambda: v.tensor_scalar(mcol[:, :], mcol[:, :], 1.0 / D, None, ALU.mult))
                bld.V(v, lambda: v.tensor_scalar(xc[:, :], buf[:, :], mcol[:, :], None, ALU.subtract))

            def sln(s):
                bld.w(s, "vv")
                bld.S(s, lambda: s.activation(out=sqd[:, :], in_=xc[:, :], func=AF.Square,
                                              accum_out=scol2[:, :]))
            bld.phase(vector=vln, scalar=sln, order=("vector", "scalar"))

            def vln2(v):
                bld.w(v, "ss")
                bld.V(v, lambda: v.tensor_scalar(scol2[:, :], scol2[:, :], 1.0 / D, EPS, ALU.mult, ALU.add))

            def sln2(s):
                bld.w(s, "vv")
                bld.S(s, lambda: s.activation(out=scol2[:, :], in_=scol2[:, :], func=AF.Ln))
                bld.S(s, lambda: s.activation(out=scol2[:, :], in_=scol2[:, :], func=AF.Exp, scale=-0.5))
            bld.phase(vector=vln2, order=("vector",))
            bld.phase(scalar=sln2, order=("scalar",))

            def vln3(v):
                bld.w(v, "ss")
                bld.V(v, lambda: v.tensor_scalar(lnb[:, :], xc[:, :], scol2[:, :], None, ALU.mult))
            bld.phase(vector=vln3, order=("vector",))

            items = []
            for c in range(DC):
                src = lnb[:, c * 128:(c + 1) * 128]
                dst = lnf[:, c * 128:(c + 1) * 128]
                ev = (lambda dst: (lambda v, pt: v.tensor_copy(dst, pt)))(dst)
                items.append((src, 128, 128, ev))
            emit_tpose_pipe(bld, items)

            if into_resid is not None:
                items = []
                for c in range(DC):
                    src = buf[:, c * 128:(c + 1) * 128]
                    dst = into_resid[c]
                    ev = (lambda dst: (lambda v, pt: v.tensor_copy(dst, pt)))(dst)
                    items.append((src, 128, 128, ev))
                emit_tpose_pipe(bld, items, f32=True)

            if into_scratch is not None:
                def sst(sy):
                    bld.w(sy, "vv")
                    for c in range(DC):
                        bld.inc("sq", sy.dma_start(out=into_scratch[c], in_=lnf[:, c * 128:(c + 1) * 128]), 16)
                bld.phase(sync=sst)


        QWALL = E(nc.sbuf_tensor("QWALL", [128, 64 * 128], BF16))

        def gq0(g):
            for c in range(DC):
                bld.inc("gq", g.dma_start(out=QWALL[:, c * 1024:(c + 1) * 1024],
                                          in_=dram["L0_attn_Wq"][c * 128:(c + 1) * 128, :]), 16)
        bld.phase(gpsimd=gq0)

        for ti in range(NT // 128):
            ln_tile(dram["xrows"][ti * 128:(ti + 1) * 128, :], ti,
                    [resid[:, c * NT + ti * 128: c * NT + (ti + 1) * 128] for c in range(DC)],
                    False, None)
            for p in range(8):
                def tq(t, p=p, ti=ti):
                    bld.w(t, "vv")
                    bld.w(t, "mm")
                    bld.w(t, "gq")
                    for c in range(DC):
                        bld.inc("mm", t.matmul(bld.P[0][:, 0:128],
                                               QWALL[:, c * 1024 + p * 128: c * 1024 + (p + 1) * 128],
                                               lnf[:, c * 128:(c + 1) * 128],
                                               start=(c == 0), stop=(c == DC - 1)))

                def vq(v, p=p, ti=ti):
                    bld.w(v, "mm")
                    bld.V(v, lambda: v.tensor_copy(QTall[:, p * NT + ti * 128:(p * NT + (ti + 1) * 128)],
                                                   bld.P[0][:, 0:128]))
                bld.phase(tensor=tq, vector=vq, order=("tensor", "vector"))

        for ti in range(NKT // 128):
            dst = dram["ln1s"].rearrange("(c x) t -> c x t", x=128)[:, :, ti * 128:(ti + 1) * 128]
            ln_tile(dram["xfull"][ti * 128:(ti + 1) * 128, :], ti, None, False, dst)


def attn_dram(nc, dram):
    dram["L0_attn_Wq"] = nc.dram_tensor("L0_attn_Wq", [D, D], F32, kind="ExternalInput").ap()
    dram["L0_attn_Wk"] = nc.dram_tensor("L0_attn_Wk", [D, D], F32, kind="ExternalInput").ap()
    dram["L0_attn_Wv"] = nc.dram_tensor("L0_attn_Wv", [D, D], F32, kind="ExternalInput").ap()
    dram["L0_attn_Wo"] = nc.dram_tensor("L0_attn_Wo", [D, D], F32, kind="ExternalInput").ap()
    dram["L0_attn_quadrant_weights"] = nc.dram_tensor("L0_attn_quadrant_weights", [4], F32,
                                                      kind="ExternalInput").ap()
    dram["xrows"] = nc.dram_tensor("xrows", [NT, D], F32, kind="ExternalInput").ap()
    dram["xfull"] = nc.dram_tensor("xfull", [NKT, D], F32, kind="ExternalInput").ap()
    dram["ln1s"] = nc.dram_tensor("ln1s", [DC * 128, NKT], BF16, kind="Internal").ap()


def build_l0attn_launch():
    nc = bass.Bass("TRN2", target_bir_lowering=False, debug=False)
    dram = {}
    dram["resid_out"] = nc.dram_tensor("resid_out", [DC, 128, NT], F32, kind="ExternalOutput").ap()
    common_dram(nc, dram)
    attn_dram(nc, dram)
    with ExitStack() as ctx:
        E = ctx.enter_context
        bld = Bld(nc, ctx)
        alloc_consts(bld, E)
        resid = E(nc.sbuf_tensor("resid", [128, DC * NT], F32))
        QTall = E(nc.sbuf_tensor("QTall", [128, 8 * NT], BF16))
        emit_load_consts(bld, dram)
        emit_p0(bld, resid, QTall, dram)
        emit_l0attn(bld, resid, QTall, dram)

        def st(sy):
            bld.w(sy, "vv")
            bld.w(sy, "mm")
            bld.w(sy, "ss")
            for c in range(DC):
                bld.inc("sq", sy.dma_start(out=dram["resid_out"][c], in_=resid[:, c * NT:(c + 1) * NT]), 16)
            bld.w(sy, "sq")
        bld.phase(sync=st)
    return nc


def emit_l1ssm(bld, resid, dram):
    nc = bld.nc
    NTILES = 128
    with ExitStack() as ctx:
        E = ctx.enter_context
        Astr = E(nc.sbuf_tensor("Astr", [128, NTILES * 16], BF16))
        Vall = E(nc.sbuf_tensor("Vall", [128, NTILES * 128], BF16))
        Tall = E(nc.sbuf_tensor("Tall", [128, NTILES * 128], BF16))
        Pall = E(nc.sbuf_tensor("Pall", [128, NTILES * 128], BF16))
        Cblk = E(nc.sbuf_tensor("Cblk", [128, NTILES * 8], BF16))
        bca = E(nc.sbuf_tensor("bca", [128, 128], F32))
        cca = E(nc.sbuf_tensor("cca", [128, 128], F32))
        dca = E(nc.sbuf_tensor("dca", [128, 8], F32))
        bcaT = E(nc.sbuf_tensor("bcaT", [128, 128], F32))
        ccaT = E(nc.sbuf_tensor("ccaT", [128, 128], F32))
        dcaT = E(nc.sbuf_tensor("dcaT", [128, 128], F32))
        gst = E(nc.sbuf_tensor("gst", [128, 512], F32))
        grev = E(nc.sbuf_tensor("grev", [128, 512], BF16))
        gTsb = E(nc.sbuf_tensor("gTsb", [128, 1024], BF16))
        zb = E(nc.sbuf_tensor("zb", [128, 1024], BF16))

        def g0(g):
            for dloc in range(8):
                for th in range(2):
                    src = bass.AP(dram["L1_attn_A"].tensor, (th * 64 * 8 + dloc) * 256,
                                  [[16, 16], [8 * 256, 64], [1, 16]])
                    dst = Astr[dloc * 16:(dloc + 1) * 16, th * 64 * 16:(th + 1) * 64 * 16] \
                        .rearrange("n (t m) -> n t m", m=16)
                    bld.inc("gq", g.dma_start(out=dst, in_=src), 16)
            bld.inc("gq", g.dma_start(out=bca[:, :],
                                      in_=dram["L1_attn_Bm"].rearrange("(t a) n -> t (a n)", a=8)), 16)
            bld.inc("gq", g.dma_start(out=cca[:, :],
                                      in_=dram["L1_attn_Cm"].rearrange("(t a) n -> t (a n)", a=8)), 16)
            bld.inc("gq", g.dma_start(out=dca[:, :],
                                      in_=dram["L1_attn_Dm"].rearrange("(t p) -> t p", p=8)), 16)

        def v0(v):
            bld.V(v, lambda: v.memset(zb[0:64, :], 0.0))
        bld.phase(gpsimd=g0, vector=v0)

        for (srcT, dstT, rows) in [(bca, bcaT, 128), (cca, ccaT, 128), (dca, dcaT, 8)]:
            def tt(t, srcT=srcT, rows=rows):
                bld.w(t, "gq")
                bld.w(t, "vv")
                bld.w(t, "mm")
                bld.inc("mm", t.matmul(bld.PTF[0:rows, 0:128], srcT[:, 0:rows], bld.identf[:, :],
                                       start=True, stop=True, is_transpose=True))

            def vv(v, dstT=dstT, rows=rows):
                bld.w(v, "mm")
                bld.V(v, lambda: v.tensor_copy(dstT[0:rows, 0:128], bld.PTF[0:rows, 0:128]))
            bld.phase(tensor=tt, vector=vv, order=("tensor", "vector"))

        def vA(v):
            bld.w(v, "gq")
            for dl in range(8):
                def op(dl=dl):
                    o = Pall[:, :].rearrange("p (t f) -> p t f", f=128)[:, :, dl * 16:(dl + 1) * 16]
                    i0 = Astr[:, :].rearrange("p (t m) -> p t m", m=16)
                    i1 = dram
                    return v.scalar_tensor_tensor(o, i0, bld.msk8[:, dl:dl + 1], i0, ALU.bypass, ALU.mult)
                o = Pall[:, :].rearrange("p (t f) -> p t f", f=128)[:, :, dl * 16:(dl + 1) * 16]
                i0 = Astr[:, :].rearrange("p (t m) -> p t m", m=16)
                bld.V(v, lambda o=o, i0=i0, dl=dl: v.tensor_scalar(o, i0, bld.msk8[:, dl:dl + 1],
                                                                   None, ALU.mult))
            o = Vall[:, :].rearrange("p (t f) -> p t f", f=128)[:, :, 127]
            bld.V(v, lambda o=o: v.tensor_copy(o, bcaT[:, :]))
            for dl in range(8):
                o = Cblk[:, :].rearrange("p (t f) -> p t f", f=8)[:, :, dl]
                bld.V(v, lambda o=o, dl=dl: v.tensor_scalar(o, ccaT[:, :], bld.msk8[:, dl:dl + 1],
                                                            None, ALU.mult))
        bld.phase(vector=vA)

        for t_ in range(NTILES):
            def ttr(t, t_=t_):
                bld.w(t, "vv")
                bld.w(t, "mm")
                bld.inc("mm", t.matmul(bld.PT[:, :], Pall[:, t_ * 128:(t_ + 1) * 128],
                                       bld.identb[:, :], start=True, stop=True, is_transpose=True))

            def vtr(v, t_=t_):
                bld.w(v, "mm")
                bld.V(v, lambda: v.tensor_copy(Tall[:, t_ * 128:(t_ + 1) * 128], bld.PT[:, :]))
            bld.phase(tensor=ttr, vector=vtr, order=("tensor", "vector"))

        for k in range(7):
            wk = 1 << k
            per = max(1, 512 // (wk if wk > 0 else 1)) if wk else 8
            per = min(8, 512 // wk)
            grp = 0
            t_ = 0
            while t_ < NTILES:
                n = min(per, NTILES - t_)
                pb = bld.P[0] if grp % 2 == 0 else bld.P[1]
                def tmv(t, t_=t_, n=n, wk=wk, pb=pb):
                    bld.w(t, "vv")
                    bld.w(t, "ss")
                    bld.w(t, "mm")
                    for j in range(n):
                        bld.inc("mm", t.matmul(pb[:, j * wk:(j + 1) * wk],
                                               Tall[:, (t_ + j) * 128:(t_ + j + 1) * 128],
                                               Vall[:, (t_ + j) * 128 + 128 - wk: (t_ + j) * 128 + 128],
                                               start=True, stop=True))

                def smv(s, t_=t_, n=n, wk=wk, pb=pb):
                    bld.w(s, "mm")
                    for j in range(n):
                        bld.S(s, lambda j=j: s.copy(
                            Vall[:, (t_ + j) * 128 + 128 - 2 * wk: (t_ + j) * 128 + 128 - wk],
                            pb[:, j * wk:(j + 1) * wk]))
                bld.phase(tensor=tmv, scalar=smv, order=("tensor", "scalar"))
                t_ += n
                grp += 1
            if k == 6:
                break
            t_ = 0
            grp = 0
            while t_ < NTILES:
                n = min(4, NTILES - t_)
                pb = bld.PW0 if grp % 2 == 0 else bld.PW1
                pb2 = bld.P[0] if grp % 2 == 0 else bld.P[1]
                def tsq(t, t_=t_, n=n, pb=pb, pb2=pb2):
                    bld.w(t, "vv")
                    bld.w(t, "ss")
                    bld.w(t, "mm")
                    for j in range(n):
                        sl = slice((t_ + j) * 128, (t_ + j + 1) * 128)
                        bld.inc("mm", t.matmul(pb[:, j * 128:(j + 1) * 128] if j < 8 else None,
                                               Tall[:, sl], Pall[:, sl], start=True, stop=True))
                    for j in range(n):
                        sl = slice((t_ + j) * 128, (t_ + j + 1) * 128)
                        bld.inc("mm", t.matmul(pb2[:, j * 128:(j + 1) * 128],
                                               Pall[:, sl], Tall[:, sl], start=True, stop=True))

                def ssq(s, t_=t_, n=n, pb=pb, pb2=pb2):
                    bld.w(s, "mm")
                    for j in range(n):
                        sl = slice((t_ + j) * 128, (t_ + j + 1) * 128)
                        bld.S(s, lambda sl=sl, j=j: s.copy(Pall[:, sl], pb[:, j * 128:(j + 1) * 128]))

                def vsq(v, t_=t_, n=n, pb=pb, pb2=pb2):
                    bld.w(v, "mm")
                    for j in range(n):
                        sl = slice((t_ + j) * 128, (t_ + j + 1) * 128)
                        bld.V(v, lambda sl=sl, j=j: v.tensor_copy(Tall[:, sl],
                                                                  pb2[:, j * 128:(j + 1) * 128]))
                bld.phase(tensor=tsq, scalar=ssq, vector=vsq, order=("tensor", "scalar", "vector"))
                t_ += n
                grp += 1

        for t0_ in range(0, NTILES, 4):
            pb = bld.P[0]
            def tge(t, t0_=t0_, pb=pb):
                bld.w(t, "vv")
                bld.w(t, "ss")
                bld.w(t, "mm")
                for j in range(4):
                    sl = slice((t0_ + j) * 128, (t0_ + j + 1) * 128)
                    bld.inc("mm", t.matmul(pb[0:8, j * 128:(j + 1) * 128],
                                           Cblk[:, (t0_ + j) * 8:(t0_ + j + 1) * 8],
                                           Vall[:, sl], start=True, stop=True))

            def vge(v, t0_=t0_, pb=pb):
                bld.w(v, "mm")
                bld.V(v, lambda: v.tensor_copy(gst[0:8, :], pb[0:8, 0:512]))
                bld.V(v, lambda: v.tensor_copy(grev[0:8, :], gst[0:8, :]))
                o = grev[0:8, :].rearrange("p (t j) -> p t j", j=128)[:, :, 127]
                bld.V(v, lambda o=o: v.tensor_tensor(o, o, dcaT[0:8, t0_:t0_ + 4], ALU.add))
            bld.phase(tensor=tge, vector=vge, order=("tensor", "vector"))

            for j in range(4):
                def ttg(t, t0_=t0_, j=j):
                    bld.w(t, "vv")
                    bld.w(t, "mm")
                    bld.inc("mm", t.matmul(bld.PT[0:128, j * 8:(j + 1) * 8],
                                           grev[0:8, j * 128:(j + 1) * 128],
                                           bld.identb[0:8, 0:8], start=True, stop=True,
                                           is_transpose=True))
                bld.phase(tensor=ttg)

            def vgt(v, t0_=t0_):
                bld.w(v, "mm")
                bld.V(v, lambda: v.tensor_copy(gTsb[:, t0_ * 8:(t0_ + 4) * 8], bld.PT[0:128, 0:32]))
            bld.phase(vector=vgt)

        def sw(sy):
            bld.w(sy, "vv")
            bld.inc("sq", sy.dma_start(out=dram["gTdR"][0:128, :], in_=gTsb[:, :]), 16)
            bld.inc("sq", sy.dma_start(out=dram["gTdR"][128:192, :], in_=zb[0:64, :]), 16)
            bld.w(sy, "sq")
        bld.phase(sync=sw)
        emit_dbg(bld, dram, "dbg_g", gTsb[:, :])
        emit_dbg(bld, dram, "dbg_V", Vall[:, 0:2048])

    with ExitStack() as ctx:
        E = ctx.enter_context
        ln3f = E(nc.sbuf_tensor("ln3f", [128, DC * NT], BF16))
        y64 = E(nc.sbuf_tensor("y64", [128, 16 * 1024], BF16))
        lnwc = E(nc.sbuf_tensor("lnwc", [128, 8], F32))
        lnbc = E(nc.sbuf_tensor("lnbc", [128, 8], F32))
        boc = E(nc.sbuf_tensor("boc", [128, 8], F32))
        cmk = E(nc.sbuf_tensor("cmk", [128, 1], F32))
        with ExitStack() as ctxl:
            lns3 = alloc_ln_scr(bld, ctxl.enter_context, "s")
            emit_ln_fm(bld, lns3, resid, NT, 0, NT, ln3f, NT, 0)

        with ExitStack() as ctxw:
            tmpW = ctxw.enter_context(nc.sbuf_tensor("tmpW", [128, DC * 1024], BF16))
            WoT = ctxw.enter_context(nc.sbuf_tensor("WoTs", [128, DC * 1024], BF16))

            def gwo(g):
                bld.inc("gq", g.dma_start(out=cmk[0:64, :], in_=dram["convmask"][:, :]), 16)
                for dc in range(DC):
                    bld.inc("gq", g.dma_start(out=tmpW[:, dc * 1024:(dc + 1) * 1024],
                                              in_=dram["L1_attn_W_out"][dc * 128:(dc + 1) * 128, :]), 16)
            bld.phase(gpsimd=gwo)
            items = []
            for dc in range(DC):
                for c in range(DC):
                    src = tmpW[:, dc * 1024 + c * 128: dc * 1024 + (c + 1) * 128]
                    dst = WoT[:, c * 1024 + dc * 128: c * 1024 + (dc + 1) * 128]
                    ev = (lambda dst: (lambda v, pt: v.tensor_copy(dst, pt)))(dst)
                    items.append((src, 128, 128, ev))
            emit_tpose_pipe(bld, items)

            def swot(sy):
                bld.w(sy, "vv")
                bld.inc("sq", sy.dma_start(out=dram["wotT"][0:64, :], in_=WoT[0:64, 0:8192]), 16)
                bld.inc("sq", sy.dma_start(out=dram["wotT"][64:128, :], in_=WoT[64:128, 0:8192]), 16)
                bld.w(sy, "sq")
            bld.phase(sync=swot)

        with ExitStack() as ctxr:
            lrow = ctxr.enter_context(nc.sbuf_tensor("lrow", [1, 1024], F32))
            for (name, dstc) in [("L1_attn_ln_w", lnwc), ("L1_attn_ln_b", lnbc), ("L1_attn_b_out", boc)]:
                def gl(g, name=name):
                    bld.w(g, "vv")
                    bld.inc("gq", g.dma_start(out=lrow[:, :], in_=dram[name][None, :]), 16)
                bld.phase(gpsimd=gl)
                for c in range(DC):
                    def tl2(t, c=c):
                        bld.w(t, "gq")
                        bld.w(t, "vv")
                        bld.w(t, "mm")
                        bld.inc("mm", t.matmul(bld.PTF[0:128, 0:1], lrow[:, c * 128:(c + 1) * 128],
                                               bld.identf[0:1, 0:1], start=True, stop=True, is_transpose=True))

                    def vl2(v, c=c, dstc=dstc):
                        bld.w(v, "mm")
                        bld.V(v, lambda: v.tensor_copy(dstc[:, c:c + 1], bld.PTF[0:128, 0:1]))
                    bld.phase(tensor=tl2, vector=vl2, order=("tensor", "vector"))

        for (hb, (t0, n)) in enumerate([(128, 1024), (1152, 1024)]):
            nblk = n // 64
            b0 = t0 // 64
            wb0 = b0 - 1
            with ExitStack() as ctx1:
                E1 = ctx1.enter_context
                x64 = E1(nc.sbuf_tensor(f"x64_{hb}", [128, 17 * 1024], BF16))
                Gt = E1(nc.sbuf_tensor(f"Gt_{hb}", [128, 64 * 256], BF16))
                items = []
                for wb in range(17):
                    blkt = wb0 + wb
                    for c in range(DC):
                        dst = x64[0:64, wb * 1024 + c * 128: wb * 1024 + (c + 1) * 128]
                        src = ln3f[:, c * NT + blkt * 64: c * NT + (blkt + 1) * 64]
                        if blkt < 2:
                            ev = (lambda dst: (lambda v, pt: v.tensor_scalar(
                                dst, pt, cmk[0:64, :], None, ALU.mult)))(dst)
                        else:
                            ev = (lambda dst: (lambda v, pt: v.tensor_copy(dst, pt)))(dst)
                        items.append((src, 128, 64, ev))
                emit_tpose_pipe(bld, items)

                for ct in range(8):
                    def ga(sy, ct=ct):
                        bld.w(sy, "mm")
                        for i in range(64):
                            src = bass.AP(dram["gTdR"].tensor, (63 - i) * 1024 + ct * 128,
                                          [[1024, 64], [64 * 1024, 2], [1, 128]])
                            dst = Gt[0:64, i * 256:(i + 1) * 256].rearrange("w (h c) -> w h c", c=128)
                            bld.inc("sq", sy.dma_start(out=dst, in_=src), 16)
                    bld.phase(sync=ga)
                    sqm = bld.cnt["sq"]

                    ssb = bld.cnt["ss"]
                    mmb = bld.cnt["mm"]
                    for g4_ in range(32):
                        pb = bld.P[0] if g4_ % 2 == 0 else bld.P[1]
                        def tcv(t, ct=ct, g4_=g4_, pb=pb, sqm=sqm, ssb=ssb):
                            t.wait_ge(bld.sem["sq"], sqm)
                            if g4_ == 0:
                                bld.w(t, "ss")
                                bld.w(t, "mm")
                            elif g4_ > 1:
                                t.wait_ge(bld.sem["ss"], ssb + g4_ - 1)
                            for cj in range(4):
                                ch = g4_ * 4 + cj
                                lh = Gt[0:64, :].rearrange("w (i h c) -> w i h c", h=2, c=128)
                                rh = x64[0:64, :].rearrange("w (b f) -> w b f", f=1024)
                                bld.inc("mm", t.matmul(pb[0:64, cj * 16:(cj + 1) * 16],
                                                       lh[:, :, 0, ch],
                                                       rh[:, 0:16, ct * 128 + ch], start=True, stop=False))
                                bld.inc("mm", t.matmul(pb[0:64, cj * 16:(cj + 1) * 16],
                                                       lh[:, :, 1, ch],
                                                       rh[:, 1:17, ct * 128 + ch], start=False, stop=True))

                        def scv(s, ct=ct, g4_=g4_, pb=pb, mmb=mmb):
                            s.wait_ge(bld.sem["mm"], mmb + 8 * (g4_ + 1))
                            o = y64[0:64, :].rearrange("w (b f) -> w b f", f=1024) \
                                [:, :, ct * 128 + g4_ * 4: ct * 128 + (g4_ + 1) * 4]
                            bld.inc("ss", s.copy(o, pb[0:64, 0:64].rearrange("w (c b) -> w b c", b=16)))
                        bld.phase(tensor=tcv, scalar=scv, order=("tensor", "scalar"))

            if hb == 0:
                emit_dbg(bld, dram, "dbg_x64", x64[0:64, :])
                emit_dbg(bld, dram, "dbg_y64", y64[0:64, :])
            with ExitStack() as ctx2:
                E2 = ctx2.enter_context
                m64 = E2(nc.sbuf_tensor(f"m64_{hb}", [128, 1], F32))
                s64 = E2(nc.sbuf_tensor(f"s64_{hb}", [128, 1], F32))
                sq64 = E2(nc.sbuf_tensor(f"sq64_{hb}", [128, 1024], BF16))
                yT = E2(nc.sbuf_tensor(f"yT_{hb}", [128, DC * 1024], BF16))
                vwtmp = E2(nc.sbuf_tensor(f"vwtmp_{hb}", [128, 512], F32))
                WoT = E2(nc.sbuf_tensor(f"WoT2_{hb}", [128, DC * 1024], BF16))
                dbgpb = E2(nc.sbuf_tensor(f"dbgpb_{hb}", [128, 512], F32))
                vwtmp2 = E2(nc.sbuf_tensor(f"vwtmp2_{hb}", [128, 512], F32))

                def gwot(g):
                    bld.w(g, "mm")
                    bld.w(g, "vv")
                    bld.inc("gq", g.dma_start(out=WoT[:, :], in_=dram["wotT"][:, :]), 16)
                bld.phase(gpsimd=gwot)
                gq_wot = bld.cnt["gq"]
                if hb == 0:
                    def wotbar(t, gq_wot=gq_wot):
                        t.wait_ge(bld.sem["gq"], gq_wot)
                    bld.phase(tensor=wotbar, order=("tensor",))
                    emit_dbg(bld, dram, "dbg_wot", WoT[:, :])
                for b in range(16):
                    def vst(v, b=b):
                        bld.w(v, "ss")
                        bld.V(v, lambda: v.tensor_reduce(m64[0:64, :], y64[0:64, b * 1024:(b + 1) * 1024],
                                                         AX.X, ALU.add))
                        bld.V(v, lambda: v.tensor_scalar(m64[0:64, :], m64[0:64, :], 1.0 / D, None, ALU.mult))
                        bld.V(v, lambda: v.tensor_scalar(sq64[0:64, :], y64[0:64, b * 1024:(b + 1) * 1024],
                                                         m64[0:64, :], None, ALU.subtract))

                    def sst2(s, b=b):
                        bld.w(s, "vv")
                        bld.S(s, lambda: s.activation(out=y64[0:64, b * 1024:(b + 1) * 1024],
                                                      in_=sq64[0:64, :], func=AF.Square,
                                                      accum_out=s64[0:64, :]))
                    bld.phase(vector=vst, scalar=sst2, order=("vector", "scalar"))

                    def vst2(v, b=b):
                        bld.w(v, "ss")
                        bld.V(v, lambda: v.tensor_scalar(s64[0:64, :], s64[0:64, :], 1.0 / D, EPS, ALU.mult, ALU.add))
                    bld.phase(vector=vst2)

                    def sst3(s, b=b):
                        bld.w(s, "vv")
                        bld.S(s, lambda: s.activation(out=s64[0:64, :], in_=s64[0:64, :], func=AF.Ln))
                        bld.S(s, lambda: s.activation(out=s64[0:64, :], in_=s64[0:64, :], func=AF.Exp,
                                                      scale=-0.5))
                    bld.phase(scalar=sst3)

                    def vst3(v, b=b):
                        bld.w(v, "ss")
                        bld.V(v, lambda: v.tensor_scalar(y64[0:64, b * 1024:(b + 1) * 1024],
                                                         sq64[0:64, :], s64[0:64, :], None, ALU.mult))
                    bld.phase(vector=vst3)
                    if hb == 0 and b == 0:
                        emit_dbg(bld, dram, "dbg_m64", m64[0:64, :])
                        emit_dbg(bld, dram, "dbg_s64", s64[0:64, :])
                        emit_dbg(bld, dram, "dbg_yb0", y64[0:64, 0:1024])
                        emit_dbg(bld, dram, "dbg_sq0", sq64[0:64, :])

                items = []
                for b in range(16):
                    for c in range(DC):
                        src = y64[0:64, b * 1024 + c * 128: b * 1024 + (c + 1) * 128]
                        dst = yT[:, c * 1024 + b * 64: c * 1024 + (b + 1) * 64]
                        ev = (lambda dst, c: (lambda v, pt: v.tensor_scalar(
                            dst, pt, lnwc[:, c:c + 1], lnbc[:, c:c + 1], ALU.mult, ALU.add)))(dst, c)
                        items.append((src, 64, 128, ev))
                emit_tpose_pipe(bld, items)

                if hb == 0:
                    emit_dbg(bld, dram, "dbg_y64b", y64[0:64, :])
                    emit_dbg(bld, dram, "dbg_z", yT[:, :])

                def sg2(s):
                    bld.w(s, "vv")
                    for c in range(DC):
                        bld.S(s, lambda c=c: s.activation(out=yT[:, c * 1024:(c + 1) * 1024],
                                                          in_=yT[:, c * 1024:(c + 1) * 1024], func=AF.Gelu))
                bld.phase(scalar=sg2)
                if hb == 0:
                    emit_dbg(bld, dram, "dbg_gel", yT[:, :])

                for dc in range(DC):
                    pbA, pbB = bld.P[3], bld.P[5]
                    def tw(t, dc=dc, gq_wot=gq_wot):
                        t.wait_ge(bld.sem["gq"], gq_wot)
                        bld.w(t, "gq")
                        bld.w(t, "sq")
                        bld.w(t, "vv")
                        bld.w(t, "ss")
                        bld.w(t, "mm")
                        for c in range(DC):
                            bld.inc("mm", t.matmul(pbA[:, 0:512],
                                                   WoT[:, c * 1024 + dc * 128: c * 1024 + (dc + 1) * 128],
                                                   yT[:, c * 1024: c * 1024 + 512],
                                                   start=(c == 0), stop=(c == DC - 1)))
                            bld.inc("mm", t.matmul(pbB[:, 0:512],
                                                   WoT[:, c * 1024 + dc * 128: c * 1024 + (dc + 1) * 128],
                                                   yT[:, c * 1024 + 512: c * 1024 + 1024],
                                                   start=(c == 0), stop=(c == DC - 1)))

                    def sw2(s, dc=dc):
                        bld.w(s, "mm")
                        bld.w(s, "vv")
                        bld.S(s, lambda: s.copy(vwtmp[:, 0:512], pbA[:, 0:512]))
                        bld.S(s, lambda: s.copy(vwtmp2[:, 0:512], pbB[:, 0:512]))

                    def vw(v, dc=dc, t0=t0):
                        bld.w(v, "ss")
                        for (o, buf) in [(0, vwtmp), (512, vwtmp2)]:
                            bld.V(v, lambda o=o, buf=buf: v.scalar_tensor_tensor(
                                buf[:, 0:512], buf[:, 0:512], boc[:, dc:dc + 1],
                                ln3f[:, dc * NT + t0 + o: dc * NT + t0 + o + 512], ALU.add, ALU.add))
                            sl = resid[:, dc * NT + t0 + o: dc * NT + t0 + o + 512]
                            bld.V(v, lambda sl=sl, buf=buf: v.tensor_tensor(sl, sl, buf[:, 0:512], ALU.add))
                    bld.phase(tensor=tw, scalar=sw2, vector=vw, order=("tensor", "scalar", "vector"))
                if hb == 0:
                    emit_dbg(bld, dram, "dbg_res0c", resid[:, 0:2176])
                    emit_dbg(bld, dram, "dbg_pb", dbgpb[:, :])


def ssm_dram(nc, dram):
    dram["L1_attn_A"] = nc.dram_tensor("L1_attn_A", [D, 16, 16], F32, kind="ExternalInput").ap()
    dram["L1_attn_Bm"] = nc.dram_tensor("L1_attn_Bm", [D, 16], F32, kind="ExternalInput").ap()
    dram["L1_attn_Cm"] = nc.dram_tensor("L1_attn_Cm", [D, 16], F32, kind="ExternalInput").ap()
    dram["L1_attn_Dm"] = nc.dram_tensor("L1_attn_Dm", [D], F32, kind="ExternalInput").ap()
    dram["L1_attn_ln_w"] = nc.dram_tensor("L1_attn_ln_w", [D], F32, kind="ExternalInput").ap()
    dram["L1_attn_ln_b"] = nc.dram_tensor("L1_attn_ln_b", [D], F32, kind="ExternalInput").ap()
    dram["L1_attn_W_out"] = nc.dram_tensor("L1_attn_W_out", [D, D], F32, kind="ExternalInput").ap()
    dram["L1_attn_b_out"] = nc.dram_tensor("L1_attn_b_out", [D], F32, kind="ExternalInput").ap()
    dram["convmask"] = nc.dram_tensor("convmask", [64, 1], F32, kind="ExternalInput").ap()
    dram["gTdR"] = nc.dram_tensor("gTdR", [192, D], BF16, kind="Internal").ap()
    dram["wotT"] = nc.dram_tensor("wotT", [128, 8192], BF16, kind="Internal").ap()
    if DBG:
        dram["dbg_g"] = nc.dram_tensor("dbg_g", [128, 1024], BF16, kind="ExternalOutput").ap()
        dram["dbg_V"] = nc.dram_tensor("dbg_V", [128, 2048], BF16, kind="ExternalOutput").ap()
        dram["dbg_x64"] = nc.dram_tensor("dbg_x64", [64, 17 * 1024], BF16, kind="ExternalOutput").ap()
        dram["dbg_y64"] = nc.dram_tensor("dbg_y64", [64, 16 * 1024], BF16, kind="ExternalOutput").ap()
        dram["dbg_z"] = nc.dram_tensor("dbg_z", [128, 8192], BF16, kind="ExternalOutput").ap()
        dram["dbg_gel"] = nc.dram_tensor("dbg_gel", [128, 8192], BF16, kind="ExternalOutput").ap()
        dram["dbg_y64b"] = nc.dram_tensor("dbg_y64b", [64, 16384], BF16, kind="ExternalOutput").ap()
        dram["dbg_m64"] = nc.dram_tensor("dbg_m64", [64, 1], F32, kind="ExternalOutput").ap()
        dram["dbg_s64"] = nc.dram_tensor("dbg_s64", [64, 1], F32, kind="ExternalOutput").ap()
        dram["dbg_yb0"] = nc.dram_tensor("dbg_yb0", [64, 1024], BF16, kind="ExternalOutput").ap()
        dram["dbg_sq0"] = nc.dram_tensor("dbg_sq0", [64, 1024], BF16, kind="ExternalOutput").ap()
        dram["dbg_wot"] = nc.dram_tensor("dbg_wot", [128, 8192], BF16, kind="ExternalOutput").ap()
        dram["dbg_res0c"] = nc.dram_tensor("dbg_res0c", [128, 2176], F32, kind="ExternalOutput").ap()
        dram["dbg_vwt"] = nc.dram_tensor("dbg_vwt", [128, 512], F32, kind="ExternalOutput").ap()
        dram["dbg_pb"] = nc.dram_tensor("dbg_pb", [128, 512], F32, kind="ExternalOutput").ap()


def build_l1ssm_launch():
    nc = bass.Bass("TRN2", target_bir_lowering=False, debug=False)
    dram = {}
    dram["resid_in"] = nc.dram_tensor("resid_in", [DC, 128, NT], F32, kind="ExternalInput").ap()
    dram["resid_out"] = nc.dram_tensor("resid_out", [DC, 128, NT], F32, kind="ExternalOutput").ap()
    common_dram(nc, dram)
    ssm_dram(nc, dram)
    with ExitStack() as ctx:
        E = ctx.enter_context
        bld = Bld(nc, ctx)
        alloc_consts(bld, E)
        bld.msk8 = E(nc.sbuf_tensor("msk8", [128, 8], F32))
        resid = E(nc.sbuf_tensor("resid", [128, DC * NT], F32))
        dram["blockmask8"] = nc.dram_tensor("blockmask8", [128, 8], F32, kind="ExternalInput").ap()
        emit_load_consts(bld, dram)

        def ld(sy):
            for c in range(DC):
                bld.inc("sq", sy.dma_start(out=resid[:, c * NT:(c + 1) * NT], in_=dram["resid_in"][c]), 16)
        bld.phase(sync=ld)

        def g1(g):
            bld.inc("gq", g.dma_start(out=bld.msk8[:, :], in_=dram["blockmask8"][:, :]), 16)
        bld.phase(gpsimd=g1)

        def bar(t):
            bld.w(t, "sq")
            bld.w(t, "gq")
        bld.phase(tensor=bar, order=("tensor",))

        emit_l1ssm(bld, resid, dram)

        def st(sy):
            bld.w(sy, "vv")
            bld.w(sy, "mm")
            bld.w(sy, "ss")
            for c in range(DC):
                bld.inc("sq", sy.dma_start(out=dram["resid_out"][c], in_=resid[:, c * NT:(c + 1) * NT]), 16)
            bld.w(sy, "sq")
        bld.phase(sync=st)
    return nc
